# version 03
# baseline (speedup 1.0000x reference)
import contextlib
import numpy as np
import ml_dtypes
import concourse.bass as bass
import concourse.mybir as mybir
from concourse.bass_utils import run_bass_kernel_spmd

F32 = mybir.dt.float32
BF16 = mybir.dt.bfloat16
I32 = mybir.dt.int32
AF = mybir.ActivationFunctionType
ALU = mybir.AluOpType
AX = mybir.AxisListType

T = 2048
D = 1024
NC_ = 8
bf16 = ml_dtypes.bfloat16


class KB:

    def __init__(self, nc, stack):
        self.nc = nc
        self.stack = stack
        self._sems = {}
        self.cnt = {}

    def sem(self, name):
        if name not in self._sems:
            self._sems[name] = self.stack.enter_context(self.nc.semaphore(name))
            self.cnt[name] = 0
        return self._sems[name]

    def _waits(self, eng, waits):
        for (s, c) in waits:
            if c > 0:
                eng.wait_ge(self.sem(s), c)

    def _chain(self, engname, semname, fn, *a, waits=(), **k):
        eng = getattr(self.nc, engname)
        s = self.sem(semname)
        if self.cnt[semname] > 0:
            eng.wait_ge(s, self.cnt[semname])
        self._waits(eng, waits)
        instr = getattr(eng, fn)(*a, **k)
        instr.then_inc(s, 1)
        self.cnt[semname] += 1
        return (semname, self.cnt[semname])

    def V(self, fn, *a, waits=(), **k):
        return self._chain('vector', 'Vc', fn, *a, waits=waits, **k)

    def S(self, fn, *a, waits=(), **k):
        return self._chain('scalar', 'Sc', fn, *a, waits=waits, **k)

    def G(self, fn, *a, waits=(), **k):
        return self._chain('gpsimd', 'Gc', fn, *a, waits=waits, **k)

    def T(self, fn, *a, waits=(), **k):
        eng = self.nc.tensor
        self._waits(eng, waits)
        return getattr(eng, fn)(*a, **k)

    def t_wait(self, waits):
        self._waits(self.nc.tensor, waits)

    def mark(self, instr, semname='PE'):
        instr.then_inc(self.sem(semname), 1)
        self.cnt[semname] += 1
        return (semname, self.cnt[semname])

    def dma(self, out, in_, sem, queue='sync', waits=(), **k):
        eng = getattr(self.nc, queue)
        self._waits(eng, waits)
        instr = eng.dma_start(out=out, in_=in_, **k)
        instr.then_inc(self.sem(sem), 16)
        self.cnt[sem] += 16
        return (sem, self.cnt[sem])

    def idma(self, out, out_offset, in_, in_offset, sem, waits=(), **k):
        eng = self.nc.gpsimd
        self._waits(eng, waits)
        instr = eng.indirect_dma_start(out, out_offset, in_, in_offset, **k)
        instr.then_inc(self.sem(sem), 16)
        self.cnt[sem] += 16
        return (sem, self.cnt[sem])

    def evt(self, semname):
        return (semname, self.cnt.get(semname, 0))

    def sb(self, name, shape, dtype, stack=None):
        return (stack or self.stack).enter_context(
            self.nc.sbuf_tensor(name, list(shape), dtype))

    def ps(self, name, shape, dtype=F32, stack=None):
        return (stack or self.stack).enter_context(
            self.nc.psum_tensor(name, list(shape), dtype))


def emit_ln_fm(kb, x, xln, consts, pfx, x_evt=(), eps=1e-5, var_scale=None,
               var_bias=None, affine=None):
    with contextlib.ExitStack() as st:
        sq = kb.sb(pfx + '_lnsq', [128, NC_, 512], F32, stack=st)
        stats = kb.sb(pfx + '_lnst', [1, 2, T], F32, stack=st)
        mr = kb.sb(pfx + '_lnmr', [1, T], F32, stack=st)
        rr = kb.sb(pfx + '_lnrr', [1, T], F32, stack=st)
        ps_s = kb.ps(pfx + '_lnps', [1, 512], stack=st)
        ps_q = kb.ps(pfx + '_lnpq', [1, 512], stack=st)
        ps_m = kb.ps(pfx + '_lnpm', [128, 512], stack=st)
        ps_r = kb.ps(pfx + '_lnpr', [128, 512], stack=st)
        eps_c = kb.sb(pfx + '_lneps', [1, 1], F32, stack=st)
        kb.V('memset', eps_c[:], float(eps))
        ones_col = consts['ones_col']
        ones_row = consts['ones_row']

        for tc in range(4):
            sl = slice(tc * 512, (tc + 1) * 512)
            for c in range(NC_):
                sev = kb.V('tensor_tensor', out=sq[:, c, :], in0=x[:, c, sl],
                           in1=x[:, c, sl], op=ALU.mult,
                           waits=(list(x_evt) + [kb.evt('PE')]) if c == 0 else ())
            for c in range(NC_):
                kb.T('matmul', ps_s[:, :], ones_col[:], x[:, c, sl],
                     start=(c == 0), stop=(c == NC_ - 1),
                     waits=(list(x_evt) + [kb.evt('Vc')]) if (tc == 0 and c == 0) else ())
            for c in range(NC_):
                mi = kb.T('matmul', ps_q[:, :], ones_col[:], sq[:, c, :],
                          start=(c == 0), stop=(c == NC_ - 1),
                          waits=[sev] if c == 0 else ())
            pe = kb.mark(mi, 'PE')
            kb.V('tensor_copy', out=stats[:, 0, sl], in_=ps_s[:, :], waits=[pe])
            cev = kb.V('tensor_copy', out=stats[:, 1, sl], in_=ps_q[:, :])
            kb.t_wait([cev])

        kb.V('tensor_scalar', out=mr[:], in0=stats[:, 0, :],
             scalar1=1.0 / D, scalar2=None, op0=ALU.mult)
        kb.V('tensor_scalar', out=stats[:, 1, :], in0=stats[:, 1, :],
             scalar1=1.0 / D, scalar2=None, op0=ALU.mult)
        kb.V('tensor_tensor', out=stats[:, 0, :], in0=mr[:], in1=mr[:],
             op=ALU.mult)
        kb.V('tensor_tensor', out=stats[:, 1, :], in0=stats[:, 1, :],
             in1=stats[:, 0, :], op=ALU.subtract)
        if var_scale is not None:
            kb.V('tensor_scalar', out=stats[:, 1, :], in0=stats[:, 1, :],
                 scalar1=float(var_scale), scalar2=float(var_bias),
                 op0=ALU.mult, op1=ALU.add)
            kb.V('memset', eps_c[:], 0.0)
        sev = kb.S('activation', out=stats[:, 1, :], in_=stats[:, 1, :],
                   func=AF.Sqrt, bias=eps_c[:], scale=1.0,
                   waits=[kb.evt('Vc')])
        rev = kb.V('reciprocal', out=rr[:], in_=stats[:, 1, :], waits=[sev])

        for tc in range(4):
            sl = slice(tc * 512, (tc + 1) * 512)
            kb.T('matmul', ps_m[:, :], ones_row[:], mr[:, sl], start=True,
                 stop=True, waits=[rev] if tc == 0 else ())
            mi = kb.T('matmul', ps_r[:, :], ones_row[:], rr[:, sl], start=True,
                      stop=True)
            pe = kb.mark(mi, 'PE')
            for c in range(NC_):
                kb.V('tensor_tensor', out=sq[:, c, :], in0=x[:, c, sl],
                     in1=ps_m[:, :], op=ALU.subtract,
                     waits=[pe] if c == 0 else ())
                if affine is None:
                    oev = kb.V('tensor_tensor', out=xln[:, c, sl],
                               in0=sq[:, c, :], in1=ps_r[:, :], op=ALU.mult)
                else:
                    kb.V('tensor_tensor', out=sq[:, c, :], in0=sq[:, c, :],
                         in1=ps_r[:, :], op=ALU.mult)
                    oev = kb.V('tensor_scalar', out=xln[:, c, sl],
                               in0=sq[:, c, :],
                               scalar1=affine[0][:, c:c + 1],
                               scalar2=affine[1][:, c:c + 1],
                               op0=ALU.mult, op1=ALU.add)
            kb.t_wait([oev])
        return oev


def emit_l0f(kb, d):
    x, xln = d['x'], d['xln']
    wev, lev = d['wevt'], d['lnevt']
    with contextlib.ExitStack() as st:
        gval = kb.sb('l0f_g', [128, 16, 8], BF16, stack=st)
        gtmp = kb.sb('l0f_gt', [128, 8], F32, stack=st)
        m8 = kb.sb('l0f_m8', [128, 8], F32, stack=st)
        nm1 = kb.sb('l0f_nm1', [128, 1], F32, stack=st)
        ssum = kb.sb('l0f_ss', [128, 1], F32, stack=st)
        srec = kb.sb('l0f_sr', [128, 1], F32, stack=st)
        mask = kb.sb('l0f_msk', [128, 8], F32, stack=st)
        gT = kb.sb('l0f_gTs', [8, 16, 128], BF16, stack=st)
        tall = kb.sb('l0f_tall', [32, T], BF16, stack=st)
        u = kb.sb('l0f_u', [32, T], BF16, stack=st)
        glog = kb.ps('l0f_glog', [128, 128], stack=st)
        ps_gt = kb.ps('l0f_psgt', [8, 128], BF16, stack=st)
        ps_t = kb.ps('l0f_pst', [32, 512], stack=st)
        ps_gr = kb.ps('l0f_psgr', [32, 512], stack=st)
        ps_o0 = kb.ps('l0f_o0', [128, 512], stack=st)
        ps_o1 = kb.ps('l0f_o1', [128, 512], stack=st)

        for tt in range(16):
            lg = glog[:, (tt % 2) * 8:(tt % 2) * 8 + 8]
            for dc in range(NC_):
                mi = kb.T('matmul', lg, xln[:, dc, tt * 128:(tt + 1) * 128],
                          d['gw'][:, dc, :], start=(dc == 0),
                          stop=(dc == NC_ - 1),
                          waits=(list(wev) + [lev]) if (tt == 0 and dc == 0)
                          else ([kb.evt('Vc')] if dc == 0 else ()))
            pe = kb.mark(mi, 'PE')
            kb.V('max', out=m8[:], in_=lg, waits=[pe])
            kb.V('tensor_scalar', out=nm1[:], in0=m8[:, 0:1], scalar1=-1.0,
                 scalar2=None, op0=ALU.mult)
            ev = kb.S('activation', out=gtmp[:], in_=lg, func=AF.Exp,
                      bias=nm1[:], scale=1.0, waits=[kb.evt('Vc')])
            kb.V('tensor_reduce', out=ssum[:], in_=gtmp[:], axis=AX.X,
                 op=ALU.add, waits=[ev])
            kb.V('reciprocal', out=srec[:], in_=ssum[:])
            kb.V('tensor_scalar', out=mask[:], in0=lg, scalar1=m8[:, 1:2],
                 scalar2=None, op0=ALU.is_ge)
            gev = kb.V('scalar_tensor_tensor', out=gval[:, tt, :], in0=gtmp[:],
                       scalar=srec[:], in1=mask[:], op0=ALU.mult, op1=ALU.mult)
            tr = kb.T('transpose', ps_gt[:, :], gval[:, tt, :], d['idbf'][:],
                      waits=[gev])
            pe = kb.mark(tr, 'PE')
            cev = kb.V('tensor_copy', out=gT[:, tt, :], in_=ps_gt[:, :],
                       waits=[pe])
            kb.t_wait([cev])

        for tc in range(4):
            sl = slice(tc * 512, (tc + 1) * 512)
            for dc in range(NC_):
                mi = kb.T('matmul', ps_t[:, :], d['leftc'][:, dc, :],
                          xln[:, dc, sl], start=(dc == 0),
                          stop=(dc == NC_ - 1))
            pe1 = kb.mark(mi, 'PE')
            mi = kb.T('matmul', ps_gr[:, :], d['e8'][:],
                      gT[:, tc * 4:(tc + 1) * 4, :].rearrange(
                          'p a b -> p (a b)'), start=True, stop=True)
            pe2 = kb.mark(mi, 'PE')
            kb.V('tensor_copy', out=tall[:, sl], in_=ps_t[:, :], waits=[pe1])
            uev = kb.V('tensor_tensor', out=u[:, sl], in0=tall[:, sl],
                       in1=ps_gr[:, :], op=ALU.mult, waits=[pe2])
            kb.t_wait([uev])

        for fc in range(NC_):
            for tc in range(4):
                sl = slice(tc * 512, (tc + 1) * 512)
                pso = ps_o0 if (tc % 2 == 0) else ps_o1
                for dc in range(NC_):
                    kb.T('matmul', pso[:, :], d['wlin'][:, dc, fc, :],
                         xln[:, dc, sl], start=(dc == 0), stop=False,
                         waits=[uev] if (fc == 0 and tc == 0 and dc == 0)
                         else ())
                mi = kb.T('matmul', pso[:, :], d['rightc'][:, fc, :], u[:, sl],
                          start=False, stop=True)
                pe = kb.mark(mi, 'PE')
                rev = kb.V('scalar_tensor_tensor', out=x[:, fc, sl],
                           in0=pso[:, :], scalar=d['blin'][:, fc:fc + 1],
                           in1=x[:, fc, sl], op0=ALU.add, op1=ALU.add,
                           waits=[pe])
                kb.t_wait([rev])
        return rev


SEG = 384
NSLOT = 8 * SEG


def emit_router_scatter(kb, d, pfx, st):
    pst = st.enter_context(contextlib.ExitStack())
    lg_ps = kb.ps(pfx + '_lg', [128, 128], stack=pst)
    pos_ps = kb.ps(pfx + '_pos', [128, 128], stack=pst)
    cnt1_ps = kb.ps(pfx + '_c1', [1, 8], stack=pst)
    cntr_ps = kb.ps(pfx + '_cr', [128, 8], stack=pst)
    m8 = kb.sb(pfx + '_m8', [128, 8], F32, stack=st)
    emask = kb.sb(pfx + '_oh', [128, 16, 8], BF16, stack=st)
    ohf = kb.sb(pfx + '_ohf', [128, 8], F32, stack=st)
    gate = kb.sb(pfx + '_gate', [128, 16], F32, stack=st)
    idxf = kb.sb(pfx + '_idxf', [128, 16], F32, stack=st)
    posd = kb.sb(pfx + '_posd', [128, 16], F32, stack=st)
    slotf = kb.sb(pfx + '_slotf', [128, 16], F32, stack=st)
    slot_i = kb.sb(pfx + '_sloti', [128, 16], I32, stack=st)
    tmp8 = kb.sb(pfx + '_tmp8', [128, 8], F32, stack=st)
    etmp = kb.sb(pfx + '_etmp', [128, 8], F32, stack=st)
    ovf = kb.sb(pfx + '_ovf', [128, 1], F32, stack=st)
    cnt_row = kb.sb(pfx + '_cnt', [1, 8], F32, stack=st)
    offrep = kb.sb(pfx + '_offrep', [128, 8], F32, stack=st)
    ssum = kb.sb(pfx + '_ssum', [128, 1], F32, stack=st)
    nm1 = kb.sb(pfx + '_nm1', [128, 1], F32, stack=st)
    xtok = kb.sb(pfx + '_xtok', [128, 1024], BF16, stack=st)
    kb.V('memset', offrep[:], 0.0)

    for c in range(NC_):
        kb.dma(d['xlnfm_dram'][c * 128:(c + 1) * 128, :], d['xln'][:, c, :],
               'XW', queue='sync', waits=[d['lnevt']] if c == 0 else ())
    tile_ev = {}
    for tt in range(16):
        lg = lg_ps[:, (tt % 2) * 8:(tt % 2) * 8 + 8]
        have_rb = d.get('rb') is not None
        for dc in range(NC_):
            mi = kb.T('matmul', lg, d['xln'][:, dc, tt * 128:(tt + 1) * 128],
                      d['rw'][:, dc, :], start=(dc == 0),
                      stop=(dc == NC_ - 1 and not have_rb),
                      waits=(list(d['wevt']) + [d['lnevt']])
                      if (tt == 0 and dc == 0)
                      else ([kb.evt('Vc'), kb.evt('Sc')] if dc == 0 else ()))
        if have_rb:
            mi = kb.T('matmul', lg, d['consts']['ones_row'][:], d['rb'][:],
                      start=False, stop=True)
        pe = kb.mark(mi, 'PE')
        kb.V('max', out=m8[:], in_=lg, waits=[pe])
        kb.V('tensor_scalar', out=nm1[:], in0=m8[:, 0:1], scalar1=-1.0,
             scalar2=None, op0=ALU.mult)
        ev = kb.S('activation', out=tmp8[:], in_=lg, func=AF.Exp,
                  bias=nm1[:], scale=1.0, waits=[kb.evt('Vc')])
        kb.V('tensor_reduce', out=ssum[:], in_=tmp8[:], axis=AX.X,
             op=ALU.add, waits=[ev])
        kb.V('reciprocal', out=gate[:, tt:tt + 1], in_=ssum[:])
        kb.V('tensor_scalar', out=emask[:, tt, :], in0=lg, scalar1=m8[:, 0:1],
             scalar2=None, op0=ALU.is_ge)
        kb.V('tensor_copy', out=ohf[:], in_=emask[:, tt, :])
        kb.V('tensor_tensor', out=etmp[:], in0=ohf[:],
             in1=d['consts']['iota8_rep'][:], op=ALU.mult)
        kb.V('tensor_reduce', out=idxf[:, tt:tt + 1], in_=etmp[:], axis=AX.X,
             op=ALU.add)
        oev = kb.evt('Vc')
        mi = kb.T('matmul', pos_ps[:, (tt % 2) * 8:(tt % 2) * 8 + 8],
                  d['consts']['triu128'][:], emask[:, tt, :],
                  start=True, stop=True, waits=[oev])
        mi2 = kb.T('matmul', cnt1_ps[:, :], d['consts']['ones_col_bf'][:],
                   emask[:, tt, :], start=True, stop=True)
        pe = kb.mark(mi2, 'PE')
        pos = pos_ps[:, (tt % 2) * 8:(tt % 2) * 8 + 8]
        kb.V('tensor_tensor', out=etmp[:], in0=ohf[:], in1=pos, op=ALU.mult,
             waits=[pe])
        kb.V('tensor_reduce', out=posd[:, tt:tt + 1], in_=etmp[:], axis=AX.X,
             op=ALU.add)
        kb.V('tensor_tensor', out=etmp[:], in0=ohf[:], in1=offrep[:],
             op=ALU.mult)
        kb.V('tensor_reduce', out=tmp8[:, 0:1], in_=etmp[:], axis=AX.X,
             op=ALU.add)
        kb.V('scalar_tensor_tensor', out=posd[:, tt:tt + 1],
             in0=posd[:, tt:tt + 1], scalar=-1.0, in1=tmp8[:, 0:1],
             op0=ALU.add, op1=ALU.add)
        kb.V('scalar_tensor_tensor', out=slotf[:, tt:tt + 1],
             in0=idxf[:, tt:tt + 1], scalar=float(SEG),
             in1=posd[:, tt:tt + 1], op0=ALU.mult, op1=ALU.add)
        kb.V('tensor_scalar', out=ovf[:], in0=posd[:, tt:tt + 1],
             scalar1=float(SEG), scalar2=100000.0, op0=ALU.is_ge, op1=ALU.mult)
        kb.V('tensor_tensor', out=slotf[:, tt:tt + 1],
             in0=slotf[:, tt:tt + 1], in1=ovf[:], op=ALU.add)
        cev = kb.V('tensor_copy', out=slot_i[:, tt:tt + 1],
                   in_=slotf[:, tt:tt + 1])
        kb.V('tensor_copy', out=cnt_row[:], in_=cnt1_ps[:, :])
        crev = kb.evt('Vc')
        mi = kb.T('matmul', cntr_ps[:, :], d['consts']['ones_row'][:],
                  cnt_row[:], start=True, stop=True, waits=[crev])
        pe2 = kb.mark(mi, 'PE')
        uev = kb.V('tensor_tensor', out=offrep[:], in0=offrep[:],
                   in1=cntr_ps[:, :], op=ALU.add, waits=[pe2])
        kb.t_wait([uev])
        tile_ev[tt] = cev
    sev = None
    for tt in range(16):
        tl = kb.dma(xtok[:], d['xlnfm_dram'][:, tt * 128:(tt + 1) * 128],
                    'TL', queue='sync', transpose=True,
                    waits=[kb.evt('SC')] if tt > 0 else [kb.evt('XW')])
        sev = kb.idma(d['xsort'], bass.IndirectOffsetOnAxis(
            ap=slot_i[:, tt:tt + 1], axis=0), xtok[:], None, 'SC',
            waits=[tl, tile_ev[tt], kb.evt('ZF')],
            bounds_check=NSLOT - 1, oob_is_err=False)
    d['slot_i'] = slot_i
    d['gate'] = gate
    d['emask_bf'] = emask
    d['idxf'] = idxf
    d['posd'] = posd
    kb._l1f_offrep = offrep
    kb.t_wait([kb.evt('Vc'), kb.evt('Sc')])
    pst.close()
    return sev


def emit_zero_fill(kb, d, pfx, st, bufs):
    zb = kb.sb(pfx + '_zb', [128, 1024], BF16, stack=st)
    zev = kb.V('memset', zb[:], 0.0)
    for buf, rows in bufs:
        for i in range(rows // 128):
            kb.dma(buf[i * 128:(i + 1) * 128, :], zb[:], 'ZF',
                   queue='scalar', waits=[zev] if i == 0 else ())


def emit_l3f(kb, d):
    with contextlib.ExitStack() as st:
        xln = d['xln']
        wev, lev = d['wevt'], d['lnevt']
        lg_ps = kb.ps('l3f_lg', [128, 128], stack=st)
        m8 = kb.sb('l3f_m8', [128, 8], F32, stack=st)
        ohf = kb.sb('l3f_ohf', [128, 16, 8], F32, stack=st)
        gate = kb.sb('l3f_gate', [128, 16], F32, stack=st)
        tmp8 = kb.sb('l3f_tmp8', [128, 8], F32, stack=st)
        ssum = kb.sb('l3f_ssum', [128, 1], F32, stack=st)
        nm1 = kb.sb('l3f_nm1', [128, 1], F32, stack=st)
        acc = kb.sb('l3f_acc', [128, 8, 1024], F32, stack=st)
        ysb = kb.sb('l3f_ysb', [128, 1024], BF16, stack=st)
        ew_sb = [kb.sb('l3f_ew0', [128, NC_, NC_, 128], BF16, stack=st),
                 kb.sb('l3f_ew1', [128, NC_, NC_, 128], BF16, stack=st)]
        ps_y = [kb.ps('l3f_py0', [128, 512], stack=st),
                kb.ps('l3f_py1', [128, 512], stack=st)]
        for tt in range(16):
            lg = lg_ps[:, (tt % 2) * 8:(tt % 2) * 8 + 8]
            for dc in range(NC_):
                kb.T('matmul', lg, xln[:, dc, tt * 128:(tt + 1) * 128],
                     d['rw'][:, dc, :], start=(dc == 0), stop=False,
                     waits=(list(wev) + [lev]) if (tt == 0 and dc == 0)
                     else ([kb.evt('Vc'), kb.evt('Sc')] if dc == 0 else ()))
            mi = kb.T('matmul', lg, d['consts']['ones_row'][:], d['rb'][:],
                      start=False, stop=True)
            pe = kb.mark(mi, 'PE')
            kb.V('max', out=m8[:], in_=lg, waits=[pe])
            kb.V('tensor_scalar', out=nm1[:], in0=m8[:, 0:1], scalar1=-1.0,
                 scalar2=None, op0=ALU.mult)
            ev = kb.S('activation', out=tmp8[:], in_=lg, func=AF.Exp,
                      bias=nm1[:], scale=1.0, waits=[kb.evt('Vc')])
            kb.V('tensor_reduce', out=ssum[:], in_=tmp8[:], axis=AX.X,
                 op=ALU.add, waits=[ev])
            kb.V('reciprocal', out=gate[:, tt:tt + 1], in_=ssum[:])
            kb.V('tensor_scalar', out=ohf[:, tt, :], in0=lg,
                 scalar1=m8[:, 0:1], scalar2=None, op0=ALU.is_ge)
            ge = kb.V('tensor_scalar', out=ohf[:, tt, :], in0=ohf[:, tt, :],
                      scalar1=gate[:, tt:tt + 1], scalar2=None, op0=ALU.mult)
            kb.t_wait([ge])
        for half in range(2):
            for e in range(8):
                b = e % 2
                wl = kb.dma(ew_sb[b][:], d['ew_dram'][e], 'EW%d' % b,
                            queue='gpsimd',
                            waits=[kb.evt('PE'), kb.evt('Vc')])
                for ti in range(8):
                    tt = half * 8 + ti
                    tsl = slice(tt * 128, (tt + 1) * 128)
                    for h2 in range(2):
                        py = ps_y[h2]
                        for hc in range(NC_):
                            kb.T('matmul', py[:, :], xln[:, hc, tsl],
                                 ew_sb[b][:, hc, 4 * h2:4 * h2 + 4, :]
                                 .rearrange('p a b -> p (a b)'),
                                 start=(hc == 0), stop=False,
                                 waits=[wl, kb.evt('Vc')]
                                 if (ti == 0 and h2 == 0 and hc == 0)
                                 else ([kb.evt('Vc')] if hc == 0 else ()))
                        mi = kb.T('matmul', py[:, :],
                                  d['consts']['ones_row'][:],
                                  d['ebrow'][:, e,
                                             512 * h2:512 * h2 + 512],
                                  start=False, stop=True)
                        pe = kb.mark(mi, 'PE')
                    for h2 in range(2):
                        csl = slice(h2 * 512, (h2 + 1) * 512)
                        if e == 0:
                            kb.V('tensor_scalar', out=acc[:, ti, csl],
                                 in0=ps_y[h2][:, :],
                                 scalar1=ohf[:, tt, 0:1], scalar2=None,
                                 op0=ALU.mult, waits=[pe])
                        else:
                            kb.V('scalar_tensor_tensor', out=acc[:, ti, csl],
                                 in0=ps_y[h2][:, :],
                                 scalar=ohf[:, tt, e:e + 1],
                                 in1=acc[:, ti, csl], op0=ALU.mult,
                                 op1=ALU.add, waits=[pe])
                    if e == 7:
                        ce = kb.V('tensor_copy', out=ysb[:],
                                  in_=acc[:, ti, :], waits=[kb.evt('FW')])
                        kb.dma(d['yfin_tok'][tsl, :], ysb[:], 'FW',
                               queue='scalar', waits=[ce])
                    kb.t_wait([kb.evt('Vc')])
        ffm = kb.sb('l3f_ffm', [128, T], BF16, stack=st)
        rev = None
        for c in range(NC_):
            tl = kb.dma(ffm[:], d['yfin_tok'][:, c * 128:(c + 1) * 128],
                        'FT', queue='sync', transpose=True,
                        waits=[kb.evt('FW')] if c == 0 else [kb.evt('Vc')])
            rev = kb.V('tensor_tensor', out=d['x'][:, c, :],
                       in0=d['x'][:, c, :], in1=ffm[:], op=ALU.add,
                       waits=[tl])
        return rev


def emit_l1f(kb, d):
    C = 2048
    with contextlib.ExitStack() as st:
        xln = d['xln']
        wev, lev = d['wevt'], d['lnevt']
        ones_row = d['consts']['ones_row']

        pst = st.enter_context(contextlib.ExitStack())
        lg_ps = kb.ps('l1f_lg', [128, 128], stack=pst)
        pos_ps = kb.ps('l1f_pos', [128, 128], stack=pst)
        cnt1_ps = kb.ps('l1f_c1', [1, 8], stack=pst)
        cntr_ps = kb.ps('l1f_cr', [128, 8], stack=pst)
        m8 = kb.sb('l1f_m8', [128, 8], F32, stack=st)
        emask = kb.sb('l1f_oh', [128, 16, 8], BF16, stack=st)
        ohf = kb.sb('l1f_ohf', [128, 8], F32, stack=st)
        gate = kb.sb('l1f_gate', [128, 16], F32, stack=st)
        idxf = kb.sb('l1f_idxf', [128, 16], F32, stack=st)
        posd = kb.sb('l1f_posd', [128, 16], F32, stack=st)
        slotf = kb.sb('l1f_slotf', [128, 16], F32, stack=st)
        slot_i = kb.sb('l1f_sloti', [128, 16], I32, stack=st)
        tmp8 = kb.sb('l1f_tmp8', [128, 8], F32, stack=st)
        etmp = kb.sb('l1f_etmp', [128, 8], F32, stack=st)
        cnt_row = kb.sb('l1f_cnt', [1, 8], F32, stack=st)
        offrep = kb.sb('l1f_offrep', [128, 8], F32, stack=st)
        ssum = kb.sb('l1f_ssum', [128, 1], F32, stack=st)
        nm1 = kb.sb('l1f_nm1', [128, 1], F32, stack=st)
        gsel = kb.sb('l1f_gsel', [128, 1], F32, stack=st)
        ag = kb.sb('l1f_ag', [8, 8], F32, stack=st)
        goff_row = kb.sb('l1f_goffr', [1, 8], F32, stack=st)
        kb.V('memset', offrep[:], 0.0)
        for c in range(NC_):
            kb.dma(d['xlnfm_dram'][c * 128:(c + 1) * 128, :], xln[:, c, :],
                   'XW', queue='sync', waits=[lev] if c == 0 else ())
        for tt in range(16):
            lg = lg_ps[:, (tt % 2) * 8:(tt % 2) * 8 + 8]
            for dc in range(NC_):
                mi = kb.T('matmul', lg, xln[:, dc, tt * 128:(tt + 1) * 128],
                          d['rw'][:, dc, :], start=(dc == 0),
                          stop=(dc == NC_ - 1),
                          waits=(list(wev) + [lev]) if (tt == 0 and dc == 0)
                          else ([kb.evt('Vc'), kb.evt('Sc')]
                                if dc == 0 else ()))
            pe = kb.mark(mi, 'PE')
            kb.V('max', out=m8[:], in_=lg, waits=[pe])
            kb.V('tensor_scalar', out=nm1[:], in0=m8[:, 0:1], scalar1=-1.0,
                 scalar2=None, op0=ALU.mult)
            ev = kb.S('activation', out=tmp8[:], in_=lg, func=AF.Exp,
                      bias=nm1[:], scale=1.0, waits=[kb.evt('Vc')])
            kb.V('tensor_reduce', out=ssum[:], in_=tmp8[:], axis=AX.X,
                 op=ALU.add, waits=[ev])
            kb.V('reciprocal', out=gate[:, tt:tt + 1], in_=ssum[:])
            kb.V('tensor_scalar', out=emask[:, tt, :], in0=lg,
                 scalar1=m8[:, 0:1], scalar2=None, op0=ALU.is_ge)
            kb.V('tensor_copy', out=ohf[:], in_=emask[:, tt, :])
            kb.V('tensor_tensor', out=etmp[:], in0=ohf[:],
                 in1=d['consts']['iota8_rep'][:], op=ALU.mult)
            kb.V('tensor_reduce', out=idxf[:, tt:tt + 1], in_=etmp[:],
                 axis=AX.X, op=ALU.add)
            oev = kb.evt('Vc')
            mi = kb.T('matmul', pos_ps[:, (tt % 2) * 8:(tt % 2) * 8 + 8],
                      d['consts']['triu128'][:], emask[:, tt, :],
                      start=True, stop=True, waits=[oev])
            mi2 = kb.T('matmul', cnt1_ps[:, :], d['consts']['ones_col_bf'][:],
                       emask[:, tt, :], start=True, stop=True)
            pe = kb.mark(mi2, 'PE')
            pos = pos_ps[:, (tt % 2) * 8:(tt % 2) * 8 + 8]
            kb.V('tensor_tensor', out=etmp[:], in0=ohf[:], in1=pos,
                 op=ALU.mult, waits=[pe])
            kb.V('tensor_reduce', out=posd[:, tt:tt + 1], in_=etmp[:],
                 axis=AX.X, op=ALU.add)
            kb.V('tensor_tensor', out=etmp[:], in0=ohf[:], in1=offrep[:],
                 op=ALU.mult)
            kb.V('tensor_reduce', out=tmp8[:, 0:1], in_=etmp[:], axis=AX.X,
                 op=ALU.add)
            kb.V('scalar_tensor_tensor', out=posd[:, tt:tt + 1],
                 in0=posd[:, tt:tt + 1], scalar=-1.0, in1=tmp8[:, 0:1],
                 op0=ALU.add, op1=ALU.add)
            kb.V('tensor_copy', out=cnt_row[:], in_=cnt1_ps[:, :])
            crev = kb.evt('Vc')
            mi = kb.T('matmul', cntr_ps[:, :], ones_row[:], cnt_row[:],
                      start=True, stop=True, waits=[crev])
            pe2 = kb.mark(mi, 'PE')
            uev = kb.V('tensor_tensor', out=offrep[:], in0=offrep[:],
                       in1=cntr_ps[:, :], op=ALU.add, waits=[pe2])
            kb.t_wait([uev])
        cw = kb.dma(d['cnt_dram'], offrep[0:1, :], 'CW', queue='scalar',
                    waits=[kb.evt('Vc')])
        if d.get('fake_cc'):
            zb8 = kb.sb('l1f_zag', [8, 8], F32, stack=st)
            zze = kb.V('memset', zb8[:], 0.0)
            cc = kb.dma(d['ag_dram'], zb8[:], 'CC', queue='scalar',
                        waits=[zze, cw])
        else:
            eng = kb.nc.gpsimd
            eng.wait_ge(kb.sem('CW'), kb.cnt['CW'])
            ci = eng.collective_compute(
                'AllGather', ALU.bypass, [list(range(8))],
                [d['cnt_dram'].opt()], [d['ag_dram'].opt()])
            ci.then_inc(kb.sem('CC'), 1)
            kb.cnt['CC'] += 1
            cc = kb.evt('CC')
        agl = kb.dma(ag[:], d['ag_dram'], 'AGL', queue='sync', waits=[cc])
        goff_ps = cnt1_ps
        mi = kb.T('matmul', goff_ps[:, :], d['coremask'][:], ag[:],
                  start=True, stop=True, waits=[agl, kb.evt('Vc')])
        pe = kb.mark(mi, 'PE')
        gev = kb.V('tensor_copy', out=goff_row[:], in_=goff_ps[:, :],
                   waits=[pe])
        mi = kb.T('matmul', cntr_ps[:, :], ones_row[:], goff_row[:],
                  start=True, stop=True, waits=[gev])
        pe = kb.mark(mi, 'PE')
        for tt in range(16):
            kb.V('tensor_tensor', out=etmp[:], in0=emask[:, tt, :],
                 in1=cntr_ps[:, :], op=ALU.mult, waits=[pe] if tt == 0 else ())
            kb.V('tensor_reduce', out=gsel[:], in_=etmp[:], axis=AX.X,
                 op=ALU.add)
            kb.V('tensor_tensor', out=posd[:, tt:tt + 1],
                 in0=posd[:, tt:tt + 1], in1=gsel[:], op=ALU.add)
            kb.V('tensor_scalar', out=gsel[:], in0=posd[:, tt:tt + 1],
                 scalar1=float(C), scalar2=None, op0=ALU.is_lt)
            kb.V('tensor_tensor', out=gate[:, tt:tt + 1],
                 in0=gate[:, tt:tt + 1], in1=gsel[:], op=ALU.mult)
            kb.V('scalar_tensor_tensor', out=slotf[:, tt:tt + 1],
                 in0=idxf[:, tt:tt + 1], scalar=float(C),
                 in1=posd[:, tt:tt + 1], op0=ALU.mult, op1=ALU.add)
            kb.V('tensor_scalar', out=gsel[:], in0=gsel[:],
                 scalar1=-1000000.0, scalar2=1000000.0, op0=ALU.mult,
                 op1=ALU.add)
            kb.V('tensor_tensor', out=slotf[:, tt:tt + 1],
                 in0=slotf[:, tt:tt + 1], in1=gsel[:], op=ALU.add)
            kev = kb.V('tensor_copy', out=slot_i[:, tt:tt + 1],
                       in_=slotf[:, tt:tt + 1])
        kb.t_wait([kev])
        pst.close()

        bcreg = kb.nc.gpsimd.to_reg(8 * C - 1)
        zb = kb.sb('l1f_zb', [128, 8192], BF16, stack=st)
        zev = kb.V('memset', zb[:], 0.0)
        for buf in (d['xc_dram'], d['yc_dram']):
            for i in range(16):
                kb.dma(buf.rearrange('(a p) d -> p a d', p=128)[
                    :, i * 8:(i + 1) * 8, :],
                    zb[:].rearrange('p (a d) -> p a d', d=1024), 'ZF',
                    queue='scalar', waits=[zev] if i == 0 else ())
        xtok = kb.sb('l1f_xtok', [128, 1024], BF16, stack=st)
        sev = None
        for tt in range(16):
            tl = kb.dma(xtok[:], d['xlnfm_dram'][:, tt * 128:(tt + 1) * 128],
                        'TL', queue='sync', transpose=True,
                        waits=[kb.evt('SC')] if tt > 0 else
                        [kb.evt('XW'), kb.evt('ZF')])
            sev = kb.idma(d['xc_dram'], bass.IndirectOffsetOnAxis(
                ap=slot_i[:, tt:tt + 1], axis=0), xtok[:], None, 'SC',
                waits=[tl, kev], bounds_check=bcreg, oob_is_err=False)
        if d.get('fake_cc'):
            ar1 = kb.dma(d['xf_dram'], d['xc_dram'], 'AR', queue='scalar',
                         waits=[sev, kb.evt('ZF')])
        else:
            eng = kb.nc.gpsimd
            eng.wait_ge(kb.sem('SC'), kb.cnt['SC'])
            eng.wait_ge(kb.sem('ZF'), kb.cnt['ZF'])
            ci = eng.collective_compute(
                'AllReduce', ALU.add, [list(range(8))],
                [d['xc_dram'].opt()], [d['xf_dram'].opt()])
            ci.then_inc(kb.sem('AR'), 1)
            kb.cnt['AR'] += 1
            ar1 = kb.evt('AR')
        ridx = d['rowidx']
        rie = kb.evt('WL')
        for i in range(16):
            ge = kb.idma(xtok[:], None, d['xf_dram'],
                         bass.IndirectOffsetOnAxis(ap=ridx[:, i:i + 1],
                                                   axis=0), 'GA',
                         waits=[ar1, rie, kb.evt('SG')] if i == 0 else
                         [kb.evt('SG')])
            kb.dma(d['xseg_dram'][i * 128:(i + 1) * 128, :], xtok[:], 'SG',
                   queue='scalar', waits=[ge])
        FH = 16
        xseg = kb.sb('l1f_xs', [128, NC_, 512], BF16, stack=st)
        w1h = kb.sb('l1f_w1h', [128, NC_, FH, 128], BF16, stack=st)
        w2h = kb.sb('l1f_w2h', [128, FH, NC_, 128], BF16, stack=st)
        hsb = kb.sb('l1f_h', [128, FH, 512], BF16, stack=st)
        ysum = kb.sb('l1f_y', [128, NC_, 512], F32, stack=st)
        yfm = kb.sb('l1f_yfm', [128, 512], BF16, stack=st)
        ps_h = [kb.ps('l1f_ph0', [128, 512], stack=st),
                kb.ps('l1f_ph1', [128, 512], stack=st)]
        ps_y = [kb.ps('l1f_py0', [128, 512], stack=st),
                kb.ps('l1f_py1', [128, 512], stack=st)]
        for sc in range(4):
            ssl = slice(sc * 512, (sc + 1) * 512)
            for dc in range(NC_):
                xl = kb.dma(xseg[:, dc, :],
                            d['xseg_dram'][ssl, dc * 128:(dc + 1) * 128],
                            'XT', queue='sync', transpose=True,
                            waits=([kb.evt('SG'), kb.evt('PE')]
                                   if dc == 0 else []))
            for hf in range(2):
                wl1 = kb.dma(w1h[:], d['w1c_dram'][:, :,
                                                   hf * FH:(hf + 1) * FH, :],
                             'EW1', queue='gpsimd', waits=[kb.evt('PE')])
                wl2 = kb.dma(w2h[:], d['w2c_dram'][:,
                                                   hf * FH:(hf + 1) * FH,
                                                   :, :],
                             'EW2', queue='gpsimd', waits=[kb.evt('PE')])
                for fc in range(FH):
                    ph = ps_h[fc % 2]
                    for dc in range(NC_):
                        mi = kb.T('matmul', ph[:, :], w1h[:, dc, fc, :],
                                  xseg[:, dc, :], start=(dc == 0),
                                  stop=(dc == NC_ - 1),
                                  waits=[wl1, xl, kb.evt('Sc'),
                                         kb.evt('Vc')]
                                  if fc == 0 and dc == 0 else
                                  ([kb.evt('Sc')] if dc == 0 else ()))
                    pe1 = kb.mark(mi, 'PE')
                    kb.S('activation', out=hsb[:, fc, :], in_=ph[:, :],
                         func=AF.Relu,
                         bias=d['b1c'][:, hf * FH + fc:hf * FH + fc + 1],
                         scale=1.0, waits=[pe1])
                for dcol in range(NC_):
                    py = ps_y[dcol % 2]
                    for fc in range(FH):
                        mi = kb.T('matmul', py[:, :], w2h[:, fc, dcol, :],
                                  hsb[:, fc, :], start=(fc == 0),
                                  stop=(fc == FH - 1),
                                  waits=[wl2, kb.evt('Sc'), kb.evt('Vc')]
                                  if dcol == 0 and fc == 0 else
                                  ([kb.evt('Vc')] if fc == 0 else ()))
                    pe2 = kb.mark(mi, 'PE')
                    if hf == 0:
                        kb.V('tensor_copy', out=ysum[:, dcol, :],
                             in_=py[:, :], waits=[pe2])
                    else:
                        kb.V('tensor_tensor', out=ysum[:, dcol, :],
                             in0=ysum[:, dcol, :], in1=py[:, :], op=ALU.add,
                             waits=[pe2])
            for dcol in range(NC_):
                ev = kb.V('tensor_scalar', out=yfm[:], in0=ysum[:, dcol, :],
                          scalar1=d['b2c'][:, dcol:dcol + 1], scalar2=None,
                          op0=ALU.add, waits=[kb.evt('YW')])
                kb.dma(d['yfm_dram'][dcol * 128:(dcol + 1) * 128, ssl],
                       yfm[:], 'YW', queue='scalar', waits=[ev])
        ytok = kb.sb('l1f_ytok', [128, 1024], BF16, stack=st)
        for i in range(16):
            tl = kb.dma(ytok[:], d['yfm_dram'][:, i * 128:(i + 1) * 128],
                        'TL', queue='sync', transpose=True,
                        waits=[kb.evt('YW')] if i == 0 else [kb.evt('SC')])
            sev = kb.idma(d['yc_dram'], bass.IndirectOffsetOnAxis(
                ap=ridx[:, i:i + 1], axis=0), ytok[:], None, 'SC',
                waits=[tl], bounds_check=bcreg, oob_is_err=False)
        if d.get('fake_cc'):
            ar2 = kb.dma(d['yf_dram'], d['yc_dram'], 'AR', queue='scalar',
                         waits=[sev])
        else:
            eng = kb.nc.gpsimd
            eng.wait_ge(kb.sem('SC'), kb.cnt['SC'])
            ci = eng.collective_compute(
                'AllReduce', ALU.add, [list(range(8))],
                [d['yc_dram'].opt()], [d['yf_dram'].opt()])
            ci.then_inc(kb.sem('AR'), 1)
            kb.cnt['AR'] += 1
            ar2 = kb.evt('AR')
        fin = kb.sb('l1f_fin', [128, 1024], BF16, stack=st)
        for tt in range(16):
            mz = kb.V('memset', fin[:], 0.0,
                      waits=[kb.evt('FW')] if tt > 0 else ())
            ge = kb.idma(fin[:], None, d['yf_dram'],
                         bass.IndirectOffsetOnAxis(
                             ap=slot_i[:, tt:tt + 1], axis=0), 'GA',
                         waits=[ar2, mz], bounds_check=bcreg,
                         oob_is_err=False)
            sev2 = kb.V('tensor_scalar', out=fin[:], in0=fin[:],
                        scalar1=gate[:, tt:tt + 1], scalar2=None,
                        op0=ALU.mult, waits=[ge])
            kb.dma(d['yfin_tok'][tt * 128:(tt + 1) * 128, :], fin[:], 'FW',
                   queue='scalar', waits=[sev2])
        ffm = kb.sb('l1f_ffm', [128, 512], BF16, stack=st)
        xsr = kb.sb('l1f_xsr', [128, 512], F32, stack=st)
        ysb2 = kb.sb('l1f_ysb2', [128, 512], F32, stack=st)
        for c in range(NC_):
            for tc in range(4):
                sl = slice(tc * 512, (tc + 1) * 512)
                tl = kb.dma(ffm[:], d['yfin_tok'][sl,
                                                  c * 128:(c + 1) * 128],
                            'FT', queue='sync', transpose=True,
                            waits=[kb.evt('FW')] if (c == 0 and tc == 0)
                            else [kb.evt('Vc')])
                xre = kb.dma(xsr[:], d['x_in'].rearrange(
                    '(c p) t -> p c t', p=128)[:, c, sl], 'XR',
                    queue='sync', waits=[kb.evt('Vc')])
                ye = kb.V('tensor_tensor', out=ysb2[:], in0=xsr[:],
                          in1=ffm[:], op=ALU.add,
                          waits=[tl, xre, kb.evt('YO')])
                kb.dma(d['y_out'].rearrange('(c p) t -> p c t', p=128)[
                    :, c, sl], ysb2[:], 'YO', queue='scalar', waits=[ye])
        return kb.evt('YO')


def emit_l3a(kb, d):
    with contextlib.ExitStack() as st:
        qfm = kb.sb('l3a_q', [128, 4, T], BF16, stack=st)
        kfm = kb.sb('l3a_k', [128, 4, T], BF16, stack=st)
        ktok = kb.sb('l3a_kt', [128, 16, 512], BF16, stack=st)
        vtok = kb.sb('l3a_vt', [128, 16, 512], BF16, stack=st)
        zfm = kb.sb('l3a_z', [128, NC_, T], BF16, stack=st)
        psb = kb.sb('l3a_psb', [128, 4, 512], BF16, stack=st)
        msb = kb.sb('l3a_msb', [128, 64], BF16, stack=st)
        msum = kb.sb('l3a_msum', [128, 64], F32, stack=st)
        ps = [kb.ps('l3a_p0', [128, 512], stack=st),
              kb.ps('l3a_p1', [128, 512], stack=st),
              kb.ps('l3a_p2', [128, 512], stack=st),
              kb.ps('l3a_p3', [128, 512], stack=st)]
        psm = kb.ps('l3a_m', [64, 64], stack=st)
        pso = kb.ps('l3a_o', [64, 512], stack=st)
        xln = d['xln']
        wev, lev = d['wevt'], d['lnevt']

        prev = kb.evt('Sc')
        for hg in range(2):
            for dst, fbase, scale in ((qfm, 4 * hg, 0.125),
                                      (kfm, 8 + 4 * hg, 1.0)):
                for fc in range(4):
                    for tc in range(4):
                        pb = ps[tc % 2]
                        sl = slice(tc * 512, (tc + 1) * 512)
                        for dc in range(NC_):
                            mi = kb.T('matmul', pb[:, :],
                                      d['wq'][:, dc, fbase + fc, :],
                                      xln[:, dc, sl], start=(dc == 0),
                                      stop=(dc == NC_ - 1),
                                      waits=(list(wev) + [lev, kb.evt('Vc')])
                                      if (fc == 0 and tc == 0 and dc == 0)
                                      else ([kb.evt('Sc')] if dc == 0 else ()))
                        pe = kb.mark(mi, 'PE')
                        kb.S('activation', out=dst[:, fc, sl], in_=pb[:, :],
                             func=AF.Copy, bias=0.0, scale=float(scale),
                             waits=[pe])
            for tt in range(16):
                for gi, (dst, fb) in enumerate(((ktok, 8 + 4 * hg),
                                                (vtok, 16 + 4 * hg))):
                    pb = ps[2 + gi]
                    for dc in range(NC_):
                        mi = kb.T('matmul', pb[:, :],
                                  xln[:, dc, tt * 128:(tt + 1) * 128],
                                  d['wq'][:, dc, fb:fb + 4, :].rearrange(
                                      'p a b -> p (a b)'),
                                  start=(dc == 0), stop=(dc == NC_ - 1),
                                  waits=[kb.evt('Sc')] if dc == 0 else ())
                    pe = kb.mark(mi, 'PE')
                    kb.S('activation', out=dst[:, tt, :], in_=pb[:, :],
                         func=AF.Copy, bias=0.0, scale=1.0, waits=[pe])
            prev = kb.evt('Sc')

            for hh in range(8):
                h = 8 * hg + hh
                hc, hr = hh // 2, (hh % 2) * 64
                hsl = slice(hr, hr + 64)
                ghc = h // 2
                ghr = (h % 2) * 64
                gsl = slice(ghr, ghr + 64)
                vc0 = hh * 64
                for qb in range(4):
                    qsl = slice(qb * 512, (qb + 1) * 512)
                    if qb > 0:
                        for j in range(4 * (qb - 1), 4 * qb):
                            mi = kb.T('matmul', psm[:, :],
                                      ktok[:, j, vc0:vc0 + 64],
                                      vtok[:, j, vc0:vc0 + 64],
                                      start=(j == 4 * (qb - 1)),
                                      stop=(j == 4 * qb - 1),
                                      waits=[prev, kb.evt('Vc')]
                                      if j == 4 * (qb - 1) else ())
                        pe = kb.mark(mi, 'PE')
                        if qb == 1:
                            kb.V('tensor_copy', out=msum[hsl, :],
                                 in_=psm[:, :], waits=[pe])
                        else:
                            kb.V('tensor_tensor', out=msum[hsl, :],
                                 in0=msum[hsl, :], in1=psm[:, :],
                                 op=ALU.add, waits=[pe])
                        mev = kb.V('tensor_copy', out=msb[hsl, :],
                                   in_=msum[hsl, :])
                    for s in range(4):
                        j = 4 * qb + s
                        n0 = s * 128
                        mi = kb.T('matmul', ps[s % 2][:, 0:512 - n0],
                                  kfm[:, hc, j * 128:(j + 1) * 128][hsl, :],
                                  qfm[:, hc,
                                      qb * 512 + n0:(qb + 1) * 512][hsl, :],
                                  start=True, stop=True,
                                  waits=[kb.evt('Vc'), prev])
                        pe = kb.mark(mi, 'PE')
                        kb.V('tensor_copy', out=psb[:, s, n0:512],
                             in_=ps[s % 2][:, 0:512 - n0], waits=[pe])
                        tev = kb.V('tensor_tensor',
                                   out=psb[:, s, n0:n0 + 128],
                                   in0=psb[:, s, n0:n0 + 128],
                                   in1=d['trimask'][:], op=ALU.mult)
                    first = (qb == 0)
                    if qb > 0:
                        mi = kb.T('matmul', pso[:, :], msb[hsl, :],
                                  qfm[:, hc, qsl][hsl, :], start=True,
                                  stop=False, waits=[mev, kb.evt('Sc')])
                    for s in range(4):
                        j = 4 * qb + s
                        n0 = s * 128
                        mi = kb.T('matmul', pso[:, n0:512],
                                  vtok[:, j, vc0:vc0 + 64],
                                  psb[:, s, n0:512],
                                  start=(first and s == 0), stop=(s == 3),
                                  waits=[tev, kb.evt('Sc')] if s == 0 else ())
                    pe = kb.mark(mi, 'PE')
                    prev = kb.S('activation', out=zfm[:, ghc, qsl][gsl, :],
                                in_=pso[:, :], func=AF.Copy, bias=0.0,
                                scale=1.0, waits=[pe])
        xs = kb.sb('l3a_xs', [128, 2, 512], F32, stack=st)
        ysb = kb.sb('l3a_ysb', [128, 512], F32, stack=st)
        for fc in range(NC_):
            for tc in range(4):
                sl = slice(tc * 512, (tc + 1) * 512)
                pb = ps[tc % 2]
                xe = kb.dma(xs[:, tc % 2, :],
                            d['x_in'].rearrange('(c p) t -> p c t', p=128)[
                                :, fc, sl], 'XR', queue='sync',
                            waits=[kb.evt('Vc')])
                for ec in range(NC_):
                    mi = kb.T('matmul', pb[:, :], d['woutT'][:, ec, fc, :],
                              zfm[:, ec, sl], start=(ec == 0),
                              stop=(ec == NC_ - 1),
                              waits=[prev, kb.evt('Vc')] if ec == 0 else ())
                pe = kb.mark(mi, 'PE')
                yev = kb.V('scalar_tensor_tensor', out=ysb[:], in0=pb[:, :],
                           scalar=1.0, in1=xs[:, tc % 2, :], op0=ALU.mult,
                           op1=ALU.add, waits=[pe, xe, kb.evt('YO')])
                kb.dma(d['y_out'].rearrange('(c p) t -> p c t', p=128)[
                    :, fc, sl], ysb[:], 'YO', queue='scalar', waits=[yev])
        return kb.evt('YO')


def emit_l2a(kb, d):
    with contextlib.ExitStack() as st:
        qfm = kb.sb('l2a_q', [128, 4, T], BF16, stack=st)
        kfm = kb.sb('l2a_k', [128, 4, T], BF16, stack=st)
        vg = kb.sb('l2a_v', [128, 16, 520], BF16, stack=st)
        zfm = kb.sb('l2a_z', [128, NC_, T], BF16, stack=st)
        ast = st.enter_context(contextlib.ExitStack())
        pbuf = kb.sb('l2a_pb', [128, 2, 1024], BF16, stack=ast)
        sqs = kb.sb('l2a_sqs', [128, 512], BF16, stack=ast)
        nmax = kb.sb('l2a_nmax', [2, 2, 4], F32, stack=ast)
        ctmp = kb.sb('l2a_ct', [2, 2], F32, stack=ast)
        cdg = kb.sb('l2a_cd', [2, 2], F32, stack=ast)
        ccols = kb.sb('l2a_cc', [128, 8], F32, stack=ast)
        rrow = kb.sb('l2a_rr', [1, 512], F32, stack=ast)
        zsb = kb.sb('l2a_zsb', [128, 512], BF16, stack=ast)
        psA = kb.ps('l2a_pA', [128, 1024], stack=st)
        psB = kb.ps('l2a_pB', [128, 1024], stack=st)
        pso = kb.ps('l2a_po', [65, 512], stack=st)
        psr = kb.ps('l2a_pr', [128, 512], stack=st)
        xln = d['xln']
        wev, lev = d['wevt'], d['lnevt']
        H2 = 0.125
        ones_row = d['consts']['ones_row']
        ones512 = d['consts']['ones_row512']

        bev = kb.V('tensor_scalar', out=d['bqkv'][:, 8:16],
                   in0=d['bqkv'][:, 8:16], scalar1=H2, scalar2=None,
                   op0=ALU.mult, waits=list(wev))

        prev = kb.evt('Sc')
        for hg in range(2):
            wq = d['wq']
            for di, (dst, fb, scale) in enumerate(((qfm, 4 * hg, 1.0),
                                                   (kfm, 8 + 4 * hg, H2))):
                for fc in range(4):
                    for tc in range(4):
                        pb = (psA if tc % 2 == 0 else psB)[:, 0:512]
                        sl = slice(tc * 512, (tc + 1) * 512)
                        for dc in range(NC_):
                            mi = kb.T('matmul', pb, wq[:, dc, fb + fc, :],
                                      xln[:, dc, sl], start=(dc == 0),
                                      stop=(dc == NC_ - 1),
                                      waits=(list(wev) + [lev, bev,
                                                          kb.evt('Vc'),
                                                          kb.evt('Sc')])
                                      if (di == 0 and fc == 0 and tc == 0
                                          and dc == 0)
                                      else ([kb.evt('Sc')] if dc == 0 else ()))
                        pe = kb.mark(mi, 'PE')
                        kb.S('activation', out=dst[:, fc, sl], in_=pb,
                             func=AF.Identity,
                             bias=d['bqkv'][:, fb + fc:fb + fc + 1],
                             scale=float(scale), waits=[pe])
            for tt in range(16):
                pb = psA[:, 0:512]
                fb = 16 + 4 * hg
                for dc in range(NC_):
                    mi = kb.T('matmul', pb,
                              xln[:, dc, tt * 128:(tt + 1) * 128],
                              wq[:, dc, fb:fb + 4, :].rearrange(
                                  'p a b -> p (a b)'),
                              start=(dc == 0), stop=False,
                              waits=[kb.evt('Sc')] if dc == 0 else ())
                mi = kb.T('matmul', pb, ones_row[:],
                          d['bqkv_row'][:, 2048 + 512 * hg:
                                        2048 + 512 * hg + 512],
                          start=False, stop=True)
                pe = kb.mark(mi, 'PE')
                kb.S('activation',
                     out=vg[:, tt, :].rearrange('p (h c) -> p h c', c=65)[
                         :, :, 0:64],
                     in_=pb, func=AF.Copy, bias=0.0, scale=1.0, waits=[pe])
            oev = kb.V('memset',
                       vg[:].rearrange('p a (h c) -> p a h c', c=65)[
                           :, :, :, 64:65], 1.0, waits=[kb.evt('Sc')])

            for fc in range(4):
                for di, src_ in enumerate((qfm, kfm)):
                    for tc in range(4):
                        sl = slice(tc * 512, (tc + 1) * 512)
                        sev = kb.S('activation', out=sqs[:],
                                   in_=src_[:, fc, sl], func=AF.Square,
                                   bias=0.0, scale=1.0,
                                   waits=[kb.evt('Vc')])
                        mi = kb.T('matmul', psB[0:2, 0:512], d['i2c'][:],
                                  sqs[:], start=True, stop=True,
                                  waits=[sev, kb.evt('Vc')])
                        pe = kb.mark(mi, 'PE')
                        kb.V('tensor_reduce', out=nmax[:, di, tc:tc + 1],
                             in_=psB[0:2, 0:512], axis=AX.X, op=ALU.max,
                             waits=[pe])
                        kb.t_wait([kb.evt('Vc')])
                    kb.V('tensor_reduce', out=ctmp[:, di:di + 1],
                         in_=nmax[:, di, :], axis=AX.X, op=ALU.max)
                kb.V('tensor_tensor', out=ctmp[:, 0:1], in0=ctmp[:, 0:1],
                     in1=ctmp[:, 1:2], op=ALU.mult)
                cse = kb.S('activation', out=ctmp[:, 0:1], in_=ctmp[:, 0:1],
                           func=AF.Sqrt, bias=0.0, scale=1.0,
                           waits=[kb.evt('Vc')])
                kb.V('tensor_scalar', out=ctmp[:, 0:1], in0=ctmp[:, 0:1],
                     scalar1=-1.0, scalar2=None, op0=ALU.mult, waits=[cse])
                dg = kb.V('tensor_tensor', out=cdg[:, :],
                          in0=ctmp[:, 0:1].to_broadcast((2, 2)),
                          in1=d['i2'][:], op=ALU.mult)
                mi = kb.T('matmul', psr[:, 0:2], d['ones2'][:], cdg[:, :],
                          start=True, stop=True, waits=[dg])
                pe = kb.mark(mi, 'PE')
                ce = kb.V('tensor_copy', out=ccols[:, 2 * fc:2 * fc + 2],
                          in_=psr[:, 0:2], waits=[pe])
                kb.t_wait([ce])
            cev = kb.evt('Vc')

            for hh in range(8):
                h = 8 * hg + hh
                hc, hr = hh // 2, (hh % 2) * 64
                hsl = slice(hr, hr + 64)
                ghc, ghr = h // 2, (h % 2) * 64
                gsl = slice(ghr, ghr + 64)
                for qb in range(4):
                    qsl = slice(qb * 512, (qb + 1) * 512)
                    for jp in range(8):
                        pS = psA if jp % 2 == 0 else psB
                        for s in range(2):
                            j = 2 * jp + s
                            mi = kb.T('matmul', pS[:, s * 512:(s + 1) * 512],
                                      kfm[:, hc, j * 128:(j + 1) * 128][hsl, :],
                                      qfm[:, hc, qsl][hsl, :],
                                      start=True, stop=True,
                                      waits=[cev, prev, kb.evt('Sc')]
                                      if s == 0 else ())
                        pe = kb.mark(mi, 'PE')
                        eev = kb.S('activation', out=pbuf[:, jp % 2, :],
                                   in_=pS[:, :], func=AF.Exp,
                                   bias=ccols[:, hh:hh + 1], scale=1.0,
                                   waits=[pe])
                        for s in range(2):
                            j = 2 * jp + s
                            mi = kb.T('matmul', pso[:, :],
                                      vg[:, j, 65 * hh:65 * hh + 65],
                                      pbuf[:, jp % 2, s * 512:(s + 1) * 512],
                                      start=(jp == 0 and s == 0),
                                      stop=(jp == 7 and s == 1),
                                      waits=[eev, oev] if s == 0 else ())
                        pe2 = kb.mark(mi, 'PE')
                    clm = kb.V('tensor_scalar', out=rrow[:],
                               in_=None, in0=pso[64:65, :], scalar1=1e-30,
                               scalar2=None, op0=ALU.max, waits=[pe2]) \
                        if False else \
                        kb.V('tensor_scalar', out=rrow[:],
                             in0=pso[64:65, :], scalar1=1e-30,
                             scalar2=None, op0=ALU.max, waits=[pe2])
                    rev = kb.V('reciprocal', out=rrow[:], in_=rrow[:])
                    mi = kb.T('matmul', psr[0:64, 0:512],
                              ones_row[:][0:1, 0:64], rrow[:],
                              start=True, stop=True, waits=[rev])
                    pe = kb.mark(mi, 'PE')
                    zev = kb.S('activation', out=zsb[0:64, :],
                               in_=pso[0:64, :], func=AF.Copy, bias=0.0,
                               scale=1.0, waits=[kb.evt('Vc')])
                    prev = kb.V('tensor_tensor', out=zfm[:, ghc, qsl][gsl, :],
                                in0=zsb[0:64, :], in1=psr[0:64, 0:512],
                                op=ALU.mult, waits=[pe, zev])
                    kb.t_wait([prev])
        ast.close()
        xs = kb.sb('l2a_xs', [128, 2, 512], F32, stack=st)
        ysb = kb.sb('l2a_ysb', [128, 512], F32, stack=st)
        for fc in range(NC_):
            for tc in range(4):
                sl = slice(tc * 512, (tc + 1) * 512)
                pb = (psA if tc % 2 == 0 else psB)[:, 0:512]
                xe = kb.dma(xs[:, tc % 2, :],
                            d['x_in'].rearrange('(c p) t -> p c t', p=128)[
                                :, fc, sl], 'XR', queue='sync',
                            waits=[kb.evt('Vc')])
                for ec in range(NC_):
                    mi = kb.T('matmul', pb, d['wo'][:, ec, fc, :],
                              zfm[:, ec, sl], start=(ec == 0), stop=False,
                              waits=[prev, kb.evt('Vc')] if ec == 0 else ())
                mi = kb.T('matmul', pb, d['bo'][:, fc * 128:(fc + 1) * 128],
                          ones512[:], start=False, stop=True)
                pe = kb.mark(mi, 'PE')
                yev = kb.V('scalar_tensor_tensor', out=ysb[:], in0=pb,
                           scalar=1.0, in1=xs[:, tc % 2, :], op0=ALU.mult,
                           op1=ALU.add, waits=[pe, xe, kb.evt('YO')])
                kb.dma(d['y_out'].rearrange('(c p) t -> p c t', p=128)[
                    :, fc, sl], ysb[:], 'YO', queue='scalar', waits=[yev])
        return kb.evt('YO')


def emit_l1a(kb, d):
    NDELTA = 19
    with contextlib.ExitStack() as st:
        ast = st.enter_context(contextlib.ExitStack())
        qh = kb.sb('l1a_qh', [128, T], BF16, stack=ast)
        kh = kb.sb('l1a_kh', [128, T], BF16, stack=ast)
        vtok = kb.sb('l1a_v', [128, 16, 8 * 129], BF16, stack=ast)
        strip = kb.sb('l1a_strip', [128, NDELTA * 128], F32, stack=ast)
        pb = kb.sb('l1a_p', [128, 16, 512], BF16, stack=ast)
        zrow = kb.sb('l1a_zr', [128, T], BF16, stack=ast)
        sqs = kb.sb('l1a_sqs', [128, 512], BF16, stack=ast)
        gb = kb.sb('l1a_gb', [128, 2, 128], BF16, stack=ast)
        sc1 = kb.sb('l1a_sc1', [1, 8, 512], F32, stack=ast)
        ccol = kb.sb('l1a_cc', [128, 1], F32, stack=ast)
        nrm = kb.sb('l1a_nrm', [1, 2], F32, stack=ast)
        stt = kb.sb('l1a_stt', [128, 10], F32, stack=ast)
        zt = kb.sb('l1a_zt', [128, 128], BF16, stack=ast)
        sq2 = kb.sb('l1a_sq2', [128, 128], F32, stack=ast)
        th = kb.sb('l1a_th', [8, 128], F32, stack=ast)
        epsc = kb.sb('l1a_eps', [128, 1], F32, stack=ast)
        thc = kb.sb('l1a_thc', [8, 1], F32, stack=ast)
        ab = kb.sb('l1a_ab', [8, 2, 128], F32, stack=ast)
        dg = kb.sb('l1a_dg', [128, 128], F32, stack=ast)
        dgi = kb.sb('l1a_dgi', [128, 128], I32, stack=ast)
        dgf = kb.sb('l1a_dgf', [128, 128], F32, stack=ast)
        thr = kb.sb('l1a_thr', [128, 8], F32, stack=ast)
        ln8 = kb.sb('l1a_ln8', [8, 1], F32, stack=ast)
        io128 = kb.sb('l1a_io', [8, 128], I32, stack=ast)
        iof = kb.sb('l1a_iof', [8, 128], F32, stack=ast)
        ps1 = kb.ps('l1a_p1', [128, 512], stack=st)
        ps2 = kb.ps('l1a_p2', [128, 512], stack=st)
        pso = kb.ps('l1a_po', [128, 129], stack=st)
        psd = kb.ps('l1a_pd', [128, 128], stack=st)
        pst = kb.ps('l1a_pt', [128, 128], BF16, stack=st)
        xln = d['xln']
        wev, lev = d['wevt'], d['lnevt']
        ones_row = d['consts']['ones_row']
        ones512 = d['consts']['ones_row512']
        ocb = d['consts']['ones_col_bf']
        SC = float(1.0 / np.sqrt(128.0))

        kb.V('memset', epsc[:], 1e-5)
        ge = kb.G('iota', io128[:], pattern=[[1, 128]], base=0,
                  channel_multiplier=0)
        ge2 = kb.G('iota', dgi[:], pattern=[[1, 128]], base=0,
                   channel_multiplier=-1)
        kb.V('tensor_copy', out=iof[:], in_=io128[:], waits=[ge, list(wev)[0]])
        kb.V('memset', ln8[:], -float(np.log(8.0)))
        kb.V('tensor_copy', out=thc[:], in_=d['thcol'][:], waits=list(wev))
        tre = kb.V('tensor_copy', out=dgf[:], in_=dgi[:], waits=[ge2])
        mi = kb.T('matmul', psd[:, 0:8], ones_row[:], d['throw'][:],
                  start=True, stop=True, waits=list(wev) + [kb.evt('Vc')])
        pe = kb.mark(mi, 'PE')
        kb.V('tensor_copy', out=thr[:], in_=psd[:, 0:8], waits=[pe])
        kb.V('tensor_scalar', out=dgf[:], in0=dgf[:], scalar1=0.0,
             scalar2=None, op0=ALU.max)
        dsum = None
        for h in range(8):
            kb.V('tensor_scalar', out=dg[:], in0=dgf[:],
                 scalar1=thr[:, h:h + 1], scalar2=None, op0=ALU.mult)
            se = kb.S('activation', out=sq2[:], in_=dg[:], func=AF.Exp,
                      bias=0.0, scale=-1.0, waits=[kb.evt('Vc')])
            if h == 0:
                dsum = kb.V('tensor_copy', out=strip[:, 3 * 128:4 * 128],
                            in_=sq2[:], waits=[se])
            else:
                dsum = kb.V('tensor_tensor', out=strip[:, 3 * 128:4 * 128],
                            in0=strip[:, 3 * 128:4 * 128], in1=sq2[:],
                            op=ALU.add, waits=[se])
        kb.V('tensor_scalar', out=strip[:, 3 * 128:4 * 128],
             in0=strip[:, 3 * 128:4 * 128], scalar1=0.125, scalar2=None,
             op0=ALU.mult)
        kb.V('memset', strip[:, 0:3 * 128], 1.0)
        for dd_ in range(1, 16):
            kb.V('tensor_scalar', out=ab[:, 0, :], in0=iof[:],
                 scalar1=-64.0 * dd_, scalar2=None, op0=ALU.add)
            kb.V('tensor_scalar', out=ab[:, 0, :], in0=ab[:, 0, :],
                 scalar1=thc[:], scalar2=None, op0=ALU.mult)
            kb.V('tensor_scalar', out=ab[:, 1, :], in0=iof[:],
                 scalar1=64.0 * dd_, scalar2=None, op0=ALU.add)
            ve = kb.V('tensor_scalar', out=ab[:, 1, :], in0=ab[:, 1, :],
                      scalar1=thc[:], scalar2=-1.0, op0=ALU.mult, op1=ALU.mult)
            kb.S('activation', out=ab[:, 0, :], in_=ab[:, 0, :], func=AF.Exp,
                 bias=ln8[:], scale=1.0, waits=[ve])
            se = kb.S('activation', out=ab[:, 1, :], in_=ab[:, 1, :],
                      func=AF.Exp, bias=0.0, scale=1.0)
            mi = kb.T('matmul', psd[:, :], ab[:, 0, :], ab[:, 1, :],
                      start=True, stop=True, waits=[se, kb.evt('Vc')])
            pe = kb.mark(mi, 'PE')
            kb.V('tensor_copy', out=strip[:, (3 + dd_) * 128:(4 + dd_) * 128],
                 in_=psd[:, :], waits=[pe])
        stre = kb.evt('Vc')

        for tt in range(16):
            for vh in range(2):
                pv = ps1 if vh == 0 else ps2
                for dc in range(NC_):
                    mi = kb.T('matmul', pv[:, :],
                              xln[:, dc, tt * 128:(tt + 1) * 128],
                              d['wv'][:, dc, 4 * vh:4 * vh + 4, :].rearrange(
                                  'p a b -> p (a b)'),
                              start=(dc == 0), stop=(dc == NC_ - 1),
                              waits=(list(wev) + [lev, kb.evt('Sc')])
                              if dc == 0 else ())
                pe = kb.mark(mi, 'PE')
                kb.S('activation',
                     out=vtok[:, tt, :].rearrange('p (h c) -> p h c', c=129)[
                         :, 4 * vh:4 * vh + 4, 0:128],
                     in_=pv[:, :], func=AF.Copy, bias=0.0, scale=1.0,
                     waits=[pe])
        onev = kb.V('memset',
                    vtok[:].rearrange('p a (h c) -> p a h c', c=129)[
                        :, :, :, 128:129], 1.0, waits=[kb.evt('Sc')])

        prev = kb.evt('Sc')
        zwe = kb.evt('ZW')
        for h in range(8):
            for di, (dst, w_, scale) in enumerate(((qh, d['wqk'], 1.0),
                                                   (kh, d['wqk'], SC))):
                for tc in range(4):
                    pv = ps1 if tc % 2 == 0 else ps2
                    sl = slice(tc * 512, (tc + 1) * 512)
                    for dc in range(NC_):
                        mi = kb.T('matmul', pv[:, :],
                                  w_[:, dc, 2 * h + di, :], xln[:, dc, sl],
                                  start=(dc == 0), stop=(dc == NC_ - 1),
                                  waits=[kb.evt('Sc'), kb.evt('Vc')]
                                  if dc == 0 else ())
                    pe = kb.mark(mi, 'PE')
                    kb.S('activation', out=dst[:, sl], in_=pv[:, :],
                         func=AF.Copy, bias=0.0, scale=float(scale),
                         waits=[pe])
            for di, src_ in enumerate((qh, kh)):
                for tc in range(4):
                    sl = slice(tc * 512, (tc + 1) * 512)
                    se = kb.S('activation', out=sqs[:], in_=src_[:, sl],
                              func=AF.Square, bias=0.0, scale=1.0,
                              waits=[kb.evt('Vc')])
                    mi = kb.T('matmul', ps1[0:1, :], ocb[:], sqs[:],
                              start=True, stop=True,
                              waits=[se, kb.evt('Vc')])
                    pe = kb.mark(mi, 'PE')
                    kb.V('tensor_copy', out=sc1[:, di * 4 + tc, :],
                         in_=ps1[0:1, :], waits=[pe])
                    kb.t_wait([kb.evt('Vc')])
                kb.V('tensor_reduce', out=nrm[:, di:di + 1],
                     in_=sc1[:, di * 4:di * 4 + 4, :].rearrange(
                         'p a b -> p (a b)'), axis=AX.X, op=ALU.max)
            kb.V('tensor_tensor', out=nrm[:, 0:1], in0=nrm[:, 0:1],
                 in1=nrm[:, 1:2], op=ALU.mult)
            cse = kb.S('activation', out=nrm[:, 0:1], in_=nrm[:, 0:1],
                       func=AF.Sqrt, bias=0.0, scale=1.0,
                       waits=[kb.evt('Vc')])
            nne = kb.V('tensor_scalar', out=nrm[:, 0:1], in0=nrm[:, 0:1],
                       scalar1=-1.0, scalar2=None, op0=ALU.mult, waits=[cse])
            mi = kb.T('matmul', psd[:, 0:1], ones_row[:], nrm[:, 0:1],
                      start=True, stop=True, waits=[nne])
            pe = kb.mark(mi, 'PE')
            cce = kb.V('tensor_copy', out=ccol[:], in_=psd[:, 0:1],
                       waits=[pe])
            mi = kb.T('matmul', psd[:, 0:128], ones_row[:],
                      d['gam'][:, h * 128:(h + 1) * 128], start=True,
                      stop=True, waits=[cce])
            mi2 = kb.T('matmul', ps1[:, 0:128], ones_row[:],
                       d['bet'][:, h * 128:(h + 1) * 128], start=True,
                       stop=True)
            pe = kb.mark(mi2, 'PE')
            kb.V('tensor_copy', out=gb[:, 0, :], in_=psd[:, 0:128],
                 waits=[pe])
            gbe = kb.V('tensor_copy', out=gb[:, 1, :], in_=ps1[:, 0:128])
            kb.t_wait([gbe])

            for qb in range(4):
                qsl = slice(qb * 512, (qb + 1) * 512)
                for j in range(4 * qb + 4):
                    n0 = max(0, (j - 4 * qb) * 128)
                    d0 = 4 * qb - j
                    mi = kb.T('matmul', (ps1 if j % 2 == 0 else ps2)[:, 0:512 - n0],
                              kh[:, j * 128:(j + 1) * 128],
                              qh[:, qb * 512 + n0:(qb + 1) * 512],
                              start=True, stop=True,
                              waits=[stre, cce, gbe, kb.evt('Vc'),
                                     kb.evt('Sc')] if j == 0 else
                              [kb.evt('Vc'), kb.evt('Sc')])
                    pe = kb.mark(mi, 'PE')
                    pS = (ps1 if j % 2 == 0 else ps2)
                    s0 = (d0 + 3) * 128 + n0
                    dve = kb.V('tensor_tensor', out=pS[:, 0:512 - n0],
                               in0=pS[:, 0:512 - n0],
                               in1=strip[:, s0:s0 + 512 - n0], op=ALU.mult,
                               waits=[pe])
                    eev = kb.S('activation', out=pb[:, j, n0:512],
                               in_=pS[:, 0:512 - n0], func=AF.Exp,
                               bias=ccol[:], scale=1.0, waits=[dve])
                    if j >= 4 * qb:
                        kb.V('tensor_tensor', out=pb[:, j, n0:n0 + 128],
                             in0=pb[:, j, n0:n0 + 128],
                             in1=d['trimask'][:], op=ALU.mult, waits=[eev])
                pmev = (kb.evt('Vc'), kb.evt('Sc'))
                for s in range(4):
                    for j in range(4 * qb + s + 1):
                        mi = kb.T('matmul', pso[:, :],
                                  pb[:, j, s * 128:(s + 1) * 128],
                                  vtok[:, j, 129 * h:129 * h + 129],
                                  start=(j == 0), stop=(j == 4 * qb + s),
                                  waits=[pmev[0], pmev[1], onev,
                                         kb.evt('Vc')] if j == 0 else ())
                    pe = kb.mark(mi, 'PE')
                    kb.V('tensor_scalar', out=stt[:, 0:1], in0=pso[:, 128:129],
                         scalar1=1e-30, scalar2=None, op0=ALU.max, waits=[pe])
                    kb.V('reciprocal', out=stt[:, 1:2], in_=stt[:, 0:1])
                    kb.V('tensor_reduce', out=stt[:, 2:3], in_=pso[:, 0:128],
                         axis=AX.X, op=ALU.add)
                    se = kb.S('activation', out=sq2[:], in_=pso[:, 0:128],
                              func=AF.Square, bias=0.0, scale=1.0,
                              waits=[kb.evt('Vc')])
                    kb.V('tensor_reduce', out=stt[:, 3:4], in_=sq2[:],
                         axis=AX.X, op=ALU.add, waits=[se])
                    kb.V('tensor_scalar', out=stt[:, 4:5], in0=stt[:, 2:3],
                         scalar1=stt[:, 1:2], scalar2=1.0 / 128.0,
                         op0=ALU.mult, op1=ALU.mult)
                    kb.V('tensor_scalar', out=stt[:, 5:6], in0=stt[:, 3:4],
                         scalar1=stt[:, 1:2], scalar2=None, op0=ALU.mult)
                    kb.V('tensor_scalar', out=stt[:, 5:6], in0=stt[:, 5:6],
                         scalar1=stt[:, 1:2], scalar2=1.0 / 128.0,
                         op0=ALU.mult, op1=ALU.mult)
                    kb.V('tensor_tensor', out=stt[:, 6:7], in0=stt[:, 4:5],
                         in1=stt[:, 4:5], op=ALU.mult)
                    kb.V('tensor_tensor', out=stt[:, 6:7], in0=stt[:, 5:6],
                         in1=stt[:, 6:7], op=ALU.subtract)
                    se = kb.S('activation', out=stt[:, 6:7], in_=stt[:, 6:7],
                              func=AF.Sqrt, bias=epsc[:], scale=1.0,
                              waits=[kb.evt('Vc')])
                    kb.V('reciprocal', out=stt[:, 7:8], in_=stt[:, 6:7],
                         waits=[se])
                    kb.V('tensor_tensor', out=stt[:, 8:9], in0=stt[:, 1:2],
                         in1=stt[:, 7:8], op=ALU.mult)
                    kb.V('scalar_tensor_tensor', out=stt[:, 9:10],
                         in0=stt[:, 4:5], scalar=-1.0, in1=stt[:, 7:8],
                         op0=ALU.mult, op1=ALU.mult)
                    kb.V('tensor_scalar', out=sq2[:], in0=pso[:, 0:128],
                         scalar1=stt[:, 8:9], scalar2=stt[:, 9:10],
                         op0=ALU.mult, op1=ALU.add)
                    kb.V('tensor_tensor', out=sq2[:], in0=sq2[:],
                         in1=gb[:, 0, :], op=ALU.mult)
                    zte = kb.V('tensor_tensor', out=zt[:], in0=sq2[:],
                               in1=gb[:, 1, :], op=ALU.add)
                    tr = kb.T('transpose', pst[:, :], zt[:], d['idbf'][:],
                              waits=[zte])
                    pe = kb.mark(tr, 'PE')
                    cpe = kb.V('tensor_copy',
                               out=zrow[:, (4 * qb + s) * 128:
                                        (4 * qb + s + 1) * 128],
                               in_=pst[:, :],
                               waits=[pe] + ([kb.evt('ZW')]
                                             if (qb == 0 and s == 0) else []))
                    kb.t_wait([cpe])
            zwe = kb.dma(d['z_dram'][h * 128:(h + 1) * 128, :], zrow[:],
                         'ZW', queue='scalar', waits=[kb.evt('Vc')])
            prev = kb.evt('Sc')

        kb.t_wait([kb.evt('Vc'), kb.evt('Sc'), kb.evt('ZW')])
        ast.close()
        zs = kb.sb('l1a_zs', [128, 2, 512], BF16, stack=st)
        xs = kb.sb('l1a_xs', [128, 2, 512], F32, stack=st)
        ysb = kb.sb('l1a_ysb', [128, 512], F32, stack=st)
        for fc in range(NC_):
            for tc in range(4):
                sl = slice(tc * 512, (tc + 1) * 512)
                pv = ps1 if tc % 2 == 0 else ps2
                xe = kb.dma(xs[:, tc % 2, :],
                            d['x_in'].rearrange('(c p) t -> p c t', p=128)[
                                :, fc, sl], 'XR', queue='sync',
                            waits=[kb.evt('Vc')])
                for ec in range(NC_):
                    ze = kb.dma(zs[:, tc % 2, :],
                                d['z_dram'][ec * 128:(ec + 1) * 128, sl],
                                'ZR', queue='sync',
                                waits=[zwe, kb.evt('PE')] if ec == 0 else
                                [kb.evt('PE')])
                    mi = kb.T('matmul', pv[:, :], d['wo'][:, ec, fc, :],
                              zs[:, tc % 2, :], start=(ec == 0), stop=False,
                              waits=[ze, kb.evt('Vc')])
                    pe = kb.mark(mi, 'PE')
                mi = kb.T('matmul', pv[:, :],
                          d['bo'][:, fc * 128:(fc + 1) * 128], ones512[:],
                          start=False, stop=True)
                pe = kb.mark(mi, 'PE')
                yev = kb.V('scalar_tensor_tensor', out=ysb[:], in0=pv[:, :],
                           scalar=1.0, in1=xs[:, tc % 2, :], op0=ALU.mult,
                           op1=ALU.add, waits=[pe, xe, kb.evt('YO')])
                kb.dma(d['y_out'].rearrange('(c p) t -> p c t', p=128)[
                    :, fc, sl], ysb[:], 'YO', queue='scalar', waits=[yev])
        return kb.evt('YO')


def emit_l0a(kb, d, trunc=None):
    with contextlib.ExitStack() as st:
        xg = d['xg']
        lq = kb.sb('l0a_lq', [128, 2, T], BF16, stack=st)
        lktok = kb.sb('l0a_lkt', [128, 16, 256], BF16, stack=st)
        linkv = kb.sb('l0a_lkv', [128, 2, T], BF16, stack=st)
        attn = kb.sb('l0a_attn', [64, 32, 64], BF16, stack=st)
        g8 = kb.sb('l0a_g8', [128, 2, 8], F32, stack=st)
        sb64 = kb.sb('l0a_sb64', [128, 64], F32, stack=st)
        wev = d['wevt']
        lev = d['lnevt']
        ones_row = d['consts']['ones_row']

        with contextlib.ExitStack() as hst:
            qq = kb.sb('l0a_qq', [128, 2, T], BF16, stack=hst)
            qkk = kb.sb('l0a_qkk', [128, 2, T], BF16, stack=hst)
            lkfm = kb.sb('l0a_lkfm', [128, 2, T], BF16, stack=hst)
            qkf = kb.sb('l0a_qkf', [128, 2, T], BF16, stack=hst)
            hbuf = kb.sb('l0a_hb', [128, 2, T], F32, stack=hst)
            hbuf2 = kb.sb('l0a_hb2', [128, 2, T], F32, stack=hst)
            cosT = kb.sb('l0a_cos', [128, T], F32, stack=hst)
            sinT = kb.sb('l0a_sin', [128, T], F32, stack=hst)
            kb.dma(cosT[:], d['cos_dram'][:], 'CS', queue='sync')
            cse_ = kb.dma(sinT[:], d['sin_dram'][:], 'CS', queue='sync')
            ps1 = kb.ps('l0a_p1', [128, 512], stack=hst)
            ps2 = kb.ps('l0a_p2', [128, 512], stack=hst)
            pst = kb.ps('l0a_pt', [128, 128], BF16, stack=hst)

            for fc in range(2):
                for tc in range(4):
                    pv = ps1 if tc % 2 == 0 else ps2
                    sl = slice(tc * 512, (tc + 1) * 512)
                    for dc in range(NC_):
                        mi = kb.T('matmul', pv[:, :], d['wqkT'][:, dc, fc, :],
                                  xg[:, dc, sl], start=(dc == 0),
                                  stop=(dc == NC_ - 1),
                                  waits=(list(wev) + [lev, kb.evt('Sc')])
                                  if (fc == 0 and tc == 0 and dc == 0)
                                  else ([kb.evt('Sc')] if dc == 0 else ()))
                    pe = kb.mark(mi, 'PE')
                    kb.S('activation', out=qkf[:, fc, sl], in_=pv[:, :],
                         func=AF.Silu, bias=d['bqk'][:, fc:fc + 1], scale=1.0,
                         waits=[pe])
            kb.V('tensor_copy', out=g8[:, 0, :], in_=d['gam'][:],
                 waits=list(wev))
            kb.V('tensor_copy', out=g8[:, 1, :], in_=d['bet'][:])
            for i in (0, 2):
                for z in range(2):
                    kb.V('tensor_scalar', out=g8[:, z, 2 * i:2 * i + 2],
                         in0=g8[:, z, 2 * i:2 * i + 2], scalar1=0.125,
                         scalar2=None, op0=ALU.mult)
            g8e = kb.evt('Vc')
            for i, dst in enumerate((qq, lq, qkk, lkfm)):
                for fc in range(2):
                    kb.V('tensor_scalar', out=hbuf[:, fc, :],
                         in0=qkf[:, fc, :],
                         scalar1=g8[:, 0, 2 * i + fc:2 * i + fc + 1],
                         scalar2=g8[:, 1, 2 * i + fc:2 * i + fc + 1],
                         op0=ALU.mult, op1=ALU.add,
                         waits=[g8e, cse_, kb.evt('Sc')] if fc == 0 else ())
                kb.V('tensor_tensor', out=hbuf2[:, 0, :], in0=hbuf[:, 0, :],
                     in1=cosT[:], op=ALU.mult)
                kb.V('tensor_tensor', out=hbuf2[:, 1, :], in0=hbuf[:, 1, :],
                     in1=sinT[:], op=ALU.mult)
                kb.V('tensor_tensor', out=dst[:, 0, :], in0=hbuf2[:, 0, :],
                     in1=hbuf2[:, 1, :], op=ALU.subtract)
                kb.V('tensor_tensor', out=hbuf2[:, 0, :], in0=hbuf[:, 1, :],
                     in1=cosT[:], op=ALU.mult)
                kb.V('tensor_tensor', out=hbuf2[:, 1, :], in0=hbuf[:, 0, :],
                     in1=sinT[:], op=ALU.mult)
                hde = kb.V('tensor_tensor', out=dst[:, 1, :],
                           in0=hbuf2[:, 0, :], in1=hbuf2[:, 1, :], op=ALU.add)
            for tt in range(16):
                for c2 in range(2):
                    tr = kb.T('transpose', pst[:, :],
                              lkfm[:, c2, tt * 128:(tt + 1) * 128],
                              d['idbf'][:], waits=[hde, kb.evt('Vc')])
                    pe = kb.mark(tr, 'PE')
                    ce = kb.V('tensor_copy',
                              out=lktok[:, tt, c2 * 128:(c2 + 1) * 128],
                              in_=pst[:, :], waits=[pe])
                    kb.t_wait([ce])
            for g in range(32):
                for dc2 in range(2):
                    mi = kb.T('matmul', ps1[0:64, 0:64],
                              qkk[:, dc2, g * 64:(g + 1) * 64],
                              qq[:, dc2, g * 64:(g + 1) * 64],
                              start=(dc2 == 0), stop=(dc2 == 1),
                              waits=[kb.evt('Vc')] if dc2 == 0 else ())
                pe = kb.mark(mi, 'PE')
                kb.V('tensor_tensor', out=sb64[0:64, :], in0=ps1[0:64, 0:64],
                     in1=d['bias64'][:], op=ALU.add, waits=[pe])
                kb.V('tensor_scalar', out=sb64[0:64, :], in0=sb64[0:64, :],
                     scalar1=0.0, scalar2=None, op0=ALU.max)
                ce = kb.V('tensor_tensor', out=attn[:, g, :],
                          in0=sb64[0:64, :], in1=sb64[0:64, :], op=ALU.mult)
                kb.t_wait([ce])
            kb.t_wait([kb.evt('Vc'), kb.evt('Sc')])
        if trunc == 'heads':
            return kb.evt('Sc')
        with contextlib.ExitStack() as vst:
            vsb = kb.sb('l0a_vsb', [128, 4, 512], BF16, stack=vst)
            ps1 = kb.ps('l0a_v1', [128, 512], stack=vst)
            ps2 = kb.ps('l0a_v2', [128, 512], stack=vst)
            for tt in range(16):
                for eg in range(4):
                    pv = ps1 if eg % 2 == 0 else ps2
                    for dc in range(NC_):
                        mi = kb.T('matmul', pv[:, :],
                                  xg[:, dc, tt * 128:(tt + 1) * 128],
                                  d['wvo'][:, dc * 2048 + 512 * eg:
                                           dc * 2048 + 512 * eg + 512],
                                  start=(dc == 0), stop=False,
                                  waits=([kb.evt('Sc'), kb.evt('Vc')] +
                                         ([kb.evt('WV')] if (tt == 0 and
                                                             eg == 0) else []))
                                  if dc == 0 else ())
                    mi = kb.T('matmul', pv[:, :], ones_row[:],
                              d['bv'][:, 512 * eg:512 * eg + 512],
                              start=False, stop=True)
                    pe = kb.mark(mi, 'PE')
                    kb.S('activation', out=vsb[:, eg, :], in_=pv[:, :],
                         func=AF.Silu, bias=0.0, scale=1.0,
                         waits=[pe] + ([kb.evt('VW')] if eg == 0 else []))
                kb.dma(d['v_dram'][tt * 128:(tt + 1) * 128, :],
                       vsb[:].rearrange('p a b -> p (a b)'), 'VW',
                       queue='scalar', waits=[kb.evt('Sc')])
            kb.t_wait([kb.evt('Sc'), kb.evt('Vc')])
        with contextlib.ExitStack() as ast:
            psl_ = [kb.ps('l0a_pl%d' % i, [128, 512], stack=ast)
                    for i in range(6)]
            vt = kb.sb('l0a_vt', [128, 2, 2048], BF16, stack=ast)
            for tt in range(16):
                le = kb.dma(vt[:, tt % 2, :],
                            d['v_dram'][tt * 128:(tt + 1) * 128, :], 'VR',
                            queue='sync',
                            waits=[kb.evt('VW')] if tt == 0 else
                            [kb.evt('PE')])
                for dc2 in range(2):
                    for eg in range(3):
                        mi = kb.T('matmul', psl_[dc2 * 3 + eg][:, :],
                                  lktok[:, tt, dc2 * 128:(dc2 + 1) * 128],
                                  vt[:, tt % 2, eg * 512:(eg + 1) * 512],
                                  start=(tt == 0), stop=(tt == 15),
                                  waits=[le] if (dc2 == 0 and eg == 0) else ())
                kb.mark(mi, 'PE')
            for dc2 in range(2):
                for eg in range(3):
                    kb.S('activation',
                         out=linkv[:, dc2, eg * 512:(eg + 1) * 512],
                         in_=psl_[dc2 * 3 + eg][:, :], func=AF.Copy, bias=0.0,
                         scale=1.0 / T, waits=[kb.evt('PE')])
            kb.t_wait([kb.evt('Sc')])
            for tt in range(16):
                le = kb.dma(vt[:, tt % 2, 0:512],
                            d['v_dram'][tt * 128:(tt + 1) * 128, 1536:2048],
                            'VR', queue='sync', waits=[kb.evt('PE')])
                for dc2 in range(2):
                    mi = kb.T('matmul', psl_[dc2][:, :],
                              lktok[:, tt, dc2 * 128:(dc2 + 1) * 128],
                              vt[:, tt % 2, 0:512],
                              start=(tt == 0), stop=(tt == 15),
                              waits=[le] if dc2 == 0 else ())
                kb.mark(mi, 'PE')
            for dc2 in range(2):
                kb.S('activation', out=linkv[:, dc2, 1536:2048],
                     in_=psl_[dc2][:, :], func=AF.Copy, bias=0.0,
                     scale=1.0 / T, waits=[kb.evt('PE')])
            kb.t_wait([kb.evt('Sc')])

        if trunc == 'phaseA':
            return kb.evt('Sc')
        kb.dma(d['wvo'][:], d['wo_dram'][:], 'WO', queue='gpsimd',
               waits=[kb.evt('PE'), kb.evt('Sc')])
        with contextlib.ExitStack() as bst:
            vsuper = kb.sb('l0a_vs', [64, 8, 2048], BF16, stack=bst)
            zsuper = kb.sb('l0a_zs', [128, 16, 512], BF16, stack=bst)
            gsb = kb.sb('l0a_gsb', [128, 512], BF16, stack=bst)
            wg = [kb.sb('l0a_wg0', [128, NC_, 128], BF16, stack=bst),
                  kb.sb('l0a_wg1', [128, NC_, 128], BF16, stack=bst)]
            outsb = kb.sb('l0a_out', [128, NC_, 512], F32, stack=bst)
            psg = kb.ps('l0a_pg', [128, 512], stack=bst)
            psq = kb.ps('l0a_pq', [128, 512], stack=bst)
            pso = [kb.ps('l0a_po0', [128, 512], stack=bst),
                   kb.ps('l0a_po1', [128, 512], stack=bst)]
            for ts_ in range(4):
                sl = slice(ts_ * 512, (ts_ + 1) * 512)
                vle = kb.dma(vsuper[:],
                             d['v_dram'].rearrange('(a p) e -> p a e', p=64)[
                                 :, 8 * ts_:8 * ts_ + 8, :], 'VS',
                             queue='sync',
                             waits=[kb.evt('Vc'), kb.evt('PE')])
                for ec in range(16):
                    wge = kb.dma(wg[ec % 2][:],
                                 d['wgT_dram'][:, :, ec, :], 'WG%d' % (ec % 2),
                                 queue='gpsimd', waits=[kb.evt('PE')])
                    for dc in range(NC_):
                        mi = kb.T('matmul', psg[:, :], wg[ec % 2][:, dc, :],
                                  xg[:, dc, sl], start=(dc == 0),
                                  stop=(dc == NC_ - 1),
                                  waits=[wge, kb.evt('Sc')] if dc == 0 else ())
                    pe = kb.mark(mi, 'PE')
                    ge = kb.S('activation', out=gsb[:], in_=psg[:, :],
                              func=AF.Silu, bias=d['bg'][:, ec:ec + 1],
                              scale=1.0, waits=[pe])
                    if trunc == 'gate' and ts_ == 0 and ec == 0:
                        return kb.evt('Sc')
                    for gg in range(8):
                        g = 8 * ts_ + gg
                        mi = kb.T('matmul', psq[:, gg * 64:(gg + 1) * 64],
                                  vsuper[:, gg, ec * 128:(ec + 1) * 128],
                                  attn[:, g, :],
                                  start=(gg == 0), stop=False,
                                  waits=[vle, kb.evt('Vc')]
                                  if gg == 0 else ())
                    for dc2 in range(2):
                        mi = kb.T('matmul', psq[:, :],
                                  linkv[:, dc2, ec * 128:(ec + 1) * 128],
                                  lq[:, dc2, sl], start=False,
                                  stop=(dc2 == 1))
                    pe = kb.mark(mi, 'PE')
                    ze = kb.V('tensor_tensor', out=zsuper[:, ec, :],
                              in0=gsb[:], in1=psq[:, :], op=ALU.mult,
                              waits=[pe, ge])
                    kb.t_wait([ze])
                    if trunc == 'ec%d' % ec and ts_ == 0:
                        return kb.evt('Vc')
                if trunc == 'ecs' and ts_ == 0:
                    return kb.evt('Vc')
                for fc in range(NC_):
                    po = pso[fc % 2]
                    for ec in range(16):
                        mi = kb.T('matmul', po[:, :],
                                  d['wvo'][:, ec * 1024 + fc * 128:
                                           ec * 1024 + fc * 128 + 128],
                                  zsuper[:, ec, :],
                                  start=(ec == 0), stop=False,
                                  waits=([kb.evt('Vc'), kb.evt('Sc')] +
                                         ([kb.evt('WO')] if (ts_ == 0 and
                                                             fc == 0) else []))
                                  if ec == 0 else ())
                    mi = kb.T('matmul', po[:, :],
                              d['bo'][:, fc * 128:(fc + 1) * 128],
                              d['consts']['ones_row512'][:],
                              start=False, stop=True)
                    pe = kb.mark(mi, 'PE')
                    kb.S('activation', out=outsb[:, fc, :], in_=po[:, :],
                         func=AF.Copy, bias=0.0, scale=1.0, waits=[pe])
                if trunc == 'phaseB1' and ts_ == 0:
                    return kb.evt('Sc')
                with contextlib.ExitStack() as lst:
                    lsq = kb.sb('l0a_lsq%d' % ts_, [128, 512], F32, stack=lst)
                    lst_ = kb.sb('l0a_lst%d' % ts_, [1, 6, 512], F32, stack=lst)
                    xsr = kb.sb('l0a_xsr%d' % ts_, [128, 512], F32, stack=lst)
                    ysb = kb.sb('l0a_ysb%d' % ts_, [128, 512], F32, stack=lst)
                    ep1 = kb.sb('l0a_ep1%d' % ts_, [1, 1], F32, stack=lst)
                    pss = kb.ps('l0a_pss%d' % ts_, [128, 512], stack=lst)
                    psq2 = kb.ps('l0a_psq2%d' % ts_, [128, 512], stack=lst)
                    kb.V('memset', ep1[:], 1e-5)
                    pe = kb.evt('PE')
                    for c in range(NC_):
                        se = kb.V('tensor_tensor', out=lsq[:], in0=outsb[:, c, :],
                                  in1=outsb[:, c, :], op=ALU.mult,
                                  waits=[kb.evt('Sc'), pe])
                        kb.T('matmul', pss[0:1, :], d['consts']['ones_col'][:],
                             outsb[:, c, :], start=(c == 0),
                             stop=(c == NC_ - 1),
                             waits=[kb.evt('Sc')] if c == 0 else ())
                        mi = kb.T('matmul', psq2[0:1, :],
                                  d['consts']['ones_col'][:], lsq[:],
                                  start=(c == 0), stop=(c == NC_ - 1),
                                  waits=[se])
                        pe = kb.mark(mi, 'PE')
                    kb.V('tensor_scalar', out=lst_[:, 0, :], in0=pss[0:1, :],
                         scalar1=1.0 / D, scalar2=None, op0=ALU.mult,
                         waits=[pe])
                    kb.V('tensor_scalar', out=lst_[:, 1, :], in0=psq2[0:1, :],
                         scalar1=1.0 / D, scalar2=None, op0=ALU.mult)
                    kb.V('tensor_tensor', out=lst_[:, 2, :],
                         in0=lst_[:, 0, :], in1=lst_[:, 0, :], op=ALU.mult)
                    kb.V('tensor_tensor', out=lst_[:, 1, :],
                         in0=lst_[:, 1, :], in1=lst_[:, 2, :],
                         op=ALU.subtract)
                    se2 = kb.S('activation', out=lst_[:, 1, :],
                               in_=lst_[:, 1, :], func=AF.Sqrt, bias=ep1[:],
                               scale=1.0, waits=[kb.evt('Vc')])
                    rve = kb.V('reciprocal', out=lst_[:, 2, :],
                               in_=lst_[:, 1, :], waits=[se2])
                    kb.T('matmul', pss[:, :], ones_row[:], lst_[:, 0, :],
                         start=True, stop=True, waits=[rve])
                    mi = kb.T('matmul', psq2[:, :], ones_row[:],
                              lst_[:, 2, :], start=True, stop=True)
                    pe = kb.mark(mi, 'PE')
                    for c in range(NC_):
                        xre = kb.dma(xsr[:],
                                     d['x_in'].rearrange(
                                         '(c p) t -> p c t', p=128)[:, c, sl],
                                     'XR', queue='sync',
                                     waits=[kb.evt('Vc')])
                        kb.V('tensor_tensor', out=lsq[:], in0=outsb[:, c, :],
                             in1=pss[:, :], op=ALU.subtract,
                             waits=[pe] if c == 0 else ())
                        kb.V('tensor_tensor', out=lsq[:], in0=lsq[:],
                             in1=psq2[:, :], op=ALU.mult)
                        kb.V('tensor_scalar', out=lsq[:], in0=lsq[:],
                             scalar1=d['ln2w'][:, c:c + 1],
                             scalar2=d['ln2b'][:, c:c + 1],
                             op0=ALU.mult, op1=ALU.add)
                        ye = kb.V('tensor_tensor', out=ysb[:], in0=lsq[:],
                                  in1=xsr[:], op=ALU.add,
                                  waits=[xre, kb.evt('YO')])
                        kb.dma(d['y_out'].rearrange(
                            '(c p) t -> p c t', p=128)[:, c, sl], ysb[:],
                            'YO', queue='scalar', waits=[ye])
                    kb.t_wait([kb.evt('Vc')])
        return kb.evt('YO')


def emit_l2f(kb, d):
    NKEY = 128
    with contextlib.ExitStack() as st:
        xln = d['xln']
        wev = d['wevt']
        lev = d['lnevt']
        ones_row = d['consts']['ones_row']

        with contextlib.ExitStack() as cst:
            cb = kb.sb('l2f_cb', [128, 2, 2048], BF16, stack=cst)
            for i in range(128):
                b = i % 2
                kb.dma(cb[:, b, 0:1024],
                       d['down_dram'][i * 128:(i + 1) * 128, :], 'CV%d' % b,
                       queue='gpsimd',
                       waits=[kb.evt('CW')] if i >= 2 else [])
                kb.dma(cb[:, b, 1024:2048],
                       d['up_dram'][i * 128:(i + 1) * 128, :], 'CV%d' % b,
                       queue='gpsimd')
                kb.dma(d['comb_dram'][i * 128:(i + 1) * 128, :],
                       cb[:, b, :], 'CW', queue='scalar',
                       waits=[kb.evt('CV%d' % b)])
        for c in range(NC_):
            kb.dma(d['xlnfm_dram'][c * 128:(c + 1) * 128, :], xln[:, c, :],
                   'XW', queue='sync', waits=[lev] if c == 0 else ())

        with contextlib.ExitStack() as pst:
            qp = kb.sb('l2f_qp', [32, 8, T], BF16, stack=pst)
            xtok = kb.sb('l2f_xt', [128, 1024], BF16, stack=pst)
            dsb = kb.sb('l2f_dsb', [128, 8, 2048], BF16, stack=pst)
            junk = kb.sb('l2f_junk', [128, 1024], BF16, stack=pst)
            simf = kb.sb('l2f_simf', [128, 128], F32, stack=pst)
            m8p = kb.sb('l2f_m8', [128, 8], F32, stack=pst)
            sxs = kb.sb('l2f_sxs', [128, 2, 8], F32, stack=pst)
            ixs = kb.sb('l2f_ixs', [128, 2, 8], F32, stack=pst)
            c64 = kb.sb('l2f_c64', [128, 64], F32, stack=pst)
            i64 = kb.sb('l2f_i64', [128, 64], F32, stack=pst)
            i64i = kb.sb('l2f_i64i', [128, 64], I32, stack=pst)
            kidx = kb.sb('l2f_kidx', [128, 8], I32, stack=pst)
            ksc = kb.sb('l2f_ksc', [128, 8], F32, stack=pst)
            ewa = kb.sb('l2f_ewa', [128, 8], F32, stack=pst)
            wk8 = kb.sb('l2f_wk8', [128, 8], F32, stack=pst)
            dgt = kb.sb('l2f_dgt', [128, 128], BF16, stack=pst)
            nm1 = kb.sb('l2f_nm1', [128, 1], F32, stack=pst)
            ssum = kb.sb('l2f_ss', [128, 1], F32, stack=pst)
            srec = kb.sb('l2f_sr', [128, 1], F32, stack=pst)
            estok = kb.sb('l2f_est', [128, 1024], BF16, stack=pst)
            psq = kb.ps('l2f_pq', [32, 512], stack=pst)
            pss = kb.ps('l2f_ps', [128, 128], stack=pst)
            pse = kb.ps('l2f_pe', [128, 1024], stack=pst)

            for ph in range(8):
                for tc in range(4):
                    sl = slice(tc * 512, (tc + 1) * 512)
                    for dc in range(NC_):
                        mi = kb.T('matmul', psq[:, :],
                                  d['wqp'][:, dc, ph, :], xln[:, dc, sl],
                                  start=(dc == 0), stop=(dc == NC_ - 1),
                                  waits=(list(wev) + [lev])
                                  if (ph == 0 and tc == 0 and dc == 0)
                                  else ([kb.evt('Sc')] if dc == 0 else ()))
                    pe = kb.mark(mi, 'PE')
                    kb.S('activation', out=qp[:, ph, sl], in_=psq[:, :],
                         func=AF.Copy, bias=0.0, scale=1.0,
                         waits=[pe] + ([kb.evt('CW')]
                                       if (ph == 0 and tc == 0) else []))
            qpe = kb.evt('Sc')

            for tt in range(16):
                tsl = slice(tt * 128, (tt + 1) * 128)
                xte = kb.dma(xtok[:], d['xlnfm_dram'][:, tsl], 'TL',
                             queue='sync',
                             waits=[kb.evt('XW'), kb.evt('CW')] if tt == 0
                             else [kb.evt('Vc')])
                for h in range(4):
                    for p in range(2):
                        mi = kb.T('matmul', pss[:, :],
                                  qp[:, p * 4 + h, tsl],
                                  d['keysT'][:, p * 4 + h, :],
                                  start=True, stop=True,
                                  waits=[qpe, kb.evt('Vc')])
                        pe = kb.mark(mi, 'PE')
                        kb.V('tensor_scalar', out=simf[:], in0=pss[:, :],
                             scalar1=4.0, scalar2=None, op0=ALU.add,
                             waits=[pe])
                        kb.V('tensor_scalar', out=simf[:].bitcast(I32),
                             in0=simf[:].bitcast(I32), scalar1=~127,
                             scalar2=None, op0=ALU.bitwise_and)
                        kb.V('tensor_tensor', out=simf[:].bitcast(I32),
                             in0=simf[:].bitcast(I32), in1=d['iotak'][:],
                             op=ALU.bitwise_or)
                        kb.V('max', out=m8p[:], in_=simf[:])
                        kb.V('tensor_scalar', out=kidx[:],
                             in0=m8p[:].bitcast(I32), scalar1=127,
                             scalar2=None, op0=ALU.bitwise_and)
                        kb.V('tensor_copy', out=ixs[:, p, :], in_=kidx[:])
                        kb.V('tensor_scalar', out=m8p[:].bitcast(I32),
                             in0=m8p[:].bitcast(I32), scalar1=~127,
                             scalar2=None, op0=ALU.bitwise_and)
                        ple = kb.V('tensor_scalar', out=sxs[:, p, :],
                                   in0=m8p[:], scalar1=-4.0, scalar2=None,
                                   op0=ALU.add)
                        kb.t_wait([ple])
                    kb.V('tensor_tensor', out=c64[:].rearrange(
                             'p (a b) -> p a b', a=8),
                         in0=sxs[:, 0, :].unsqueeze(2)
                         .to_broadcast((128, 8, 8)),
                         in1=sxs[:, 1, :].unsqueeze(1)
                         .to_broadcast((128, 8, 8)), op=ALU.add)
                    kb.V('scalar_tensor_tensor', out=i64[:].rearrange(
                             'p (a b) -> p a b', a=8),
                         in0=ixs[:, 0, :].unsqueeze(2)
                         .to_broadcast((128, 8, 8)),
                         scalar=float(NKEY),
                         in1=ixs[:, 1, :].unsqueeze(1)
                         .to_broadcast((128, 8, 8)),
                         op0=ALU.mult, op1=ALU.add)
                    kb.V('tensor_copy', out=i64i[:], in_=i64[:])
                    kb.V('tensor_scalar', out=c64[:], in0=c64[:],
                         scalar1=8.0, scalar2=None, op0=ALU.add)
                    kb.V('tensor_scalar', out=c64[:].bitcast(I32),
                         in0=c64[:].bitcast(I32), scalar1=~16383,
                         scalar2=None, op0=ALU.bitwise_and)
                    kb.V('tensor_tensor', out=c64[:].bitcast(I32),
                         in0=c64[:].bitcast(I32), in1=i64i[:],
                         op=ALU.bitwise_or)
                    kb.V('max', out=m8p[:], in_=c64[:])
                    kb.V('tensor_scalar', out=kidx[:],
                         in0=m8p[:].bitcast(I32), scalar1=16383,
                         scalar2=None, op0=ALU.bitwise_and)
                    kb.V('tensor_scalar', out=m8p[:].bitcast(I32),
                         in0=m8p[:].bitcast(I32), scalar1=~16383,
                         scalar2=None, op0=ALU.bitwise_and)
                    kb.V('tensor_scalar', out=ksc[:], in0=m8p[:],
                         scalar1=-8.0, scalar2=None, op0=ALU.add)
                    kb.V('tensor_scalar', out=nm1[:], in0=ksc[:, 0:1],
                         scalar1=-1.0, scalar2=None, op0=ALU.mult)
                    see = kb.S('activation', out=wk8[:], in_=ksc[:],
                               func=AF.Exp, bias=nm1[:], scale=1.0,
                               waits=[kb.evt('Vc')])
                    kb.V('tensor_reduce', out=ssum[:], in_=wk8[:], axis=AX.X,
                         op=ALU.add, waits=[see])
                    kb.V('reciprocal', out=srec[:], in_=ssum[:])
                    kie = kb.V('tensor_scalar', out=wk8[:], in0=wk8[:],
                               scalar1=srec[:], scalar2=None, op0=ALU.mult)
                    for k in range(8):
                        ge = kb.idma(dsb[:, k, :], None, d['comb_dram'],
                                     bass.IndirectOffsetOnAxis(
                                         ap=kidx[:, k:k + 1], axis=0), 'GA',
                                     waits=[kie, kb.evt('CW')]
                                     if k == 0 else [])
                    for k in range(8):
                        kb.V('scalar_tensor_tensor', out=junk[:],
                             in0=dsb[:, k, 0:1024], scalar=1.0, in1=xtok[:],
                             op0=ALU.mult, op1=ALU.mult,
                             accum_out=ewa[:, k:k + 1],
                             waits=[ge, xte] if k == 0 else ())
                    swe = kb.S('activation', out=m8p[:], in_=ewa[:],
                               func=AF.Exp, bias=0.0, scale=-1.0,
                               waits=[kb.evt('Vc')])
                    kb.V('tensor_scalar', out=m8p[:], in0=m8p[:],
                         scalar1=1.0, scalar2=None, op0=ALU.add, waits=[swe])
                    kb.V('reciprocal', out=m8p[:], in_=m8p[:])
                    kb.V('tensor_tensor', out=ewa[:], in0=ewa[:], in1=m8p[:],
                         op=ALU.mult)
                    wke = kb.V('tensor_tensor', out=wk8[:], in0=wk8[:],
                               in1=ewa[:], op=ALU.mult)
                    pv_prev = None
                    for k in range(8):
                        de = kb.V('tensor_scalar', out=dgt[:], in0=d['idbf'][:],
                                  scalar1=wk8[:, k:k + 1], scalar2=None,
                                  op0=ALU.mult,
                                  waits=([wke] if k == 0 else []) +
                                  ([pv_prev] if pv_prev else []))
                        for nn in range(2):
                            mi = kb.T('matmul',
                                      pse[:, nn * 512:(nn + 1) * 512],
                                      dgt[:],
                                      dsb[:, k, 1024 + nn * 512:
                                          1024 + (nn + 1) * 512],
                                      start=(h == 0 and k == 0),
                                      stop=(h == 3 and k == 7),
                                      waits=[de] if nn == 0 else ())
                        pv_prev = kb.mark(mi, 'PE')
                pe = kb.evt('PE')
                ee = kb.V('tensor_copy', out=estok[:], in_=pse[:, :],
                          waits=[pe, kb.evt('EO')])
                kb.dma(d['es_dram'][tsl, :], estok[:], 'EO', queue='scalar',
                       waits=[ee])
                kb.t_wait([ee])

        with contextlib.ExitStack() as mst:
            wg = kb.sb('l2f_wg', [128, NC_, 16, 128], BF16, stack=mst)
            wu = kb.sb('l2f_wu', [128, NC_, 16, 128], BF16, stack=mst)
            wd = kb.sb('l2f_wd', [128, 16, NC_, 128], BF16, stack=mst)
            msb = kb.sb('l2f_m', [128, 16, 512], BF16, stack=mst)
            gsb = kb.sb('l2f_g', [128, 512], BF16, stack=mst)
            xsr = kb.sb('l2f_xsr', [128, 512], F32, stack=mst)
            esr = kb.sb('l2f_esr', [128, 512], BF16, stack=mst)
            ysb = kb.sb('l2f_ysb', [128, 512], F32, stack=mst)
            ps1 = kb.ps('l2f_m1', [128, 512], stack=mst)
            ps2 = kb.ps('l2f_m2', [128, 512], stack=mst)
            ps3 = kb.ps('l2f_m3', [128, 512], stack=mst)
            we1 = kb.dma(wg[:], d['wgT_dram'][:], 'MW', queue='gpsimd',
                         waits=[kb.evt('Vc'), kb.evt('PE'), kb.evt('EO'),
                                kb.evt('GA')])
            we2 = kb.dma(wu[:], d['wuT_dram'][:], 'MW', queue='gpsimd')
            we3 = kb.dma(wd[:], d['wdT_dram'][:], 'MW', queue='gpsimd')
            for tc in range(4):
                sl = slice(tc * 512, (tc + 1) * 512)
                for ic in range(16):
                    pg = ps1 if ic % 2 == 0 else ps2
                    for dc in range(NC_):
                        kb.T('matmul', pg[:, :], wg[:, dc, ic, :],
                             xln[:, dc, sl], start=(dc == 0),
                             stop=(dc == NC_ - 1),
                             waits=[we3, kb.evt('Sc')] if dc == 0 else ())
                    for dc in range(NC_):
                        mi = kb.T('matmul', ps3[:, :], wu[:, dc, ic, :],
                                  xln[:, dc, sl], start=(dc == 0),
                                  stop=(dc == NC_ - 1),
                                  waits=[kb.evt('Vc')] if dc == 0 else ())
                    pe = kb.mark(mi, 'PE')
                    ge = kb.S('activation', out=gsb[:], in_=pg[:, :],
                              func=AF.Silu, bias=0.0, scale=1.0, waits=[pe])
                    me = kb.V('tensor_tensor', out=msb[:, ic, :], in0=gsb[:],
                              in1=ps3[:, :], op=ALU.mult, waits=[ge])
                    kb.t_wait([me])
                for fc in range(NC_):
                    po = ps1 if fc % 2 == 0 else ps2
                    xre = kb.dma(xsr[:], d['x_in'].rearrange(
                        '(c p) t -> p c t', p=128)[:, fc, sl], 'XR',
                        queue='sync', waits=[kb.evt('Vc')])
                    ere = kb.dma(esr[:], d['es_dram'][sl,
                                                       fc * 128:
                                                       (fc + 1) * 128],
                                 'ER', queue='sync', transpose=True,
                                 waits=[kb.evt('EO')] if (tc == 0 and fc == 0)
                                 else [])
                    for ic in range(16):
                        mi = kb.T('matmul', po[:, :], wd[:, ic, fc, :],
                                  msb[:, ic, :], start=(ic == 0),
                                  stop=(ic == 15),
                                  waits=[kb.evt('Vc'), kb.evt('Sc')]
                                  if ic == 0 else ())
                    pe = kb.mark(mi, 'PE')
                    kb.V('scalar_tensor_tensor', out=ysb[:], in0=po[:, :],
                         scalar=1.0, in1=xsr[:], op0=ALU.mult, op1=ALU.add,
                         waits=[pe, xre, kb.evt('YO')])
                    ye = kb.V('tensor_tensor', out=ysb[:], in0=ysb[:],
                              in1=esr[:], op=ALU.add, waits=[ere])
                    kb.dma(d['y_out'].rearrange('(c p) t -> p c t', p=128)[
                        :, fc, sl], ysb[:], 'YO', queue='scalar',
                        waits=[ye])
        return kb.evt('YO')


def _f(a):
    return np.ascontiguousarray(a, dtype=np.float32)


def prep_weights(inp):
    w = {}
    w['identity'] = np.eye(128, dtype=np.float32)
    w['e8'] = _f(np.repeat(np.eye(8, dtype=np.float32), 4, axis=1))
    w['l0f_wlin'] = _f(inp['L0_ffn_W_lin'].T.reshape(NC_, 128, NC_, 128)
                       .transpose(1, 0, 2, 3))
    w['l0f_blin'] = _f(np.asarray(inp['L0_ffn_b_lin']).reshape(NC_, 128).T)
    w['l0f_gw'] = _f(inp['L0_ffn_gate_w'].T.reshape(NC_, 128, 8)
                     .transpose(1, 0, 2))
    w['l0f_leftc'] = _f(np.transpose(inp['L0_ffn_left'], (1, 0, 2))
                        .reshape(D, 32).reshape(NC_, 128, 32)
                        .transpose(1, 0, 2))
    w['l0f_rightc'] = _f(np.asarray(inp['L0_ffn_right']).reshape(32, D)
                         .reshape(32, NC_, 128))
    w['l3f_rw'] = _f(np.asarray(inp['L3_ffn_router_w']).reshape(NC_, 128, 8)
                     .transpose(1, 0, 2))
    w['l3f_rb'] = _f(np.asarray(inp['L3_ffn_router_b']).reshape(1, 8))
    w['l3f_ew'] = _f(np.asarray(inp['L3_ffn_expert_w'])
                     .reshape(8, NC_, 128, NC_, 128).transpose(0, 2, 1, 3, 4))
    w['l3f_ebrow'] = _f(np.asarray(inp['L3_ffn_expert_b'])
                        .reshape(1, 8, 1024))
    w['l1f_rw'] = _f(np.asarray(inp['L1_ffn_router_w']).reshape(NC_, 128, 8)
                     .transpose(1, 0, 2))
    w['l1f_w1'] = _f(np.asarray(inp['L1_ffn_w1'])
                     .reshape(8, NC_, 128, 32, 128).transpose(0, 2, 1, 3, 4))
    w['l1f_w2'] = _f(np.asarray(inp['L1_ffn_w2'])
                     .reshape(8, 32, 128, NC_, 128).transpose(0, 2, 1, 3, 4))
    w['l1f_b1'] = _f(np.asarray(inp['L1_ffn_b1']).reshape(8, 32, 128)
                     .transpose(0, 2, 1))
    w['l1f_b2'] = _f(np.asarray(inp['L1_ffn_b2']).reshape(8, NC_, 128)
                     .transpose(0, 2, 1))
    w['l3a_wq'] = _f(np.asarray(inp['L3_attn_W_qkv']).T
                     .reshape(NC_, 128, 24, 128).transpose(1, 0, 2, 3))
    w['l3a_woutT'] = _f(np.asarray(inp['L3_attn_W_out']).T
                        .reshape(NC_, 128, NC_, 128).transpose(1, 0, 2, 3))
    w['trimask'] = _f(np.triu(np.ones((128, 128), dtype=np.float32)))
    _wq2 = np.asarray(inp['L2_attn_W_qkv']).reshape(D, 16, 3, 64) \
        .transpose(0, 2, 1, 3).reshape(D, 3072)
    _bq2 = np.asarray(inp['L2_attn_b_qkv']).reshape(16, 3, 64) \
        .transpose(1, 0, 2).reshape(3072)
    w['l2a_wq'] = _f(_wq2.reshape(NC_, 128, 24, 128).transpose(1, 0, 2, 3))
    w['l2a_bqkv'] = _f(_bq2.reshape(24, 128).T)
    w['l2a_bqkv_row'] = _f(_bq2.reshape(1, 3072))
    w['l2a_wo'] = _f(np.asarray(inp['L2_attn_W_o'])
                     .reshape(NC_, 128, NC_, 128).transpose(1, 0, 2, 3))
    w['l2a_bo'] = _f(np.asarray(inp['L2_attn_b_o']).reshape(1, 1024))
    w['i2'] = _f(np.eye(2, dtype=np.float32))
    w['ones2'] = _f(np.ones((2, 128), dtype=np.float32))
    w['i2c'] = _f(np.concatenate([
        np.concatenate([np.ones((64, 1)), np.zeros((64, 1))], 1),
        np.concatenate([np.zeros((64, 1)), np.ones((64, 1))], 1)],
        0).astype(np.float32))
    _wq1 = np.asarray(inp['L1_attn_Wq']).reshape(NC_, 128, 8, 128)
    _wk1 = np.asarray(inp['L1_attn_Wk']).reshape(NC_, 128, 8, 128)
    _wqk = np.empty((128, NC_, 16, 128), np.float32)
    _wqk[:, :, 0::2, :] = _wq1.transpose(1, 0, 2, 3)
    _wqk[:, :, 1::2, :] = _wk1.transpose(1, 0, 2, 3)
    w['l1a_wqk'] = _f(_wqk)
    w['l1a_wv'] = _f(np.asarray(inp['L1_attn_Wv']).reshape(NC_, 128, NC_, 128)
                     .transpose(1, 0, 2, 3))
    w['l1a_wo'] = _f(np.asarray(inp['L1_attn_Wo']).reshape(NC_, 128, NC_, 128)
                     .transpose(1, 0, 2, 3))
    w['l1a_bo'] = _f(np.asarray(inp['L1_attn_bo']).reshape(1, 1024))
    w['l1a_thcol'] = _f(np.asarray(inp['L1_attn_theta']).reshape(8, 1))
    w['l1a_throw'] = _f(np.asarray(inp['L1_attn_theta']).reshape(1, 8))
    w['l1a_gam'] = _f(np.asarray(inp['L1_attn_gamma']).reshape(1, 1024))
    w['l1a_bet'] = _f(np.asarray(inp['L1_attn_beta']).reshape(1, 1024))
    w['l0a_ln1w'] = _f(np.asarray(inp['L0_attn_ln1_w']).reshape(NC_, 128).T)
    w['l0a_ln1b'] = _f(np.asarray(inp['L0_attn_ln1_b']).reshape(NC_, 128).T)
    w['l0a_wqkT'] = _f(np.asarray(inp['L0_attn_Wqk']).T
                       .reshape(NC_, 128, 2, 128).transpose(1, 0, 2, 3))
    w['l0a_bqk'] = _f(np.asarray(inp['L0_attn_bqk']).reshape(2, 128).T)
    w['l0a_gam'] = _f(np.asarray(inp['L0_attn_gamma']).reshape(4, 2, 128)
                      .transpose(2, 0, 1).reshape(128, 8))
    w['l0a_bet'] = _f(np.asarray(inp['L0_attn_beta']).reshape(4, 2, 128)
                      .transpose(2, 0, 1).reshape(128, 8))
    _idx = np.arange(64)
    _dist = np.clip(np.abs(_idx[:, None] - _idx[None, :]), 0, 32)
    w['l0a_bias64'] = _f(np.asarray(inp['L0_attn_pos_emb'])[_dist])
    _pos = np.arange(T, dtype=np.float64)
    _invf = 10000.0 ** (-np.arange(128, dtype=np.float64) / 128.0)
    _ang = _invf[:, None] * _pos[None, :]
    w['l0a_cosT'] = _f(np.cos(_ang))
    w['l0a_sinT'] = _f(np.sin(_ang))
    w['l0a_wvT'] = _f(np.asarray(inp['L0_attn_Wv']).T
                      .reshape(NC_, 128, 2048).transpose(1, 0, 2)
                      .reshape(128, NC_ * 2048))
    w['l0a_bv'] = _f(np.asarray(inp['L0_attn_bv']).reshape(1, 2048))
    w['l0a_wgT'] = _f(np.asarray(inp['L0_attn_Wg']).T
                      .reshape(NC_, 128, 16, 128).transpose(1, 0, 2, 3))
    w['l0a_bg'] = _f(np.asarray(inp['L0_attn_bg']).reshape(16, 128).T)
    w['l0a_woT'] = _f(np.asarray(inp['L0_attn_Wo']).T
                      .reshape(16, 128, NC_, 128).transpose(1, 0, 2, 3)
                      .reshape(128, 16 * NC_ * 128))
    w['l0a_bo'] = _f(np.asarray(inp['L0_attn_bo']).reshape(1, 1024))
    w['l0a_ln2w'] = _f(np.asarray(inp['L0_attn_ln2_w']).reshape(NC_, 128).T)
    w['l0a_ln2b'] = _f(np.asarray(inp['L0_attn_ln2_b']).reshape(NC_, 128).T)
    w['l2f_down'] = _f(inp['L2_ffn_down_embed'])
    w['l2f_up'] = _f(inp['L2_ffn_up_embed'])
    w['l2f_wqp'] = _f(np.asarray(inp['L2_ffn_w_queries']).T
                      .reshape(NC_, 128, 2, 4, 32)
                      .transpose(1, 0, 2, 3, 4).reshape(128, NC_, 8, 32))
    w['l2f_keysT'] = _f(np.asarray(inp['L2_ffn_keys'])
                        .transpose(3, 2, 0, 1).reshape(32, 8, 128))
    w['l2f_wgT'] = _f(np.asarray(inp['L2_ffn_w_gate']).T
                      .reshape(NC_, 128, 16, 128).transpose(1, 0, 2, 3))
    w['l2f_wuT'] = _f(np.asarray(inp['L2_ffn_w_up']).T
                      .reshape(NC_, 128, 16, 128).transpose(1, 0, 2, 3))
    w['l2f_wdT'] = _f(np.asarray(inp['L2_ffn_w_down']).T
                      .reshape(16, 128, NC_, 128).transpose(1, 0, 2, 3))
    w['iotak'] = np.ascontiguousarray(
        np.tile(np.arange(128, dtype=np.int32), (128, 1)))
    w['iota8_rep'] = _f(np.tile(np.arange(8, dtype=np.float32), (128, 1)))
    w['triu128'] = _f(np.triu(np.ones((128, 128), dtype=np.float32)))
    return w


L3F_WEIGHTS = [
    ('l3f_rw', [128, NC_, 8], BF16),
    ('l3f_rb', [1, 8], F32),
    ('l3f_ebrow', [1, NC_, 1024], F32),
    ('l3f_ew', [8, 128, NC_, NC_, 128], F32, 'dram'),
]

L1F_WEIGHTS = [
    ('l1f_rw', [128, NC_, 8], BF16),
    ('l1f_b1c', [128, 32], F32),
    ('l1f_b2c', [128, NC_], F32),
    ('l1f_w1c', [128, NC_, 32, 128], F32, 'dram'),
    ('l1f_w2c', [128, 32, NC_, 128], F32, 'dram'),
    ('l1f_rowidx', [128, 16], I32),
    ('iota8_rep', [128, 8], F32),
    ('triu128', [128, 128], BF16),
]

L3A_WEIGHTS = [
    ('l3a_wq', [128, NC_, 24, 128], BF16),
    ('l3a_woutT', [128, NC_, NC_, 128], BF16),
    ('trimask', [128, 128], BF16),
]

L2A_WEIGHTS = [
    ('l2a_wq', [128, NC_, 24, 128], BF16),
    ('l2a_bqkv', [128, 24], F32),
    ('l2a_bqkv_row', [1, 3072], F32),
    ('l2a_wo', [128, NC_, NC_, 128], BF16),
    ('l2a_bo', [1, 1024], F32),
    ('i2', [2, 2], F32),
    ('ones2', [2, 128], F32),
    ('i2c', [128, 2], BF16),
]

L1A_WEIGHTS = [
    ('l1a_wqk', [128, NC_, 16, 128], BF16),
    ('l1a_wv', [128, NC_, NC_, 128], BF16),
    ('l1a_wo', [128, NC_, NC_, 128], BF16),
    ('l1a_bo', [1, 1024], F32),
    ('l1a_thcol', [8, 1], F32),
    ('l1a_throw', [1, 8], F32),
    ('l1a_gam', [1, 1024], F32),
    ('l1a_bet', [1, 1024], F32),
    ('trimask', [128, 128], BF16),
    ('identity', [128, 128], BF16),
]

L0A_WEIGHTS = [
    ('l0a_wqkT', [128, NC_, 2, 128], BF16),
    ('l0a_bqk', [128, 2], F32),
    ('l0a_gam', [128, 8], F32),
    ('l0a_bet', [128, 8], F32),
    ('l0a_bias64', [64, 64], F32),
    ('l0a_bv', [1, 2048], F32),
    ('l0a_bg', [128, 16], F32),
    ('l0a_bo', [1, 1024], F32),
    ('l0a_ln2w', [128, NC_], F32),
    ('l0a_ln2b', [128, NC_], F32),
    ('identity', [128, 128], BF16),
    ('l0a_wgT', [128, NC_, 16, 128], F32, 'dram'),
    ('l0a_wvT', [128, NC_ * 2048], F32, 'dram'),
    ('l0a_woT', [128, 16 * NC_ * 128], F32, 'dram'),
]

L2F_WEIGHTS = [
    ('l2f_wqp', [128, NC_, 8, 32], BF16),
    ('l2f_keysT', [32, 8, 128], BF16),
    ('iotak', [128, 128], I32),
    ('identity', [128, 128], BF16),
    ('l2f_wgT', [128, NC_, 16, 128], F32, 'dram'),
    ('l2f_wuT', [128, NC_, 16, 128], F32, 'dram'),
    ('l2f_wdT', [128, 16, NC_, 128], F32, 'dram'),
    ('l2f_down', [16384, 1024], F32, 'dram'),
    ('l2f_up', [16384, 1024], F32, 'dram'),
]

L0F_WEIGHTS = [
    ('l0f_wlin', [128, NC_, NC_, 128], BF16),
    ('l0f_blin', [128, NC_], F32),
    ('l0f_gw', [128, NC_, 8], BF16),
    ('l0f_leftc', [128, NC_, 32], BF16),
    ('l0f_rightc', [32, NC_, 128], BF16),
    ('identity', [128, 128], BF16),
    ('e8', [8, 32], BF16),
]


def declare_and_load(kb, names_shapes, queue='gpsimd', sem='WL', waits=(), sfx=''):
    nc = kb.nc
    out = {}
    first = True
    for ent in names_shapes:
        if len(ent) == 4 and ent[3] == 'dram':
            name, shape, dt, _ = ent
            out[name] = nc.dram_tensor(name + sfx, list(shape), dt,
                                       kind='ExternalInput').ap()
            continue
        name, shape, dt = ent
        ddt = I32 if dt == I32 else F32
        dram = nc.dram_tensor(name + sfx, list(shape), ddt,
                              kind='ExternalInput').ap()
        sbuf = kb.sb(name + '_sb' + sfx, shape, dt)
        kb.dma(sbuf[:], dram[:], sem, queue=queue,
               waits=waits if first else ())
        first = False
        out[name] = sbuf
    return out


def emit_stage(kb, nc, block, x_in, y_out, fake_cc, sfx=''):
    if True:
        if block in ('l3a', 'l2a', 'l1a', 'l0a', 'l2f', 'l1f'):
            xln = kb.sb('xln' + sfx, [128, NC_, T], BF16)
            consts = {}
            consts['ones_col'] = kb.sb('ones_col' + sfx, [128, 1], F32)
            consts['ones_row'] = kb.sb('ones_row' + sfx, [1, 128], F32)
            consts['ones_col_bf'] = kb.sb('ones_col_bf' + sfx, [128, 1], BF16)
            kb.V('memset', consts['ones_col'][:], 1.0)
            kb.V('memset', consts['ones_col_bf'][:], 1.0)
            ocev = kb.V('memset', consts['ones_row'][:], 1.0)
            affine = None
            vs = vb = None
            if block == 'l0a':
                ln1w_d = nc.dram_tensor('l0a_ln1w' + sfx, [128, NC_], F32,
                                        kind='ExternalInput').ap()
                ln1b_d = nc.dram_tensor('l0a_ln1b' + sfx, [128, NC_], F32,
                                        kind='ExternalInput').ap()
                ln1w = kb.sb('ln1w_sb' + sfx, [128, NC_], F32)
                ln1b = kb.sb('ln1b_sb' + sfx, [128, NC_], F32)
                kb.dma(ln1w[:], ln1w_d[:], 'LW', queue='sync')
                lwe = kb.dma(ln1b[:], ln1b_d[:], 'LW', queue='sync')
                affine = (ln1w, ln1b)
                vs, vb = 1.0 + 1e-5, 1e-10
            with contextlib.ExitStack() as xst:
                x = kb.sb('x' + sfx, [128, NC_, T], F32, stack=xst)
                xe = kb.dma(x[:], x_in.rearrange('(c p) t -> p c t', p=128),
                            'XL')
                lev = emit_ln_fm(kb, x, xln, consts, block,
                                 x_evt=[xe, ocev] + ([lwe] if affine else []),
                                 var_scale=vs, var_bias=vb, affine=affine)
            sbufs = declare_and_load(kb, globals()[block.upper() + '_WEIGHTS'],
                                     waits=[lev], sfx=sfx)
            if block == 'l3a':
                dd = dict(xln=xln, wevt=[kb.evt('WL')], lnevt=lev,
                          consts=consts, wq=sbufs['l3a_wq'],
                          woutT=sbufs['l3a_woutT'],
                          trimask=sbufs['trimask'], x_in=x_in, y_out=y_out)
                emit_l3a(kb, dd)
            elif block == 'l0a':
                consts['ones_row512'] = kb.sb('ones_row512' + sfx, [1, 512], F32)
                kb.V('memset', consts['ones_row512'][:], 1.0)
                cosd = nc.dram_tensor('l0a_cosT' + sfx, [128, T], F32,
                                      kind='ExternalInput').ap()
                sind = nc.dram_tensor('l0a_sinT' + sfx, [128, T], F32,
                                      kind='ExternalInput').ap()
                wvo = kb.sb('wvo_sb' + sfx, [128, NC_ * 2048], BF16)
                kb.dma(wvo[:], sbufs['l0a_wvT'][:], 'WV', queue='gpsimd')
                dd = dict(xg=xln, wevt=[kb.evt('WL')], lnevt=lev,
                          consts=consts, wqkT=sbufs['l0a_wqkT'],
                          bqk=sbufs['l0a_bqk'], gam=sbufs['l0a_gam'],
                          bet=sbufs['l0a_bet'], bias64=sbufs['l0a_bias64'],
                          cos_dram=cosd, sin_dram=sind,
                          idbf=sbufs['identity'],
                          wvo=wvo, bv=sbufs['l0a_bv'],
                          wgT_dram=sbufs['l0a_wgT'], bg=sbufs['l0a_bg'],
                          bo=sbufs['l0a_bo'], ln2w=sbufs['l0a_ln2w'],
                          ln2b=sbufs['l0a_ln2b'], x_in=x_in, y_out=y_out)
                dd['v_dram'] = nc.dram_tensor('v_dram' + sfx, [T, 2048], BF16,
                                              kind='Internal').ap()
                dd['wo_dram'] = sbufs['l0a_woT']
                dd['wvo_handle'] = wvo
                emit_l0a(kb, dd, trunc=globals().get('_L0A_TRUNC'))
            elif block == 'l1f':
                cmask = nc.dram_tensor('coremask' + sfx, [8, 1], F32,
                                       kind='ExternalInput').ap()
                cm_sb = kb.sb('coremask_sb' + sfx, [8, 1], F32)
                kb.dma(cm_sb[:], cmask[:], 'WL', queue='gpsimd')
                consts['iota8_rep'] = sbufs['iota8_rep']
                consts['triu128'] = sbufs['triu128']
                dd = dict(xln=xln, wevt=[kb.evt('WL')], lnevt=lev,
                          consts=consts, rw=sbufs['l1f_rw'],
                          b1c=sbufs['l1f_b1c'], b2c=sbufs['l1f_b2c'],
                          w1c_dram=sbufs['l1f_w1c'],
                          w2c_dram=sbufs['l1f_w2c'],
                          rowidx=sbufs['l1f_rowidx'],
                          coremask=cm_sb, fake_cc=fake_cc,
                          x_in=x_in, y_out=y_out)
                for nm, shp, dt in (('xc_dram', [8 * T, D], BF16),
                                    ('xf_dram', [8 * T, D], BF16),
                                    ('yc_dram', [8 * T, D], BF16),
                                    ('yf_dram', [8 * T, D], BF16),
                                    ('xseg_dram', [T, D], BF16),
                                    ('yfm_dram', [D, T], BF16),
                                    ('yfin_tok', [T, D], BF16),
                                    ('xlnfm_dram', [D, T], BF16)):
                    dd[nm] = nc.dram_tensor(nm + sfx, shp, dt,
                                            kind='Internal').ap()
                for nm, shp in (('cnt_dram', [1, 8]), ('ag_dram', [8, 8])):
                    dd[nm] = nc.dram_tensor(nm + sfx, shp, F32).ap()
                emit_l1f(kb, dd)
            elif block == 'l2f':
                dd = dict(xln=xln, wevt=[kb.evt('WL')], lnevt=lev,
                          consts=consts, wqp=sbufs['l2f_wqp'],
                          keysT=sbufs['l2f_keysT'], iotak=sbufs['iotak'],
                          idbf=sbufs['identity'],
                          wgT_dram=sbufs['l2f_wgT'],
                          wuT_dram=sbufs['l2f_wuT'],
                          wdT_dram=sbufs['l2f_wdT'],
                          down_dram=sbufs['l2f_down'],
                          up_dram=sbufs['l2f_up'], x_in=x_in, y_out=y_out)
                for nm, shp, dt in (('comb_dram', [16384, 2048], BF16),
                                    ('es_dram', [T, D], BF16),
                                    ('xlnfm_dram', [D, T], BF16)):
                    dd[nm] = nc.dram_tensor(nm + sfx, shp, dt, kind='Internal').ap()
                emit_l2f(kb, dd)
            elif block == 'l1a':
                consts['ones_row512'] = kb.sb('ones_row512' + sfx, [1, 512], F32)
                kb.V('memset', consts['ones_row512'][:], 1.0)
                dd = dict(xln=xln, wevt=[kb.evt('WL')], lnevt=lev,
                          consts=consts, wqk=sbufs['l1a_wqk'],
                          wv=sbufs['l1a_wv'], wo=sbufs['l1a_wo'],
                          bo=sbufs['l1a_bo'], thcol=sbufs['l1a_thcol'],
                          throw=sbufs['l1a_throw'], gam=sbufs['l1a_gam'],
                          bet=sbufs['l1a_bet'], trimask=sbufs['trimask'],
                          idbf=sbufs['identity'], x_in=x_in, y_out=y_out)
                dd['z_dram'] = nc.dram_tensor('z_dram' + sfx, [D, T], BF16,
                                              kind='Internal').ap()
                emit_l1a(kb, dd)
            elif block == 'l2a':
                consts['ones_row512'] = kb.sb('ones_row512' + sfx, [1, 512], F32)
                kb.V('memset', consts['ones_row512'][:], 1.0)
                dd = dict(xln=xln, wevt=[kb.evt('WL')], lnevt=lev,
                          consts=consts, wq=sbufs['l2a_wq'],
                          bqkv=sbufs['l2a_bqkv'],
                          bqkv_row=sbufs['l2a_bqkv_row'],
                          wo=sbufs['l2a_wo'], bo=sbufs['l2a_bo'],
                          i2=sbufs['i2'], ones2=sbufs['ones2'],
                          i2c=sbufs['i2c'], x_in=x_in, y_out=y_out)
                emit_l2a(kb, dd)
            return
        x = kb.sb('x' + sfx, [128, NC_, T], F32)
        xln = kb.sb('xln' + sfx, [128, NC_, T], BF16)
        xe = kb.dma(x[:], x_in.rearrange('(c p) t -> p c t', p=128), 'XL')

        consts = {}
        consts['ones_col'] = kb.sb('ones_col' + sfx, [128, 1], F32)
        consts['ones_row'] = kb.sb('ones_row' + sfx, [1, 128], F32)
        consts['ones_col_bf'] = kb.sb('ones_col_bf' + sfx, [128, 1], BF16)
        kb.V('memset', consts['ones_col'][:], 1.0)
        kb.V('memset', consts['ones_col_bf'][:], 1.0)
        ocev = kb.V('memset', consts['ones_row'][:], 1.0)

        if block == 'l0f':
            sbufs = declare_and_load(kb, L0F_WEIGHTS, sfx=sfx)
            lev = emit_ln_fm(kb, x, xln, consts, 'l0f', x_evt=[xe, ocev])
            dd = dict(x=x, xln=xln, wevt=[kb.evt('WL')], lnevt=lev,
                      consts=consts, idbf=sbufs['identity'], e8=sbufs['e8'],
                      wlin=sbufs['l0f_wlin'], blin=sbufs['l0f_blin'],
                      gw=sbufs['l0f_gw'], leftc=sbufs['l0f_leftc'],
                      rightc=sbufs['l0f_rightc'])
            oev = emit_l0f(kb, dd)
        elif block == 'l3f':
            sbufs = declare_and_load(kb, L3F_WEIGHTS, sfx=sfx)
            lev = emit_ln_fm(kb, x, xln, consts, 'l3f', x_evt=[xe, ocev])
            dd = dict(x=x, xln=xln, wevt=[kb.evt('WL')], lnevt=lev,
                      consts=consts, rw=sbufs['l3f_rw'], rb=sbufs['l3f_rb'],
                      ebrow=sbufs['l3f_ebrow'], ew_dram=sbufs['l3f_ew'])
            dd['yfin_tok'] = nc.dram_tensor('yfin_tok' + sfx, [T, D], BF16,
                                            kind='Internal').ap()
            oev = emit_l3f(kb, dd)
        else:
            raise ValueError(block)

        kb.dma(y_out.rearrange('(c p) t -> p c t', p=128), x[:], 'ST',
               waits=[oev])
    return




def _stage_barrier(kb):
    for eng in (kb.nc.tensor, kb.nc.vector, kb.nc.scalar, kb.nc.gpsimd,
                kb.nc.sync):
        for name, c in list(kb.cnt.items()):
            if c > 0:
                eng.wait_ge(kb.sem(name), c)


def build_block_program(block, fake_cc=False):
    nc = bass.Bass('TRN2', target_bir_lowering=False)
    with contextlib.ExitStack() as stack:
        kb = KB(nc, stack)
        x_in = nc.dram_tensor('x_fm', [D, T], F32, kind='ExternalInput').ap()
        y_out = nc.dram_tensor('y_fm', [D, T], F32,
                               kind='ExternalOutput').ap()
        emit_stage(kb, nc, block, x_in, y_out, fake_cc)
        for name in list(kb.cnt):
            if kb.cnt[name] > 0:
                kb.nc.sync.wait_ge(kb.sem(name), kb.cnt[name])
    return nc


def build_fused_program(fake_cc=False):
    nc = bass.Bass('TRN2', target_bir_lowering=False)
    with contextlib.ExitStack() as stack:
        kb = KB(nc, stack)
        x_ext = nc.dram_tensor('x_fm', [D, T], F32, kind='ExternalInput').ap()
        y_ext = nc.dram_tensor('y_fm', [D, T], F32,
                               kind='ExternalOutput').ap()
        chain = [x_ext]
        for k in range(7):
            chain.append(nc.dram_tensor('xchain%d' % k, [D, T], F32,
                                        kind='Internal').ap())
        chain.append(y_ext)
        outer = kb.stack
        for k, block in enumerate(BLOCK_ORDER):
            with contextlib.ExitStack() as sst:
                kb.stack = sst
                emit_stage(kb, nc, block, chain[k], chain[k + 1], fake_cc,
                           sfx='_s%d' % k)
                _stage_barrier(kb)
            kb.stack = outer
        for name in list(kb.cnt):
            if kb.cnt[name] > 0:
                kb.nc.sync.wait_ge(kb.sem(name), kb.cnt[name])
    return nc

BLOCK_ORDER = ['l0a', 'l0f', 'l1a', 'l1f', 'l2a', 'l2f', 'l3a', 'l3f']

BLOCK_EXTRAS = {
    'l0a': ['l0a_ln1w', 'l0a_ln1b', 'l0a_cosT', 'l0a_sinT'],
}

_PROG_CACHE = {}


def _get_prog(block):
    if block not in _PROG_CACHE:
        _PROG_CACHE[block] = build_block_program(block)
    return _PROG_CACHE[block]


def _block_input_names(block):
    names = [e[0] for e in globals()[block.upper() + '_WEIGHTS']
             if e[0] not in ('l1f_w1c', 'l1f_w2c', 'l1f_b1c', 'l1f_b2c',
                             'l1f_rowidx')]
    names += BLOCK_EXTRAS.get(block, [])
    return names


def _percore_extras(m, block, c, w, sfx=''):
    if block == 'l1f':
        cm = np.zeros((8, 1), np.float32)
        cm[:c] = 1.0
        m['coremask' + sfx] = cm
        m['l1f_w1c' + sfx] = w['l1f_w1'][c]
        m['l1f_w2c' + sfx] = w['l1f_w2'][c]
        m['l1f_b1c' + sfx] = w['l1f_b1'][c]
        m['l1f_b2c' + sfx] = w['l1f_b2'][c]
        m['l1f_rowidx' + sfx] = np.ascontiguousarray(
            np.arange(c * T, (c + 1) * T, dtype=np.int32)
            .reshape(16, 128).T)


def kernel(**inputs):
    w = prep_weights(inputs)
    if 'prog' not in _PROG_CACHE:
        _PROG_CACHE['prog'] = build_fused_program()
    nc = _PROG_CACHE['prog']
    x = np.asarray(inputs['L0_attn_x'], dtype=np.float32)
    maps = []
    for c in range(8):
        m = {'x_fm': np.ascontiguousarray(x[c].T)}
        for k, block in enumerate(BLOCK_ORDER):
            sfx = '_s%d' % k
            for n in _block_input_names(block):
                m[n + sfx] = w[n]
            _percore_extras(m, block, c, w, sfx)
        maps.append(m)
    res = run_bass_kernel_spmd(nc, maps, core_ids=list(range(8)))
    out = np.stack([np.ascontiguousarray(np.asarray(
        res.results[c]['y_fm']).T) for c in range(8)])
    return out.astype(np.float32)


def kernel_unfused(**inputs):
    w = prep_weights(inputs)
    x = np.asarray(inputs['L0_attn_x'], dtype=np.float32)
    xfm = [np.ascontiguousarray(x[c].T) for c in range(8)]
    for block in BLOCK_ORDER:
        nc = _get_prog(block)
        maps = []
        for c in range(8):
            m = {'x_fm': xfm[c]}
            for n in _block_input_names(block):
                m[n] = w[n]
            _percore_extras(m, block, c, w)
            maps.append(m)
        res = run_bass_kernel_spmd(nc, maps, core_ids=list(range(8)))
        xfm = [np.asarray(res.results[c]['y_fm']) for c in range(8)]
    out = np.stack([np.ascontiguousarray(xfm[c].T) for c in range(8)])
    return out.astype(np.float32)


# version 04
# speedup vs baseline: 1.0902x; 1.0902x over previous
import contextlib
import numpy as np
import ml_dtypes
import concourse.bass as bass
import concourse.mybir as mybir
from concourse.bass_utils import run_bass_kernel_spmd

F32 = mybir.dt.float32
BF16 = mybir.dt.bfloat16
I32 = mybir.dt.int32
AF = mybir.ActivationFunctionType
ALU = mybir.AluOpType
AX = mybir.AxisListType

T = 2048
D = 1024
NC_ = 8
bf16 = ml_dtypes.bfloat16


class KB:

    def __init__(self, nc, stack):
        self.nc = nc
        self.stack = stack
        self._sems = {}
        self.cnt = {}

    def sem(self, name):
        if name not in self._sems:
            self._sems[name] = self.stack.enter_context(self.nc.semaphore(name))
            self.cnt[name] = 0
        return self._sems[name]

    def _waits(self, eng, waits):
        for (s, c) in waits:
            if c > 0:
                eng.wait_ge(self.sem(s), c)

    def _chain(self, engname, semname, fn, *a, waits=(), **k):
        eng = getattr(self.nc, engname)
        s = self.sem(semname)
        if self.cnt[semname] > 0:
            eng.wait_ge(s, self.cnt[semname])
        self._waits(eng, waits)
        instr = getattr(eng, fn)(*a, **k)
        instr.then_inc(s, 1)
        self.cnt[semname] += 1
        return (semname, self.cnt[semname])

    def V(self, fn, *a, waits=(), **k):
        return self._chain('vector', 'Vc', fn, *a, waits=waits, **k)

    def S(self, fn, *a, waits=(), **k):
        return self._chain('scalar', 'Sc', fn, *a, waits=waits, **k)

    def G(self, fn, *a, waits=(), **k):
        return self._chain('gpsimd', 'Gc', fn, *a, waits=waits, **k)

    def T(self, fn, *a, waits=(), **k):
        eng = self.nc.tensor
        self._waits(eng, waits)
        return getattr(eng, fn)(*a, **k)

    def t_wait(self, waits):
        self._waits(self.nc.tensor, waits)

    def mark(self, instr, semname='PE'):
        instr.then_inc(self.sem(semname), 1)
        self.cnt[semname] += 1
        return (semname, self.cnt[semname])

    def dma(self, out, in_, sem, queue='sync', waits=(), **k):
        eng = getattr(self.nc, queue)
        self._waits(eng, waits)
        instr = eng.dma_start(out=out, in_=in_, **k)
        instr.then_inc(self.sem(sem), 16)
        self.cnt[sem] += 16
        return (sem, self.cnt[sem])

    def idma(self, out, out_offset, in_, in_offset, sem, waits=(), **k):
        eng = self.nc.gpsimd
        self._waits(eng, waits)
        instr = eng.indirect_dma_start(out, out_offset, in_, in_offset, **k)
        instr.then_inc(self.sem(sem), 16)
        self.cnt[sem] += 16
        return (sem, self.cnt[sem])

    def evt(self, semname):
        return (semname, self.cnt.get(semname, 0))

    def sb(self, name, shape, dtype, stack=None):
        return (stack or self.stack).enter_context(
            self.nc.sbuf_tensor(name, list(shape), dtype))

    def ps(self, name, shape, dtype=F32, stack=None):
        return (stack or self.stack).enter_context(
            self.nc.psum_tensor(name, list(shape), dtype))


def emit_ln_fm(kb, x, xln, consts, pfx, x_evt=(), eps=1e-5, var_scale=None,
               var_bias=None, affine=None):
    with contextlib.ExitStack() as st:
        sq = kb.sb(pfx + '_lnsq', [128, NC_, 512], F32, stack=st)
        stats = kb.sb(pfx + '_lnst', [1, 2, T], F32, stack=st)
        mr = kb.sb(pfx + '_lnmr', [1, T], F32, stack=st)
        rr = kb.sb(pfx + '_lnrr', [1, T], F32, stack=st)
        ps_s = kb.ps(pfx + '_lnps', [1, 512], stack=st)
        ps_q = kb.ps(pfx + '_lnpq', [1, 512], stack=st)
        ps_m = kb.ps(pfx + '_lnpm', [128, 512], stack=st)
        ps_r = kb.ps(pfx + '_lnpr', [128, 512], stack=st)
        eps_c = kb.sb(pfx + '_lneps', [1, 1], F32, stack=st)
        kb.V('memset', eps_c[:], float(eps))
        ones_col = consts['ones_col']
        ones_row = consts['ones_row']

        for tc in range(4):
            sl = slice(tc * 512, (tc + 1) * 512)
            for c in range(NC_):
                sev = kb.V('tensor_tensor', out=sq[:, c, :], in0=x[:, c, sl],
                           in1=x[:, c, sl], op=ALU.mult,
                           waits=(list(x_evt) + [kb.evt('PE')]) if c == 0 else ())
            for c in range(NC_):
                kb.T('matmul', ps_s[:, :], ones_col[:], x[:, c, sl],
                     start=(c == 0), stop=(c == NC_ - 1),
                     waits=(list(x_evt) + [kb.evt('Vc')]) if (tc == 0 and c == 0) else ())
            for c in range(NC_):
                mi = kb.T('matmul', ps_q[:, :], ones_col[:], sq[:, c, :],
                          start=(c == 0), stop=(c == NC_ - 1),
                          waits=[sev] if c == 0 else ())
            pe = kb.mark(mi, 'PE')
            kb.V('tensor_copy', out=stats[:, 0, sl], in_=ps_s[:, :], waits=[pe])
            cev = kb.V('tensor_copy', out=stats[:, 1, sl], in_=ps_q[:, :])
            kb.t_wait([cev])

        kb.V('tensor_scalar', out=mr[:], in0=stats[:, 0, :],
             scalar1=1.0 / D, scalar2=None, op0=ALU.mult)
        kb.V('tensor_scalar', out=stats[:, 1, :], in0=stats[:, 1, :],
             scalar1=1.0 / D, scalar2=None, op0=ALU.mult)
        kb.V('tensor_tensor', out=stats[:, 0, :], in0=mr[:], in1=mr[:],
             op=ALU.mult)
        kb.V('tensor_tensor', out=stats[:, 1, :], in0=stats[:, 1, :],
             in1=stats[:, 0, :], op=ALU.subtract)
        if var_scale is not None:
            kb.V('tensor_scalar', out=stats[:, 1, :], in0=stats[:, 1, :],
                 scalar1=float(var_scale), scalar2=float(var_bias),
                 op0=ALU.mult, op1=ALU.add)
            kb.V('memset', eps_c[:], 0.0)
        sev = kb.S('activation', out=stats[:, 1, :], in_=stats[:, 1, :],
                   func=AF.Sqrt, bias=eps_c[:], scale=1.0,
                   waits=[kb.evt('Vc')])
        rev = kb.V('reciprocal', out=rr[:], in_=stats[:, 1, :], waits=[sev])

        for tc in range(4):
            sl = slice(tc * 512, (tc + 1) * 512)
            kb.T('matmul', ps_m[:, :], ones_row[:], mr[:, sl], start=True,
                 stop=True, waits=[rev] if tc == 0 else ())
            mi = kb.T('matmul', ps_r[:, :], ones_row[:], rr[:, sl], start=True,
                      stop=True)
            pe = kb.mark(mi, 'PE')
            for c in range(NC_):
                kb.V('tensor_tensor', out=sq[:, c, :], in0=x[:, c, sl],
                     in1=ps_m[:, :], op=ALU.subtract,
                     waits=[pe] if c == 0 else ())
                if affine is None:
                    oev = kb.V('tensor_tensor', out=xln[:, c, sl],
                               in0=sq[:, c, :], in1=ps_r[:, :], op=ALU.mult)
                else:
                    kb.V('tensor_tensor', out=sq[:, c, :], in0=sq[:, c, :],
                         in1=ps_r[:, :], op=ALU.mult)
                    oev = kb.V('tensor_scalar', out=xln[:, c, sl],
                               in0=sq[:, c, :],
                               scalar1=affine[0][:, c:c + 1],
                               scalar2=affine[1][:, c:c + 1],
                               op0=ALU.mult, op1=ALU.add)
            kb.t_wait([oev])
        return oev


def emit_l0f(kb, d):
    x, xln = d['x'], d['xln']
    wev, lev = d['wevt'], d['lnevt']
    with contextlib.ExitStack() as st:
        gval = kb.sb('l0f_g', [128, 16, 8], BF16, stack=st)
        gtmp = kb.sb('l0f_gt', [128, 8], F32, stack=st)
        m8 = kb.sb('l0f_m8', [128, 8], F32, stack=st)
        nm1 = kb.sb('l0f_nm1', [128, 1], F32, stack=st)
        ssum = kb.sb('l0f_ss', [128, 1], F32, stack=st)
        srec = kb.sb('l0f_sr', [128, 1], F32, stack=st)
        mask = kb.sb('l0f_msk', [128, 8], F32, stack=st)
        gT = kb.sb('l0f_gTs', [8, 16, 128], BF16, stack=st)
        tall = kb.sb('l0f_tall', [32, T], BF16, stack=st)
        u = kb.sb('l0f_u', [32, T], BF16, stack=st)
        glog = kb.ps('l0f_glog', [128, 128], stack=st)
        ps_gt = kb.ps('l0f_psgt', [8, 128], BF16, stack=st)
        ps_t = kb.ps('l0f_pst', [32, 512], stack=st)
        ps_gr = kb.ps('l0f_psgr', [32, 512], stack=st)
        ps_o0 = kb.ps('l0f_o0', [128, 512], stack=st)
        ps_o1 = kb.ps('l0f_o1', [128, 512], stack=st)

        for tt in range(16):
            lg = glog[:, (tt % 2) * 8:(tt % 2) * 8 + 8]
            for dc in range(NC_):
                mi = kb.T('matmul', lg, xln[:, dc, tt * 128:(tt + 1) * 128],
                          d['gw'][:, dc, :], start=(dc == 0),
                          stop=(dc == NC_ - 1),
                          waits=(list(wev) + [lev]) if (tt == 0 and dc == 0)
                          else ([kb.evt('Vc')] if dc == 0 else ()))
            pe = kb.mark(mi, 'PE')
            kb.V('max', out=m8[:], in_=lg, waits=[pe])
            kb.V('tensor_scalar', out=nm1[:], in0=m8[:, 0:1], scalar1=-1.0,
                 scalar2=None, op0=ALU.mult)
            ev = kb.S('activation', out=gtmp[:], in_=lg, func=AF.Exp,
                      bias=nm1[:], scale=1.0, waits=[kb.evt('Vc')])
            kb.V('tensor_reduce', out=ssum[:], in_=gtmp[:], axis=AX.X,
                 op=ALU.add, waits=[ev])
            kb.V('reciprocal', out=srec[:], in_=ssum[:])
            kb.V('tensor_scalar', out=mask[:], in0=lg, scalar1=m8[:, 1:2],
                 scalar2=None, op0=ALU.is_ge)
            gev = kb.V('scalar_tensor_tensor', out=gval[:, tt, :], in0=gtmp[:],
                       scalar=srec[:], in1=mask[:], op0=ALU.mult, op1=ALU.mult)
            tr = kb.T('transpose', ps_gt[:, :], gval[:, tt, :], d['idbf'][:],
                      waits=[gev])
            pe = kb.mark(tr, 'PE')
            cev = kb.V('tensor_copy', out=gT[:, tt, :], in_=ps_gt[:, :],
                       waits=[pe])
            kb.t_wait([cev])

        for tc in range(4):
            sl = slice(tc * 512, (tc + 1) * 512)
            for dc in range(NC_):
                mi = kb.T('matmul', ps_t[:, :], d['leftc'][:, dc, :],
                          xln[:, dc, sl], start=(dc == 0),
                          stop=(dc == NC_ - 1))
            pe1 = kb.mark(mi, 'PE')
            mi = kb.T('matmul', ps_gr[:, :], d['e8'][:],
                      gT[:, tc * 4:(tc + 1) * 4, :].rearrange(
                          'p a b -> p (a b)'), start=True, stop=True)
            pe2 = kb.mark(mi, 'PE')
            kb.V('tensor_copy', out=tall[:, sl], in_=ps_t[:, :], waits=[pe1])
            uev = kb.V('tensor_tensor', out=u[:, sl], in0=tall[:, sl],
                       in1=ps_gr[:, :], op=ALU.mult, waits=[pe2])
            kb.t_wait([uev])

        for fc in range(NC_):
            for tc in range(4):
                sl = slice(tc * 512, (tc + 1) * 512)
                pso = ps_o0 if (tc % 2 == 0) else ps_o1
                for dc in range(NC_):
                    kb.T('matmul', pso[:, :], d['wlin'][:, dc, fc, :],
                         xln[:, dc, sl], start=(dc == 0), stop=False,
                         waits=[uev] if (fc == 0 and tc == 0 and dc == 0)
                         else ())
                mi = kb.T('matmul', pso[:, :], d['rightc'][:, fc, :], u[:, sl],
                          start=False, stop=True)
                pe = kb.mark(mi, 'PE')
                rev = kb.V('scalar_tensor_tensor', out=x[:, fc, sl],
                           in0=pso[:, :], scalar=d['blin'][:, fc:fc + 1],
                           in1=x[:, fc, sl], op0=ALU.add, op1=ALU.add,
                           waits=[pe])
                kb.t_wait([rev])
        return rev


SEG = 384
NSLOT = 8 * SEG


def emit_router_scatter(kb, d, pfx, st):
    pst = st.enter_context(contextlib.ExitStack())
    lg_ps = kb.ps(pfx + '_lg', [128, 128], stack=pst)
    pos_ps = kb.ps(pfx + '_pos', [128, 128], stack=pst)
    cnt1_ps = kb.ps(pfx + '_c1', [1, 8], stack=pst)
    cntr_ps = kb.ps(pfx + '_cr', [128, 8], stack=pst)
    m8 = kb.sb(pfx + '_m8', [128, 8], F32, stack=st)
    emask = kb.sb(pfx + '_oh', [128, 16, 8], BF16, stack=st)
    ohf = kb.sb(pfx + '_ohf', [128, 8], F32, stack=st)
    gate = kb.sb(pfx + '_gate', [128, 16], F32, stack=st)
    idxf = kb.sb(pfx + '_idxf', [128, 16], F32, stack=st)
    posd = kb.sb(pfx + '_posd', [128, 16], F32, stack=st)
    slotf = kb.sb(pfx + '_slotf', [128, 16], F32, stack=st)
    slot_i = kb.sb(pfx + '_sloti', [128, 16], I32, stack=st)
    tmp8 = kb.sb(pfx + '_tmp8', [128, 8], F32, stack=st)
    etmp = kb.sb(pfx + '_etmp', [128, 8], F32, stack=st)
    ovf = kb.sb(pfx + '_ovf', [128, 1], F32, stack=st)
    cnt_row = kb.sb(pfx + '_cnt', [1, 8], F32, stack=st)
    offrep = kb.sb(pfx + '_offrep', [128, 8], F32, stack=st)
    ssum = kb.sb(pfx + '_ssum', [128, 1], F32, stack=st)
    nm1 = kb.sb(pfx + '_nm1', [128, 1], F32, stack=st)
    xtok = kb.sb(pfx + '_xtok', [128, 1024], BF16, stack=st)
    kb.V('memset', offrep[:], 0.0)

    for c in range(NC_):
        kb.dma(d['xlnfm_dram'][c * 128:(c + 1) * 128, :], d['xln'][:, c, :],
               'XW', queue='sync', waits=[d['lnevt']] if c == 0 else ())
    tile_ev = {}
    for tt in range(16):
        lg = lg_ps[:, (tt % 2) * 8:(tt % 2) * 8 + 8]
        have_rb = d.get('rb') is not None
        for dc in range(NC_):
            mi = kb.T('matmul', lg, d['xln'][:, dc, tt * 128:(tt + 1) * 128],
                      d['rw'][:, dc, :], start=(dc == 0),
                      stop=(dc == NC_ - 1 and not have_rb),
                      waits=(list(d['wevt']) + [d['lnevt']])
                      if (tt == 0 and dc == 0)
                      else ([kb.evt('Vc'), kb.evt('Sc')] if dc == 0 else ()))
        if have_rb:
            mi = kb.T('matmul', lg, d['consts']['ones_row'][:], d['rb'][:],
                      start=False, stop=True)
        pe = kb.mark(mi, 'PE')
        kb.V('max', out=m8[:], in_=lg, waits=[pe])
        kb.V('tensor_scalar', out=nm1[:], in0=m8[:, 0:1], scalar1=-1.0,
             scalar2=None, op0=ALU.mult)
        ev = kb.S('activation', out=tmp8[:], in_=lg, func=AF.Exp,
                  bias=nm1[:], scale=1.0, waits=[kb.evt('Vc')])
        kb.V('tensor_reduce', out=ssum[:], in_=tmp8[:], axis=AX.X,
             op=ALU.add, waits=[ev])
        kb.V('reciprocal', out=gate[:, tt:tt + 1], in_=ssum[:])
        kb.V('tensor_scalar', out=emask[:, tt, :], in0=lg, scalar1=m8[:, 0:1],
             scalar2=None, op0=ALU.is_ge)
        kb.V('tensor_copy', out=ohf[:], in_=emask[:, tt, :])
        kb.V('tensor_tensor', out=etmp[:], in0=ohf[:],
             in1=d['consts']['iota8_rep'][:], op=ALU.mult)
        kb.V('tensor_reduce', out=idxf[:, tt:tt + 1], in_=etmp[:], axis=AX.X,
             op=ALU.add)
        oev = kb.evt('Vc')
        mi = kb.T('matmul', pos_ps[:, (tt % 2) * 8:(tt % 2) * 8 + 8],
                  d['consts']['triu128'][:], emask[:, tt, :],
                  start=True, stop=True, waits=[oev])
        mi2 = kb.T('matmul', cnt1_ps[:, :], d['consts']['ones_col_bf'][:],
                   emask[:, tt, :], start=True, stop=True)
        pe = kb.mark(mi2, 'PE')
        pos = pos_ps[:, (tt % 2) * 8:(tt % 2) * 8 + 8]
        kb.V('tensor_tensor', out=etmp[:], in0=ohf[:], in1=pos, op=ALU.mult,
             waits=[pe])
        kb.V('tensor_reduce', out=posd[:, tt:tt + 1], in_=etmp[:], axis=AX.X,
             op=ALU.add)
        kb.V('tensor_tensor', out=etmp[:], in0=ohf[:], in1=offrep[:],
             op=ALU.mult)
        kb.V('tensor_reduce', out=tmp8[:, 0:1], in_=etmp[:], axis=AX.X,
             op=ALU.add)
        kb.V('scalar_tensor_tensor', out=posd[:, tt:tt + 1],
             in0=posd[:, tt:tt + 1], scalar=-1.0, in1=tmp8[:, 0:1],
             op0=ALU.add, op1=ALU.add)
        kb.V('scalar_tensor_tensor', out=slotf[:, tt:tt + 1],
             in0=idxf[:, tt:tt + 1], scalar=float(SEG),
             in1=posd[:, tt:tt + 1], op0=ALU.mult, op1=ALU.add)
        kb.V('tensor_scalar', out=ovf[:], in0=posd[:, tt:tt + 1],
             scalar1=float(SEG), scalar2=100000.0, op0=ALU.is_ge, op1=ALU.mult)
        kb.V('tensor_tensor', out=slotf[:, tt:tt + 1],
             in0=slotf[:, tt:tt + 1], in1=ovf[:], op=ALU.add)
        cev = kb.V('tensor_copy', out=slot_i[:, tt:tt + 1],
                   in_=slotf[:, tt:tt + 1])
        kb.V('tensor_copy', out=cnt_row[:], in_=cnt1_ps[:, :])
        crev = kb.evt('Vc')
        mi = kb.T('matmul', cntr_ps[:, :], d['consts']['ones_row'][:],
                  cnt_row[:], start=True, stop=True, waits=[crev])
        pe2 = kb.mark(mi, 'PE')
        uev = kb.V('tensor_tensor', out=offrep[:], in0=offrep[:],
                   in1=cntr_ps[:, :], op=ALU.add, waits=[pe2])
        kb.t_wait([uev])
        tile_ev[tt] = cev
    sev = None
    for tt in range(16):
        tl = kb.dma(xtok[:], d['xlnfm_dram'][:, tt * 128:(tt + 1) * 128],
                    'TL', queue='sync', transpose=True,
                    waits=[kb.evt('SC')] if tt > 0 else [kb.evt('XW')])
        sev = kb.idma(d['xsort'], bass.IndirectOffsetOnAxis(
            ap=slot_i[:, tt:tt + 1], axis=0), xtok[:], None, 'SC',
            waits=[tl, tile_ev[tt], kb.evt('ZF')],
            bounds_check=NSLOT - 1, oob_is_err=False)
    d['slot_i'] = slot_i
    d['gate'] = gate
    d['emask_bf'] = emask
    d['idxf'] = idxf
    d['posd'] = posd
    kb._l1f_offrep = offrep
    kb.t_wait([kb.evt('Vc'), kb.evt('Sc')])
    pst.close()
    return sev


def emit_zero_fill(kb, d, pfx, st, bufs):
    zb = kb.sb(pfx + '_zb', [128, 1024], BF16, stack=st)
    zev = kb.V('memset', zb[:], 0.0)
    for buf, rows in bufs:
        for i in range(rows // 128):
            kb.dma(buf[i * 128:(i + 1) * 128, :], zb[:], 'ZF',
                   queue='scalar', waits=[zev] if i == 0 else ())


def emit_l3f(kb, d):
    with contextlib.ExitStack() as st:
        xln = d['xln']
        wev, lev = d['wevt'], d['lnevt']
        lg_ps = kb.ps('l3f_lg', [128, 128], stack=st)
        m8 = kb.sb('l3f_m8', [128, 8], F32, stack=st)
        ohf = kb.sb('l3f_ohf', [128, 16, 8], F32, stack=st)
        gate = kb.sb('l3f_gate', [128, 16], F32, stack=st)
        tmp8 = kb.sb('l3f_tmp8', [128, 8], F32, stack=st)
        ssum = kb.sb('l3f_ssum', [128, 1], F32, stack=st)
        nm1 = kb.sb('l3f_nm1', [128, 1], F32, stack=st)
        acc = kb.sb('l3f_acc', [128, 8, 1024], F32, stack=st)
        ysb = kb.sb('l3f_ysb', [128, 1024], BF16, stack=st)
        ew_sb = [kb.sb('l3f_ew0', [128, NC_, NC_, 128], BF16, stack=st),
                 kb.sb('l3f_ew1', [128, NC_, NC_, 128], BF16, stack=st)]
        ps_y = [kb.ps('l3f_py0', [128, 512], stack=st),
                kb.ps('l3f_py1', [128, 512], stack=st)]
        for tt in range(16):
            lg = lg_ps[:, (tt % 2) * 8:(tt % 2) * 8 + 8]
            for dc in range(NC_):
                kb.T('matmul', lg, xln[:, dc, tt * 128:(tt + 1) * 128],
                     d['rw'][:, dc, :], start=(dc == 0), stop=False,
                     waits=(list(wev) + [lev]) if (tt == 0 and dc == 0)
                     else ([kb.evt('Vc'), kb.evt('Sc')] if dc == 0 else ()))
            mi = kb.T('matmul', lg, d['consts']['ones_row'][:], d['rb'][:],
                      start=False, stop=True)
            pe = kb.mark(mi, 'PE')
            kb.V('max', out=m8[:], in_=lg, waits=[pe])
            kb.V('tensor_scalar', out=nm1[:], in0=m8[:, 0:1], scalar1=-1.0,
                 scalar2=None, op0=ALU.mult)
            ev = kb.S('activation', out=tmp8[:], in_=lg, func=AF.Exp,
                      bias=nm1[:], scale=1.0, waits=[kb.evt('Vc')])
            kb.V('tensor_reduce', out=ssum[:], in_=tmp8[:], axis=AX.X,
                 op=ALU.add, waits=[ev])
            kb.V('reciprocal', out=gate[:, tt:tt + 1], in_=ssum[:])
            kb.V('tensor_scalar', out=ohf[:, tt, :], in0=lg,
                 scalar1=m8[:, 0:1], scalar2=None, op0=ALU.is_ge)
            ge = kb.V('tensor_scalar', out=ohf[:, tt, :], in0=ohf[:, tt, :],
                      scalar1=gate[:, tt:tt + 1], scalar2=None, op0=ALU.mult)
            kb.t_wait([ge])
        for half in range(2):
            for e in range(8):
                b = e % 2
                wl = kb.dma(ew_sb[b][:], d['ew_dram'][e], 'EW%d' % b,
                            queue='gpsimd',
                            waits=[kb.evt('PE'), kb.evt('Vc')])
                for ti in range(8):
                    tt = half * 8 + ti
                    tsl = slice(tt * 128, (tt + 1) * 128)
                    for h2 in range(2):
                        py = ps_y[h2]
                        for hc in range(NC_):
                            kb.T('matmul', py[:, :], xln[:, hc, tsl],
                                 ew_sb[b][:, hc, 4 * h2:4 * h2 + 4, :]
                                 .rearrange('p a b -> p (a b)'),
                                 start=(hc == 0), stop=False,
                                 waits=[wl, kb.evt('Vc')]
                                 if (ti == 0 and h2 == 0 and hc == 0)
                                 else ([kb.evt('Vc')] if hc == 0 else ()))
                        mi = kb.T('matmul', py[:, :],
                                  d['consts']['ones_row'][:],
                                  d['ebrow'][:, e,
                                             512 * h2:512 * h2 + 512],
                                  start=False, stop=True)
                        pe = kb.mark(mi, 'PE')
                    for h2 in range(2):
                        csl = slice(h2 * 512, (h2 + 1) * 512)
                        if e == 0:
                            kb.V('tensor_scalar', out=acc[:, ti, csl],
                                 in0=ps_y[h2][:, :],
                                 scalar1=ohf[:, tt, 0:1], scalar2=None,
                                 op0=ALU.mult, waits=[pe])
                        else:
                            kb.V('scalar_tensor_tensor', out=acc[:, ti, csl],
                                 in0=ps_y[h2][:, :],
                                 scalar=ohf[:, tt, e:e + 1],
                                 in1=acc[:, ti, csl], op0=ALU.mult,
                                 op1=ALU.add, waits=[pe])
                    if e == 7:
                        ce = kb.V('tensor_copy', out=ysb[:],
                                  in_=acc[:, ti, :], waits=[kb.evt('FW')])
                        kb.dma(d['yfin_tok'][tsl, :], ysb[:], 'FW',
                               queue='scalar', waits=[ce])
                    kb.t_wait([kb.evt('Vc')])
        ffm = kb.sb('l3f_ffm', [128, T], BF16, stack=st)
        rev = None
        for c in range(NC_):
            tl = kb.dma(ffm[:], d['yfin_tok'][:, c * 128:(c + 1) * 128],
                        'FT', queue='sync', transpose=True,
                        waits=[kb.evt('FW')] if c == 0 else [kb.evt('Vc')])
            rev = kb.V('tensor_tensor', out=d['x'][:, c, :],
                       in0=d['x'][:, c, :], in1=ffm[:], op=ALU.add,
                       waits=[tl])
        return rev


def emit_l1f(kb, d):
    C = 2048
    with contextlib.ExitStack() as st:
        xln = d['xln']
        wev, lev = d['wevt'], d['lnevt']
        ones_row = d['consts']['ones_row']

        pst = st.enter_context(contextlib.ExitStack())
        lg_ps = kb.ps('l1f_lg', [128, 128], stack=pst)
        pos_ps = kb.ps('l1f_pos', [128, 128], stack=pst)
        cnt1_ps = kb.ps('l1f_c1', [1, 8], stack=pst)
        cntr_ps = kb.ps('l1f_cr', [128, 8], stack=pst)
        m8 = kb.sb('l1f_m8', [128, 8], F32, stack=st)
        emask = kb.sb('l1f_oh', [128, 16, 8], BF16, stack=st)
        ohf = kb.sb('l1f_ohf', [128, 8], F32, stack=st)
        gate = kb.sb('l1f_gate', [128, 16], F32, stack=st)
        idxf = kb.sb('l1f_idxf', [128, 16], F32, stack=st)
        posd = kb.sb('l1f_posd', [128, 16], F32, stack=st)
        slotf = kb.sb('l1f_slotf', [128, 16], F32, stack=st)
        slot_i = kb.sb('l1f_sloti', [128, 16], I32, stack=st)
        tmp8 = kb.sb('l1f_tmp8', [128, 8], F32, stack=st)
        etmp = kb.sb('l1f_etmp', [128, 8], F32, stack=st)
        cnt_row = kb.sb('l1f_cnt', [1, 8], F32, stack=st)
        offrep = kb.sb('l1f_offrep', [128, 8], F32, stack=st)
        ssum = kb.sb('l1f_ssum', [128, 1], F32, stack=st)
        nm1 = kb.sb('l1f_nm1', [128, 1], F32, stack=st)
        gsel = kb.sb('l1f_gsel', [128, 1], F32, stack=st)
        ag = kb.sb('l1f_ag', [8, 8], F32, stack=st)
        goff_row = kb.sb('l1f_goffr', [1, 8], F32, stack=st)
        kb.V('memset', offrep[:], 0.0)
        for c in range(NC_):
            kb.dma(d['xlnfm_dram'][c * 128:(c + 1) * 128, :], xln[:, c, :],
                   'XW', queue='sync', waits=[lev] if c == 0 else ())
        for tt in range(16):
            lg = lg_ps[:, (tt % 2) * 8:(tt % 2) * 8 + 8]
            for dc in range(NC_):
                mi = kb.T('matmul', lg, xln[:, dc, tt * 128:(tt + 1) * 128],
                          d['rw'][:, dc, :], start=(dc == 0),
                          stop=(dc == NC_ - 1),
                          waits=(list(wev) + [lev]) if (tt == 0 and dc == 0)
                          else ([kb.evt('Vc'), kb.evt('Sc')]
                                if dc == 0 else ()))
            pe = kb.mark(mi, 'PE')
            kb.V('max', out=m8[:], in_=lg, waits=[pe])
            kb.V('tensor_scalar', out=nm1[:], in0=m8[:, 0:1], scalar1=-1.0,
                 scalar2=None, op0=ALU.mult)
            ev = kb.S('activation', out=tmp8[:], in_=lg, func=AF.Exp,
                      bias=nm1[:], scale=1.0, waits=[kb.evt('Vc')])
            kb.V('tensor_reduce', out=ssum[:], in_=tmp8[:], axis=AX.X,
                 op=ALU.add, waits=[ev])
            kb.V('reciprocal', out=gate[:, tt:tt + 1], in_=ssum[:])
            kb.V('tensor_scalar', out=emask[:, tt, :], in0=lg,
                 scalar1=m8[:, 0:1], scalar2=None, op0=ALU.is_ge)
            kb.V('tensor_copy', out=ohf[:], in_=emask[:, tt, :])
            kb.V('tensor_tensor', out=etmp[:], in0=ohf[:],
                 in1=d['consts']['iota8_rep'][:], op=ALU.mult)
            kb.V('tensor_reduce', out=idxf[:, tt:tt + 1], in_=etmp[:],
                 axis=AX.X, op=ALU.add)
            oev = kb.evt('Vc')
            mi = kb.T('matmul', pos_ps[:, (tt % 2) * 8:(tt % 2) * 8 + 8],
                      d['consts']['triu128'][:], emask[:, tt, :],
                      start=True, stop=True, waits=[oev])
            mi2 = kb.T('matmul', cnt1_ps[:, :], d['consts']['ones_col_bf'][:],
                       emask[:, tt, :], start=True, stop=True)
            pe = kb.mark(mi2, 'PE')
            pos = pos_ps[:, (tt % 2) * 8:(tt % 2) * 8 + 8]
            kb.V('tensor_tensor', out=etmp[:], in0=ohf[:], in1=pos,
                 op=ALU.mult, waits=[pe])
            kb.V('tensor_reduce', out=posd[:, tt:tt + 1], in_=etmp[:],
                 axis=AX.X, op=ALU.add)
            kb.V('tensor_tensor', out=etmp[:], in0=ohf[:], in1=offrep[:],
                 op=ALU.mult)
            kb.V('tensor_reduce', out=tmp8[:, 0:1], in_=etmp[:], axis=AX.X,
                 op=ALU.add)
            kb.V('scalar_tensor_tensor', out=posd[:, tt:tt + 1],
                 in0=posd[:, tt:tt + 1], scalar=-1.0, in1=tmp8[:, 0:1],
                 op0=ALU.add, op1=ALU.add)
            kb.V('tensor_copy', out=cnt_row[:], in_=cnt1_ps[:, :])
            crev = kb.evt('Vc')
            mi = kb.T('matmul', cntr_ps[:, :], ones_row[:], cnt_row[:],
                      start=True, stop=True, waits=[crev])
            pe2 = kb.mark(mi, 'PE')
            uev = kb.V('tensor_tensor', out=offrep[:], in0=offrep[:],
                       in1=cntr_ps[:, :], op=ALU.add, waits=[pe2])
            kb.t_wait([uev])
        cw = kb.dma(d['cnt_dram'], offrep[0:1, :], 'CW', queue='scalar',
                    waits=[kb.evt('Vc')])
        if d.get('fake_cc'):
            zb8 = kb.sb('l1f_zag', [8, 8], F32, stack=st)
            zze = kb.V('memset', zb8[:], 0.0)
            cc = kb.dma(d['ag_dram'], zb8[:], 'CC', queue='scalar',
                        waits=[zze, cw])
        else:
            eng = kb.nc.gpsimd
            eng.wait_ge(kb.sem('CW'), kb.cnt['CW'])
            ci = eng.collective_compute(
                'AllGather', ALU.bypass, [list(range(8))],
                [d['cnt_dram'].opt()], [d['ag_dram'].opt()])
            ci.then_inc(kb.sem('CC'), 1)
            kb.cnt['CC'] += 1
            cc = kb.evt('CC')
        agl = kb.dma(ag[:], d['ag_dram'], 'AGL', queue='sync', waits=[cc])
        goff_ps = cnt1_ps
        mi = kb.T('matmul', goff_ps[:, :], d['coremask'][:], ag[:],
                  start=True, stop=True, waits=[agl, kb.evt('Vc')])
        pe = kb.mark(mi, 'PE')
        gev = kb.V('tensor_copy', out=goff_row[:], in_=goff_ps[:, :],
                   waits=[pe])
        mi = kb.T('matmul', cntr_ps[:, :], ones_row[:], goff_row[:],
                  start=True, stop=True, waits=[gev])
        pe = kb.mark(mi, 'PE')
        for tt in range(16):
            kb.V('tensor_tensor', out=etmp[:], in0=emask[:, tt, :],
                 in1=cntr_ps[:, :], op=ALU.mult, waits=[pe] if tt == 0 else ())
            kb.V('tensor_reduce', out=gsel[:], in_=etmp[:], axis=AX.X,
                 op=ALU.add)
            kb.V('tensor_tensor', out=posd[:, tt:tt + 1],
                 in0=posd[:, tt:tt + 1], in1=gsel[:], op=ALU.add)
            kb.V('tensor_scalar', out=gsel[:], in0=posd[:, tt:tt + 1],
                 scalar1=float(C), scalar2=None, op0=ALU.is_lt)
            kb.V('tensor_tensor', out=gate[:, tt:tt + 1],
                 in0=gate[:, tt:tt + 1], in1=gsel[:], op=ALU.mult)
            kb.V('scalar_tensor_tensor', out=slotf[:, tt:tt + 1],
                 in0=idxf[:, tt:tt + 1], scalar=float(C),
                 in1=posd[:, tt:tt + 1], op0=ALU.mult, op1=ALU.add)
            kb.V('tensor_scalar', out=gsel[:], in0=gsel[:],
                 scalar1=-1000000.0, scalar2=1000000.0, op0=ALU.mult,
                 op1=ALU.add)
            kb.V('tensor_tensor', out=slotf[:, tt:tt + 1],
                 in0=slotf[:, tt:tt + 1], in1=gsel[:], op=ALU.add)
            kev = kb.V('tensor_copy', out=slot_i[:, tt:tt + 1],
                       in_=slotf[:, tt:tt + 1])
        kb.t_wait([kev])
        pst.close()

        bcreg = kb.nc.gpsimd.to_reg(8 * C - 1)
        zb = kb.sb('l1f_zb', [128, 8192], BF16, stack=st)
        zev = kb.V('memset', zb[:], 0.0)
        for buf in (d['xc_dram'], d['yc_dram']):
            for i in range(16):
                kb.dma(buf.rearrange('(a p) d -> p a d', p=128)[
                    :, i * 8:(i + 1) * 8, :],
                    zb[:].rearrange('p (a d) -> p a d', d=1024), 'ZF',
                    queue='scalar', waits=[zev] if i == 0 else ())
        xtok = kb.sb('l1f_xtok', [128, 1024], BF16, stack=st)
        sev = None
        for tt in range(16):
            tl = kb.dma(xtok[:], d['xlnfm_dram'][:, tt * 128:(tt + 1) * 128],
                        'TL', queue='sync', transpose=True,
                        waits=[kb.evt('SC')] if tt > 0 else
                        [kb.evt('XW'), kb.evt('ZF')])
            sev = kb.idma(d['xc_dram'], bass.IndirectOffsetOnAxis(
                ap=slot_i[:, tt:tt + 1], axis=0), xtok[:], None, 'SC',
                waits=[tl, kev], bounds_check=bcreg, oob_is_err=False)
        if d.get('fake_cc'):
            ar1 = kb.dma(d['xf_dram'], d['xc_dram'], 'AR', queue='scalar',
                         waits=[sev, kb.evt('ZF')])
        else:
            eng = kb.nc.gpsimd
            eng.wait_ge(kb.sem('SC'), kb.cnt['SC'])
            eng.wait_ge(kb.sem('ZF'), kb.cnt['ZF'])
            ci = eng.collective_compute(
                'AllReduce', ALU.add, [list(range(8))],
                [d['xc_dram'].opt()], [d['xf_dram'].opt()])
            ci.then_inc(kb.sem('AR'), 1)
            kb.cnt['AR'] += 1
            ar1 = kb.evt('AR')
        ridx = d['rowidx']
        rie = kb.evt('WL')
        for i in range(16):
            ge = kb.idma(xtok[:], None, d['xf_dram'],
                         bass.IndirectOffsetOnAxis(ap=ridx[:, i:i + 1],
                                                   axis=0), 'GA',
                         waits=[ar1, rie, kb.evt('SG')] if i == 0 else
                         [kb.evt('SG')])
            kb.dma(d['xseg_dram'][i * 128:(i + 1) * 128, :], xtok[:], 'SG',
                   queue='scalar', waits=[ge])
        FH = 16
        xseg = kb.sb('l1f_xs', [128, NC_, 512], BF16, stack=st)
        w1h = kb.sb('l1f_w1h', [128, NC_, FH, 128], BF16, stack=st)
        w2h = kb.sb('l1f_w2h', [128, FH, NC_, 128], BF16, stack=st)
        hsb = kb.sb('l1f_h', [128, FH, 512], BF16, stack=st)
        ysum = kb.sb('l1f_y', [128, NC_, 512], F32, stack=st)
        yfm = kb.sb('l1f_yfm', [128, 512], BF16, stack=st)
        ps_h = [kb.ps('l1f_ph0', [128, 512], stack=st),
                kb.ps('l1f_ph1', [128, 512], stack=st)]
        ps_y = [kb.ps('l1f_py0', [128, 512], stack=st),
                kb.ps('l1f_py1', [128, 512], stack=st)]
        for sc in range(4):
            ssl = slice(sc * 512, (sc + 1) * 512)
            for dc in range(NC_):
                xl = kb.dma(xseg[:, dc, :],
                            d['xseg_dram'][ssl, dc * 128:(dc + 1) * 128],
                            'XT', queue='sync', transpose=True,
                            waits=([kb.evt('SG'), kb.evt('PE')]
                                   if dc == 0 else []))
            for hf in range(2):
                wl1 = kb.dma(w1h[:], d['w1c_dram'][:, :,
                                                   hf * FH:(hf + 1) * FH, :],
                             'EW1', queue='gpsimd', waits=[kb.evt('PE')])
                wl2 = kb.dma(w2h[:], d['w2c_dram'][:,
                                                   hf * FH:(hf + 1) * FH,
                                                   :, :],
                             'EW2', queue='gpsimd', waits=[kb.evt('PE')])
                for fc in range(FH):
                    ph = ps_h[fc % 2]
                    for dc in range(NC_):
                        mi = kb.T('matmul', ph[:, :], w1h[:, dc, fc, :],
                                  xseg[:, dc, :], start=(dc == 0),
                                  stop=(dc == NC_ - 1),
                                  waits=[wl1, xl, kb.evt('Sc'),
                                         kb.evt('Vc')]
                                  if fc == 0 and dc == 0 else
                                  ([kb.evt('Sc')] if dc == 0 else ()))
                    pe1 = kb.mark(mi, 'PE')
                    kb.S('activation', out=hsb[:, fc, :], in_=ph[:, :],
                         func=AF.Relu,
                         bias=d['b1c'][:, hf * FH + fc:hf * FH + fc + 1],
                         scale=1.0, waits=[pe1])
                for dcol in range(NC_):
                    py = ps_y[dcol % 2]
                    for fc in range(FH):
                        mi = kb.T('matmul', py[:, :], w2h[:, fc, dcol, :],
                                  hsb[:, fc, :], start=(fc == 0),
                                  stop=(fc == FH - 1),
                                  waits=[wl2, kb.evt('Sc'), kb.evt('Vc')]
                                  if dcol == 0 and fc == 0 else
                                  ([kb.evt('Vc')] if fc == 0 else ()))
                    pe2 = kb.mark(mi, 'PE')
                    if hf == 0:
                        kb.V('tensor_copy', out=ysum[:, dcol, :],
                             in_=py[:, :], waits=[pe2])
                    else:
                        kb.V('tensor_tensor', out=ysum[:, dcol, :],
                             in0=ysum[:, dcol, :], in1=py[:, :], op=ALU.add,
                             waits=[pe2])
            for dcol in range(NC_):
                ev = kb.V('tensor_scalar', out=yfm[:], in0=ysum[:, dcol, :],
                          scalar1=d['b2c'][:, dcol:dcol + 1], scalar2=None,
                          op0=ALU.add, waits=[kb.evt('YW')])
                kb.dma(d['yfm_dram'][dcol * 128:(dcol + 1) * 128, ssl],
                       yfm[:], 'YW', queue='scalar', waits=[ev])
        ytok = kb.sb('l1f_ytok', [128, 1024], BF16, stack=st)
        for i in range(16):
            tl = kb.dma(ytok[:], d['yfm_dram'][:, i * 128:(i + 1) * 128],
                        'TL', queue='sync', transpose=True,
                        waits=[kb.evt('YW')] if i == 0 else [kb.evt('SC')])
            sev = kb.idma(d['yc_dram'], bass.IndirectOffsetOnAxis(
                ap=ridx[:, i:i + 1], axis=0), ytok[:], None, 'SC',
                waits=[tl], bounds_check=bcreg, oob_is_err=False)
        if d.get('fake_cc'):
            ar2 = kb.dma(d['yf_dram'], d['yc_dram'], 'AR', queue='scalar',
                         waits=[sev])
        else:
            eng = kb.nc.gpsimd
            eng.wait_ge(kb.sem('SC'), kb.cnt['SC'])
            ci = eng.collective_compute(
                'AllReduce', ALU.add, [list(range(8))],
                [d['yc_dram'].opt()], [d['yf_dram'].opt()])
            ci.then_inc(kb.sem('AR'), 1)
            kb.cnt['AR'] += 1
            ar2 = kb.evt('AR')
        fin = kb.sb('l1f_fin', [128, 1024], BF16, stack=st)
        for tt in range(16):
            mz = kb.V('memset', fin[:], 0.0,
                      waits=[kb.evt('FW')] if tt > 0 else ())
            ge = kb.idma(fin[:], None, d['yf_dram'],
                         bass.IndirectOffsetOnAxis(
                             ap=slot_i[:, tt:tt + 1], axis=0), 'GA',
                         waits=[ar2, mz], bounds_check=bcreg,
                         oob_is_err=False)
            sev2 = kb.V('tensor_scalar', out=fin[:], in0=fin[:],
                        scalar1=gate[:, tt:tt + 1], scalar2=None,
                        op0=ALU.mult, waits=[ge])
            kb.dma(d['yfin_tok'][tt * 128:(tt + 1) * 128, :], fin[:], 'FW',
                   queue='scalar', waits=[sev2])
        ffm = kb.sb('l1f_ffm', [128, 512], BF16, stack=st)
        xsr = kb.sb('l1f_xsr', [128, 512], F32, stack=st)
        ysb2 = kb.sb('l1f_ysb2', [128, 512], F32, stack=st)
        for c in range(NC_):
            for tc in range(4):
                sl = slice(tc * 512, (tc + 1) * 512)
                tl = kb.dma(ffm[:], d['yfin_tok'][sl,
                                                  c * 128:(c + 1) * 128],
                            'FT', queue='sync', transpose=True,
                            waits=[kb.evt('FW')] if (c == 0 and tc == 0)
                            else [kb.evt('Vc')])
                xre = kb.dma(xsr[:], d['x_in'].rearrange(
                    '(c p) t -> p c t', p=128)[:, c, sl], 'XR',
                    queue='sync', waits=[kb.evt('Vc')])
                ye = kb.V('tensor_tensor', out=ysb2[:], in0=xsr[:],
                          in1=ffm[:], op=ALU.add,
                          waits=[tl, xre, kb.evt('YO')])
                kb.dma(d['y_out'].rearrange('(c p) t -> p c t', p=128)[
                    :, c, sl], ysb2[:], 'YO', queue='scalar', waits=[ye])
        return kb.evt('YO')


def emit_l3a(kb, d):
    with contextlib.ExitStack() as st:
        qfm = kb.sb('l3a_q', [128, 4, T], BF16, stack=st)
        kfm = kb.sb('l3a_k', [128, 4, T], BF16, stack=st)
        ktok = kb.sb('l3a_kt', [128, 16, 512], BF16, stack=st)
        vtok = kb.sb('l3a_vt', [128, 16, 512], BF16, stack=st)
        zfm = kb.sb('l3a_z', [128, NC_, T], BF16, stack=st)
        psb = kb.sb('l3a_psb', [128, 4, 512], BF16, stack=st)
        msb = kb.sb('l3a_msb', [128, 64], BF16, stack=st)
        msum = kb.sb('l3a_msum', [128, 64], F32, stack=st)
        ps = [kb.ps('l3a_p0', [128, 512], stack=st),
              kb.ps('l3a_p1', [128, 512], stack=st),
              kb.ps('l3a_p2', [128, 512], stack=st),
              kb.ps('l3a_p3', [128, 512], stack=st)]
        psm = kb.ps('l3a_m', [64, 64], stack=st)
        pso = kb.ps('l3a_o', [64, 512], stack=st)
        xln = d['xln']
        wev, lev = d['wevt'], d['lnevt']

        prev = kb.evt('Sc')
        for hg in range(2):
            for dst, fbase, scale in ((qfm, 4 * hg, 0.125),
                                      (kfm, 8 + 4 * hg, 1.0)):
                for fc in range(4):
                    for tc in range(4):
                        pb = ps[tc % 2]
                        sl = slice(tc * 512, (tc + 1) * 512)
                        for dc in range(NC_):
                            mi = kb.T('matmul', pb[:, :],
                                      d['wq'][:, dc, fbase + fc, :],
                                      xln[:, dc, sl], start=(dc == 0),
                                      stop=(dc == NC_ - 1),
                                      waits=(list(wev) + [lev, kb.evt('Vc')])
                                      if (fc == 0 and tc == 0 and dc == 0)
                                      else ([kb.evt('Sc')] if dc == 0 else ()))
                        pe = kb.mark(mi, 'PE')
                        kb.S('activation', out=dst[:, fc, sl], in_=pb[:, :],
                             func=AF.Copy, bias=0.0, scale=float(scale),
                             waits=[pe])
            for tt in range(16):
                for gi, (dst, fb) in enumerate(((ktok, 8 + 4 * hg),
                                                (vtok, 16 + 4 * hg))):
                    pb = ps[2 + gi]
                    for dc in range(NC_):
                        mi = kb.T('matmul', pb[:, :],
                                  xln[:, dc, tt * 128:(tt + 1) * 128],
                                  d['wq'][:, dc, fb:fb + 4, :].rearrange(
                                      'p a b -> p (a b)'),
                                  start=(dc == 0), stop=(dc == NC_ - 1),
                                  waits=[kb.evt('Sc')] if dc == 0 else ())
                    pe = kb.mark(mi, 'PE')
                    kb.S('activation', out=dst[:, tt, :], in_=pb[:, :],
                         func=AF.Copy, bias=0.0, scale=1.0, waits=[pe])
            prev = kb.evt('Sc')

            for hh in range(8):
                h = 8 * hg + hh
                hc, hr = hh // 2, (hh % 2) * 64
                hsl = slice(hr, hr + 64)
                ghc = h // 2
                ghr = (h % 2) * 64
                gsl = slice(ghr, ghr + 64)
                vc0 = hh * 64
                for qb in range(4):
                    qsl = slice(qb * 512, (qb + 1) * 512)
                    if qb > 0:
                        for j in range(4 * (qb - 1), 4 * qb):
                            mi = kb.T('matmul', psm[:, :],
                                      ktok[:, j, vc0:vc0 + 64],
                                      vtok[:, j, vc0:vc0 + 64],
                                      start=(j == 4 * (qb - 1)),
                                      stop=(j == 4 * qb - 1),
                                      waits=[prev, kb.evt('Vc')]
                                      if j == 4 * (qb - 1) else ())
                        pe = kb.mark(mi, 'PE')
                        if qb == 1:
                            kb.V('tensor_copy', out=msum[hsl, :],
                                 in_=psm[:, :], waits=[pe])
                        else:
                            kb.V('tensor_tensor', out=msum[hsl, :],
                                 in0=msum[hsl, :], in1=psm[:, :],
                                 op=ALU.add, waits=[pe])
                        mev = kb.V('tensor_copy', out=msb[hsl, :],
                                   in_=msum[hsl, :])
                    for s in range(4):
                        j = 4 * qb + s
                        n0 = s * 128
                        mi = kb.T('matmul', ps[s % 2][:, 0:512 - n0],
                                  kfm[:, hc, j * 128:(j + 1) * 128][hsl, :],
                                  qfm[:, hc,
                                      qb * 512 + n0:(qb + 1) * 512][hsl, :],
                                  start=True, stop=True,
                                  waits=[kb.evt('Vc'), prev])
                        pe = kb.mark(mi, 'PE')
                        kb.V('tensor_copy', out=psb[:, s, n0:512],
                             in_=ps[s % 2][:, 0:512 - n0], waits=[pe])
                        tev = kb.V('tensor_tensor',
                                   out=psb[:, s, n0:n0 + 128],
                                   in0=psb[:, s, n0:n0 + 128],
                                   in1=d['trimask'][:], op=ALU.mult)
                    first = (qb == 0)
                    if qb > 0:
                        mi = kb.T('matmul', pso[:, :], msb[hsl, :],
                                  qfm[:, hc, qsl][hsl, :], start=True,
                                  stop=False, waits=[mev, kb.evt('Sc')])
                    for s in range(4):
                        j = 4 * qb + s
                        n0 = s * 128
                        mi = kb.T('matmul', pso[:, n0:512],
                                  vtok[:, j, vc0:vc0 + 64],
                                  psb[:, s, n0:512],
                                  start=(first and s == 0), stop=(s == 3),
                                  waits=[tev, kb.evt('Sc')] if s == 0 else ())
                    pe = kb.mark(mi, 'PE')
                    prev = kb.S('activation', out=zfm[:, ghc, qsl][gsl, :],
                                in_=pso[:, :], func=AF.Copy, bias=0.0,
                                scale=1.0, waits=[pe])
        xs = kb.sb('l3a_xs', [128, 2, 512], F32, stack=st)
        ysb = kb.sb('l3a_ysb', [128, 512], F32, stack=st)
        for fc in range(NC_):
            for tc in range(4):
                sl = slice(tc * 512, (tc + 1) * 512)
                pb = ps[tc % 2]
                xe = kb.dma(xs[:, tc % 2, :],
                            d['x_in'].rearrange('(c p) t -> p c t', p=128)[
                                :, fc, sl], 'XR', queue='sync',
                            waits=[kb.evt('Vc')])
                for ec in range(NC_):
                    mi = kb.T('matmul', pb[:, :], d['woutT'][:, ec, fc, :],
                              zfm[:, ec, sl], start=(ec == 0),
                              stop=(ec == NC_ - 1),
                              waits=[prev, kb.evt('Vc')] if ec == 0 else ())
                pe = kb.mark(mi, 'PE')
                yev = kb.V('scalar_tensor_tensor', out=ysb[:], in0=pb[:, :],
                           scalar=1.0, in1=xs[:, tc % 2, :], op0=ALU.mult,
                           op1=ALU.add, waits=[pe, xe, kb.evt('YO')])
                kb.dma(d['y_out'].rearrange('(c p) t -> p c t', p=128)[
                    :, fc, sl], ysb[:], 'YO', queue='scalar', waits=[yev])
        return kb.evt('YO')


def emit_l2a(kb, d):
    with contextlib.ExitStack() as st:
        qfm = kb.sb('l2a_q', [128, 4, T], BF16, stack=st)
        kfm = kb.sb('l2a_k', [128, 4, T], BF16, stack=st)
        vg = kb.sb('l2a_v', [128, 16, 520], BF16, stack=st)
        zfm = kb.sb('l2a_z', [128, NC_, T], BF16, stack=st)
        ast = st.enter_context(contextlib.ExitStack())
        pbuf = kb.sb('l2a_pb', [128, 2, 1024], BF16, stack=ast)
        sqs = kb.sb('l2a_sqs', [128, 512], BF16, stack=ast)
        nmax = kb.sb('l2a_nmax', [2, 2, 4], F32, stack=ast)
        ctmp = kb.sb('l2a_ct', [2, 2], F32, stack=ast)
        cdg = kb.sb('l2a_cd', [2, 2], F32, stack=ast)
        ccols = kb.sb('l2a_cc', [128, 8], F32, stack=ast)
        rrow = kb.sb('l2a_rr', [1, 512], F32, stack=ast)
        zsb = kb.sb('l2a_zsb', [128, 512], BF16, stack=ast)
        psA = kb.ps('l2a_pA', [128, 1024], stack=st)
        psB = kb.ps('l2a_pB', [128, 1024], stack=st)
        pso = kb.ps('l2a_po', [65, 512], stack=st)
        psr = kb.ps('l2a_pr', [128, 512], stack=st)
        xln = d['xln']
        wev, lev = d['wevt'], d['lnevt']
        H2 = 0.125
        ones_row = d['consts']['ones_row']
        ones512 = d['consts']['ones_row512']

        bev = kb.V('tensor_scalar', out=d['bqkv'][:, 8:16],
                   in0=d['bqkv'][:, 8:16], scalar1=H2, scalar2=None,
                   op0=ALU.mult, waits=list(wev))

        prev = kb.evt('Sc')
        for hg in range(2):
            wq = d['wq']
            for di, (dst, fb, scale) in enumerate(((qfm, 4 * hg, 1.0),
                                                   (kfm, 8 + 4 * hg, H2))):
                for fc in range(4):
                    for tc in range(4):
                        pb = (psA if tc % 2 == 0 else psB)[:, 0:512]
                        sl = slice(tc * 512, (tc + 1) * 512)
                        for dc in range(NC_):
                            mi = kb.T('matmul', pb, wq[:, dc, fb + fc, :],
                                      xln[:, dc, sl], start=(dc == 0),
                                      stop=(dc == NC_ - 1),
                                      waits=(list(wev) + [lev, bev,
                                                          kb.evt('Vc'),
                                                          kb.evt('Sc')])
                                      if (di == 0 and fc == 0 and tc == 0
                                          and dc == 0)
                                      else ([kb.evt('Sc')] if dc == 0 else ()))
                        pe = kb.mark(mi, 'PE')
                        kb.S('activation', out=dst[:, fc, sl], in_=pb,
                             func=AF.Identity,
                             bias=d['bqkv'][:, fb + fc:fb + fc + 1],
                             scale=float(scale), waits=[pe])
            for tt in range(16):
                pb = psA[:, 0:512]
                fb = 16 + 4 * hg
                for dc in range(NC_):
                    mi = kb.T('matmul', pb,
                              xln[:, dc, tt * 128:(tt + 1) * 128],
                              wq[:, dc, fb:fb + 4, :].rearrange(
                                  'p a b -> p (a b)'),
                              start=(dc == 0), stop=False,
                              waits=[kb.evt('Sc')] if dc == 0 else ())
                mi = kb.T('matmul', pb, ones_row[:],
                          d['bqkv_row'][:, 2048 + 512 * hg:
                                        2048 + 512 * hg + 512],
                          start=False, stop=True)
                pe = kb.mark(mi, 'PE')
                kb.S('activation',
                     out=vg[:, tt, :].rearrange('p (h c) -> p h c', c=65)[
                         :, :, 0:64],
                     in_=pb, func=AF.Copy, bias=0.0, scale=1.0, waits=[pe])
            oev = kb.V('memset',
                       vg[:].rearrange('p a (h c) -> p a h c', c=65)[
                           :, :, :, 64:65], 1.0, waits=[kb.evt('Sc')])

            for fc in range(4):
                for di, src_ in enumerate((qfm, kfm)):
                    for tc in range(4):
                        sl = slice(tc * 512, (tc + 1) * 512)
                        sev = kb.S('activation', out=sqs[:],
                                   in_=src_[:, fc, sl], func=AF.Square,
                                   bias=0.0, scale=1.0,
                                   waits=[kb.evt('Vc')])
                        mi = kb.T('matmul', psB[0:2, 0:512], d['i2c'][:],
                                  sqs[:], start=True, stop=True,
                                  waits=[sev, kb.evt('Vc')])
                        pe = kb.mark(mi, 'PE')
                        kb.V('tensor_reduce', out=nmax[:, di, tc:tc + 1],
                             in_=psB[0:2, 0:512], axis=AX.X, op=ALU.max,
                             waits=[pe])
                        kb.t_wait([kb.evt('Vc')])
                    kb.V('tensor_reduce', out=ctmp[:, di:di + 1],
                         in_=nmax[:, di, :], axis=AX.X, op=ALU.max)
                kb.V('tensor_tensor', out=ctmp[:, 0:1], in0=ctmp[:, 0:1],
                     in1=ctmp[:, 1:2], op=ALU.mult)
                cse = kb.S('activation', out=ctmp[:, 0:1], in_=ctmp[:, 0:1],
                           func=AF.Sqrt, bias=0.0, scale=1.0,
                           waits=[kb.evt('Vc')])
                kb.V('tensor_scalar', out=ctmp[:, 0:1], in0=ctmp[:, 0:1],
                     scalar1=-1.0, scalar2=None, op0=ALU.mult, waits=[cse])
                dg = kb.V('tensor_tensor', out=cdg[:, :],
                          in0=ctmp[:, 0:1].to_broadcast((2, 2)),
                          in1=d['i2'][:], op=ALU.mult)
                mi = kb.T('matmul', psr[:, 0:2], d['ones2'][:], cdg[:, :],
                          start=True, stop=True, waits=[dg])
                pe = kb.mark(mi, 'PE')
                ce = kb.V('tensor_copy', out=ccols[:, 2 * fc:2 * fc + 2],
                          in_=psr[:, 0:2], waits=[pe])
                kb.t_wait([ce])
            cev = kb.evt('Vc')

            for hh in range(8):
                h = 8 * hg + hh
                hc, hr = hh // 2, (hh % 2) * 64
                hsl = slice(hr, hr + 64)
                ghc, ghr = h // 2, (h % 2) * 64
                gsl = slice(ghr, ghr + 64)
                for qb in range(4):
                    qsl = slice(qb * 512, (qb + 1) * 512)

                    def s_mm(jp, extra=()):
                        pS = psA if jp % 2 == 0 else psB
                        for s in range(2):
                            j = 2 * jp + s
                            mi = kb.T('matmul',
                                      pS[:, s * 512:(s + 1) * 512],
                                      kfm[:, hc,
                                          j * 128:(j + 1) * 128][hsl, :],
                                      qfm[:, hc, qsl][hsl, :],
                                      start=True, stop=True,
                                      waits=list(extra) if s == 0 else ())
                        return kb.mark(mi, 'PE')
                    ex_h = {0: None, 1: None}
                    pv_h = {0: None, 1: None}
                    pe_s = {0: s_mm(0, [cev, prev, kb.evt('Sc')])}
                    for jp in range(8):
                        par = jp % 2
                        eev = kb.S('activation', out=pbuf[:, par, :],
                                   in_=(psA if par == 0 else psB)[:, :],
                                   func=AF.Exp, bias=ccols[:, hh:hh + 1],
                                   scale=1.0,
                                   waits=[pe_s[jp]] +
                                   ([pv_h[par]] if pv_h[par] else []))
                        ex_h[par] = eev
                        if jp < 7:
                            npar = (jp + 1) % 2
                            pe_s[jp + 1] = s_mm(
                                jp + 1,
                                [ex_h[npar]] if ex_h[npar] else [])
                        for s in range(2):
                            j = 2 * jp + s
                            mi = kb.T('matmul', pso[:, :],
                                      vg[:, j, 65 * hh:65 * hh + 65],
                                      pbuf[:, par, s * 512:(s + 1) * 512],
                                      start=(jp == 0 and s == 0),
                                      stop=(jp == 7 and s == 1),
                                      waits=[eev, oev] if s == 0 else ())
                        pe2 = kb.mark(mi, 'PE')
                        pv_h[par] = pe2
                    clm = kb.V('tensor_scalar', out=rrow[:],
                               in_=None, in0=pso[64:65, :], scalar1=1e-30,
                               scalar2=None, op0=ALU.max, waits=[pe2]) \
                        if False else \
                        kb.V('tensor_scalar', out=rrow[:],
                             in0=pso[64:65, :], scalar1=1e-30,
                             scalar2=None, op0=ALU.max, waits=[pe2])
                    rev = kb.V('reciprocal', out=rrow[:], in_=rrow[:])
                    mi = kb.T('matmul', psr[0:64, 0:512],
                              ones_row[:][0:1, 0:64], rrow[:],
                              start=True, stop=True, waits=[rev])
                    pe = kb.mark(mi, 'PE')
                    zev = kb.S('activation', out=zsb[0:64, :],
                               in_=pso[0:64, :], func=AF.Copy, bias=0.0,
                               scale=1.0, waits=[kb.evt('Vc')])
                    prev = kb.V('tensor_tensor', out=zfm[:, ghc, qsl][gsl, :],
                                in0=zsb[0:64, :], in1=psr[0:64, 0:512],
                                op=ALU.mult, waits=[pe, zev])
                    kb.t_wait([prev])
        ast.close()
        xs = kb.sb('l2a_xs', [128, 2, 512], F32, stack=st)
        ysb = kb.sb('l2a_ysb', [128, 512], F32, stack=st)
        for fc in range(NC_):
            for tc in range(4):
                sl = slice(tc * 512, (tc + 1) * 512)
                pb = (psA if tc % 2 == 0 else psB)[:, 0:512]
                xe = kb.dma(xs[:, tc % 2, :],
                            d['x_in'].rearrange('(c p) t -> p c t', p=128)[
                                :, fc, sl], 'XR', queue='sync',
                            waits=[kb.evt('Vc')])
                for ec in range(NC_):
                    mi = kb.T('matmul', pb, d['wo'][:, ec, fc, :],
                              zfm[:, ec, sl], start=(ec == 0), stop=False,
                              waits=[prev, kb.evt('Vc')] if ec == 0 else ())
                mi = kb.T('matmul', pb, d['bo'][:, fc * 128:(fc + 1) * 128],
                          ones512[:], start=False, stop=True)
                pe = kb.mark(mi, 'PE')
                yev = kb.V('scalar_tensor_tensor', out=ysb[:], in0=pb,
                           scalar=1.0, in1=xs[:, tc % 2, :], op0=ALU.mult,
                           op1=ALU.add, waits=[pe, xe, kb.evt('YO')])
                kb.dma(d['y_out'].rearrange('(c p) t -> p c t', p=128)[
                    :, fc, sl], ysb[:], 'YO', queue='scalar', waits=[yev])
        return kb.evt('YO')


def emit_l1a(kb, d):
    NDELTA = 19
    with contextlib.ExitStack() as st:
        ast = st.enter_context(contextlib.ExitStack())
        qh = kb.sb('l1a_qh', [128, T], BF16, stack=ast)
        kh = kb.sb('l1a_kh', [128, T], BF16, stack=ast)
        vtok = kb.sb('l1a_v', [128, 16, 8 * 129], BF16, stack=ast)
        strip = kb.sb('l1a_strip', [128, NDELTA * 128], F32, stack=ast)
        pb = kb.sb('l1a_p', [128, 16, 512], BF16, stack=ast)
        zrow = kb.sb('l1a_zr', [128, T], BF16, stack=ast)
        sqs = kb.sb('l1a_sqs', [128, 512], BF16, stack=ast)
        gb = kb.sb('l1a_gb', [128, 2, 128], BF16, stack=ast)
        sc1 = kb.sb('l1a_sc1', [1, 8, 512], F32, stack=ast)
        ccol = kb.sb('l1a_cc', [128, 1], F32, stack=ast)
        nrm = kb.sb('l1a_nrm', [1, 2], F32, stack=ast)
        stt = kb.sb('l1a_stt', [128, 10], F32, stack=ast)
        zt = kb.sb('l1a_zt', [128, 128], BF16, stack=ast)
        sq2 = kb.sb('l1a_sq2', [128, 128], F32, stack=ast)
        th = kb.sb('l1a_th', [8, 128], F32, stack=ast)
        epsc = kb.sb('l1a_eps', [128, 1], F32, stack=ast)
        thc = kb.sb('l1a_thc', [8, 1], F32, stack=ast)
        ab = kb.sb('l1a_ab', [8, 2, 128], F32, stack=ast)
        dg = kb.sb('l1a_dg', [128, 128], F32, stack=ast)
        dgi = kb.sb('l1a_dgi', [128, 128], I32, stack=ast)
        dgf = kb.sb('l1a_dgf', [128, 128], F32, stack=ast)
        thr = kb.sb('l1a_thr', [128, 8], F32, stack=ast)
        ln8 = kb.sb('l1a_ln8', [8, 1], F32, stack=ast)
        io128 = kb.sb('l1a_io', [8, 128], I32, stack=ast)
        iof = kb.sb('l1a_iof', [8, 128], F32, stack=ast)
        ps1 = kb.ps('l1a_p1', [128, 512], stack=st)
        ps2 = kb.ps('l1a_p2', [128, 512], stack=st)
        pso = kb.ps('l1a_po', [128, 129], stack=st)
        psd = kb.ps('l1a_pd', [128, 128], stack=st)
        pst = kb.ps('l1a_pt', [128, 128], BF16, stack=st)
        xln = d['xln']
        wev, lev = d['wevt'], d['lnevt']
        ones_row = d['consts']['ones_row']
        ones512 = d['consts']['ones_row512']
        ocb = d['consts']['ones_col_bf']
        SC = float(1.0 / np.sqrt(128.0))

        kb.V('memset', epsc[:], 1e-5)
        ge = kb.G('iota', io128[:], pattern=[[1, 128]], base=0,
                  channel_multiplier=0)
        ge2 = kb.G('iota', dgi[:], pattern=[[1, 128]], base=0,
                   channel_multiplier=-1)
        kb.V('tensor_copy', out=iof[:], in_=io128[:], waits=[ge, list(wev)[0]])
        kb.V('memset', ln8[:], -float(np.log(8.0)))
        kb.V('tensor_copy', out=thc[:], in_=d['thcol'][:], waits=list(wev))
        tre = kb.V('tensor_copy', out=dgf[:], in_=dgi[:], waits=[ge2])
        mi = kb.T('matmul', psd[:, 0:8], ones_row[:], d['throw'][:],
                  start=True, stop=True, waits=list(wev) + [kb.evt('Vc')])
        pe = kb.mark(mi, 'PE')
        kb.V('tensor_copy', out=thr[:], in_=psd[:, 0:8], waits=[pe])
        kb.V('tensor_scalar', out=dgf[:], in0=dgf[:], scalar1=0.0,
             scalar2=None, op0=ALU.max)
        dsum = None
        for h in range(8):
            kb.V('tensor_scalar', out=dg[:], in0=dgf[:],
                 scalar1=thr[:, h:h + 1], scalar2=None, op0=ALU.mult)
            se = kb.S('activation', out=sq2[:], in_=dg[:], func=AF.Exp,
                      bias=0.0, scale=-1.0, waits=[kb.evt('Vc')])
            if h == 0:
                dsum = kb.V('tensor_copy', out=strip[:, 3 * 128:4 * 128],
                            in_=sq2[:], waits=[se])
            else:
                dsum = kb.V('tensor_tensor', out=strip[:, 3 * 128:4 * 128],
                            in0=strip[:, 3 * 128:4 * 128], in1=sq2[:],
                            op=ALU.add, waits=[se])
        kb.V('tensor_scalar', out=strip[:, 3 * 128:4 * 128],
             in0=strip[:, 3 * 128:4 * 128], scalar1=0.125, scalar2=None,
             op0=ALU.mult)
        kb.V('memset', strip[:, 0:3 * 128], 1.0)
        for dd_ in range(1, 16):
            kb.V('tensor_scalar', out=ab[:, 0, :], in0=iof[:],
                 scalar1=-64.0 * dd_, scalar2=None, op0=ALU.add)
            kb.V('tensor_scalar', out=ab[:, 0, :], in0=ab[:, 0, :],
                 scalar1=thc[:], scalar2=None, op0=ALU.mult)
            kb.V('tensor_scalar', out=ab[:, 1, :], in0=iof[:],
                 scalar1=64.0 * dd_, scalar2=None, op0=ALU.add)
            ve = kb.V('tensor_scalar', out=ab[:, 1, :], in0=ab[:, 1, :],
                      scalar1=thc[:], scalar2=-1.0, op0=ALU.mult, op1=ALU.mult)
            kb.S('activation', out=ab[:, 0, :], in_=ab[:, 0, :], func=AF.Exp,
                 bias=ln8[:], scale=1.0, waits=[ve])
            se = kb.S('activation', out=ab[:, 1, :], in_=ab[:, 1, :],
                      func=AF.Exp, bias=0.0, scale=1.0)
            mi = kb.T('matmul', psd[:, :], ab[:, 0, :], ab[:, 1, :],
                      start=True, stop=True, waits=[se, kb.evt('Vc')])
            pe = kb.mark(mi, 'PE')
            kb.V('tensor_copy', out=strip[:, (3 + dd_) * 128:(4 + dd_) * 128],
                 in_=psd[:, :], waits=[pe])
        stre = kb.evt('Vc')

        for tt in range(16):
            for vh in range(2):
                pv = ps1 if vh == 0 else ps2
                for dc in range(NC_):
                    mi = kb.T('matmul', pv[:, :],
                              xln[:, dc, tt * 128:(tt + 1) * 128],
                              d['wv'][:, dc, 4 * vh:4 * vh + 4, :].rearrange(
                                  'p a b -> p (a b)'),
                              start=(dc == 0), stop=(dc == NC_ - 1),
                              waits=(list(wev) + [lev, kb.evt('Sc')])
                              if dc == 0 else ())
                pe = kb.mark(mi, 'PE')
                kb.S('activation',
                     out=vtok[:, tt, :].rearrange('p (h c) -> p h c', c=129)[
                         :, 4 * vh:4 * vh + 4, 0:128],
                     in_=pv[:, :], func=AF.Copy, bias=0.0, scale=1.0,
                     waits=[pe])
        onev = kb.V('memset',
                    vtok[:].rearrange('p a (h c) -> p a h c', c=129)[
                        :, :, :, 128:129], 1.0, waits=[kb.evt('Sc')])

        prev = kb.evt('Sc')
        zwe = kb.evt('ZW')
        for h in range(8):
            for di, (dst, w_, scale) in enumerate(((qh, d['wqk'], 1.0),
                                                   (kh, d['wqk'], SC))):
                for tc in range(4):
                    pv = ps1 if tc % 2 == 0 else ps2
                    sl = slice(tc * 512, (tc + 1) * 512)
                    for dc in range(NC_):
                        mi = kb.T('matmul', pv[:, :],
                                  w_[:, dc, 2 * h + di, :], xln[:, dc, sl],
                                  start=(dc == 0), stop=(dc == NC_ - 1),
                                  waits=[kb.evt('Sc'), kb.evt('Vc')]
                                  if dc == 0 else ())
                    pe = kb.mark(mi, 'PE')
                    kb.S('activation', out=dst[:, sl], in_=pv[:, :],
                         func=AF.Copy, bias=0.0, scale=float(scale),
                         waits=[pe])
            for di, src_ in enumerate((qh, kh)):
                for tc in range(4):
                    sl = slice(tc * 512, (tc + 1) * 512)
                    se = kb.S('activation', out=sqs[:], in_=src_[:, sl],
                              func=AF.Square, bias=0.0, scale=1.0,
                              waits=[kb.evt('Vc')])
                    mi = kb.T('matmul', ps1[0:1, :], ocb[:], sqs[:],
                              start=True, stop=True,
                              waits=[se, kb.evt('Vc')])
                    pe = kb.mark(mi, 'PE')
                    kb.V('tensor_copy', out=sc1[:, di * 4 + tc, :],
                         in_=ps1[0:1, :], waits=[pe])
                    kb.t_wait([kb.evt('Vc')])
                kb.V('tensor_reduce', out=nrm[:, di:di + 1],
                     in_=sc1[:, di * 4:di * 4 + 4, :].rearrange(
                         'p a b -> p (a b)'), axis=AX.X, op=ALU.max)
            kb.V('tensor_tensor', out=nrm[:, 0:1], in0=nrm[:, 0:1],
                 in1=nrm[:, 1:2], op=ALU.mult)
            cse = kb.S('activation', out=nrm[:, 0:1], in_=nrm[:, 0:1],
                       func=AF.Sqrt, bias=0.0, scale=1.0,
                       waits=[kb.evt('Vc')])
            nne = kb.V('tensor_scalar', out=nrm[:, 0:1], in0=nrm[:, 0:1],
                       scalar1=-1.0, scalar2=None, op0=ALU.mult, waits=[cse])
            mi = kb.T('matmul', psd[:, 0:1], ones_row[:], nrm[:, 0:1],
                      start=True, stop=True, waits=[nne])
            pe = kb.mark(mi, 'PE')
            cce = kb.V('tensor_copy', out=ccol[:], in_=psd[:, 0:1],
                       waits=[pe])
            mi = kb.T('matmul', psd[:, 0:128], ones_row[:],
                      d['gam'][:, h * 128:(h + 1) * 128], start=True,
                      stop=True, waits=[cce])
            mi2 = kb.T('matmul', ps1[:, 0:128], ones_row[:],
                       d['bet'][:, h * 128:(h + 1) * 128], start=True,
                       stop=True)
            pe = kb.mark(mi2, 'PE')
            kb.V('tensor_copy', out=gb[:, 0, :], in_=psd[:, 0:128],
                 waits=[pe])
            gbe = kb.V('tensor_copy', out=gb[:, 1, :], in_=ps1[:, 0:128])
            kb.t_wait([gbe])

            for qb in range(4):
                qsl = slice(qb * 512, (qb + 1) * 512)
                for j in range(4 * qb + 4):
                    n0 = max(0, (j - 4 * qb) * 128)
                    d0 = 4 * qb - j
                    mi = kb.T('matmul', (ps1 if j % 2 == 0 else ps2)[:, 0:512 - n0],
                              kh[:, j * 128:(j + 1) * 128],
                              qh[:, qb * 512 + n0:(qb + 1) * 512],
                              start=True, stop=True,
                              waits=[stre, cce, gbe, kb.evt('Vc'),
                                     kb.evt('Sc')] if j == 0 else
                              [kb.evt('Vc'), kb.evt('Sc')])
                    pe = kb.mark(mi, 'PE')
                    pS = (ps1 if j % 2 == 0 else ps2)
                    s0 = (d0 + 3) * 128 + n0
                    dve = kb.V('tensor_tensor', out=pS[:, 0:512 - n0],
                               in0=pS[:, 0:512 - n0],
                               in1=strip[:, s0:s0 + 512 - n0], op=ALU.mult,
                               waits=[pe])
                    eev = kb.S('activation', out=pb[:, j, n0:512],
                               in_=pS[:, 0:512 - n0], func=AF.Exp,
                               bias=ccol[:], scale=1.0, waits=[dve])
                    if j >= 4 * qb:
                        kb.V('tensor_tensor', out=pb[:, j, n0:n0 + 128],
                             in0=pb[:, j, n0:n0 + 128],
                             in1=d['trimask'][:], op=ALU.mult, waits=[eev])
                pmev = (kb.evt('Vc'), kb.evt('Sc'))
                for s in range(4):
                    for j in range(4 * qb + s + 1):
                        mi = kb.T('matmul', pso[:, :],
                                  pb[:, j, s * 128:(s + 1) * 128],
                                  vtok[:, j, 129 * h:129 * h + 129],
                                  start=(j == 0), stop=(j == 4 * qb + s),
                                  waits=[pmev[0], pmev[1], onev,
                                         kb.evt('Vc')] if j == 0 else ())
                    pe = kb.mark(mi, 'PE')
                    kb.V('tensor_scalar', out=stt[:, 0:1], in0=pso[:, 128:129],
                         scalar1=1e-30, scalar2=None, op0=ALU.max, waits=[pe])
                    kb.V('reciprocal', out=stt[:, 1:2], in_=stt[:, 0:1])
                    kb.V('tensor_reduce', out=stt[:, 2:3], in_=pso[:, 0:128],
                         axis=AX.X, op=ALU.add)
                    se = kb.S('activation', out=sq2[:], in_=pso[:, 0:128],
                              func=AF.Square, bias=0.0, scale=1.0,
                              waits=[kb.evt('Vc')])
                    kb.V('tensor_reduce', out=stt[:, 3:4], in_=sq2[:],
                         axis=AX.X, op=ALU.add, waits=[se])
                    kb.V('tensor_scalar', out=stt[:, 4:5], in0=stt[:, 2:3],
                         scalar1=stt[:, 1:2], scalar2=1.0 / 128.0,
                         op0=ALU.mult, op1=ALU.mult)
                    kb.V('tensor_scalar', out=stt[:, 5:6], in0=stt[:, 3:4],
                         scalar1=stt[:, 1:2], scalar2=None, op0=ALU.mult)
                    kb.V('tensor_scalar', out=stt[:, 5:6], in0=stt[:, 5:6],
                         scalar1=stt[:, 1:2], scalar2=1.0 / 128.0,
                         op0=ALU.mult, op1=ALU.mult)
                    kb.V('tensor_tensor', out=stt[:, 6:7], in0=stt[:, 4:5],
                         in1=stt[:, 4:5], op=ALU.mult)
                    kb.V('tensor_tensor', out=stt[:, 6:7], in0=stt[:, 5:6],
                         in1=stt[:, 6:7], op=ALU.subtract)
                    se = kb.S('activation', out=stt[:, 6:7], in_=stt[:, 6:7],
                              func=AF.Sqrt, bias=epsc[:], scale=1.0,
                              waits=[kb.evt('Vc')])
                    kb.V('reciprocal', out=stt[:, 7:8], in_=stt[:, 6:7],
                         waits=[se])
                    kb.V('tensor_tensor', out=stt[:, 8:9], in0=stt[:, 1:2],
                         in1=stt[:, 7:8], op=ALU.mult)
                    kb.V('scalar_tensor_tensor', out=stt[:, 9:10],
                         in0=stt[:, 4:5], scalar=-1.0, in1=stt[:, 7:8],
                         op0=ALU.mult, op1=ALU.mult)
                    kb.V('tensor_scalar', out=sq2[:], in0=pso[:, 0:128],
                         scalar1=stt[:, 8:9], scalar2=stt[:, 9:10],
                         op0=ALU.mult, op1=ALU.add)
                    kb.V('tensor_tensor', out=sq2[:], in0=sq2[:],
                         in1=gb[:, 0, :], op=ALU.mult)
                    zte = kb.V('tensor_tensor', out=zt[:], in0=sq2[:],
                               in1=gb[:, 1, :], op=ALU.add)
                    tr = kb.T('transpose', pst[:, :], zt[:], d['idbf'][:],
                              waits=[zte])
                    pe = kb.mark(tr, 'PE')
                    cpe = kb.V('tensor_copy',
                               out=zrow[:, (4 * qb + s) * 128:
                                        (4 * qb + s + 1) * 128],
                               in_=pst[:, :],
                               waits=[pe] + ([kb.evt('ZW')]
                                             if (qb == 0 and s == 0) else []))
                    kb.t_wait([cpe])
            zwe = kb.dma(d['z_dram'][h * 128:(h + 1) * 128, :], zrow[:],
                         'ZW', queue='scalar', waits=[kb.evt('Vc')])
            prev = kb.evt('Sc')

        kb.t_wait([kb.evt('Vc'), kb.evt('Sc'), kb.evt('ZW')])
        ast.close()
        zs = kb.sb('l1a_zs', [128, 2, 512], BF16, stack=st)
        xs = kb.sb('l1a_xs', [128, 2, 512], F32, stack=st)
        ysb = kb.sb('l1a_ysb', [128, 512], F32, stack=st)
        for fc in range(NC_):
            for tc in range(4):
                sl = slice(tc * 512, (tc + 1) * 512)
                pv = ps1 if tc % 2 == 0 else ps2
                xe = kb.dma(xs[:, tc % 2, :],
                            d['x_in'].rearrange('(c p) t -> p c t', p=128)[
                                :, fc, sl], 'XR', queue='sync',
                            waits=[kb.evt('Vc')])
                for ec in range(NC_):
                    ze = kb.dma(zs[:, tc % 2, :],
                                d['z_dram'][ec * 128:(ec + 1) * 128, sl],
                                'ZR', queue='sync',
                                waits=[zwe, kb.evt('PE')] if ec == 0 else
                                [kb.evt('PE')])
                    mi = kb.T('matmul', pv[:, :], d['wo'][:, ec, fc, :],
                              zs[:, tc % 2, :], start=(ec == 0), stop=False,
                              waits=[ze, kb.evt('Vc')])
                    pe = kb.mark(mi, 'PE')
                mi = kb.T('matmul', pv[:, :],
                          d['bo'][:, fc * 128:(fc + 1) * 128], ones512[:],
                          start=False, stop=True)
                pe = kb.mark(mi, 'PE')
                yev = kb.V('scalar_tensor_tensor', out=ysb[:], in0=pv[:, :],
                           scalar=1.0, in1=xs[:, tc % 2, :], op0=ALU.mult,
                           op1=ALU.add, waits=[pe, xe, kb.evt('YO')])
                kb.dma(d['y_out'].rearrange('(c p) t -> p c t', p=128)[
                    :, fc, sl], ysb[:], 'YO', queue='scalar', waits=[yev])
        return kb.evt('YO')


def emit_l0a(kb, d, trunc=None):
    with contextlib.ExitStack() as st:
        xg = d['xg']
        lq = kb.sb('l0a_lq', [128, 2, T], BF16, stack=st)
        lktok = kb.sb('l0a_lkt', [128, 16, 256], BF16, stack=st)
        linkv = kb.sb('l0a_lkv', [128, 2, T], BF16, stack=st)
        attn = kb.sb('l0a_attn', [64, 32, 64], BF16, stack=st)
        g8 = kb.sb('l0a_g8', [128, 2, 8], F32, stack=st)
        sb64 = kb.sb('l0a_sb64', [128, 64], F32, stack=st)
        wev = d['wevt']
        lev = d['lnevt']
        ones_row = d['consts']['ones_row']

        with contextlib.ExitStack() as hst:
            qq = kb.sb('l0a_qq', [128, 2, T], BF16, stack=hst)
            qkk = kb.sb('l0a_qkk', [128, 2, T], BF16, stack=hst)
            lkfm = kb.sb('l0a_lkfm', [128, 2, T], BF16, stack=hst)
            qkf = kb.sb('l0a_qkf', [128, 2, T], BF16, stack=hst)
            hbuf = kb.sb('l0a_hb', [128, 2, T], F32, stack=hst)
            hbuf2 = kb.sb('l0a_hb2', [128, 2, T], F32, stack=hst)
            cosT = kb.sb('l0a_cos', [128, T], F32, stack=hst)
            sinT = kb.sb('l0a_sin', [128, T], F32, stack=hst)
            kb.dma(cosT[:], d['cos_dram'][:], 'CS', queue='sync')
            cse_ = kb.dma(sinT[:], d['sin_dram'][:], 'CS', queue='sync')
            ps1 = kb.ps('l0a_p1', [128, 512], stack=hst)
            ps2 = kb.ps('l0a_p2', [128, 512], stack=hst)
            pst = kb.ps('l0a_pt', [128, 128], BF16, stack=hst)

            for fc in range(2):
                for tc in range(4):
                    pv = ps1 if tc % 2 == 0 else ps2
                    sl = slice(tc * 512, (tc + 1) * 512)
                    for dc in range(NC_):
                        mi = kb.T('matmul', pv[:, :], d['wqkT'][:, dc, fc, :],
                                  xg[:, dc, sl], start=(dc == 0),
                                  stop=(dc == NC_ - 1),
                                  waits=(list(wev) + [lev, kb.evt('Sc')])
                                  if (fc == 0 and tc == 0 and dc == 0)
                                  else ([kb.evt('Sc')] if dc == 0 else ()))
                    pe = kb.mark(mi, 'PE')
                    kb.S('activation', out=qkf[:, fc, sl], in_=pv[:, :],
                         func=AF.Silu, bias=d['bqk'][:, fc:fc + 1], scale=1.0,
                         waits=[pe])
            kb.V('tensor_copy', out=g8[:, 0, :], in_=d['gam'][:],
                 waits=list(wev))
            kb.V('tensor_copy', out=g8[:, 1, :], in_=d['bet'][:])
            for i in (0, 2):
                for z in range(2):
                    kb.V('tensor_scalar', out=g8[:, z, 2 * i:2 * i + 2],
                         in0=g8[:, z, 2 * i:2 * i + 2], scalar1=0.125,
                         scalar2=None, op0=ALU.mult)
            g8e = kb.evt('Vc')
            for i, dst in enumerate((qq, lq, qkk, lkfm)):
                for fc in range(2):
                    kb.V('tensor_scalar', out=hbuf[:, fc, :],
                         in0=qkf[:, fc, :],
                         scalar1=g8[:, 0, 2 * i + fc:2 * i + fc + 1],
                         scalar2=g8[:, 1, 2 * i + fc:2 * i + fc + 1],
                         op0=ALU.mult, op1=ALU.add,
                         waits=[g8e, cse_, kb.evt('Sc')] if fc == 0 else ())
                kb.V('tensor_tensor', out=hbuf2[:, 0, :], in0=hbuf[:, 0, :],
                     in1=cosT[:], op=ALU.mult)
                kb.V('tensor_tensor', out=hbuf2[:, 1, :], in0=hbuf[:, 1, :],
                     in1=sinT[:], op=ALU.mult)
                kb.V('tensor_tensor', out=dst[:, 0, :], in0=hbuf2[:, 0, :],
                     in1=hbuf2[:, 1, :], op=ALU.subtract)
                kb.V('tensor_tensor', out=hbuf2[:, 0, :], in0=hbuf[:, 1, :],
                     in1=cosT[:], op=ALU.mult)
                kb.V('tensor_tensor', out=hbuf2[:, 1, :], in0=hbuf[:, 0, :],
                     in1=sinT[:], op=ALU.mult)
                hde = kb.V('tensor_tensor', out=dst[:, 1, :],
                           in0=hbuf2[:, 0, :], in1=hbuf2[:, 1, :], op=ALU.add)
            for tt in range(16):
                for c2 in range(2):
                    tr = kb.T('transpose', pst[:, :],
                              lkfm[:, c2, tt * 128:(tt + 1) * 128],
                              d['idbf'][:], waits=[hde, kb.evt('Vc')])
                    pe = kb.mark(tr, 'PE')
                    ce = kb.V('tensor_copy',
                              out=lktok[:, tt, c2 * 128:(c2 + 1) * 128],
                              in_=pst[:, :], waits=[pe])
                    kb.t_wait([ce])
            for g in range(32):
                for dc2 in range(2):
                    mi = kb.T('matmul', ps1[0:64, 0:64],
                              qkk[:, dc2, g * 64:(g + 1) * 64],
                              qq[:, dc2, g * 64:(g + 1) * 64],
                              start=(dc2 == 0), stop=(dc2 == 1),
                              waits=[kb.evt('Vc')] if dc2 == 0 else ())
                pe = kb.mark(mi, 'PE')
                kb.V('tensor_tensor', out=sb64[0:64, :], in0=ps1[0:64, 0:64],
                     in1=d['bias64'][:], op=ALU.add, waits=[pe])
                kb.V('tensor_scalar', out=sb64[0:64, :], in0=sb64[0:64, :],
                     scalar1=0.0, scalar2=None, op0=ALU.max)
                ce = kb.V('tensor_tensor', out=attn[:, g, :],
                          in0=sb64[0:64, :], in1=sb64[0:64, :], op=ALU.mult)
                kb.t_wait([ce])
            kb.t_wait([kb.evt('Vc'), kb.evt('Sc')])
        if trunc == 'heads':
            return kb.evt('Sc')
        with contextlib.ExitStack() as vst:
            vsb = kb.sb('l0a_vsb', [128, 4, 512], BF16, stack=vst)
            ps1 = kb.ps('l0a_v1', [128, 512], stack=vst)
            ps2 = kb.ps('l0a_v2', [128, 512], stack=vst)
            for tt in range(16):
                for eg in range(4):
                    pv = ps1 if eg % 2 == 0 else ps2
                    for dc in range(NC_):
                        mi = kb.T('matmul', pv[:, :],
                                  xg[:, dc, tt * 128:(tt + 1) * 128],
                                  d['wvo'][:, dc * 2048 + 512 * eg:
                                           dc * 2048 + 512 * eg + 512],
                                  start=(dc == 0), stop=False,
                                  waits=([kb.evt('Sc'), kb.evt('Vc')] +
                                         ([kb.evt('WV')] if (tt == 0 and
                                                             eg == 0) else []))
                                  if dc == 0 else ())
                    mi = kb.T('matmul', pv[:, :], ones_row[:],
                              d['bv'][:, 512 * eg:512 * eg + 512],
                              start=False, stop=True)
                    pe = kb.mark(mi, 'PE')
                    kb.S('activation', out=vsb[:, eg, :], in_=pv[:, :],
                         func=AF.Silu, bias=0.0, scale=1.0,
                         waits=[pe] + ([kb.evt('VW')] if eg == 0 else []))
                kb.dma(d['v_dram'][tt * 128:(tt + 1) * 128, :],
                       vsb[:].rearrange('p a b -> p (a b)'), 'VW',
                       queue='scalar', waits=[kb.evt('Sc')])
            kb.t_wait([kb.evt('Sc'), kb.evt('Vc')])
        with contextlib.ExitStack() as ast:
            psl_ = [kb.ps('l0a_pl%d' % i, [128, 512], stack=ast)
                    for i in range(6)]
            vt = kb.sb('l0a_vt', [128, 2, 2048], BF16, stack=ast)
            for tt in range(16):
                le = kb.dma(vt[:, tt % 2, :],
                            d['v_dram'][tt * 128:(tt + 1) * 128, :], 'VR',
                            queue='sync',
                            waits=[kb.evt('VW')] if tt == 0 else
                            [kb.evt('PE')])
                for dc2 in range(2):
                    for eg in range(3):
                        mi = kb.T('matmul', psl_[dc2 * 3 + eg][:, :],
                                  lktok[:, tt, dc2 * 128:(dc2 + 1) * 128],
                                  vt[:, tt % 2, eg * 512:(eg + 1) * 512],
                                  start=(tt == 0), stop=(tt == 15),
                                  waits=[le] if (dc2 == 0 and eg == 0) else ())
                kb.mark(mi, 'PE')
            for dc2 in range(2):
                for eg in range(3):
                    kb.S('activation',
                         out=linkv[:, dc2, eg * 512:(eg + 1) * 512],
                         in_=psl_[dc2 * 3 + eg][:, :], func=AF.Copy, bias=0.0,
                         scale=1.0 / T, waits=[kb.evt('PE')])
            kb.t_wait([kb.evt('Sc')])
            for tt in range(16):
                le = kb.dma(vt[:, tt % 2, 0:512],
                            d['v_dram'][tt * 128:(tt + 1) * 128, 1536:2048],
                            'VR', queue='sync', waits=[kb.evt('PE')])
                for dc2 in range(2):
                    mi = kb.T('matmul', psl_[dc2][:, :],
                              lktok[:, tt, dc2 * 128:(dc2 + 1) * 128],
                              vt[:, tt % 2, 0:512],
                              start=(tt == 0), stop=(tt == 15),
                              waits=[le] if dc2 == 0 else ())
                kb.mark(mi, 'PE')
            for dc2 in range(2):
                kb.S('activation', out=linkv[:, dc2, 1536:2048],
                     in_=psl_[dc2][:, :], func=AF.Copy, bias=0.0,
                     scale=1.0 / T, waits=[kb.evt('PE')])
            kb.t_wait([kb.evt('Sc')])

        if trunc == 'phaseA':
            return kb.evt('Sc')
        kb.dma(d['wvo'][:], d['wo_dram'][:], 'WO', queue='gpsimd',
               waits=[kb.evt('PE'), kb.evt('Sc')])
        with contextlib.ExitStack() as bst:
            vsuper = kb.sb('l0a_vs', [64, 8, 2048], BF16, stack=bst)
            zsuper = kb.sb('l0a_zs', [128, 16, 512], BF16, stack=bst)
            gsb = kb.sb('l0a_gsb', [128, 512], BF16, stack=bst)
            wg = [kb.sb('l0a_wg0', [128, NC_, 128], BF16, stack=bst),
                  kb.sb('l0a_wg1', [128, NC_, 128], BF16, stack=bst)]
            outsb = kb.sb('l0a_out', [128, NC_, 512], F32, stack=bst)
            psg = kb.ps('l0a_pg', [128, 512], stack=bst)
            psq = kb.ps('l0a_pq', [128, 512], stack=bst)
            pso = [kb.ps('l0a_po0', [128, 512], stack=bst),
                   kb.ps('l0a_po1', [128, 512], stack=bst)]
            for ts_ in range(4):
                sl = slice(ts_ * 512, (ts_ + 1) * 512)
                vle = kb.dma(vsuper[:],
                             d['v_dram'].rearrange('(a p) e -> p a e', p=64)[
                                 :, 8 * ts_:8 * ts_ + 8, :], 'VS',
                             queue='sync',
                             waits=[kb.evt('Vc'), kb.evt('PE')])
                for ec in range(16):
                    wge = kb.dma(wg[ec % 2][:],
                                 d['wgT_dram'][:, :, ec, :], 'WG%d' % (ec % 2),
                                 queue='gpsimd', waits=[kb.evt('PE')])
                    for dc in range(NC_):
                        mi = kb.T('matmul', psg[:, :], wg[ec % 2][:, dc, :],
                                  xg[:, dc, sl], start=(dc == 0),
                                  stop=(dc == NC_ - 1),
                                  waits=[wge, kb.evt('Sc')] if dc == 0 else ())
                    pe = kb.mark(mi, 'PE')
                    ge = kb.S('activation', out=gsb[:], in_=psg[:, :],
                              func=AF.Silu, bias=d['bg'][:, ec:ec + 1],
                              scale=1.0, waits=[pe])
                    if trunc == 'gate' and ts_ == 0 and ec == 0:
                        return kb.evt('Sc')
                    for gg in range(8):
                        g = 8 * ts_ + gg
                        mi = kb.T('matmul', psq[:, gg * 64:(gg + 1) * 64],
                                  vsuper[:, gg, ec * 128:(ec + 1) * 128],
                                  attn[:, g, :],
                                  start=(gg == 0), stop=False,
                                  waits=[vle, kb.evt('Vc')]
                                  if gg == 0 else ())
                    for dc2 in range(2):
                        mi = kb.T('matmul', psq[:, :],
                                  linkv[:, dc2, ec * 128:(ec + 1) * 128],
                                  lq[:, dc2, sl], start=False,
                                  stop=(dc2 == 1))
                    pe = kb.mark(mi, 'PE')
                    ze = kb.V('tensor_tensor', out=zsuper[:, ec, :],
                              in0=gsb[:], in1=psq[:, :], op=ALU.mult,
                              waits=[pe, ge])
                    kb.t_wait([ze])
                    if trunc == 'ec%d' % ec and ts_ == 0:
                        return kb.evt('Vc')
                if trunc == 'ecs' and ts_ == 0:
                    return kb.evt('Vc')
                for fc in range(NC_):
                    po = pso[fc % 2]
                    for ec in range(16):
                        mi = kb.T('matmul', po[:, :],
                                  d['wvo'][:, ec * 1024 + fc * 128:
                                           ec * 1024 + fc * 128 + 128],
                                  zsuper[:, ec, :],
                                  start=(ec == 0), stop=False,
                                  waits=([kb.evt('Vc'), kb.evt('Sc')] +
                                         ([kb.evt('WO')] if (ts_ == 0 and
                                                             fc == 0) else []))
                                  if ec == 0 else ())
                    mi = kb.T('matmul', po[:, :],
                              d['bo'][:, fc * 128:(fc + 1) * 128],
                              d['consts']['ones_row512'][:],
                              start=False, stop=True)
                    pe = kb.mark(mi, 'PE')
                    kb.S('activation', out=outsb[:, fc, :], in_=po[:, :],
                         func=AF.Copy, bias=0.0, scale=1.0, waits=[pe])
                if trunc == 'phaseB1' and ts_ == 0:
                    return kb.evt('Sc')
                with contextlib.ExitStack() as lst:
                    lsq = kb.sb('l0a_lsq%d' % ts_, [128, 512], F32, stack=lst)
                    lst_ = kb.sb('l0a_lst%d' % ts_, [1, 6, 512], F32, stack=lst)
                    xsr = kb.sb('l0a_xsr%d' % ts_, [128, 512], F32, stack=lst)
                    ysb = kb.sb('l0a_ysb%d' % ts_, [128, 512], F32, stack=lst)
                    ep1 = kb.sb('l0a_ep1%d' % ts_, [1, 1], F32, stack=lst)
                    pss = kb.ps('l0a_pss%d' % ts_, [128, 512], stack=lst)
                    psq2 = kb.ps('l0a_psq2%d' % ts_, [128, 512], stack=lst)
                    kb.V('memset', ep1[:], 1e-5)
                    pe = kb.evt('PE')
                    for c in range(NC_):
                        se = kb.V('tensor_tensor', out=lsq[:], in0=outsb[:, c, :],
                                  in1=outsb[:, c, :], op=ALU.mult,
                                  waits=[kb.evt('Sc'), pe])
                        kb.T('matmul', pss[0:1, :], d['consts']['ones_col'][:],
                             outsb[:, c, :], start=(c == 0),
                             stop=(c == NC_ - 1),
                             waits=[kb.evt('Sc')] if c == 0 else ())
                        mi = kb.T('matmul', psq2[0:1, :],
                                  d['consts']['ones_col'][:], lsq[:],
                                  start=(c == 0), stop=(c == NC_ - 1),
                                  waits=[se])
                        pe = kb.mark(mi, 'PE')
                    kb.V('tensor_scalar', out=lst_[:, 0, :], in0=pss[0:1, :],
                         scalar1=1.0 / D, scalar2=None, op0=ALU.mult,
                         waits=[pe])
                    kb.V('tensor_scalar', out=lst_[:, 1, :], in0=psq2[0:1, :],
                         scalar1=1.0 / D, scalar2=None, op0=ALU.mult)
                    kb.V('tensor_tensor', out=lst_[:, 2, :],
                         in0=lst_[:, 0, :], in1=lst_[:, 0, :], op=ALU.mult)
                    kb.V('tensor_tensor', out=lst_[:, 1, :],
                         in0=lst_[:, 1, :], in1=lst_[:, 2, :],
                         op=ALU.subtract)
                    se2 = kb.S('activation', out=lst_[:, 1, :],
                               in_=lst_[:, 1, :], func=AF.Sqrt, bias=ep1[:],
                               scale=1.0, waits=[kb.evt('Vc')])
                    rve = kb.V('reciprocal', out=lst_[:, 2, :],
                               in_=lst_[:, 1, :], waits=[se2])
                    kb.T('matmul', pss[:, :], ones_row[:], lst_[:, 0, :],
                         start=True, stop=True, waits=[rve])
                    mi = kb.T('matmul', psq2[:, :], ones_row[:],
                              lst_[:, 2, :], start=True, stop=True)
                    pe = kb.mark(mi, 'PE')
                    for c in range(NC_):
                        xre = kb.dma(xsr[:],
                                     d['x_in'].rearrange(
                                         '(c p) t -> p c t', p=128)[:, c, sl],
                                     'XR', queue='sync',
                                     waits=[kb.evt('Vc')])
                        kb.V('tensor_tensor', out=lsq[:], in0=outsb[:, c, :],
                             in1=pss[:, :], op=ALU.subtract,
                             waits=[pe] if c == 0 else ())
                        kb.V('tensor_tensor', out=lsq[:], in0=lsq[:],
                             in1=psq2[:, :], op=ALU.mult)
                        kb.V('tensor_scalar', out=lsq[:], in0=lsq[:],
                             scalar1=d['ln2w'][:, c:c + 1],
                             scalar2=d['ln2b'][:, c:c + 1],
                             op0=ALU.mult, op1=ALU.add)
                        ye = kb.V('tensor_tensor', out=ysb[:], in0=lsq[:],
                                  in1=xsr[:], op=ALU.add,
                                  waits=[xre, kb.evt('YO')])
                        kb.dma(d['y_out'].rearrange(
                            '(c p) t -> p c t', p=128)[:, c, sl], ysb[:],
                            'YO', queue='scalar', waits=[ye])
                    kb.t_wait([kb.evt('Vc')])
        return kb.evt('YO')


def emit_l2f(kb, d):
    NKEY = 128
    with contextlib.ExitStack() as st:
        xln = d['xln']
        wev = d['wevt']
        lev = d['lnevt']
        ones_row = d['consts']['ones_row']

        with contextlib.ExitStack() as cst:
            cb = kb.sb('l2f_cb', [128, 2, 2048], BF16, stack=cst)
            for i in range(128):
                b = i % 2
                kb.dma(cb[:, b, 0:1024],
                       d['down_dram'][i * 128:(i + 1) * 128, :], 'CV%d' % b,
                       queue='gpsimd',
                       waits=[kb.evt('CW')] if i >= 2 else [])
                kb.dma(cb[:, b, 1024:2048],
                       d['up_dram'][i * 128:(i + 1) * 128, :], 'CV%d' % b,
                       queue='gpsimd')
                kb.dma(d['comb_dram'][i * 128:(i + 1) * 128, :],
                       cb[:, b, :], 'CW', queue='scalar',
                       waits=[kb.evt('CV%d' % b)])
        for c in range(NC_):
            kb.dma(d['xlnfm_dram'][c * 128:(c + 1) * 128, :], xln[:, c, :],
                   'XW', queue='sync', waits=[lev] if c == 0 else ())

        with contextlib.ExitStack() as pst:
            qp = kb.sb('l2f_qp', [32, 8, T], BF16, stack=pst)
            xtok = kb.sb('l2f_xt', [128, 2, 1024], BF16, stack=pst)
            dsb = kb.sb('l2f_dsb', [128, 2, 8, 2048], BF16, stack=pst)
            junk = kb.sb('l2f_junk', [128, 1024], BF16, stack=pst)
            simf = kb.sb('l2f_simf', [128, 128], F32, stack=pst)
            m8p = kb.sb('l2f_m8', [128, 8], F32, stack=pst)
            sxs = kb.sb('l2f_sxs', [128, 2, 8], F32, stack=pst)
            ixs = kb.sb('l2f_ixs', [128, 2, 8], F32, stack=pst)
            c64 = kb.sb('l2f_c64', [128, 64], F32, stack=pst)
            i64 = kb.sb('l2f_i64', [128, 64], F32, stack=pst)
            i64i = kb.sb('l2f_i64i', [128, 64], I32, stack=pst)
            kidx = kb.sb('l2f_kidx', [128, 2, 8], I32, stack=pst)
            ksc = kb.sb('l2f_ksc', [128, 8], F32, stack=pst)
            ewa = kb.sb('l2f_ewa', [128, 8], F32, stack=pst)
            wk8 = kb.sb('l2f_wk8', [128, 2, 8], F32, stack=pst)
            m8b = kb.sb('l2f_m8b', [128, 8], F32, stack=pst)
            dgt = kb.sb('l2f_dgt', [128, 128], BF16, stack=pst)
            nm1 = kb.sb('l2f_nm1', [128, 1], F32, stack=pst)
            ssum = kb.sb('l2f_ss', [128, 1], F32, stack=pst)
            srec = kb.sb('l2f_sr', [128, 1], F32, stack=pst)
            estok = kb.sb('l2f_est', [128, 1024], BF16, stack=pst)
            psq = kb.ps('l2f_pq', [32, 512], stack=pst)
            pss = kb.ps('l2f_ps', [128, 4, 128], stack=pst)
            pse = kb.ps('l2f_pe', [128, 1024], stack=pst)

            for ph in range(8):
                for tc in range(4):
                    sl = slice(tc * 512, (tc + 1) * 512)
                    for dc in range(NC_):
                        mi = kb.T('matmul', psq[:, :],
                                  d['wqp'][:, dc, ph, :], xln[:, dc, sl],
                                  start=(dc == 0), stop=(dc == NC_ - 1),
                                  waits=(list(wev) + [lev])
                                  if (ph == 0 and tc == 0 and dc == 0)
                                  else ([kb.evt('Sc')] if dc == 0 else ()))
                    pe = kb.mark(mi, 'PE')
                    kb.S('activation', out=qp[:, ph, sl], in_=psq[:, :],
                         func=AF.Copy, bias=0.0, scale=1.0,
                         waits=[pe] + ([kb.evt('CW')]
                                       if (ph == 0 and tc == 0) else []))
            qpe = kb.evt('Sc')

            def select_front(u):
                tt, h = divmod(u, 4)
                par = u % 2
                tsl = slice(tt * 128, (tt + 1) * 128)
                if h == 0:
                    kb.dma(xtok[:, tt % 2, :], d['xlnfm_dram'][:, tsl], 'TL',
                           queue='sync',
                           waits=[kb.evt('XW'), kb.evt('CW')] if tt == 0
                           else [kb.evt('Vc')])
                for p in range(2):
                    mi = kb.T('matmul', pss[:, 2 * par + p, :],
                              qp[:, p * 4 + h, tsl],
                              d['keysT'][:, p * 4 + h, :],
                              start=True, stop=True,
                              waits=[qpe, kb.evt('Vc')])
                    pe = kb.mark(mi, 'PE')
                    kb.V('tensor_scalar', out=simf[:],
                         in0=pss[:, 2 * par + p, :],
                         scalar1=4.0, scalar2=None, op0=ALU.add, waits=[pe])
                    kb.V('tensor_scalar', out=simf[:].bitcast(I32),
                         in0=simf[:].bitcast(I32), scalar1=~127,
                         scalar2=None, op0=ALU.bitwise_and)
                    kb.V('tensor_tensor', out=simf[:].bitcast(I32),
                         in0=simf[:].bitcast(I32), in1=d['iotak'][:],
                         op=ALU.bitwise_or)
                    kb.V('max', out=m8p[:], in_=simf[:])
                    kb.V('tensor_scalar', out=kidx[:, par, :],
                         in0=m8p[:].bitcast(I32), scalar1=127,
                         scalar2=None, op0=ALU.bitwise_and)
                    kb.V('tensor_copy', out=ixs[:, p, :],
                         in_=kidx[:, par, :])
                    kb.V('tensor_scalar', out=m8p[:].bitcast(I32),
                         in0=m8p[:].bitcast(I32), scalar1=~127,
                         scalar2=None, op0=ALU.bitwise_and)
                    kb.V('tensor_scalar', out=sxs[:, p, :],
                         in0=m8p[:], scalar1=-4.0, scalar2=None, op0=ALU.add)
                kb.V('tensor_tensor', out=c64[:].rearrange(
                         'p (a b) -> p a b', a=8),
                     in0=sxs[:, 0, :].unsqueeze(2).to_broadcast((128, 8, 8)),
                     in1=sxs[:, 1, :].unsqueeze(1).to_broadcast((128, 8, 8)),
                     op=ALU.add)
                kb.V('scalar_tensor_tensor', out=i64[:].rearrange(
                         'p (a b) -> p a b', a=8),
                     in0=ixs[:, 0, :].unsqueeze(2).to_broadcast((128, 8, 8)),
                     scalar=float(NKEY),
                     in1=ixs[:, 1, :].unsqueeze(1).to_broadcast((128, 8, 8)),
                     op0=ALU.mult, op1=ALU.add)
                kb.V('tensor_copy', out=i64i[:], in_=i64[:])
                kb.V('tensor_scalar', out=c64[:], in0=c64[:], scalar1=8.0,
                     scalar2=None, op0=ALU.add)
                kb.V('tensor_scalar', out=c64[:].bitcast(I32),
                     in0=c64[:].bitcast(I32), scalar1=~16383, scalar2=None,
                     op0=ALU.bitwise_and)
                kb.V('tensor_tensor', out=c64[:].bitcast(I32),
                     in0=c64[:].bitcast(I32), in1=i64i[:],
                     op=ALU.bitwise_or)
                kb.V('max', out=m8p[:], in_=c64[:])
                kb.V('tensor_scalar', out=kidx[:, par, :],
                     in0=m8p[:].bitcast(I32), scalar1=16383, scalar2=None,
                     op0=ALU.bitwise_and)
                kb.V('tensor_scalar', out=m8p[:].bitcast(I32),
                     in0=m8p[:].bitcast(I32), scalar1=~16383, scalar2=None,
                     op0=ALU.bitwise_and)
                kb.V('tensor_scalar', out=ksc[:], in0=m8p[:], scalar1=-8.0,
                     scalar2=None, op0=ALU.add)
                kb.V('tensor_scalar', out=nm1[:], in0=ksc[:, 0:1],
                     scalar1=-1.0, scalar2=None, op0=ALU.mult)
                see = kb.S('activation', out=wk8[:, par, :], in_=ksc[:],
                           func=AF.Exp, bias=nm1[:], scale=1.0,
                           waits=[kb.evt('Vc')])
                kb.V('tensor_reduce', out=ssum[:], in_=wk8[:, par, :],
                     axis=AX.X, op=ALU.add, waits=[see])
                kb.V('reciprocal', out=srec[:], in_=ssum[:])
                kie = kb.V('tensor_scalar', out=wk8[:, par, :],
                           in0=wk8[:, par, :], scalar1=srec[:], scalar2=None,
                           op0=ALU.mult)
                ge = None
                for k in range(8):
                    ge = kb.idma(dsb[:, par, k, :], None, d['comb_dram'],
                                 bass.IndirectOffsetOnAxis(
                                     ap=kidx[:, par, k:k + 1], axis=0),
                                 'GA%d' % par,
                                 waits=[kie, kb.evt('CW'), kb.evt('PE')]
                                 if k == 0 else [])
                return ge

            def back_half(u, ge):
                tt, h = divmod(u, 4)
                par = u % 2
                tsl = slice(tt * 128, (tt + 1) * 128)
                for k in range(8):
                    kb.V('scalar_tensor_tensor', out=junk[:],
                         in0=dsb[:, par, k, 0:1024], scalar=1.0,
                         in1=xtok[:, tt % 2, :], op0=ALU.mult, op1=ALU.mult,
                         accum_out=ewa[:, k:k + 1],
                         waits=[ge, kb.evt('TL')] if k == 0 else ())
                swe = kb.S('activation', out=m8b[:], in_=ewa[:],
                           func=AF.Exp, bias=0.0, scale=-1.0,
                           waits=[kb.evt('Vc')])
                kb.V('tensor_scalar', out=m8b[:], in0=m8b[:], scalar1=1.0,
                     scalar2=None, op0=ALU.add, waits=[swe])
                kb.V('reciprocal', out=m8b[:], in_=m8b[:])
                kb.V('tensor_tensor', out=ewa[:], in0=ewa[:], in1=m8b[:],
                     op=ALU.mult)
                wke = kb.V('tensor_tensor', out=wk8[:, par, :],
                           in0=wk8[:, par, :], in1=ewa[:], op=ALU.mult)
                pv_prev = None
                for k in range(8):
                    de = kb.V('tensor_scalar', out=dgt[:], in0=d['idbf'][:],
                              scalar1=wk8[:, par, k:k + 1], scalar2=None,
                              op0=ALU.mult,
                              waits=([wke, kb.evt('PE')] if k == 0 else []) +
                              ([pv_prev] if pv_prev else []))
                    for nn in range(2):
                        mi = kb.T('matmul', pse[:, nn * 512:(nn + 1) * 512],
                                  dgt[:],
                                  dsb[:, par, k, 1024 + nn * 512:
                                      1024 + (nn + 1) * 512],
                                  start=(h == 0 and k == 0),
                                  stop=(h == 3 and k == 7),
                                  waits=[de] if nn == 0 else ())
                    pv_prev = kb.mark(mi, 'PE')
                if h == 3:
                    ee = kb.V('tensor_copy', out=estok[:], in_=pse[:, :],
                              waits=[pv_prev, kb.evt('EO')])
                    kb.dma(d['es_dram'][tsl, :], estok[:], 'EO',
                           queue='scalar', waits=[ee])

            pending = None
            for u in range(64):
                ge = select_front(u)
                if pending is not None:
                    back_half(*pending)
                pending = (u, ge)
            back_half(*pending)
            kb.t_wait([kb.evt('Vc')])

        with contextlib.ExitStack() as mst:
            wg = kb.sb('l2f_wg', [128, NC_, 16, 128], BF16, stack=mst)
            wu = kb.sb('l2f_wu', [128, NC_, 16, 128], BF16, stack=mst)
            wd = kb.sb('l2f_wd', [128, 16, NC_, 128], BF16, stack=mst)
            msb = kb.sb('l2f_m', [128, 16, 512], BF16, stack=mst)
            gsb = kb.sb('l2f_g', [128, 512], BF16, stack=mst)
            xsr = kb.sb('l2f_xsr', [128, 512], F32, stack=mst)
            esr = kb.sb('l2f_esr', [128, 512], BF16, stack=mst)
            ysb = kb.sb('l2f_ysb', [128, 512], F32, stack=mst)
            ps1 = kb.ps('l2f_m1', [128, 512], stack=mst)
            ps2 = kb.ps('l2f_m2', [128, 512], stack=mst)
            ps3 = kb.ps('l2f_m3', [128, 512], stack=mst)
            we1 = kb.dma(wg[:], d['wgT_dram'][:], 'MW', queue='gpsimd',
                         waits=[kb.evt('Vc'), kb.evt('PE'), kb.evt('EO'),
                                kb.evt('GA0'), kb.evt('GA1')])
            we2 = kb.dma(wu[:], d['wuT_dram'][:], 'MW', queue='gpsimd')
            we3 = kb.dma(wd[:], d['wdT_dram'][:], 'MW', queue='gpsimd')
            for tc in range(4):
                sl = slice(tc * 512, (tc + 1) * 512)
                for ic in range(16):
                    pg = ps1 if ic % 2 == 0 else ps2
                    for dc in range(NC_):
                        kb.T('matmul', pg[:, :], wg[:, dc, ic, :],
                             xln[:, dc, sl], start=(dc == 0),
                             stop=(dc == NC_ - 1),
                             waits=[we3, kb.evt('Sc')] if dc == 0 else ())
                    for dc in range(NC_):
                        mi = kb.T('matmul', ps3[:, :], wu[:, dc, ic, :],
                                  xln[:, dc, sl], start=(dc == 0),
                                  stop=(dc == NC_ - 1),
                                  waits=[kb.evt('Vc')] if dc == 0 else ())
                    pe = kb.mark(mi, 'PE')
                    ge = kb.S('activation', out=gsb[:], in_=pg[:, :],
                              func=AF.Silu, bias=0.0, scale=1.0, waits=[pe])
                    me = kb.V('tensor_tensor', out=msb[:, ic, :], in0=gsb[:],
                              in1=ps3[:, :], op=ALU.mult, waits=[ge])
                    kb.t_wait([me])
                for fc in range(NC_):
                    po = ps1 if fc % 2 == 0 else ps2
                    xre = kb.dma(xsr[:], d['x_in'].rearrange(
                        '(c p) t -> p c t', p=128)[:, fc, sl], 'XR',
                        queue='sync', waits=[kb.evt('Vc')])
                    ere = kb.dma(esr[:], d['es_dram'][sl,
                                                       fc * 128:
                                                       (fc + 1) * 128],
                                 'ER', queue='sync', transpose=True,
                                 waits=[kb.evt('EO')] if (tc == 0 and fc == 0)
                                 else [])
                    for ic in range(16):
                        mi = kb.T('matmul', po[:, :], wd[:, ic, fc, :],
                                  msb[:, ic, :], start=(ic == 0),
                                  stop=(ic == 15),
                                  waits=[kb.evt('Vc'), kb.evt('Sc')]
                                  if ic == 0 else ())
                    pe = kb.mark(mi, 'PE')
                    kb.V('scalar_tensor_tensor', out=ysb[:], in0=po[:, :],
                         scalar=1.0, in1=xsr[:], op0=ALU.mult, op1=ALU.add,
                         waits=[pe, xre, kb.evt('YO')])
                    ye = kb.V('tensor_tensor', out=ysb[:], in0=ysb[:],
                              in1=esr[:], op=ALU.add, waits=[ere])
                    kb.dma(d['y_out'].rearrange('(c p) t -> p c t', p=128)[
                        :, fc, sl], ysb[:], 'YO', queue='scalar',
                        waits=[ye])
        return kb.evt('YO')


def _f(a):
    return np.ascontiguousarray(a, dtype=np.float32)


def prep_weights(inp):
    w = {}
    w['identity'] = np.eye(128, dtype=np.float32)
    w['e8'] = _f(np.repeat(np.eye(8, dtype=np.float32), 4, axis=1))
    w['l0f_wlin'] = _f(inp['L0_ffn_W_lin'].T.reshape(NC_, 128, NC_, 128)
                       .transpose(1, 0, 2, 3))
    w['l0f_blin'] = _f(np.asarray(inp['L0_ffn_b_lin']).reshape(NC_, 128).T)
    w['l0f_gw'] = _f(inp['L0_ffn_gate_w'].T.reshape(NC_, 128, 8)
                     .transpose(1, 0, 2))
    w['l0f_leftc'] = _f(np.transpose(inp['L0_ffn_left'], (1, 0, 2))
                        .reshape(D, 32).reshape(NC_, 128, 32)
                        .transpose(1, 0, 2))
    w['l0f_rightc'] = _f(np.asarray(inp['L0_ffn_right']).reshape(32, D)
                         .reshape(32, NC_, 128))
    w['l3f_rw'] = _f(np.asarray(inp['L3_ffn_router_w']).reshape(NC_, 128, 8)
                     .transpose(1, 0, 2))
    w['l3f_rb'] = _f(np.asarray(inp['L3_ffn_router_b']).reshape(1, 8))
    w['l3f_ew'] = _f(np.asarray(inp['L3_ffn_expert_w'])
                     .reshape(8, NC_, 128, NC_, 128).transpose(0, 2, 1, 3, 4))
    w['l3f_ebrow'] = _f(np.asarray(inp['L3_ffn_expert_b'])
                        .reshape(1, 8, 1024))
    w['l1f_rw'] = _f(np.asarray(inp['L1_ffn_router_w']).reshape(NC_, 128, 8)
                     .transpose(1, 0, 2))
    w['l1f_w1'] = _f(np.asarray(inp['L1_ffn_w1'])
                     .reshape(8, NC_, 128, 32, 128).transpose(0, 2, 1, 3, 4))
    w['l1f_w2'] = _f(np.asarray(inp['L1_ffn_w2'])
                     .reshape(8, 32, 128, NC_, 128).transpose(0, 2, 1, 3, 4))
    w['l1f_b1'] = _f(np.asarray(inp['L1_ffn_b1']).reshape(8, 32, 128)
                     .transpose(0, 2, 1))
    w['l1f_b2'] = _f(np.asarray(inp['L1_ffn_b2']).reshape(8, NC_, 128)
                     .transpose(0, 2, 1))
    w['l3a_wq'] = _f(np.asarray(inp['L3_attn_W_qkv']).T
                     .reshape(NC_, 128, 24, 128).transpose(1, 0, 2, 3))
    w['l3a_woutT'] = _f(np.asarray(inp['L3_attn_W_out']).T
                        .reshape(NC_, 128, NC_, 128).transpose(1, 0, 2, 3))
    w['trimask'] = _f(np.triu(np.ones((128, 128), dtype=np.float32)))
    _wq2 = np.asarray(inp['L2_attn_W_qkv']).reshape(D, 16, 3, 64) \
        .transpose(0, 2, 1, 3).reshape(D, 3072)
    _bq2 = np.asarray(inp['L2_attn_b_qkv']).reshape(16, 3, 64) \
        .transpose(1, 0, 2).reshape(3072)
    w['l2a_wq'] = _f(_wq2.reshape(NC_, 128, 24, 128).transpose(1, 0, 2, 3))
    w['l2a_bqkv'] = _f(_bq2.reshape(24, 128).T)
    w['l2a_bqkv_row'] = _f(_bq2.reshape(1, 3072))
    w['l2a_wo'] = _f(np.asarray(inp['L2_attn_W_o'])
                     .reshape(NC_, 128, NC_, 128).transpose(1, 0, 2, 3))
    w['l2a_bo'] = _f(np.asarray(inp['L2_attn_b_o']).reshape(1, 1024))
    w['i2'] = _f(np.eye(2, dtype=np.float32))
    w['ones2'] = _f(np.ones((2, 128), dtype=np.float32))
    w['i2c'] = _f(np.concatenate([
        np.concatenate([np.ones((64, 1)), np.zeros((64, 1))], 1),
        np.concatenate([np.zeros((64, 1)), np.ones((64, 1))], 1)],
        0).astype(np.float32))
    _wq1 = np.asarray(inp['L1_attn_Wq']).reshape(NC_, 128, 8, 128)
    _wk1 = np.asarray(inp['L1_attn_Wk']).reshape(NC_, 128, 8, 128)
    _wqk = np.empty((128, NC_, 16, 128), np.float32)
    _wqk[:, :, 0::2, :] = _wq1.transpose(1, 0, 2, 3)
    _wqk[:, :, 1::2, :] = _wk1.transpose(1, 0, 2, 3)
    w['l1a_wqk'] = _f(_wqk)
    w['l1a_wv'] = _f(np.asarray(inp['L1_attn_Wv']).reshape(NC_, 128, NC_, 128)
                     .transpose(1, 0, 2, 3))
    w['l1a_wo'] = _f(np.asarray(inp['L1_attn_Wo']).reshape(NC_, 128, NC_, 128)
                     .transpose(1, 0, 2, 3))
    w['l1a_bo'] = _f(np.asarray(inp['L1_attn_bo']).reshape(1, 1024))
    w['l1a_thcol'] = _f(np.asarray(inp['L1_attn_theta']).reshape(8, 1))
    w['l1a_throw'] = _f(np.asarray(inp['L1_attn_theta']).reshape(1, 8))
    w['l1a_gam'] = _f(np.asarray(inp['L1_attn_gamma']).reshape(1, 1024))
    w['l1a_bet'] = _f(np.asarray(inp['L1_attn_beta']).reshape(1, 1024))
    w['l0a_ln1w'] = _f(np.asarray(inp['L0_attn_ln1_w']).reshape(NC_, 128).T)
    w['l0a_ln1b'] = _f(np.asarray(inp['L0_attn_ln1_b']).reshape(NC_, 128).T)
    w['l0a_wqkT'] = _f(np.asarray(inp['L0_attn_Wqk']).T
                       .reshape(NC_, 128, 2, 128).transpose(1, 0, 2, 3))
    w['l0a_bqk'] = _f(np.asarray(inp['L0_attn_bqk']).reshape(2, 128).T)
    w['l0a_gam'] = _f(np.asarray(inp['L0_attn_gamma']).reshape(4, 2, 128)
                      .transpose(2, 0, 1).reshape(128, 8))
    w['l0a_bet'] = _f(np.asarray(inp['L0_attn_beta']).reshape(4, 2, 128)
                      .transpose(2, 0, 1).reshape(128, 8))
    _idx = np.arange(64)
    _dist = np.clip(np.abs(_idx[:, None] - _idx[None, :]), 0, 32)
    w['l0a_bias64'] = _f(np.asarray(inp['L0_attn_pos_emb'])[_dist])
    _pos = np.arange(T, dtype=np.float64)
    _invf = 10000.0 ** (-np.arange(128, dtype=np.float64) / 128.0)
    _ang = _invf[:, None] * _pos[None, :]
    w['l0a_cosT'] = _f(np.cos(_ang))
    w['l0a_sinT'] = _f(np.sin(_ang))
    w['l0a_wvT'] = _f(np.asarray(inp['L0_attn_Wv']).T
                      .reshape(NC_, 128, 2048).transpose(1, 0, 2)
                      .reshape(128, NC_ * 2048))
    w['l0a_bv'] = _f(np.asarray(inp['L0_attn_bv']).reshape(1, 2048))
    w['l0a_wgT'] = _f(np.asarray(inp['L0_attn_Wg']).T
                      .reshape(NC_, 128, 16, 128).transpose(1, 0, 2, 3))
    w['l0a_bg'] = _f(np.asarray(inp['L0_attn_bg']).reshape(16, 128).T)
    w['l0a_woT'] = _f(np.asarray(inp['L0_attn_Wo']).T
                      .reshape(16, 128, NC_, 128).transpose(1, 0, 2, 3)
                      .reshape(128, 16 * NC_ * 128))
    w['l0a_bo'] = _f(np.asarray(inp['L0_attn_bo']).reshape(1, 1024))
    w['l0a_ln2w'] = _f(np.asarray(inp['L0_attn_ln2_w']).reshape(NC_, 128).T)
    w['l0a_ln2b'] = _f(np.asarray(inp['L0_attn_ln2_b']).reshape(NC_, 128).T)
    w['l2f_down'] = _f(inp['L2_ffn_down_embed'])
    w['l2f_up'] = _f(inp['L2_ffn_up_embed'])
    w['l2f_wqp'] = _f(np.asarray(inp['L2_ffn_w_queries']).T
                      .reshape(NC_, 128, 2, 4, 32)
                      .transpose(1, 0, 2, 3, 4).reshape(128, NC_, 8, 32))
    w['l2f_keysT'] = _f(np.asarray(inp['L2_ffn_keys'])
                        .transpose(3, 2, 0, 1).reshape(32, 8, 128))
    w['l2f_wgT'] = _f(np.asarray(inp['L2_ffn_w_gate']).T
                      .reshape(NC_, 128, 16, 128).transpose(1, 0, 2, 3))
    w['l2f_wuT'] = _f(np.asarray(inp['L2_ffn_w_up']).T
                      .reshape(NC_, 128, 16, 128).transpose(1, 0, 2, 3))
    w['l2f_wdT'] = _f(np.asarray(inp['L2_ffn_w_down']).T
                      .reshape(16, 128, NC_, 128).transpose(1, 0, 2, 3))
    w['iotak'] = np.ascontiguousarray(
        np.tile(np.arange(128, dtype=np.int32), (128, 1)))
    w['iota8_rep'] = _f(np.tile(np.arange(8, dtype=np.float32), (128, 1)))
    w['triu128'] = _f(np.triu(np.ones((128, 128), dtype=np.float32)))
    return w


L3F_WEIGHTS = [
    ('l3f_rw', [128, NC_, 8], BF16),
    ('l3f_rb', [1, 8], F32),
    ('l3f_ebrow', [1, NC_, 1024], F32),
    ('l3f_ew', [8, 128, NC_, NC_, 128], F32, 'dram'),
]

L1F_WEIGHTS = [
    ('l1f_rw', [128, NC_, 8], BF16),
    ('l1f_b1c', [128, 32], F32),
    ('l1f_b2c', [128, NC_], F32),
    ('l1f_w1c', [128, NC_, 32, 128], F32, 'dram'),
    ('l1f_w2c', [128, 32, NC_, 128], F32, 'dram'),
    ('l1f_rowidx', [128, 16], I32),
    ('iota8_rep', [128, 8], F32),
    ('triu128', [128, 128], BF16),
]

L3A_WEIGHTS = [
    ('l3a_wq', [128, NC_, 24, 128], BF16),
    ('l3a_woutT', [128, NC_, NC_, 128], BF16),
    ('trimask', [128, 128], BF16),
]

L2A_WEIGHTS = [
    ('l2a_wq', [128, NC_, 24, 128], BF16),
    ('l2a_bqkv', [128, 24], F32),
    ('l2a_bqkv_row', [1, 3072], F32),
    ('l2a_wo', [128, NC_, NC_, 128], BF16),
    ('l2a_bo', [1, 1024], F32),
    ('i2', [2, 2], F32),
    ('ones2', [2, 128], F32),
    ('i2c', [128, 2], BF16),
]

L1A_WEIGHTS = [
    ('l1a_wqk', [128, NC_, 16, 128], BF16),
    ('l1a_wv', [128, NC_, NC_, 128], BF16),
    ('l1a_wo', [128, NC_, NC_, 128], BF16),
    ('l1a_bo', [1, 1024], F32),
    ('l1a_thcol', [8, 1], F32),
    ('l1a_throw', [1, 8], F32),
    ('l1a_gam', [1, 1024], F32),
    ('l1a_bet', [1, 1024], F32),
    ('trimask', [128, 128], BF16),
    ('identity', [128, 128], BF16),
]

L0A_WEIGHTS = [
    ('l0a_wqkT', [128, NC_, 2, 128], BF16),
    ('l0a_bqk', [128, 2], F32),
    ('l0a_gam', [128, 8], F32),
    ('l0a_bet', [128, 8], F32),
    ('l0a_bias64', [64, 64], F32),
    ('l0a_bv', [1, 2048], F32),
    ('l0a_bg', [128, 16], F32),
    ('l0a_bo', [1, 1024], F32),
    ('l0a_ln2w', [128, NC_], F32),
    ('l0a_ln2b', [128, NC_], F32),
    ('identity', [128, 128], BF16),
    ('l0a_wgT', [128, NC_, 16, 128], F32, 'dram'),
    ('l0a_wvT', [128, NC_ * 2048], F32, 'dram'),
    ('l0a_woT', [128, 16 * NC_ * 128], F32, 'dram'),
]

L2F_WEIGHTS = [
    ('l2f_wqp', [128, NC_, 8, 32], BF16),
    ('l2f_keysT', [32, 8, 128], BF16),
    ('iotak', [128, 128], I32),
    ('identity', [128, 128], BF16),
    ('l2f_wgT', [128, NC_, 16, 128], F32, 'dram'),
    ('l2f_wuT', [128, NC_, 16, 128], F32, 'dram'),
    ('l2f_wdT', [128, 16, NC_, 128], F32, 'dram'),
    ('l2f_down', [16384, 1024], F32, 'dram'),
    ('l2f_up', [16384, 1024], F32, 'dram'),
]

L0F_WEIGHTS = [
    ('l0f_wlin', [128, NC_, NC_, 128], BF16),
    ('l0f_blin', [128, NC_], F32),
    ('l0f_gw', [128, NC_, 8], BF16),
    ('l0f_leftc', [128, NC_, 32], BF16),
    ('l0f_rightc', [32, NC_, 128], BF16),
    ('identity', [128, 128], BF16),
    ('e8', [8, 32], BF16),
]


def declare_and_load(kb, names_shapes, queue='gpsimd', sem='WL', waits=(), sfx=''):
    nc = kb.nc
    out = {}
    first = True
    for ent in names_shapes:
        if len(ent) == 4 and ent[3] == 'dram':
            name, shape, dt, _ = ent
            out[name] = nc.dram_tensor(name + sfx, list(shape), dt,
                                       kind='ExternalInput').ap()
            continue
        name, shape, dt = ent
        ddt = I32 if dt == I32 else F32
        dram = nc.dram_tensor(name + sfx, list(shape), ddt,
                              kind='ExternalInput').ap()
        sbuf = kb.sb(name + '_sb' + sfx, shape, dt)
        kb.dma(sbuf[:], dram[:], sem, queue=queue,
               waits=waits if first else ())
        first = False
        out[name] = sbuf
    return out


def emit_stage(kb, nc, block, x_in, y_out, fake_cc, sfx=''):
    if True:
        if block in ('l3a', 'l2a', 'l1a', 'l0a', 'l2f', 'l1f'):
            xln = kb.sb('xln' + sfx, [128, NC_, T], BF16)
            consts = {}
            consts['ones_col'] = kb.sb('ones_col' + sfx, [128, 1], F32)
            consts['ones_row'] = kb.sb('ones_row' + sfx, [1, 128], F32)
            consts['ones_col_bf'] = kb.sb('ones_col_bf' + sfx, [128, 1], BF16)
            kb.V('memset', consts['ones_col'][:], 1.0)
            kb.V('memset', consts['ones_col_bf'][:], 1.0)
            ocev = kb.V('memset', consts['ones_row'][:], 1.0)
            affine = None
            vs = vb = None
            if block == 'l0a':
                ln1w_d = nc.dram_tensor('l0a_ln1w' + sfx, [128, NC_], F32,
                                        kind='ExternalInput').ap()
                ln1b_d = nc.dram_tensor('l0a_ln1b' + sfx, [128, NC_], F32,
                                        kind='ExternalInput').ap()
                ln1w = kb.sb('ln1w_sb' + sfx, [128, NC_], F32)
                ln1b = kb.sb('ln1b_sb' + sfx, [128, NC_], F32)
                kb.dma(ln1w[:], ln1w_d[:], 'LW', queue='sync')
                lwe = kb.dma(ln1b[:], ln1b_d[:], 'LW', queue='sync')
                affine = (ln1w, ln1b)
                vs, vb = 1.0 + 1e-5, 1e-10
            with contextlib.ExitStack() as xst:
                x = kb.sb('x' + sfx, [128, NC_, T], F32, stack=xst)
                xe = kb.dma(x[:], x_in.rearrange('(c p) t -> p c t', p=128),
                            'XL')
                lev = emit_ln_fm(kb, x, xln, consts, block,
                                 x_evt=[xe, ocev] + ([lwe] if affine else []),
                                 var_scale=vs, var_bias=vb, affine=affine)
            sbufs = declare_and_load(kb, globals()[block.upper() + '_WEIGHTS'],
                                     waits=[lev], sfx=sfx)
            if block == 'l3a':
                dd = dict(xln=xln, wevt=[kb.evt('WL')], lnevt=lev,
                          consts=consts, wq=sbufs['l3a_wq'],
                          woutT=sbufs['l3a_woutT'],
                          trimask=sbufs['trimask'], x_in=x_in, y_out=y_out)
                emit_l3a(kb, dd)
            elif block == 'l0a':
                consts['ones_row512'] = kb.sb('ones_row512' + sfx, [1, 512], F32)
                kb.V('memset', consts['ones_row512'][:], 1.0)
                cosd = nc.dram_tensor('l0a_cosT' + sfx, [128, T], F32,
                                      kind='ExternalInput').ap()
                sind = nc.dram_tensor('l0a_sinT' + sfx, [128, T], F32,
                                      kind='ExternalInput').ap()
                wvo = kb.sb('wvo_sb' + sfx, [128, NC_ * 2048], BF16)
                kb.dma(wvo[:], sbufs['l0a_wvT'][:], 'WV', queue='gpsimd')
                dd = dict(xg=xln, wevt=[kb.evt('WL')], lnevt=lev,
                          consts=consts, wqkT=sbufs['l0a_wqkT'],
                          bqk=sbufs['l0a_bqk'], gam=sbufs['l0a_gam'],
                          bet=sbufs['l0a_bet'], bias64=sbufs['l0a_bias64'],
                          cos_dram=cosd, sin_dram=sind,
                          idbf=sbufs['identity'],
                          wvo=wvo, bv=sbufs['l0a_bv'],
                          wgT_dram=sbufs['l0a_wgT'], bg=sbufs['l0a_bg'],
                          bo=sbufs['l0a_bo'], ln2w=sbufs['l0a_ln2w'],
                          ln2b=sbufs['l0a_ln2b'], x_in=x_in, y_out=y_out)
                dd['v_dram'] = nc.dram_tensor('v_dram' + sfx, [T, 2048], BF16,
                                              kind='Internal').ap()
                dd['wo_dram'] = sbufs['l0a_woT']
                dd['wvo_handle'] = wvo
                emit_l0a(kb, dd, trunc=globals().get('_L0A_TRUNC'))
            elif block == 'l1f':
                cmask = nc.dram_tensor('coremask' + sfx, [8, 1], F32,
                                       kind='ExternalInput').ap()
                cm_sb = kb.sb('coremask_sb' + sfx, [8, 1], F32)
                kb.dma(cm_sb[:], cmask[:], 'WL', queue='gpsimd')
                consts['iota8_rep'] = sbufs['iota8_rep']
                consts['triu128'] = sbufs['triu128']
                dd = dict(xln=xln, wevt=[kb.evt('WL')], lnevt=lev,
                          consts=consts, rw=sbufs['l1f_rw'],
                          b1c=sbufs['l1f_b1c'], b2c=sbufs['l1f_b2c'],
                          w1c_dram=sbufs['l1f_w1c'],
                          w2c_dram=sbufs['l1f_w2c'],
                          rowidx=sbufs['l1f_rowidx'],
                          coremask=cm_sb, fake_cc=fake_cc,
                          x_in=x_in, y_out=y_out)
                for nm, shp, dt in (('xc_dram', [8 * T, D], BF16),
                                    ('xf_dram', [8 * T, D], BF16),
                                    ('yc_dram', [8 * T, D], BF16),
                                    ('yf_dram', [8 * T, D], BF16),
                                    ('xseg_dram', [T, D], BF16),
                                    ('yfm_dram', [D, T], BF16),
                                    ('yfin_tok', [T, D], BF16),
                                    ('xlnfm_dram', [D, T], BF16)):
                    dd[nm] = nc.dram_tensor(nm + sfx, shp, dt,
                                            kind='Internal').ap()
                for nm, shp in (('cnt_dram', [1, 8]), ('ag_dram', [8, 8])):
                    dd[nm] = nc.dram_tensor(nm + sfx, shp, F32).ap()
                emit_l1f(kb, dd)
            elif block == 'l2f':
                dd = dict(xln=xln, wevt=[kb.evt('WL')], lnevt=lev,
                          consts=consts, wqp=sbufs['l2f_wqp'],
                          keysT=sbufs['l2f_keysT'], iotak=sbufs['iotak'],
                          idbf=sbufs['identity'],
                          wgT_dram=sbufs['l2f_wgT'],
                          wuT_dram=sbufs['l2f_wuT'],
                          wdT_dram=sbufs['l2f_wdT'],
                          down_dram=sbufs['l2f_down'],
                          up_dram=sbufs['l2f_up'], x_in=x_in, y_out=y_out)
                for nm, shp, dt in (('comb_dram', [16384, 2048], BF16),
                                    ('es_dram', [T, D], BF16),
                                    ('xlnfm_dram', [D, T], BF16)):
                    dd[nm] = nc.dram_tensor(nm + sfx, shp, dt, kind='Internal').ap()
                emit_l2f(kb, dd)
            elif block == 'l1a':
                consts['ones_row512'] = kb.sb('ones_row512' + sfx, [1, 512], F32)
                kb.V('memset', consts['ones_row512'][:], 1.0)
                dd = dict(xln=xln, wevt=[kb.evt('WL')], lnevt=lev,
                          consts=consts, wqk=sbufs['l1a_wqk'],
                          wv=sbufs['l1a_wv'], wo=sbufs['l1a_wo'],
                          bo=sbufs['l1a_bo'], thcol=sbufs['l1a_thcol'],
                          throw=sbufs['l1a_throw'], gam=sbufs['l1a_gam'],
                          bet=sbufs['l1a_bet'], trimask=sbufs['trimask'],
                          idbf=sbufs['identity'], x_in=x_in, y_out=y_out)
                dd['z_dram'] = nc.dram_tensor('z_dram' + sfx, [D, T], BF16,
                                              kind='Internal').ap()
                emit_l1a(kb, dd)
            elif block == 'l2a':
                consts['ones_row512'] = kb.sb('ones_row512' + sfx, [1, 512], F32)
                kb.V('memset', consts['ones_row512'][:], 1.0)
                dd = dict(xln=xln, wevt=[kb.evt('WL')], lnevt=lev,
                          consts=consts, wq=sbufs['l2a_wq'],
                          bqkv=sbufs['l2a_bqkv'],
                          bqkv_row=sbufs['l2a_bqkv_row'],
                          wo=sbufs['l2a_wo'], bo=sbufs['l2a_bo'],
                          i2=sbufs['i2'], ones2=sbufs['ones2'],
                          i2c=sbufs['i2c'], x_in=x_in, y_out=y_out)
                emit_l2a(kb, dd)
            return
        x = kb.sb('x' + sfx, [128, NC_, T], F32)
        xln = kb.sb('xln' + sfx, [128, NC_, T], BF16)
        xe = kb.dma(x[:], x_in.rearrange('(c p) t -> p c t', p=128), 'XL')

        consts = {}
        consts['ones_col'] = kb.sb('ones_col' + sfx, [128, 1], F32)
        consts['ones_row'] = kb.sb('ones_row' + sfx, [1, 128], F32)
        consts['ones_col_bf'] = kb.sb('ones_col_bf' + sfx, [128, 1], BF16)
        kb.V('memset', consts['ones_col'][:], 1.0)
        kb.V('memset', consts['ones_col_bf'][:], 1.0)
        ocev = kb.V('memset', consts['ones_row'][:], 1.0)

        if block == 'l0f':
            sbufs = declare_and_load(kb, L0F_WEIGHTS, sfx=sfx)
            lev = emit_ln_fm(kb, x, xln, consts, 'l0f', x_evt=[xe, ocev])
            dd = dict(x=x, xln=xln, wevt=[kb.evt('WL')], lnevt=lev,
                      consts=consts, idbf=sbufs['identity'], e8=sbufs['e8'],
                      wlin=sbufs['l0f_wlin'], blin=sbufs['l0f_blin'],
                      gw=sbufs['l0f_gw'], leftc=sbufs['l0f_leftc'],
                      rightc=sbufs['l0f_rightc'])
            oev = emit_l0f(kb, dd)
        elif block == 'l3f':
            sbufs = declare_and_load(kb, L3F_WEIGHTS, sfx=sfx)
            lev = emit_ln_fm(kb, x, xln, consts, 'l3f', x_evt=[xe, ocev])
            dd = dict(x=x, xln=xln, wevt=[kb.evt('WL')], lnevt=lev,
                      consts=consts, rw=sbufs['l3f_rw'], rb=sbufs['l3f_rb'],
                      ebrow=sbufs['l3f_ebrow'], ew_dram=sbufs['l3f_ew'])
            dd['yfin_tok'] = nc.dram_tensor('yfin_tok' + sfx, [T, D], BF16,
                                            kind='Internal').ap()
            oev = emit_l3f(kb, dd)
        else:
            raise ValueError(block)

        kb.dma(y_out.rearrange('(c p) t -> p c t', p=128), x[:], 'ST',
               waits=[oev])
    return




def _stage_barrier(kb):
    for eng in (kb.nc.tensor, kb.nc.vector, kb.nc.scalar, kb.nc.gpsimd,
                kb.nc.sync):
        for name, c in list(kb.cnt.items()):
            if c > 0:
                eng.wait_ge(kb.sem(name), c)


def build_block_program(block, fake_cc=False):
    nc = bass.Bass('TRN2', target_bir_lowering=False)
    with contextlib.ExitStack() as stack:
        kb = KB(nc, stack)
        x_in = nc.dram_tensor('x_fm', [D, T], F32, kind='ExternalInput').ap()
        y_out = nc.dram_tensor('y_fm', [D, T], F32,
                               kind='ExternalOutput').ap()
        emit_stage(kb, nc, block, x_in, y_out, fake_cc)
        for name in list(kb.cnt):
            if kb.cnt[name] > 0:
                kb.nc.sync.wait_ge(kb.sem(name), kb.cnt[name])
    return nc


def build_fused_program(fake_cc=False):
    nc = bass.Bass('TRN2', target_bir_lowering=False)
    with contextlib.ExitStack() as stack:
        kb = KB(nc, stack)
        x_ext = nc.dram_tensor('x_fm', [D, T], F32, kind='ExternalInput').ap()
        y_ext = nc.dram_tensor('y_fm', [D, T], F32,
                               kind='ExternalOutput').ap()
        chain = [x_ext]
        for k in range(7):
            chain.append(nc.dram_tensor('xchain%d' % k, [D, T], F32,
                                        kind='Internal').ap())
        chain.append(y_ext)
        outer = kb.stack
        for k, block in enumerate(BLOCK_ORDER):
            with contextlib.ExitStack() as sst:
                kb.stack = sst
                emit_stage(kb, nc, block, chain[k], chain[k + 1], fake_cc,
                           sfx='_s%d' % k)
                _stage_barrier(kb)
            kb.stack = outer
        for name in list(kb.cnt):
            if kb.cnt[name] > 0:
                kb.nc.sync.wait_ge(kb.sem(name), kb.cnt[name])
    return nc

BLOCK_ORDER = ['l0a', 'l0f', 'l1a', 'l1f', 'l2a', 'l2f', 'l3a', 'l3f']

BLOCK_EXTRAS = {
    'l0a': ['l0a_ln1w', 'l0a_ln1b', 'l0a_cosT', 'l0a_sinT'],
}

_PROG_CACHE = {}


def _get_prog(block):
    if block not in _PROG_CACHE:
        _PROG_CACHE[block] = build_block_program(block)
    return _PROG_CACHE[block]


def _block_input_names(block):
    names = [e[0] for e in globals()[block.upper() + '_WEIGHTS']
             if e[0] not in ('l1f_w1c', 'l1f_w2c', 'l1f_b1c', 'l1f_b2c',
                             'l1f_rowidx')]
    names += BLOCK_EXTRAS.get(block, [])
    return names


def _percore_extras(m, block, c, w, sfx=''):
    if block == 'l1f':
        cm = np.zeros((8, 1), np.float32)
        cm[:c] = 1.0
        m['coremask' + sfx] = cm
        m['l1f_w1c' + sfx] = w['l1f_w1'][c]
        m['l1f_w2c' + sfx] = w['l1f_w2'][c]
        m['l1f_b1c' + sfx] = w['l1f_b1'][c]
        m['l1f_b2c' + sfx] = w['l1f_b2'][c]
        m['l1f_rowidx' + sfx] = np.ascontiguousarray(
            np.arange(c * T, (c + 1) * T, dtype=np.int32)
            .reshape(16, 128).T)


def kernel(**inputs):
    w = prep_weights(inputs)
    if 'prog' not in _PROG_CACHE:
        _PROG_CACHE['prog'] = build_fused_program()
    nc = _PROG_CACHE['prog']
    x = np.asarray(inputs['L0_attn_x'], dtype=np.float32)
    maps = []
    for c in range(8):
        m = {'x_fm': np.ascontiguousarray(x[c].T)}
        for k, block in enumerate(BLOCK_ORDER):
            sfx = '_s%d' % k
            for n in _block_input_names(block):
                m[n + sfx] = w[n]
            _percore_extras(m, block, c, w, sfx)
        maps.append(m)
    res = run_bass_kernel_spmd(nc, maps, core_ids=list(range(8)))
    out = np.stack([np.ascontiguousarray(np.asarray(
        res.results[c]['y_fm']).T) for c in range(8)])
    return out.astype(np.float32)


def kernel_unfused(**inputs):
    w = prep_weights(inputs)
    x = np.asarray(inputs['L0_attn_x'], dtype=np.float32)
    xfm = [np.ascontiguousarray(x[c].T) for c in range(8)]
    for block in BLOCK_ORDER:
        nc = _get_prog(block)
        maps = []
        for c in range(8):
            m = {'x_fm': xfm[c]}
            for n in _block_input_names(block):
                m[n] = w[n]
            _percore_extras(m, block, c, w)
            maps.append(m)
        res = run_bass_kernel_spmd(nc, maps, core_ids=list(range(8)))
        xfm = [np.asarray(res.results[c]['y_fm']) for c in range(8)]
    out = np.stack([np.ascontiguousarray(xfm[c].T) for c in range(8)])
    return out.astype(np.float32)


# version 05
# speedup vs baseline: 1.1657x; 1.0692x over previous
import contextlib
import numpy as np
import ml_dtypes
import concourse.bass as bass
import concourse.mybir as mybir
from concourse.bass_utils import run_bass_kernel_spmd

F32 = mybir.dt.float32
BF16 = mybir.dt.bfloat16
I32 = mybir.dt.int32
AF = mybir.ActivationFunctionType
ALU = mybir.AluOpType
AX = mybir.AxisListType

T = 2048
D = 1024
NC_ = 8
bf16 = ml_dtypes.bfloat16


class KB:

    def __init__(self, nc, stack):
        self.nc = nc
        self.stack = stack
        self._sems = {}
        self.cnt = {}

    def sem(self, name):
        if name not in self._sems:
            self._sems[name] = self.stack.enter_context(self.nc.semaphore(name))
            self.cnt[name] = 0
        return self._sems[name]

    def _waits(self, eng, waits):
        for (s, c) in waits:
            if c > 0:
                eng.wait_ge(self.sem(s), c)

    def _chain(self, engname, semname, fn, *a, waits=(), **k):
        eng = getattr(self.nc, engname)
        s = self.sem(semname)
        if self.cnt[semname] > 0:
            eng.wait_ge(s, self.cnt[semname])
        self._waits(eng, waits)
        instr = getattr(eng, fn)(*a, **k)
        instr.then_inc(s, 1)
        self.cnt[semname] += 1
        return (semname, self.cnt[semname])

    def V(self, fn, *a, waits=(), **k):
        return self._chain('vector', 'Vc', fn, *a, waits=waits, **k)

    def S(self, fn, *a, waits=(), **k):
        return self._chain('scalar', 'Sc', fn, *a, waits=waits, **k)

    def G(self, fn, *a, waits=(), **k):
        return self._chain('gpsimd', 'Gc', fn, *a, waits=waits, **k)

    def T(self, fn, *a, waits=(), **k):
        eng = self.nc.tensor
        self._waits(eng, waits)
        return getattr(eng, fn)(*a, **k)

    def t_wait(self, waits):
        self._waits(self.nc.tensor, waits)

    def mark(self, instr, semname='PE'):
        instr.then_inc(self.sem(semname), 1)
        self.cnt[semname] += 1
        return (semname, self.cnt[semname])

    def dma(self, out, in_, sem, queue='sync', waits=(), **k):
        eng = getattr(self.nc, queue)
        self._waits(eng, waits)
        instr = eng.dma_start(out=out, in_=in_, **k)
        instr.then_inc(self.sem(sem), 16)
        self.cnt[sem] += 16
        return (sem, self.cnt[sem])

    def idma(self, out, out_offset, in_, in_offset, sem, waits=(), **k):
        eng = self.nc.gpsimd
        self._waits(eng, waits)
        instr = eng.indirect_dma_start(out, out_offset, in_, in_offset, **k)
        instr.then_inc(self.sem(sem), 16)
        self.cnt[sem] += 16
        return (sem, self.cnt[sem])

    def evt(self, semname):
        return (semname, self.cnt.get(semname, 0))

    def sb(self, name, shape, dtype, stack=None):
        return (stack or self.stack).enter_context(
            self.nc.sbuf_tensor(name, list(shape), dtype))

    def ps(self, name, shape, dtype=F32, stack=None):
        return (stack or self.stack).enter_context(
            self.nc.psum_tensor(name, list(shape), dtype))


def emit_ln_fm(kb, x, xln, consts, pfx, x_evt=(), eps=1e-5, var_scale=None,
               var_bias=None, affine=None):
    with contextlib.ExitStack() as st:
        sq = kb.sb(pfx + '_lnsq', [128, NC_, 512], F32, stack=st)
        stats = kb.sb(pfx + '_lnst', [1, 2, T], F32, stack=st)
        mr = kb.sb(pfx + '_lnmr', [1, T], F32, stack=st)
        rr = kb.sb(pfx + '_lnrr', [1, T], F32, stack=st)
        ps_s = kb.ps(pfx + '_lnps', [1, 512], stack=st)
        ps_q = kb.ps(pfx + '_lnpq', [1, 512], stack=st)
        ps_m = kb.ps(pfx + '_lnpm', [128, 512], stack=st)
        ps_r = kb.ps(pfx + '_lnpr', [128, 512], stack=st)
        eps_c = kb.sb(pfx + '_lneps', [1, 1], F32, stack=st)
        kb.V('memset', eps_c[:], float(eps))
        ones_col = consts['ones_col']
        ones_row = consts['ones_row']

        for tc in range(4):
            sl = slice(tc * 512, (tc + 1) * 512)
            for c in range(NC_):
                sev = kb.V('tensor_tensor', out=sq[:, c, :], in0=x[:, c, sl],
                           in1=x[:, c, sl], op=ALU.mult,
                           waits=(list(x_evt) + [kb.evt('PE')]) if c == 0 else ())
            for c in range(NC_):
                kb.T('matmul', ps_s[:, :], ones_col[:], x[:, c, sl],
                     start=(c == 0), stop=(c == NC_ - 1),
                     waits=(list(x_evt) + [kb.evt('Vc')]) if (tc == 0 and c == 0) else ())
            for c in range(NC_):
                mi = kb.T('matmul', ps_q[:, :], ones_col[:], sq[:, c, :],
                          start=(c == 0), stop=(c == NC_ - 1),
                          waits=[sev] if c == 0 else ())
            pe = kb.mark(mi, 'PE')
            kb.V('tensor_copy', out=stats[:, 0, sl], in_=ps_s[:, :], waits=[pe])
            cev = kb.V('tensor_copy', out=stats[:, 1, sl], in_=ps_q[:, :])
            kb.t_wait([cev])

        kb.V('tensor_scalar', out=mr[:], in0=stats[:, 0, :],
             scalar1=1.0 / D, scalar2=None, op0=ALU.mult)
        kb.V('tensor_scalar', out=stats[:, 1, :], in0=stats[:, 1, :],
             scalar1=1.0 / D, scalar2=None, op0=ALU.mult)
        kb.V('tensor_tensor', out=stats[:, 0, :], in0=mr[:], in1=mr[:],
             op=ALU.mult)
        kb.V('tensor_tensor', out=stats[:, 1, :], in0=stats[:, 1, :],
             in1=stats[:, 0, :], op=ALU.subtract)
        if var_scale is not None:
            kb.V('tensor_scalar', out=stats[:, 1, :], in0=stats[:, 1, :],
                 scalar1=float(var_scale), scalar2=float(var_bias),
                 op0=ALU.mult, op1=ALU.add)
            kb.V('memset', eps_c[:], 0.0)
        sev = kb.S('activation', out=stats[:, 1, :], in_=stats[:, 1, :],
                   func=AF.Sqrt, bias=eps_c[:], scale=1.0,
                   waits=[kb.evt('Vc')])
        rev = kb.V('reciprocal', out=rr[:], in_=stats[:, 1, :], waits=[sev])

        for tc in range(4):
            sl = slice(tc * 512, (tc + 1) * 512)
            kb.T('matmul', ps_m[:, :], ones_row[:], mr[:, sl], start=True,
                 stop=True, waits=[rev] if tc == 0 else ())
            mi = kb.T('matmul', ps_r[:, :], ones_row[:], rr[:, sl], start=True,
                      stop=True)
            pe = kb.mark(mi, 'PE')
            for c in range(NC_):
                kb.V('tensor_tensor', out=sq[:, c, :], in0=x[:, c, sl],
                     in1=ps_m[:, :], op=ALU.subtract,
                     waits=[pe] if c == 0 else ())
                if affine is None:
                    oev = kb.V('tensor_tensor', out=xln[:, c, sl],
                               in0=sq[:, c, :], in1=ps_r[:, :], op=ALU.mult)
                else:
                    kb.V('tensor_tensor', out=sq[:, c, :], in0=sq[:, c, :],
                         in1=ps_r[:, :], op=ALU.mult)
                    oev = kb.V('tensor_scalar', out=xln[:, c, sl],
                               in0=sq[:, c, :],
                               scalar1=affine[0][:, c:c + 1],
                               scalar2=affine[1][:, c:c + 1],
                               op0=ALU.mult, op1=ALU.add)
            kb.t_wait([oev])
        return oev


def emit_l0f(kb, d):
    x, xln = d['x'], d['xln']
    wev, lev = d['wevt'], d['lnevt']
    with contextlib.ExitStack() as st:
        gval = kb.sb('l0f_g', [128, 16, 8], BF16, stack=st)
        gtmp = kb.sb('l0f_gt', [128, 8], F32, stack=st)
        m8 = kb.sb('l0f_m8', [128, 8], F32, stack=st)
        nm1 = kb.sb('l0f_nm1', [128, 1], F32, stack=st)
        ssum = kb.sb('l0f_ss', [128, 1], F32, stack=st)
        srec = kb.sb('l0f_sr', [128, 1], F32, stack=st)
        mask = kb.sb('l0f_msk', [128, 8], F32, stack=st)
        gT = kb.sb('l0f_gTs', [8, 16, 128], BF16, stack=st)
        tall = kb.sb('l0f_tall', [32, T], BF16, stack=st)
        u = kb.sb('l0f_u', [32, T], BF16, stack=st)
        glog = kb.ps('l0f_glog', [128, 128], stack=st)
        ps_gt = kb.ps('l0f_psgt', [8, 128], BF16, stack=st)
        ps_t = kb.ps('l0f_pst', [32, 512], stack=st)
        ps_gr = kb.ps('l0f_psgr', [32, 512], stack=st)
        ps_o0 = kb.ps('l0f_o0', [128, 512], stack=st)
        ps_o1 = kb.ps('l0f_o1', [128, 512], stack=st)

        for tt in range(16):
            lg = glog[:, (tt % 2) * 8:(tt % 2) * 8 + 8]
            for dc in range(NC_):
                mi = kb.T('matmul', lg, xln[:, dc, tt * 128:(tt + 1) * 128],
                          d['gw'][:, dc, :], start=(dc == 0),
                          stop=(dc == NC_ - 1),
                          waits=(list(wev) + [lev]) if (tt == 0 and dc == 0)
                          else ([kb.evt('Vc')] if dc == 0 else ()))
            pe = kb.mark(mi, 'PE')
            kb.V('max', out=m8[:], in_=lg, waits=[pe])
            kb.V('tensor_scalar', out=nm1[:], in0=m8[:, 0:1], scalar1=-1.0,
                 scalar2=None, op0=ALU.mult)
            ev = kb.S('activation', out=gtmp[:], in_=lg, func=AF.Exp,
                      bias=nm1[:], scale=1.0, waits=[kb.evt('Vc')])
            kb.V('tensor_reduce', out=ssum[:], in_=gtmp[:], axis=AX.X,
                 op=ALU.add, waits=[ev])
            kb.V('reciprocal', out=srec[:], in_=ssum[:])
            kb.V('tensor_scalar', out=mask[:], in0=lg, scalar1=m8[:, 1:2],
                 scalar2=None, op0=ALU.is_ge)
            gev = kb.V('scalar_tensor_tensor', out=gval[:, tt, :], in0=gtmp[:],
                       scalar=srec[:], in1=mask[:], op0=ALU.mult, op1=ALU.mult)
            tr = kb.T('transpose', ps_gt[:, :], gval[:, tt, :], d['idbf'][:],
                      waits=[gev])
            pe = kb.mark(tr, 'PE')
            cev = kb.V('tensor_copy', out=gT[:, tt, :], in_=ps_gt[:, :],
                       waits=[pe])
            kb.t_wait([cev])

        for tc in range(4):
            sl = slice(tc * 512, (tc + 1) * 512)
            for dc in range(NC_):
                mi = kb.T('matmul', ps_t[:, :], d['leftc'][:, dc, :],
                          xln[:, dc, sl], start=(dc == 0),
                          stop=(dc == NC_ - 1))
            pe1 = kb.mark(mi, 'PE')
            mi = kb.T('matmul', ps_gr[:, :], d['e8'][:],
                      gT[:, tc * 4:(tc + 1) * 4, :].rearrange(
                          'p a b -> p (a b)'), start=True, stop=True)
            pe2 = kb.mark(mi, 'PE')
            kb.V('tensor_copy', out=tall[:, sl], in_=ps_t[:, :], waits=[pe1])
            uev = kb.V('tensor_tensor', out=u[:, sl], in0=tall[:, sl],
                       in1=ps_gr[:, :], op=ALU.mult, waits=[pe2])
            kb.t_wait([uev])

        for fc in range(NC_):
            for tc in range(4):
                sl = slice(tc * 512, (tc + 1) * 512)
                pso = ps_o0 if (tc % 2 == 0) else ps_o1
                for dc in range(NC_):
                    kb.T('matmul', pso[:, :], d['wlin'][:, dc, fc, :],
                         xln[:, dc, sl], start=(dc == 0), stop=False,
                         waits=[uev] if (fc == 0 and tc == 0 and dc == 0)
                         else ())
                mi = kb.T('matmul', pso[:, :], d['rightc'][:, fc, :], u[:, sl],
                          start=False, stop=True)
                pe = kb.mark(mi, 'PE')
                rev = kb.V('scalar_tensor_tensor', out=x[:, fc, sl],
                           in0=pso[:, :], scalar=d['blin'][:, fc:fc + 1],
                           in1=x[:, fc, sl], op0=ALU.add, op1=ALU.add,
                           waits=[pe])
                kb.t_wait([rev])
        return rev


SEG = 384
NSLOT = 8 * SEG


def emit_router_scatter(kb, d, pfx, st):
    pst = st.enter_context(contextlib.ExitStack())
    lg_ps = kb.ps(pfx + '_lg', [128, 128], stack=pst)
    pos_ps = kb.ps(pfx + '_pos', [128, 128], stack=pst)
    cnt1_ps = kb.ps(pfx + '_c1', [1, 8], stack=pst)
    cntr_ps = kb.ps(pfx + '_cr', [128, 8], stack=pst)
    m8 = kb.sb(pfx + '_m8', [128, 8], F32, stack=st)
    emask = kb.sb(pfx + '_oh', [128, 16, 8], BF16, stack=st)
    ohf = kb.sb(pfx + '_ohf', [128, 8], F32, stack=st)
    gate = kb.sb(pfx + '_gate', [128, 16], F32, stack=st)
    idxf = kb.sb(pfx + '_idxf', [128, 16], F32, stack=st)
    posd = kb.sb(pfx + '_posd', [128, 16], F32, stack=st)
    slotf = kb.sb(pfx + '_slotf', [128, 16], F32, stack=st)
    slot_i = kb.sb(pfx + '_sloti', [128, 16], I32, stack=st)
    tmp8 = kb.sb(pfx + '_tmp8', [128, 8], F32, stack=st)
    etmp = kb.sb(pfx + '_etmp', [128, 8], F32, stack=st)
    ovf = kb.sb(pfx + '_ovf', [128, 1], F32, stack=st)
    cnt_row = kb.sb(pfx + '_cnt', [1, 8], F32, stack=st)
    offrep = kb.sb(pfx + '_offrep', [128, 8], F32, stack=st)
    ssum = kb.sb(pfx + '_ssum', [128, 1], F32, stack=st)
    nm1 = kb.sb(pfx + '_nm1', [128, 1], F32, stack=st)
    xtok = kb.sb(pfx + '_xtok', [128, 1024], BF16, stack=st)
    kb.V('memset', offrep[:], 0.0)

    for c in range(NC_):
        kb.dma(d['xlnfm_dram'][c * 128:(c + 1) * 128, :], d['xln'][:, c, :],
               'XW', queue='sync', waits=[d['lnevt']] if c == 0 else ())
    tile_ev = {}
    for tt in range(16):
        lg = lg_ps[:, (tt % 2) * 8:(tt % 2) * 8 + 8]
        have_rb = d.get('rb') is not None
        for dc in range(NC_):
            mi = kb.T('matmul', lg, d['xln'][:, dc, tt * 128:(tt + 1) * 128],
                      d['rw'][:, dc, :], start=(dc == 0),
                      stop=(dc == NC_ - 1 and not have_rb),
                      waits=(list(d['wevt']) + [d['lnevt']])
                      if (tt == 0 and dc == 0)
                      else ([kb.evt('Vc'), kb.evt('Sc')] if dc == 0 else ()))
        if have_rb:
            mi = kb.T('matmul', lg, d['consts']['ones_row'][:], d['rb'][:],
                      start=False, stop=True)
        pe = kb.mark(mi, 'PE')
        kb.V('max', out=m8[:], in_=lg, waits=[pe])
        kb.V('tensor_scalar', out=nm1[:], in0=m8[:, 0:1], scalar1=-1.0,
             scalar2=None, op0=ALU.mult)
        ev = kb.S('activation', out=tmp8[:], in_=lg, func=AF.Exp,
                  bias=nm1[:], scale=1.0, waits=[kb.evt('Vc')])
        kb.V('tensor_reduce', out=ssum[:], in_=tmp8[:], axis=AX.X,
             op=ALU.add, waits=[ev])
        kb.V('reciprocal', out=gate[:, tt:tt + 1], in_=ssum[:])
        kb.V('tensor_scalar', out=emask[:, tt, :], in0=lg, scalar1=m8[:, 0:1],
             scalar2=None, op0=ALU.is_ge)
        kb.V('tensor_copy', out=ohf[:], in_=emask[:, tt, :])
        kb.V('tensor_tensor', out=etmp[:], in0=ohf[:],
             in1=d['consts']['iota8_rep'][:], op=ALU.mult)
        kb.V('tensor_reduce', out=idxf[:, tt:tt + 1], in_=etmp[:], axis=AX.X,
             op=ALU.add)
        oev = kb.evt('Vc')
        mi = kb.T('matmul', pos_ps[:, (tt % 2) * 8:(tt % 2) * 8 + 8],
                  d['consts']['triu128'][:], emask[:, tt, :],
                  start=True, stop=True, waits=[oev])
        mi2 = kb.T('matmul', cnt1_ps[:, :], d['consts']['ones_col_bf'][:],
                   emask[:, tt, :], start=True, stop=True)
        pe = kb.mark(mi2, 'PE')
        pos = pos_ps[:, (tt % 2) * 8:(tt % 2) * 8 + 8]
        kb.V('tensor_tensor', out=etmp[:], in0=ohf[:], in1=pos, op=ALU.mult,
             waits=[pe])
        kb.V('tensor_reduce', out=posd[:, tt:tt + 1], in_=etmp[:], axis=AX.X,
             op=ALU.add)
        kb.V('tensor_tensor', out=etmp[:], in0=ohf[:], in1=offrep[:],
             op=ALU.mult)
        kb.V('tensor_reduce', out=tmp8[:, 0:1], in_=etmp[:], axis=AX.X,
             op=ALU.add)
        kb.V('scalar_tensor_tensor', out=posd[:, tt:tt + 1],
             in0=posd[:, tt:tt + 1], scalar=-1.0, in1=tmp8[:, 0:1],
             op0=ALU.add, op1=ALU.add)
        kb.V('scalar_tensor_tensor', out=slotf[:, tt:tt + 1],
             in0=idxf[:, tt:tt + 1], scalar=float(SEG),
             in1=posd[:, tt:tt + 1], op0=ALU.mult, op1=ALU.add)
        kb.V('tensor_scalar', out=ovf[:], in0=posd[:, tt:tt + 1],
             scalar1=float(SEG), scalar2=100000.0, op0=ALU.is_ge, op1=ALU.mult)
        kb.V('tensor_tensor', out=slotf[:, tt:tt + 1],
             in0=slotf[:, tt:tt + 1], in1=ovf[:], op=ALU.add)
        cev = kb.V('tensor_copy', out=slot_i[:, tt:tt + 1],
                   in_=slotf[:, tt:tt + 1])
        kb.V('tensor_copy', out=cnt_row[:], in_=cnt1_ps[:, :])
        crev = kb.evt('Vc')
        mi = kb.T('matmul', cntr_ps[:, :], d['consts']['ones_row'][:],
                  cnt_row[:], start=True, stop=True, waits=[crev])
        pe2 = kb.mark(mi, 'PE')
        uev = kb.V('tensor_tensor', out=offrep[:], in0=offrep[:],
                   in1=cntr_ps[:, :], op=ALU.add, waits=[pe2])
        kb.t_wait([uev])
        tile_ev[tt] = cev
    sev = None
    for tt in range(16):
        tl = kb.dma(xtok[:], d['xlnfm_dram'][:, tt * 128:(tt + 1) * 128],
                    'TL', queue='sync', transpose=True,
                    waits=[kb.evt('SC')] if tt > 0 else [kb.evt('XW')])
        sev = kb.idma(d['xsort'], bass.IndirectOffsetOnAxis(
            ap=slot_i[:, tt:tt + 1], axis=0), xtok[:], None, 'SC',
            waits=[tl, tile_ev[tt], kb.evt('ZF')],
            bounds_check=NSLOT - 1, oob_is_err=False)
    d['slot_i'] = slot_i
    d['gate'] = gate
    d['emask_bf'] = emask
    d['idxf'] = idxf
    d['posd'] = posd
    kb._l1f_offrep = offrep
    kb.t_wait([kb.evt('Vc'), kb.evt('Sc')])
    pst.close()
    return sev


def emit_zero_fill(kb, d, pfx, st, bufs):
    zb = kb.sb(pfx + '_zb', [128, 1024], BF16, stack=st)
    zev = kb.V('memset', zb[:], 0.0)
    for buf, rows in bufs:
        for i in range(rows // 128):
            kb.dma(buf[i * 128:(i + 1) * 128, :], zb[:], 'ZF',
                   queue='scalar', waits=[zev] if i == 0 else ())


def emit_l3f(kb, d):
    with contextlib.ExitStack() as st:
        xln = d['xln']
        wev, lev = d['wevt'], d['lnevt']
        lg_ps = kb.ps('l3f_lg', [128, 128], stack=st)
        m8 = kb.sb('l3f_m8', [128, 8], F32, stack=st)
        ohf = kb.sb('l3f_ohf', [128, 16, 8], F32, stack=st)
        gate = kb.sb('l3f_gate', [128, 16], F32, stack=st)
        tmp8 = kb.sb('l3f_tmp8', [128, 8], F32, stack=st)
        ssum = kb.sb('l3f_ssum', [128, 1], F32, stack=st)
        nm1 = kb.sb('l3f_nm1', [128, 1], F32, stack=st)
        acc = kb.sb('l3f_acc', [128, 8, 1024], F32, stack=st)
        ysb = kb.sb('l3f_ysb', [128, 1024], BF16, stack=st)
        ew_sb = [kb.sb('l3f_ew0', [128, NC_, NC_, 128], BF16, stack=st),
                 kb.sb('l3f_ew1', [128, NC_, NC_, 128], BF16, stack=st)]
        ps_y = [kb.ps('l3f_py0', [128, 512], stack=st),
                kb.ps('l3f_py1', [128, 512], stack=st)]
        for tt in range(16):
            lg = lg_ps[:, (tt % 2) * 8:(tt % 2) * 8 + 8]
            for dc in range(NC_):
                kb.T('matmul', lg, xln[:, dc, tt * 128:(tt + 1) * 128],
                     d['rw'][:, dc, :], start=(dc == 0), stop=False,
                     waits=(list(wev) + [lev]) if (tt == 0 and dc == 0)
                     else ([kb.evt('Vc'), kb.evt('Sc')] if dc == 0 else ()))
            mi = kb.T('matmul', lg, d['consts']['ones_row'][:], d['rb'][:],
                      start=False, stop=True)
            pe = kb.mark(mi, 'PE')
            kb.V('max', out=m8[:], in_=lg, waits=[pe])
            kb.V('tensor_scalar', out=nm1[:], in0=m8[:, 0:1], scalar1=-1.0,
                 scalar2=None, op0=ALU.mult)
            ev = kb.S('activation', out=tmp8[:], in_=lg, func=AF.Exp,
                      bias=nm1[:], scale=1.0, waits=[kb.evt('Vc')])
            kb.V('tensor_reduce', out=ssum[:], in_=tmp8[:], axis=AX.X,
                 op=ALU.add, waits=[ev])
            kb.V('reciprocal', out=gate[:, tt:tt + 1], in_=ssum[:])
            kb.V('tensor_scalar', out=ohf[:, tt, :], in0=lg,
                 scalar1=m8[:, 0:1], scalar2=None, op0=ALU.is_ge)
            ge = kb.V('tensor_scalar', out=ohf[:, tt, :], in0=ohf[:, tt, :],
                      scalar1=gate[:, tt:tt + 1], scalar2=None, op0=ALU.mult)
            kb.t_wait([ge])
        for half in range(2):
            for e in range(8):
                b = e % 2
                wl = kb.dma(ew_sb[b][:], d['ew_dram'][e], 'EW%d' % b,
                            queue='gpsimd',
                            waits=[kb.evt('PE'), kb.evt('Vc')])
                for ti in range(8):
                    tt = half * 8 + ti
                    tsl = slice(tt * 128, (tt + 1) * 128)
                    for h2 in range(2):
                        py = ps_y[h2]
                        for hc in range(NC_):
                            kb.T('matmul', py[:, :], xln[:, hc, tsl],
                                 ew_sb[b][:, hc, 4 * h2:4 * h2 + 4, :]
                                 .rearrange('p a b -> p (a b)'),
                                 start=(hc == 0), stop=False,
                                 waits=[wl, kb.evt('Vc')]
                                 if (ti == 0 and h2 == 0 and hc == 0)
                                 else ([kb.evt('Vc')] if hc == 0 else ()))
                        mi = kb.T('matmul', py[:, :],
                                  d['consts']['ones_row'][:],
                                  d['ebrow'][:, e,
                                             512 * h2:512 * h2 + 512],
                                  start=False, stop=True)
                        pe = kb.mark(mi, 'PE')
                    for h2 in range(2):
                        csl = slice(h2 * 512, (h2 + 1) * 512)
                        if e == 0:
                            kb.V('tensor_scalar', out=acc[:, ti, csl],
                                 in0=ps_y[h2][:, :],
                                 scalar1=ohf[:, tt, 0:1], scalar2=None,
                                 op0=ALU.mult, waits=[pe])
                        else:
                            kb.V('scalar_tensor_tensor', out=acc[:, ti, csl],
                                 in0=ps_y[h2][:, :],
                                 scalar=ohf[:, tt, e:e + 1],
                                 in1=acc[:, ti, csl], op0=ALU.mult,
                                 op1=ALU.add, waits=[pe])
                    if e == 7:
                        ce = kb.V('tensor_copy', out=ysb[:],
                                  in_=acc[:, ti, :], waits=[kb.evt('FW')])
                        kb.dma(d['yfin_tok'][tsl, :], ysb[:], 'FW',
                               queue='scalar', waits=[ce])
                    kb.t_wait([kb.evt('Vc')])
        ffm = kb.sb('l3f_ffm', [128, T], BF16, stack=st)
        rev = None
        for c in range(NC_):
            tl = kb.dma(ffm[:], d['yfin_tok'][:, c * 128:(c + 1) * 128],
                        'FT', queue='sync', transpose=True,
                        waits=[kb.evt('FW')] if c == 0 else [kb.evt('Vc')])
            rev = kb.V('tensor_tensor', out=d['x'][:, c, :],
                       in0=d['x'][:, c, :], in1=ffm[:], op=ALU.add,
                       waits=[tl])
        return rev


def emit_l1f(kb, d):
    C = 2048
    with contextlib.ExitStack() as st:
        xln = d['xln']
        wev, lev = d['wevt'], d['lnevt']
        ones_row = d['consts']['ones_row']

        pst = st.enter_context(contextlib.ExitStack())
        lg_ps = kb.ps('l1f_lg', [128, 128], stack=pst)
        pos_ps = kb.ps('l1f_pos', [128, 128], stack=pst)
        cnt1_ps = kb.ps('l1f_c1', [1, 8], stack=pst)
        cntr_ps = kb.ps('l1f_cr', [128, 8], stack=pst)
        m8 = kb.sb('l1f_m8', [128, 8], F32, stack=st)
        emask = kb.sb('l1f_oh', [128, 16, 8], BF16, stack=st)
        ohf = kb.sb('l1f_ohf', [128, 8], F32, stack=st)
        gate = kb.sb('l1f_gate', [128, 16], F32, stack=st)
        idxf = kb.sb('l1f_idxf', [128, 16], F32, stack=st)
        posd = kb.sb('l1f_posd', [128, 16], F32, stack=st)
        slotf = kb.sb('l1f_slotf', [128, 16], F32, stack=st)
        slot_i = kb.sb('l1f_sloti', [128, 16], I32, stack=st)
        tmp8 = kb.sb('l1f_tmp8', [128, 8], F32, stack=st)
        etmp = kb.sb('l1f_etmp', [128, 8], F32, stack=st)
        cnt_row = kb.sb('l1f_cnt', [1, 8], F32, stack=st)
        offrep = kb.sb('l1f_offrep', [128, 8], F32, stack=st)
        ssum = kb.sb('l1f_ssum', [128, 1], F32, stack=st)
        nm1 = kb.sb('l1f_nm1', [128, 1], F32, stack=st)
        gsel = kb.sb('l1f_gsel', [128, 1], F32, stack=st)
        ag = kb.sb('l1f_ag', [8, 8], F32, stack=st)
        goff_row = kb.sb('l1f_goffr', [1, 8], F32, stack=st)
        kb.V('memset', offrep[:], 0.0)
        for c in range(NC_):
            kb.dma(d['xlnfm_dram'][c * 128:(c + 1) * 128, :], xln[:, c, :],
                   'XW', queue='sync', waits=[lev] if c == 0 else ())
        for tt in range(16):
            lg = lg_ps[:, (tt % 2) * 8:(tt % 2) * 8 + 8]
            for dc in range(NC_):
                mi = kb.T('matmul', lg, xln[:, dc, tt * 128:(tt + 1) * 128],
                          d['rw'][:, dc, :], start=(dc == 0),
                          stop=(dc == NC_ - 1),
                          waits=(list(wev) + [lev]) if (tt == 0 and dc == 0)
                          else ([kb.evt('Vc'), kb.evt('Sc')]
                                if dc == 0 else ()))
            pe = kb.mark(mi, 'PE')
            kb.V('max', out=m8[:], in_=lg, waits=[pe])
            kb.V('tensor_scalar', out=nm1[:], in0=m8[:, 0:1], scalar1=-1.0,
                 scalar2=None, op0=ALU.mult)
            ev = kb.S('activation', out=tmp8[:], in_=lg, func=AF.Exp,
                      bias=nm1[:], scale=1.0, waits=[kb.evt('Vc')])
            kb.V('tensor_reduce', out=ssum[:], in_=tmp8[:], axis=AX.X,
                 op=ALU.add, waits=[ev])
            kb.V('reciprocal', out=gate[:, tt:tt + 1], in_=ssum[:])
            kb.V('tensor_scalar', out=emask[:, tt, :], in0=lg,
                 scalar1=m8[:, 0:1], scalar2=None, op0=ALU.is_ge)
            kb.V('tensor_copy', out=ohf[:], in_=emask[:, tt, :])
            kb.V('tensor_tensor', out=etmp[:], in0=ohf[:],
                 in1=d['consts']['iota8_rep'][:], op=ALU.mult)
            kb.V('tensor_reduce', out=idxf[:, tt:tt + 1], in_=etmp[:],
                 axis=AX.X, op=ALU.add)
            oev = kb.evt('Vc')
            mi = kb.T('matmul', pos_ps[:, (tt % 2) * 8:(tt % 2) * 8 + 8],
                      d['consts']['triu128'][:], emask[:, tt, :],
                      start=True, stop=True, waits=[oev])
            mi2 = kb.T('matmul', cnt1_ps[:, :], d['consts']['ones_col_bf'][:],
                       emask[:, tt, :], start=True, stop=True)
            pe = kb.mark(mi2, 'PE')
            pos = pos_ps[:, (tt % 2) * 8:(tt % 2) * 8 + 8]
            kb.V('tensor_tensor', out=etmp[:], in0=ohf[:], in1=pos,
                 op=ALU.mult, waits=[pe])
            kb.V('tensor_reduce', out=posd[:, tt:tt + 1], in_=etmp[:],
                 axis=AX.X, op=ALU.add)
            kb.V('tensor_tensor', out=etmp[:], in0=ohf[:], in1=offrep[:],
                 op=ALU.mult)
            kb.V('tensor_reduce', out=tmp8[:, 0:1], in_=etmp[:], axis=AX.X,
                 op=ALU.add)
            kb.V('scalar_tensor_tensor', out=posd[:, tt:tt + 1],
                 in0=posd[:, tt:tt + 1], scalar=-1.0, in1=tmp8[:, 0:1],
                 op0=ALU.add, op1=ALU.add)
            kb.V('tensor_copy', out=cnt_row[:], in_=cnt1_ps[:, :])
            crev = kb.evt('Vc')
            mi = kb.T('matmul', cntr_ps[:, :], ones_row[:], cnt_row[:],
                      start=True, stop=True, waits=[crev])
            pe2 = kb.mark(mi, 'PE')
            uev = kb.V('tensor_tensor', out=offrep[:], in0=offrep[:],
                       in1=cntr_ps[:, :], op=ALU.add, waits=[pe2])
            kb.t_wait([uev])
        cw = kb.dma(d['cnt_dram'], offrep[0:1, :], 'CW', queue='scalar',
                    waits=[kb.evt('Vc')])
        if d.get('fake_cc'):
            zb8 = kb.sb('l1f_zag', [8, 8], F32, stack=st)
            zze = kb.V('memset', zb8[:], 0.0)
            cc = kb.dma(d['ag_dram'], zb8[:], 'CC', queue='scalar',
                        waits=[zze, cw])
        else:
            eng = kb.nc.gpsimd
            eng.wait_ge(kb.sem('CW'), kb.cnt['CW'])
            ci = eng.collective_compute(
                'AllGather', ALU.bypass, [list(range(8))],
                [d['cnt_dram'].opt()], [d['ag_dram'].opt()])
            ci.then_inc(kb.sem('CC'), 1)
            kb.cnt['CC'] += 1
            cc = kb.evt('CC')
        agl = kb.dma(ag[:], d['ag_dram'], 'AGL', queue='sync', waits=[cc])
        goff_ps = cnt1_ps
        mi = kb.T('matmul', goff_ps[:, :], d['coremask'][:], ag[:],
                  start=True, stop=True, waits=[agl, kb.evt('Vc')])
        pe = kb.mark(mi, 'PE')
        gev = kb.V('tensor_copy', out=goff_row[:], in_=goff_ps[:, :],
                   waits=[pe])
        mi = kb.T('matmul', cntr_ps[:, :], ones_row[:], goff_row[:],
                  start=True, stop=True, waits=[gev])
        pe = kb.mark(mi, 'PE')
        for tt in range(16):
            kb.V('tensor_tensor', out=etmp[:], in0=emask[:, tt, :],
                 in1=cntr_ps[:, :], op=ALU.mult, waits=[pe] if tt == 0 else ())
            kb.V('tensor_reduce', out=gsel[:], in_=etmp[:], axis=AX.X,
                 op=ALU.add)
            kb.V('tensor_tensor', out=posd[:, tt:tt + 1],
                 in0=posd[:, tt:tt + 1], in1=gsel[:], op=ALU.add)
            kb.V('tensor_scalar', out=gsel[:], in0=posd[:, tt:tt + 1],
                 scalar1=float(C), scalar2=None, op0=ALU.is_lt)
            kb.V('tensor_tensor', out=gate[:, tt:tt + 1],
                 in0=gate[:, tt:tt + 1], in1=gsel[:], op=ALU.mult)
            kb.V('scalar_tensor_tensor', out=slotf[:, tt:tt + 1],
                 in0=idxf[:, tt:tt + 1], scalar=float(C),
                 in1=posd[:, tt:tt + 1], op0=ALU.mult, op1=ALU.add)
            kb.V('tensor_scalar', out=gsel[:], in0=gsel[:],
                 scalar1=-1000000.0, scalar2=1000000.0, op0=ALU.mult,
                 op1=ALU.add)
            kb.V('tensor_tensor', out=slotf[:, tt:tt + 1],
                 in0=slotf[:, tt:tt + 1], in1=gsel[:], op=ALU.add)
            kev = kb.V('tensor_copy', out=slot_i[:, tt:tt + 1],
                       in_=slotf[:, tt:tt + 1])
        kb.t_wait([kev])
        pst.close()

        bcreg = kb.nc.gpsimd.to_reg(8 * C - 1)
        zb = kb.sb('l1f_zb', [128, 8192], BF16, stack=st)
        zev = kb.V('memset', zb[:], 0.0)
        for buf in (d['xc_dram'], d['yc_dram']):
            for i in range(16):
                kb.dma(buf.rearrange('(a p) d -> p a d', p=128)[
                    :, i * 8:(i + 1) * 8, :],
                    zb[:].rearrange('p (a d) -> p a d', d=1024), 'ZF',
                    queue='scalar', waits=[zev] if i == 0 else ())
        xtok = kb.sb('l1f_xtok', [128, 1024], BF16, stack=st)
        sev = None
        for tt in range(16):
            tl = kb.dma(xtok[:], d['xlnfm_dram'][:, tt * 128:(tt + 1) * 128],
                        'TL', queue='sync', transpose=True,
                        waits=[kb.evt('SC')] if tt > 0 else
                        [kb.evt('XW'), kb.evt('ZF')])
            sev = kb.idma(d['xc_dram'], bass.IndirectOffsetOnAxis(
                ap=slot_i[:, tt:tt + 1], axis=0), xtok[:], None, 'SC',
                waits=[tl, kev], bounds_check=bcreg, oob_is_err=False)
        if d.get('fake_cc'):
            ar1 = kb.dma(d['xf_dram'], d['xc_dram'], 'AR', queue='scalar',
                         waits=[sev, kb.evt('ZF')])
        else:
            eng = kb.nc.gpsimd
            eng.wait_ge(kb.sem('SC'), kb.cnt['SC'])
            eng.wait_ge(kb.sem('ZF'), kb.cnt['ZF'])
            ci = eng.collective_compute(
                'AllReduce', ALU.add, [list(range(8))],
                [d['xc_dram'].opt()], [d['xf_dram'].opt()])
            ci.then_inc(kb.sem('AR'), 1)
            kb.cnt['AR'] += 1
            ar1 = kb.evt('AR')
        ridx = d['rowidx']
        rie = kb.evt('WL')
        for i in range(16):
            ge = kb.idma(xtok[:], None, d['xf_dram'],
                         bass.IndirectOffsetOnAxis(ap=ridx[:, i:i + 1],
                                                   axis=0), 'GA',
                         waits=[ar1, rie, kb.evt('SG')] if i == 0 else
                         [kb.evt('SG')])
            kb.dma(d['xseg_dram'][i * 128:(i + 1) * 128, :], xtok[:], 'SG',
                   queue='scalar', waits=[ge])
        FH = 16
        xseg = kb.sb('l1f_xs', [128, NC_, 512], BF16, stack=st)
        w1h = kb.sb('l1f_w1h', [128, NC_, FH, 128], BF16, stack=st)
        w2h = kb.sb('l1f_w2h', [128, FH, NC_, 128], BF16, stack=st)
        hsb = kb.sb('l1f_h', [128, FH, 512], BF16, stack=st)
        ysum = kb.sb('l1f_y', [128, NC_, 512], F32, stack=st)
        yfm = kb.sb('l1f_yfm', [128, 512], BF16, stack=st)
        ps_h = [kb.ps('l1f_ph0', [128, 512], stack=st),
                kb.ps('l1f_ph1', [128, 512], stack=st)]
        ps_y = [kb.ps('l1f_py0', [128, 512], stack=st),
                kb.ps('l1f_py1', [128, 512], stack=st)]
        for sc in range(4):
            ssl = slice(sc * 512, (sc + 1) * 512)
            for dc in range(NC_):
                xl = kb.dma(xseg[:, dc, :],
                            d['xseg_dram'][ssl, dc * 128:(dc + 1) * 128],
                            'XT', queue='sync', transpose=True,
                            waits=([kb.evt('SG'), kb.evt('PE')]
                                   if dc == 0 else []))
            for hf in range(2):
                wl1 = kb.dma(w1h[:], d['w1c_dram'][:, :,
                                                   hf * FH:(hf + 1) * FH, :],
                             'EW1', queue='gpsimd', waits=[kb.evt('PE')])
                wl2 = kb.dma(w2h[:], d['w2c_dram'][:,
                                                   hf * FH:(hf + 1) * FH,
                                                   :, :],
                             'EW2', queue='gpsimd', waits=[kb.evt('PE')])
                for fc in range(FH):
                    ph = ps_h[fc % 2]
                    for dc in range(NC_):
                        mi = kb.T('matmul', ph[:, :], w1h[:, dc, fc, :],
                                  xseg[:, dc, :], start=(dc == 0),
                                  stop=(dc == NC_ - 1),
                                  waits=[wl1, xl, kb.evt('Sc'),
                                         kb.evt('Vc')]
                                  if fc == 0 and dc == 0 else
                                  ([kb.evt('Sc')] if dc == 0 else ()))
                    pe1 = kb.mark(mi, 'PE')
                    kb.S('activation', out=hsb[:, fc, :], in_=ph[:, :],
                         func=AF.Relu,
                         bias=d['b1c'][:, hf * FH + fc:hf * FH + fc + 1],
                         scale=1.0, waits=[pe1])
                for dcol in range(NC_):
                    py = ps_y[dcol % 2]
                    for fc in range(FH):
                        mi = kb.T('matmul', py[:, :], w2h[:, fc, dcol, :],
                                  hsb[:, fc, :], start=(fc == 0),
                                  stop=(fc == FH - 1),
                                  waits=[wl2, kb.evt('Sc'), kb.evt('Vc')]
                                  if dcol == 0 and fc == 0 else
                                  ([kb.evt('Vc')] if fc == 0 else ()))
                    pe2 = kb.mark(mi, 'PE')
                    if hf == 0:
                        kb.V('tensor_copy', out=ysum[:, dcol, :],
                             in_=py[:, :], waits=[pe2])
                    else:
                        kb.V('tensor_tensor', out=ysum[:, dcol, :],
                             in0=ysum[:, dcol, :], in1=py[:, :], op=ALU.add,
                             waits=[pe2])
            for dcol in range(NC_):
                ev = kb.V('tensor_scalar', out=yfm[:], in0=ysum[:, dcol, :],
                          scalar1=d['b2c'][:, dcol:dcol + 1], scalar2=None,
                          op0=ALU.add, waits=[kb.evt('YW')])
                kb.dma(d['yfm_dram'][dcol * 128:(dcol + 1) * 128, ssl],
                       yfm[:], 'YW', queue='scalar', waits=[ev])
        ytok = kb.sb('l1f_ytok', [128, 1024], BF16, stack=st)
        for i in range(16):
            tl = kb.dma(ytok[:], d['yfm_dram'][:, i * 128:(i + 1) * 128],
                        'TL', queue='sync', transpose=True,
                        waits=[kb.evt('YW')] if i == 0 else [kb.evt('SC')])
            sev = kb.idma(d['yc_dram'], bass.IndirectOffsetOnAxis(
                ap=ridx[:, i:i + 1], axis=0), ytok[:], None, 'SC',
                waits=[tl], bounds_check=bcreg, oob_is_err=False)
        if d.get('fake_cc'):
            ar2 = kb.dma(d['yf_dram'], d['yc_dram'], 'AR', queue='scalar',
                         waits=[sev])
        else:
            eng = kb.nc.gpsimd
            eng.wait_ge(kb.sem('SC'), kb.cnt['SC'])
            ci = eng.collective_compute(
                'AllReduce', ALU.add, [list(range(8))],
                [d['yc_dram'].opt()], [d['yf_dram'].opt()])
            ci.then_inc(kb.sem('AR'), 1)
            kb.cnt['AR'] += 1
            ar2 = kb.evt('AR')
        fin = kb.sb('l1f_fin', [128, 1024], BF16, stack=st)
        for tt in range(16):
            mz = kb.V('memset', fin[:], 0.0,
                      waits=[kb.evt('FW')] if tt > 0 else ())
            ge = kb.idma(fin[:], None, d['yf_dram'],
                         bass.IndirectOffsetOnAxis(
                             ap=slot_i[:, tt:tt + 1], axis=0), 'GA',
                         waits=[ar2, mz], bounds_check=bcreg,
                         oob_is_err=False)
            sev2 = kb.V('tensor_scalar', out=fin[:], in0=fin[:],
                        scalar1=gate[:, tt:tt + 1], scalar2=None,
                        op0=ALU.mult, waits=[ge])
            kb.dma(d['yfin_tok'][tt * 128:(tt + 1) * 128, :], fin[:], 'FW',
                   queue='scalar', waits=[sev2])
        ffm = kb.sb('l1f_ffm', [128, 512], BF16, stack=st)
        xsr = kb.sb('l1f_xsr', [128, 512], F32, stack=st)
        ysb2 = kb.sb('l1f_ysb2', [128, 512], F32, stack=st)
        for c in range(NC_):
            for tc in range(4):
                sl = slice(tc * 512, (tc + 1) * 512)
                tl = kb.dma(ffm[:], d['yfin_tok'][sl,
                                                  c * 128:(c + 1) * 128],
                            'FT', queue='sync', transpose=True,
                            waits=[kb.evt('FW')] if (c == 0 and tc == 0)
                            else [kb.evt('Vc')])
                xre = kb.dma(xsr[:], d['x_in'].rearrange(
                    '(c p) t -> p c t', p=128)[:, c, sl], 'XR',
                    queue='sync', waits=[kb.evt('Vc')])
                ye = kb.V('tensor_tensor', out=ysb2[:], in0=xsr[:],
                          in1=ffm[:], op=ALU.add,
                          waits=[tl, xre, kb.evt('YO')])
                kb.dma(d['y_out'].rearrange('(c p) t -> p c t', p=128)[
                    :, c, sl], ysb2[:], 'YO', queue='scalar', waits=[ye])
        return kb.evt('YO')


def emit_l3a(kb, d):
    with contextlib.ExitStack() as st:
        qfm = kb.sb('l3a_q', [128, 4, T], BF16, stack=st)
        kfm = kb.sb('l3a_k', [128, 4, T], BF16, stack=st)
        ktok = kb.sb('l3a_kt', [128, 16, 512], BF16, stack=st)
        vtok = kb.sb('l3a_vt', [128, 16, 512], BF16, stack=st)
        zfm = kb.sb('l3a_z', [128, NC_, T], BF16, stack=st)
        psb = kb.sb('l3a_psb', [128, 4, 512], BF16, stack=st)
        msb = kb.sb('l3a_msb', [128, 64], BF16, stack=st)
        msum = kb.sb('l3a_msum', [128, 64], F32, stack=st)
        ps = [kb.ps('l3a_p0', [128, 512], stack=st),
              kb.ps('l3a_p1', [128, 512], stack=st),
              kb.ps('l3a_p2', [128, 512], stack=st),
              kb.ps('l3a_p3', [128, 512], stack=st)]
        psm = kb.ps('l3a_m', [64, 64], stack=st)
        pso = kb.ps('l3a_o', [64, 512], stack=st)
        xln = d['xln']
        wev, lev = d['wevt'], d['lnevt']

        prev = kb.evt('Sc')
        for hg in range(2):
            for dst, fbase, scale in ((qfm, 4 * hg, 0.125),
                                      (kfm, 8 + 4 * hg, 1.0)):
                for fc in range(4):
                    for tc in range(4):
                        pb = ps[tc % 2]
                        sl = slice(tc * 512, (tc + 1) * 512)
                        for dc in range(NC_):
                            mi = kb.T('matmul', pb[:, :],
                                      d['wq'][:, dc, fbase + fc, :],
                                      xln[:, dc, sl], start=(dc == 0),
                                      stop=(dc == NC_ - 1),
                                      waits=(list(wev) + [lev, kb.evt('Vc')])
                                      if (fc == 0 and tc == 0 and dc == 0)
                                      else ([kb.evt('Sc')] if dc == 0 else ()))
                        pe = kb.mark(mi, 'PE')
                        kb.S('activation', out=dst[:, fc, sl], in_=pb[:, :],
                             func=AF.Copy, bias=0.0, scale=float(scale),
                             waits=[pe])
            for tt in range(16):
                for gi, (dst, fb) in enumerate(((ktok, 8 + 4 * hg),
                                                (vtok, 16 + 4 * hg))):
                    pb = ps[2 + gi]
                    for dc in range(NC_):
                        mi = kb.T('matmul', pb[:, :],
                                  xln[:, dc, tt * 128:(tt + 1) * 128],
                                  d['wq'][:, dc, fb:fb + 4, :].rearrange(
                                      'p a b -> p (a b)'),
                                  start=(dc == 0), stop=(dc == NC_ - 1),
                                  waits=[kb.evt('Sc')] if dc == 0 else ())
                    pe = kb.mark(mi, 'PE')
                    kb.S('activation', out=dst[:, tt, :], in_=pb[:, :],
                         func=AF.Copy, bias=0.0, scale=1.0, waits=[pe])
            prev = kb.evt('Sc')

            for hh in range(8):
                h = 8 * hg + hh
                hc, hr = hh // 2, (hh % 2) * 64
                hsl = slice(hr, hr + 64)
                ghc = h // 2
                ghr = (h % 2) * 64
                gsl = slice(ghr, ghr + 64)
                vc0 = hh * 64
                for qb in range(4):
                    qsl = slice(qb * 512, (qb + 1) * 512)
                    if qb > 0:
                        for j in range(4 * (qb - 1), 4 * qb):
                            mi = kb.T('matmul', psm[:, :],
                                      ktok[:, j, vc0:vc0 + 64],
                                      vtok[:, j, vc0:vc0 + 64],
                                      start=(j == 4 * (qb - 1)),
                                      stop=(j == 4 * qb - 1),
                                      waits=[prev, kb.evt('Vc')]
                                      if j == 4 * (qb - 1) else ())
                        pe = kb.mark(mi, 'PE')
                        if qb == 1:
                            kb.V('tensor_copy', out=msum[hsl, :],
                                 in_=psm[:, :], waits=[pe])
                        else:
                            kb.V('tensor_tensor', out=msum[hsl, :],
                                 in0=msum[hsl, :], in1=psm[:, :],
                                 op=ALU.add, waits=[pe])
                        mev = kb.V('tensor_copy', out=msb[hsl, :],
                                   in_=msum[hsl, :])
                    for s in range(4):
                        j = 4 * qb + s
                        n0 = s * 128
                        mi = kb.T('matmul', ps[s % 2][:, 0:512 - n0],
                                  kfm[:, hc, j * 128:(j + 1) * 128][hsl, :],
                                  qfm[:, hc,
                                      qb * 512 + n0:(qb + 1) * 512][hsl, :],
                                  start=True, stop=True,
                                  waits=[kb.evt('Vc'), prev])
                        pe = kb.mark(mi, 'PE')
                        kb.V('tensor_copy', out=psb[:, s, n0:512],
                             in_=ps[s % 2][:, 0:512 - n0], waits=[pe])
                        tev = kb.V('tensor_tensor',
                                   out=psb[:, s, n0:n0 + 128],
                                   in0=psb[:, s, n0:n0 + 128],
                                   in1=d['trimask'][:], op=ALU.mult)
                    first = (qb == 0)
                    if qb > 0:
                        mi = kb.T('matmul', pso[:, :], msb[hsl, :],
                                  qfm[:, hc, qsl][hsl, :], start=True,
                                  stop=False, waits=[mev, kb.evt('Sc')])
                    for s in range(4):
                        j = 4 * qb + s
                        n0 = s * 128
                        mi = kb.T('matmul', pso[:, n0:512],
                                  vtok[:, j, vc0:vc0 + 64],
                                  psb[:, s, n0:512],
                                  start=(first and s == 0), stop=(s == 3),
                                  waits=[tev, kb.evt('Sc')] if s == 0 else ())
                    pe = kb.mark(mi, 'PE')
                    prev = kb.S('activation', out=zfm[:, ghc, qsl][gsl, :],
                                in_=pso[:, :], func=AF.Copy, bias=0.0,
                                scale=1.0, waits=[pe])
        xs = kb.sb('l3a_xs', [128, 2, 512], F32, stack=st)
        ysb = kb.sb('l3a_ysb', [128, 512], F32, stack=st)
        for fc in range(NC_):
            for tc in range(4):
                sl = slice(tc * 512, (tc + 1) * 512)
                pb = ps[tc % 2]
                xe = kb.dma(xs[:, tc % 2, :],
                            d['x_in'].rearrange('(c p) t -> p c t', p=128)[
                                :, fc, sl], 'XR', queue='sync',
                            waits=[kb.evt('Vc')])
                for ec in range(NC_):
                    mi = kb.T('matmul', pb[:, :], d['woutT'][:, ec, fc, :],
                              zfm[:, ec, sl], start=(ec == 0),
                              stop=(ec == NC_ - 1),
                              waits=[prev, kb.evt('Vc')] if ec == 0 else ())
                pe = kb.mark(mi, 'PE')
                yev = kb.V('scalar_tensor_tensor', out=ysb[:], in0=pb[:, :],
                           scalar=1.0, in1=xs[:, tc % 2, :], op0=ALU.mult,
                           op1=ALU.add, waits=[pe, xe, kb.evt('YO')])
                kb.dma(d['y_out'].rearrange('(c p) t -> p c t', p=128)[
                    :, fc, sl], ysb[:], 'YO', queue='scalar', waits=[yev])
        return kb.evt('YO')


def emit_l2a(kb, d):
    with contextlib.ExitStack() as st:
        qfm = kb.sb('l2a_q', [128, 4, T], BF16, stack=st)
        kfm = kb.sb('l2a_k', [128, 4, T], BF16, stack=st)
        vg = kb.sb('l2a_v', [128, 16, 520], BF16, stack=st)
        zfm = kb.sb('l2a_z', [128, NC_, T], BF16, stack=st)
        ast = st.enter_context(contextlib.ExitStack())
        pbuf = kb.sb('l2a_pb', [128, 2, 1024], BF16, stack=ast)
        sqs = kb.sb('l2a_sqs', [128, 512], BF16, stack=ast)
        nmax = kb.sb('l2a_nmax', [2, 2, 4], F32, stack=ast)
        ctmp = kb.sb('l2a_ct', [2, 2], F32, stack=ast)
        cdg = kb.sb('l2a_cd', [2, 2], F32, stack=ast)
        ccols = kb.sb('l2a_cc', [128, 8], F32, stack=ast)
        rrow = kb.sb('l2a_rr', [1, 512], F32, stack=ast)
        zsb = kb.sb('l2a_zsb', [128, 512], BF16, stack=ast)
        psA = kb.ps('l2a_pA', [128, 1024], stack=st)
        psB = kb.ps('l2a_pB', [128, 1024], stack=st)
        pso = kb.ps('l2a_po', [65, 512], stack=st)
        psr = kb.ps('l2a_pr', [128, 512], stack=st)
        xln = d['xln']
        wev, lev = d['wevt'], d['lnevt']
        H2 = 0.125
        ones_row = d['consts']['ones_row']
        ones512 = d['consts']['ones_row512']

        bev = kb.V('tensor_scalar', out=d['bqkv'][:, 8:16],
                   in0=d['bqkv'][:, 8:16], scalar1=H2, scalar2=None,
                   op0=ALU.mult, waits=list(wev))

        prev = kb.evt('Sc')
        for hg in range(2):
            wq = d['wq']
            for di, (dst, fb, scale) in enumerate(((qfm, 4 * hg, 1.0),
                                                   (kfm, 8 + 4 * hg, H2))):
                for fc in range(4):
                    for tc in range(4):
                        pb = (psA if tc % 2 == 0 else psB)[:, 0:512]
                        sl = slice(tc * 512, (tc + 1) * 512)
                        for dc in range(NC_):
                            mi = kb.T('matmul', pb, wq[:, dc, fb + fc, :],
                                      xln[:, dc, sl], start=(dc == 0),
                                      stop=(dc == NC_ - 1),
                                      waits=(list(wev) + [lev, bev,
                                                          kb.evt('Vc'),
                                                          kb.evt('Sc')])
                                      if (di == 0 and fc == 0 and tc == 0
                                          and dc == 0)
                                      else ([kb.evt('Sc')] if dc == 0 else ()))
                        pe = kb.mark(mi, 'PE')
                        kb.S('activation', out=dst[:, fc, sl], in_=pb,
                             func=AF.Identity,
                             bias=d['bqkv'][:, fb + fc:fb + fc + 1],
                             scale=float(scale), waits=[pe])
            for tt in range(16):
                pb = psA[:, 0:512]
                fb = 16 + 4 * hg
                for dc in range(NC_):
                    mi = kb.T('matmul', pb,
                              xln[:, dc, tt * 128:(tt + 1) * 128],
                              wq[:, dc, fb:fb + 4, :].rearrange(
                                  'p a b -> p (a b)'),
                              start=(dc == 0), stop=False,
                              waits=[kb.evt('Sc')] if dc == 0 else ())
                mi = kb.T('matmul', pb, ones_row[:],
                          d['bqkv_row'][:, 2048 + 512 * hg:
                                        2048 + 512 * hg + 512],
                          start=False, stop=True)
                pe = kb.mark(mi, 'PE')
                kb.S('activation',
                     out=vg[:, tt, :].rearrange('p (h c) -> p h c', c=65)[
                         :, :, 0:64],
                     in_=pb, func=AF.Copy, bias=0.0, scale=1.0, waits=[pe])
            oev = kb.V('memset',
                       vg[:].rearrange('p a (h c) -> p a h c', c=65)[
                           :, :, :, 64:65], 1.0, waits=[kb.evt('Sc')])

            for fc in range(4):
                for di, src_ in enumerate((qfm, kfm)):
                    for tc in range(4):
                        sl = slice(tc * 512, (tc + 1) * 512)
                        sev = kb.S('activation', out=sqs[:],
                                   in_=src_[:, fc, sl], func=AF.Square,
                                   bias=0.0, scale=1.0,
                                   waits=[kb.evt('Vc')])
                        mi = kb.T('matmul', psB[0:2, 0:512], d['i2c'][:],
                                  sqs[:], start=True, stop=True,
                                  waits=[sev, kb.evt('Vc')])
                        pe = kb.mark(mi, 'PE')
                        kb.V('tensor_reduce', out=nmax[:, di, tc:tc + 1],
                             in_=psB[0:2, 0:512], axis=AX.X, op=ALU.max,
                             waits=[pe])
                        kb.t_wait([kb.evt('Vc')])
                    kb.V('tensor_reduce', out=ctmp[:, di:di + 1],
                         in_=nmax[:, di, :], axis=AX.X, op=ALU.max)
                kb.V('tensor_tensor', out=ctmp[:, 0:1], in0=ctmp[:, 0:1],
                     in1=ctmp[:, 1:2], op=ALU.mult)
                cse = kb.S('activation', out=ctmp[:, 0:1], in_=ctmp[:, 0:1],
                           func=AF.Sqrt, bias=0.0, scale=1.0,
                           waits=[kb.evt('Vc')])
                kb.V('tensor_scalar', out=ctmp[:, 0:1], in0=ctmp[:, 0:1],
                     scalar1=-1.0, scalar2=None, op0=ALU.mult, waits=[cse])
                dg = kb.V('tensor_tensor', out=cdg[:, :],
                          in0=ctmp[:, 0:1].to_broadcast((2, 2)),
                          in1=d['i2'][:], op=ALU.mult)
                mi = kb.T('matmul', psr[:, 0:2], d['ones2'][:], cdg[:, :],
                          start=True, stop=True, waits=[dg])
                pe = kb.mark(mi, 'PE')
                ce = kb.V('tensor_copy', out=ccols[:, 2 * fc:2 * fc + 2],
                          in_=psr[:, 0:2], waits=[pe])
                kb.t_wait([ce])
            cev = kb.evt('Vc')

            for hh in range(8):
                h = 8 * hg + hh
                hc, hr = hh // 2, (hh % 2) * 64
                hsl = slice(hr, hr + 64)
                ghc, ghr = h // 2, (h % 2) * 64
                gsl = slice(ghr, ghr + 64)
                for qb in range(4):
                    qsl = slice(qb * 512, (qb + 1) * 512)

                    def s_mm(jp, extra=()):
                        pS = psA if jp % 2 == 0 else psB
                        for s in range(2):
                            j = 2 * jp + s
                            mi = kb.T('matmul',
                                      pS[:, s * 512:(s + 1) * 512],
                                      kfm[:, hc,
                                          j * 128:(j + 1) * 128][hsl, :],
                                      qfm[:, hc, qsl][hsl, :],
                                      start=True, stop=True,
                                      waits=list(extra) if s == 0 else ())
                        return kb.mark(mi, 'PE')
                    ex_h = {0: None, 1: None}
                    pv_h = {0: None, 1: None}
                    pe_s = {0: s_mm(0, [cev, prev, kb.evt('Sc')])}
                    for jp in range(8):
                        par = jp % 2
                        eev = kb.S('activation', out=pbuf[:, par, :],
                                   in_=(psA if par == 0 else psB)[:, :],
                                   func=AF.Exp, bias=ccols[:, hh:hh + 1],
                                   scale=1.0,
                                   waits=[pe_s[jp]] +
                                   ([pv_h[par]] if pv_h[par] else []))
                        ex_h[par] = eev
                        if jp < 7:
                            npar = (jp + 1) % 2
                            pe_s[jp + 1] = s_mm(
                                jp + 1,
                                [ex_h[npar]] if ex_h[npar] else [])
                        for s in range(2):
                            j = 2 * jp + s
                            mi = kb.T('matmul', pso[:, :],
                                      vg[:, j, 65 * hh:65 * hh + 65],
                                      pbuf[:, par, s * 512:(s + 1) * 512],
                                      start=(jp == 0 and s == 0),
                                      stop=(jp == 7 and s == 1),
                                      waits=[eev, oev] if s == 0 else ())
                        pe2 = kb.mark(mi, 'PE')
                        pv_h[par] = pe2
                    clm = kb.V('tensor_scalar', out=rrow[:],
                               in_=None, in0=pso[64:65, :], scalar1=1e-30,
                               scalar2=None, op0=ALU.max, waits=[pe2]) \
                        if False else \
                        kb.V('tensor_scalar', out=rrow[:],
                             in0=pso[64:65, :], scalar1=1e-30,
                             scalar2=None, op0=ALU.max, waits=[pe2])
                    rev = kb.V('reciprocal', out=rrow[:], in_=rrow[:])
                    mi = kb.T('matmul', psr[0:64, 0:512],
                              ones_row[:][0:1, 0:64], rrow[:],
                              start=True, stop=True, waits=[rev])
                    pe = kb.mark(mi, 'PE')
                    zev = kb.S('activation', out=zsb[0:64, :],
                               in_=pso[0:64, :], func=AF.Copy, bias=0.0,
                               scale=1.0, waits=[kb.evt('Vc')])
                    prev = kb.V('tensor_tensor', out=zfm[:, ghc, qsl][gsl, :],
                                in0=zsb[0:64, :], in1=psr[0:64, 0:512],
                                op=ALU.mult, waits=[pe, zev])
                    kb.t_wait([prev])
        ast.close()
        xs = kb.sb('l2a_xs', [128, 2, 512], F32, stack=st)
        ysb = kb.sb('l2a_ysb', [128, 512], F32, stack=st)
        for fc in range(NC_):
            for tc in range(4):
                sl = slice(tc * 512, (tc + 1) * 512)
                pb = (psA if tc % 2 == 0 else psB)[:, 0:512]
                xe = kb.dma(xs[:, tc % 2, :],
                            d['x_in'].rearrange('(c p) t -> p c t', p=128)[
                                :, fc, sl], 'XR', queue='sync',
                            waits=[kb.evt('Vc')])
                for ec in range(NC_):
                    mi = kb.T('matmul', pb, d['wo'][:, ec, fc, :],
                              zfm[:, ec, sl], start=(ec == 0), stop=False,
                              waits=[prev, kb.evt('Vc')] if ec == 0 else ())
                mi = kb.T('matmul', pb, d['bo'][:, fc * 128:(fc + 1) * 128],
                          ones512[:], start=False, stop=True)
                pe = kb.mark(mi, 'PE')
                yev = kb.V('scalar_tensor_tensor', out=ysb[:], in0=pb,
                           scalar=1.0, in1=xs[:, tc % 2, :], op0=ALU.mult,
                           op1=ALU.add, waits=[pe, xe, kb.evt('YO')])
                kb.dma(d['y_out'].rearrange('(c p) t -> p c t', p=128)[
                    :, fc, sl], ysb[:], 'YO', queue='scalar', waits=[yev])
        return kb.evt('YO')


def emit_l1a(kb, d):
    NDELTA = 19
    with contextlib.ExitStack() as st:
        ast = st.enter_context(contextlib.ExitStack())
        qh = kb.sb('l1a_qh', [128, T], BF16, stack=ast)
        kh = kb.sb('l1a_kh', [128, T], BF16, stack=ast)
        vtok = kb.sb('l1a_v', [128, 16, 8 * 129], BF16, stack=ast)
        strip = kb.sb('l1a_strip', [128, NDELTA * 128], F32, stack=ast)
        pb = kb.sb('l1a_p', [128, 16, 512], BF16, stack=ast)
        zrow = kb.sb('l1a_zr', [128, T], BF16, stack=ast)
        sqs = kb.sb('l1a_sqs', [128, 512], BF16, stack=ast)
        gb = kb.sb('l1a_gb', [128, 2, 128], BF16, stack=ast)
        sc1 = kb.sb('l1a_sc1', [1, 8, 512], F32, stack=ast)
        ccol = kb.sb('l1a_cc', [128, 1], F32, stack=ast)
        nrm = kb.sb('l1a_nrm', [1, 2], F32, stack=ast)
        stt = kb.sb('l1a_stt', [128, 10], F32, stack=ast)
        zt = kb.sb('l1a_zt', [128, 2, 128], BF16, stack=ast)
        sq2 = kb.sb('l1a_sq2', [128, 128], F32, stack=ast)
        th = kb.sb('l1a_th', [8, 128], F32, stack=ast)
        epsc = kb.sb('l1a_eps', [128, 1], F32, stack=ast)
        thc = kb.sb('l1a_thc', [8, 1], F32, stack=ast)
        ab = kb.sb('l1a_ab', [8, 2, 128], F32, stack=ast)
        dg = kb.sb('l1a_dg', [128, 128], F32, stack=ast)
        dgi = kb.sb('l1a_dgi', [128, 128], I32, stack=ast)
        dgf = kb.sb('l1a_dgf', [128, 128], F32, stack=ast)
        thr = kb.sb('l1a_thr', [128, 8], F32, stack=ast)
        ln8 = kb.sb('l1a_ln8', [8, 1], F32, stack=ast)
        io128 = kb.sb('l1a_io', [8, 128], I32, stack=ast)
        iof = kb.sb('l1a_iof', [8, 128], F32, stack=ast)
        ps1 = kb.ps('l1a_p1', [128, 512], stack=st)
        ps2 = kb.ps('l1a_p2', [128, 512], stack=st)
        pso = kb.ps('l1a_po', [128, 129], stack=st)
        psoB = kb.ps('l1a_poB', [128, 129], stack=st)
        psd = kb.ps('l1a_pd', [128, 128], stack=st)
        pst = kb.ps('l1a_pt', [128, 128], BF16, stack=st)
        pstB = kb.ps('l1a_ptB', [128, 128], BF16, stack=st)
        xln = d['xln']
        wev, lev = d['wevt'], d['lnevt']
        ones_row = d['consts']['ones_row']
        ones512 = d['consts']['ones_row512']
        ocb = d['consts']['ones_col_bf']
        SC = float(1.0 / np.sqrt(128.0))

        kb.V('memset', epsc[:], 1e-5)
        ge = kb.G('iota', io128[:], pattern=[[1, 128]], base=0,
                  channel_multiplier=0)
        ge2 = kb.G('iota', dgi[:], pattern=[[1, 128]], base=0,
                   channel_multiplier=-1)
        kb.V('tensor_copy', out=iof[:], in_=io128[:], waits=[ge, list(wev)[0]])
        kb.V('memset', ln8[:], -float(np.log(8.0)))
        kb.V('tensor_copy', out=thc[:], in_=d['thcol'][:], waits=list(wev))
        tre = kb.V('tensor_copy', out=dgf[:], in_=dgi[:], waits=[ge2])
        mi = kb.T('matmul', psd[:, 0:8], ones_row[:], d['throw'][:],
                  start=True, stop=True, waits=list(wev) + [kb.evt('Vc')])
        pe = kb.mark(mi, 'PE')
        kb.V('tensor_copy', out=thr[:], in_=psd[:, 0:8], waits=[pe])
        kb.V('tensor_scalar', out=dgf[:], in0=dgf[:], scalar1=0.0,
             scalar2=None, op0=ALU.max)
        dsum = None
        for h in range(8):
            kb.V('tensor_scalar', out=dg[:], in0=dgf[:],
                 scalar1=thr[:, h:h + 1], scalar2=None, op0=ALU.mult)
            se = kb.S('activation', out=sq2[:], in_=dg[:], func=AF.Exp,
                      bias=0.0, scale=-1.0, waits=[kb.evt('Vc')])
            if h == 0:
                dsum = kb.V('tensor_copy', out=strip[:, 3 * 128:4 * 128],
                            in_=sq2[:], waits=[se])
            else:
                dsum = kb.V('tensor_tensor', out=strip[:, 3 * 128:4 * 128],
                            in0=strip[:, 3 * 128:4 * 128], in1=sq2[:],
                            op=ALU.add, waits=[se])
        kb.V('tensor_scalar', out=strip[:, 3 * 128:4 * 128],
             in0=strip[:, 3 * 128:4 * 128], scalar1=0.125, scalar2=None,
             op0=ALU.mult)
        kb.V('memset', strip[:, 0:3 * 128], 1.0)
        for dd_ in range(1, 16):
            kb.V('tensor_scalar', out=ab[:, 0, :], in0=iof[:],
                 scalar1=-64.0 * dd_, scalar2=None, op0=ALU.add)
            kb.V('tensor_scalar', out=ab[:, 0, :], in0=ab[:, 0, :],
                 scalar1=thc[:], scalar2=None, op0=ALU.mult)
            kb.V('tensor_scalar', out=ab[:, 1, :], in0=iof[:],
                 scalar1=64.0 * dd_, scalar2=None, op0=ALU.add)
            ve = kb.V('tensor_scalar', out=ab[:, 1, :], in0=ab[:, 1, :],
                      scalar1=thc[:], scalar2=-1.0, op0=ALU.mult, op1=ALU.mult)
            kb.S('activation', out=ab[:, 0, :], in_=ab[:, 0, :], func=AF.Exp,
                 bias=ln8[:], scale=1.0, waits=[ve])
            se = kb.S('activation', out=ab[:, 1, :], in_=ab[:, 1, :],
                      func=AF.Exp, bias=0.0, scale=1.0)
            mi = kb.T('matmul', psd[:, :], ab[:, 0, :], ab[:, 1, :],
                      start=True, stop=True, waits=[se, kb.evt('Vc')])
            pe = kb.mark(mi, 'PE')
            kb.V('tensor_copy', out=strip[:, (3 + dd_) * 128:(4 + dd_) * 128],
                 in_=psd[:, :], waits=[pe])
        stre = kb.evt('Vc')

        vevac = {0: None, 1: None}
        for tt in range(16):
            for vh in range(2):
                pv = ps1 if vh == 0 else ps2
                for dc in range(NC_):
                    mi = kb.T('matmul', pv[:, :],
                              xln[:, dc, tt * 128:(tt + 1) * 128],
                              d['wv'][:, dc, 4 * vh:4 * vh + 4, :].rearrange(
                                  'p a b -> p (a b)'),
                              start=(dc == 0), stop=(dc == NC_ - 1),
                              waits=(list(wev) + [lev] +
                                     ([vevac[vh]] if vevac[vh]
                                      else [kb.evt('Sc')]))
                              if dc == 0 else ())
                pe = kb.mark(mi, 'PE')
                vevac[vh] = kb.S(
                    'activation',
                    out=vtok[:, tt, :].rearrange('p (h c) -> p h c', c=129)[
                        :, 4 * vh:4 * vh + 4, 0:128],
                    in_=pv[:, :], func=AF.Copy, bias=0.0, scale=1.0,
                    waits=[pe])
        onev = kb.V('memset',
                    vtok[:].rearrange('p a (h c) -> p a h c', c=129)[
                        :, :, :, 128:129], 1.0, waits=[kb.evt('Sc')])

        prev = kb.evt('Sc')
        zwe = kb.evt('ZW')
        psoX = (pso, psoB)
        pstX = (pst, pstB)
        pso_free = {0: None, 1: None}
        cpe_h = {0: None, 1: None}
        tr_pe = {0: None, 1: None}
        trq = []

        def flush_tr():
            if not trq:
                return
            gidx_, zte_, par_, zr_ = trq.pop(0)
            tr = kb.T('transpose', pstX[par_][:, :], zt[:, par_, :],
                      d['idbf'][:],
                      waits=[zte_] + ([cpe_h[par_]] if cpe_h[par_] else []))
            tr_pe[par_] = pe2 = kb.mark(tr, 'PE')
            cpe_h[par_] = kb.V(
                'tensor_copy',
                out=zr_[:, gidx_ * 128:(gidx_ + 1) * 128],
                in_=pstX[par_][:, :],
                waits=[pe2] + ([kb.evt('ZW')] if gidx_ == 0 else []))
        for h in range(8):
            qkevac = {0: None, 1: None}
            for di, (dst, w_, scale) in enumerate(((qh, d['wqk'], 1.0),
                                                   (kh, d['wqk'], SC))):
                for tc in range(4):
                    par = tc % 2
                    pv = ps1 if par == 0 else ps2
                    sl = slice(tc * 512, (tc + 1) * 512)
                    for dc in range(NC_):
                        mi = kb.T('matmul', pv[:, :],
                                  w_[:, dc, 2 * h + di, :], xln[:, dc, sl],
                                  start=(dc == 0), stop=(dc == NC_ - 1),
                                  waits=([qkevac[par]] if qkevac[par]
                                         else [kb.evt('Sc'), kb.evt('Vc')])
                                  if dc == 0 else ())
                    pe = kb.mark(mi, 'PE')
                    qkevac[par] = kb.S('activation', out=dst[:, sl],
                                       in_=pv[:, :], func=AF.Copy, bias=0.0,
                                       scale=float(scale), waits=[pe])
            for di, src_ in enumerate((qh, kh)):
                for tc in range(4):
                    sl = slice(tc * 512, (tc + 1) * 512)
                    se = kb.S('activation', out=sqs[:], in_=src_[:, sl],
                              func=AF.Square, bias=0.0, scale=1.0,
                              waits=[kb.evt('Vc')])
                    mi = kb.T('matmul', ps1[0:1, :], ocb[:], sqs[:],
                              start=True, stop=True,
                              waits=[se, kb.evt('Vc')])
                    pe = kb.mark(mi, 'PE')
                    kb.V('tensor_copy', out=sc1[:, di * 4 + tc, :],
                         in_=ps1[0:1, :], waits=[pe])
                    kb.t_wait([kb.evt('Vc')])
                kb.V('tensor_reduce', out=nrm[:, di:di + 1],
                     in_=sc1[:, di * 4:di * 4 + 4, :].rearrange(
                         'p a b -> p (a b)'), axis=AX.X, op=ALU.max)
            kb.V('tensor_tensor', out=nrm[:, 0:1], in0=nrm[:, 0:1],
                 in1=nrm[:, 1:2], op=ALU.mult)
            cse = kb.S('activation', out=nrm[:, 0:1], in_=nrm[:, 0:1],
                       func=AF.Sqrt, bias=0.0, scale=1.0,
                       waits=[kb.evt('Vc')])
            nne = kb.V('tensor_scalar', out=nrm[:, 0:1], in0=nrm[:, 0:1],
                       scalar1=-1.0, scalar2=None, op0=ALU.mult, waits=[cse])
            mi = kb.T('matmul', psd[:, 0:1], ones_row[:], nrm[:, 0:1],
                      start=True, stop=True, waits=[nne])
            pe = kb.mark(mi, 'PE')
            cce = kb.V('tensor_copy', out=ccol[:], in_=psd[:, 0:1],
                       waits=[pe])
            mi = kb.T('matmul', psd[:, 0:128], ones_row[:],
                      d['gam'][:, h * 128:(h + 1) * 128], start=True,
                      stop=True, waits=[cce])
            mi2 = kb.T('matmul', ps1[:, 0:128], ones_row[:],
                       d['bet'][:, h * 128:(h + 1) * 128], start=True,
                       stop=True)
            pe = kb.mark(mi2, 'PE')
            kb.V('tensor_copy', out=gb[:, 0, :], in_=psd[:, 0:128],
                 waits=[pe])
            gbe = kb.V('tensor_copy', out=gb[:, 1, :], in_=ps1[:, 0:128])
            kb.t_wait([gbe])

            eev_h = {0: None, 1: None}
            for qb in range(4):
                qsl = slice(qb * 512, (qb + 1) * 512)
                for j in range(4 * qb + 4):
                    n0 = max(0, (j - 4 * qb) * 128)
                    d0 = 4 * qb - j
                    par = j % 2
                    if j == 0:
                        sw = [stre, cce, gbe, kb.evt('Vc'), kb.evt('Sc')]
                    elif eev_h[par] is not None:
                        sw = [eev_h[par]]
                    else:
                        sw = [kb.evt('Sc')]
                    mi = kb.T('matmul', (ps1 if par == 0 else ps2)[:, 0:512 - n0],
                              kh[:, j * 128:(j + 1) * 128],
                              qh[:, qb * 512 + n0:(qb + 1) * 512],
                              start=True, stop=True, waits=sw)
                    pe = kb.mark(mi, 'PE')
                    pS = (ps1 if par == 0 else ps2)
                    s0 = (d0 + 3) * 128 + n0
                    dve = kb.V('tensor_tensor', out=pS[:, 0:512 - n0],
                               in0=pS[:, 0:512 - n0],
                               in1=strip[:, s0:s0 + 512 - n0], op=ALU.mult,
                               waits=[pe])
                    eev = kb.S('activation', out=pb[:, j, n0:512],
                               in_=pS[:, 0:512 - n0], func=AF.Exp,
                               bias=ccol[:], scale=1.0, waits=[dve])
                    eev_h[par] = eev
                    if j >= 4 * qb:
                        kb.V('tensor_tensor', out=pb[:, j, n0:n0 + 128],
                             in0=pb[:, j, n0:n0 + 128],
                             in1=d['trimask'][:], op=ALU.mult, waits=[eev])
                pmev = (kb.evt('Vc'), kb.evt('Sc'))
                for s in range(4):
                    gidx = 4 * qb + s
                    par = gidx % 2
                    po = psoX[par]
                    for j in range(4 * qb + s + 1):
                        mi = kb.T('matmul', po[:, :],
                                  pb[:, j, s * 128:(s + 1) * 128],
                                  vtok[:, j, 129 * h:129 * h + 129],
                                  start=(j == 0), stop=(j == 4 * qb + s),
                                  waits=([pmev[0], pmev[1], onev] +
                                         ([pso_free[par]] if pso_free[par]
                                          else [kb.evt('Vc')]))
                                  if j == 0 else ())
                    pe = kb.mark(mi, 'PE')
                    if len(trq) >= 2:
                        flush_tr()
                    kb.V('tensor_scalar', out=stt[:, 0:1], in0=po[:, 128:129],
                         scalar1=1e-30, scalar2=None, op0=ALU.max, waits=[pe])
                    kb.V('reciprocal', out=stt[:, 1:2], in_=stt[:, 0:1])
                    kb.V('tensor_reduce', out=stt[:, 2:3], in_=po[:, 0:128],
                         axis=AX.X, op=ALU.add)
                    se = kb.S('activation', out=sq2[:], in_=po[:, 0:128],
                              func=AF.Square, bias=0.0, scale=1.0,
                              waits=[kb.evt('Vc')])
                    kb.V('tensor_reduce', out=stt[:, 3:4], in_=sq2[:],
                         axis=AX.X, op=ALU.add, waits=[se])
                    kb.V('tensor_scalar', out=stt[:, 4:5], in0=stt[:, 2:3],
                         scalar1=stt[:, 1:2], scalar2=1.0 / 128.0,
                         op0=ALU.mult, op1=ALU.mult)
                    kb.V('tensor_scalar', out=stt[:, 5:6], in0=stt[:, 3:4],
                         scalar1=stt[:, 1:2], scalar2=None, op0=ALU.mult)
                    kb.V('tensor_scalar', out=stt[:, 5:6], in0=stt[:, 5:6],
                         scalar1=stt[:, 1:2], scalar2=1.0 / 128.0,
                         op0=ALU.mult, op1=ALU.mult)
                    kb.V('tensor_tensor', out=stt[:, 6:7], in0=stt[:, 4:5],
                         in1=stt[:, 4:5], op=ALU.mult)
                    kb.V('tensor_tensor', out=stt[:, 6:7], in0=stt[:, 5:6],
                         in1=stt[:, 6:7], op=ALU.subtract)
                    se = kb.S('activation', out=stt[:, 6:7], in_=stt[:, 6:7],
                              func=AF.Sqrt, bias=epsc[:], scale=1.0,
                              waits=[kb.evt('Vc')])
                    kb.V('reciprocal', out=stt[:, 7:8], in_=stt[:, 6:7],
                         waits=[se])
                    kb.V('tensor_tensor', out=stt[:, 8:9], in0=stt[:, 1:2],
                         in1=stt[:, 7:8], op=ALU.mult)
                    kb.V('scalar_tensor_tensor', out=stt[:, 9:10],
                         in0=stt[:, 4:5], scalar=-1.0, in1=stt[:, 7:8],
                         op0=ALU.mult, op1=ALU.mult)
                    pso_free[par] = kb.V(
                        'tensor_scalar', out=sq2[:], in0=po[:, 0:128],
                        scalar1=stt[:, 8:9], scalar2=stt[:, 9:10],
                        op0=ALU.mult, op1=ALU.add)
                    kb.V('tensor_tensor', out=sq2[:], in0=sq2[:],
                         in1=gb[:, 0, :], op=ALU.mult)
                    zte = kb.V('tensor_tensor', out=zt[:, par, :], in0=sq2[:],
                               in1=gb[:, 1, :], op=ALU.add,
                               waits=[tr_pe[par]] if tr_pe[par] else ())
                    trq.append((gidx, zte, par, zrow))
            while trq:
                flush_tr()
            zwe = kb.dma(d['z_dram'][h * 128:(h + 1) * 128, :], zrow[:],
                         'ZW', queue='scalar', waits=[kb.evt('Vc')])
            prev = kb.evt('Sc')

        kb.t_wait([kb.evt('Vc'), kb.evt('Sc'), kb.evt('ZW')])
        ast.close()
        zs = kb.sb('l1a_zs', [128, 2, 512], BF16, stack=st)
        xs = kb.sb('l1a_xs', [128, 2, 512], F32, stack=st)
        ysb = kb.sb('l1a_ysb', [128, 512], F32, stack=st)
        for fc in range(NC_):
            for tc in range(4):
                sl = slice(tc * 512, (tc + 1) * 512)
                pv = ps1 if tc % 2 == 0 else ps2
                xe = kb.dma(xs[:, tc % 2, :],
                            d['x_in'].rearrange('(c p) t -> p c t', p=128)[
                                :, fc, sl], 'XR', queue='sync',
                            waits=[kb.evt('Vc')])
                for ec in range(NC_):
                    ze = kb.dma(zs[:, tc % 2, :],
                                d['z_dram'][ec * 128:(ec + 1) * 128, sl],
                                'ZR', queue='sync',
                                waits=[zwe, kb.evt('PE')] if ec == 0 else
                                [kb.evt('PE')])
                    mi = kb.T('matmul', pv[:, :], d['wo'][:, ec, fc, :],
                              zs[:, tc % 2, :], start=(ec == 0), stop=False,
                              waits=[ze, kb.evt('Vc')])
                    pe = kb.mark(mi, 'PE')
                mi = kb.T('matmul', pv[:, :],
                          d['bo'][:, fc * 128:(fc + 1) * 128], ones512[:],
                          start=False, stop=True)
                pe = kb.mark(mi, 'PE')
                yev = kb.V('scalar_tensor_tensor', out=ysb[:], in0=pv[:, :],
                           scalar=1.0, in1=xs[:, tc % 2, :], op0=ALU.mult,
                           op1=ALU.add, waits=[pe, xe, kb.evt('YO')])
                kb.dma(d['y_out'].rearrange('(c p) t -> p c t', p=128)[
                    :, fc, sl], ysb[:], 'YO', queue='scalar', waits=[yev])
        return kb.evt('YO')


def emit_l0a(kb, d, trunc=None):
    with contextlib.ExitStack() as st:
        xg = d['xg']
        lq = kb.sb('l0a_lq', [128, 2, T], BF16, stack=st)
        lktok = kb.sb('l0a_lkt', [128, 16, 256], BF16, stack=st)
        linkv = kb.sb('l0a_lkv', [128, 2, T], BF16, stack=st)
        attn = kb.sb('l0a_attn', [64, 32, 64], BF16, stack=st)
        g8 = kb.sb('l0a_g8', [128, 2, 8], F32, stack=st)
        sb64 = kb.sb('l0a_sb64', [128, 64], F32, stack=st)
        wev = d['wevt']
        lev = d['lnevt']
        ones_row = d['consts']['ones_row']

        with contextlib.ExitStack() as hst:
            qq = kb.sb('l0a_qq', [128, 2, T], BF16, stack=hst)
            qkk = kb.sb('l0a_qkk', [128, 2, T], BF16, stack=hst)
            lkfm = kb.sb('l0a_lkfm', [128, 2, T], BF16, stack=hst)
            qkf = kb.sb('l0a_qkf', [128, 2, T], BF16, stack=hst)
            hbuf = kb.sb('l0a_hb', [128, 2, T], F32, stack=hst)
            hbuf2 = kb.sb('l0a_hb2', [128, 2, T], F32, stack=hst)
            cosT = kb.sb('l0a_cos', [128, T], F32, stack=hst)
            sinT = kb.sb('l0a_sin', [128, T], F32, stack=hst)
            kb.dma(cosT[:], d['cos_dram'][:], 'CS', queue='sync')
            cse_ = kb.dma(sinT[:], d['sin_dram'][:], 'CS', queue='sync')
            ps1 = kb.ps('l0a_p1', [128, 512], stack=hst)
            ps2 = kb.ps('l0a_p2', [128, 512], stack=hst)
            pst = kb.ps('l0a_pt', [128, 128], BF16, stack=hst)
            pstB = kb.ps('l0a_ptB', [128, 128], BF16, stack=hst)

            qkevac = {0: None, 1: None}
            for fc in range(2):
                for tc in range(4):
                    par = tc % 2
                    pv = ps1 if par == 0 else ps2
                    sl = slice(tc * 512, (tc + 1) * 512)
                    for dc in range(NC_):
                        mi = kb.T('matmul', pv[:, :], d['wqkT'][:, dc, fc, :],
                                  xg[:, dc, sl], start=(dc == 0),
                                  stop=(dc == NC_ - 1),
                                  waits=(list(wev) + [lev, kb.evt('Sc')])
                                  if (fc == 0 and tc == 0 and dc == 0)
                                  else (([qkevac[par]] if qkevac[par]
                                         else [kb.evt('Sc')])
                                        if dc == 0 else ()))
                    pe = kb.mark(mi, 'PE')
                    qkevac[par] = kb.S('activation', out=qkf[:, fc, sl],
                                       in_=pv[:, :], func=AF.Silu,
                                       bias=d['bqk'][:, fc:fc + 1], scale=1.0,
                                       waits=[pe])
            kb.V('tensor_copy', out=g8[:, 0, :], in_=d['gam'][:],
                 waits=list(wev))
            kb.V('tensor_copy', out=g8[:, 1, :], in_=d['bet'][:])
            for i in (0, 2):
                for z in range(2):
                    kb.V('tensor_scalar', out=g8[:, z, 2 * i:2 * i + 2],
                         in0=g8[:, z, 2 * i:2 * i + 2], scalar1=0.125,
                         scalar2=None, op0=ALU.mult)
            g8e = kb.evt('Vc')
            for i, dst in enumerate((qq, lq, qkk, lkfm)):
                for fc in range(2):
                    kb.V('tensor_scalar', out=hbuf[:, fc, :],
                         in0=qkf[:, fc, :],
                         scalar1=g8[:, 0, 2 * i + fc:2 * i + fc + 1],
                         scalar2=g8[:, 1, 2 * i + fc:2 * i + fc + 1],
                         op0=ALU.mult, op1=ALU.add,
                         waits=[g8e, cse_, kb.evt('Sc')] if fc == 0 else ())
                kb.V('tensor_tensor', out=hbuf2[:, 0, :], in0=hbuf[:, 0, :],
                     in1=cosT[:], op=ALU.mult)
                kb.V('tensor_tensor', out=hbuf2[:, 1, :], in0=hbuf[:, 1, :],
                     in1=sinT[:], op=ALU.mult)
                kb.V('tensor_tensor', out=dst[:, 0, :], in0=hbuf2[:, 0, :],
                     in1=hbuf2[:, 1, :], op=ALU.subtract)
                kb.V('tensor_tensor', out=hbuf2[:, 0, :], in0=hbuf[:, 1, :],
                     in1=cosT[:], op=ALU.mult)
                kb.V('tensor_tensor', out=hbuf2[:, 1, :], in0=hbuf[:, 0, :],
                     in1=sinT[:], op=ALU.mult)
                hde = kb.V('tensor_tensor', out=dst[:, 1, :],
                           in0=hbuf2[:, 0, :], in1=hbuf2[:, 1, :], op=ALU.add)
            trce = {0: None, 1: None}
            for tt in range(16):
                for c2 in range(2):
                    idx = 2 * tt + c2
                    par = idx % 2
                    pp = pst if par == 0 else pstB
                    tr = kb.T('transpose', pp[:, :],
                              lkfm[:, c2, tt * 128:(tt + 1) * 128],
                              d['idbf'][:],
                              waits=([hde, kb.evt('Vc')] if idx == 0 else
                                     ([trce[par]] if trce[par] else [])))
                    pe = kb.mark(tr, 'PE')
                    trce[par] = kb.V(
                        'tensor_copy',
                        out=lktok[:, tt, c2 * 128:(c2 + 1) * 128],
                        in_=pp[:, :], waits=[pe])
            qrd = {0: None, 1: None}
            for g in range(32):
                par = g % 2
                pq_ = ps1 if par == 0 else ps2
                for dc2 in range(2):
                    mi = kb.T('matmul', pq_[0:64, 0:64],
                              qkk[:, dc2, g * 64:(g + 1) * 64],
                              qq[:, dc2, g * 64:(g + 1) * 64],
                              start=(dc2 == 0), stop=(dc2 == 1),
                              waits=(([qrd[par]] if qrd[par]
                                      else [kb.evt('Vc')])
                                     if dc2 == 0 else ()))
                pe = kb.mark(mi, 'PE')
                qrd[par] = kb.V('tensor_tensor', out=sb64[0:64, :],
                                in0=pq_[0:64, 0:64],
                                in1=d['bias64'][:], op=ALU.add, waits=[pe])
                kb.V('tensor_scalar', out=sb64[0:64, :], in0=sb64[0:64, :],
                     scalar1=0.0, scalar2=None, op0=ALU.max)
                ce = kb.V('tensor_tensor', out=attn[:, g, :],
                          in0=sb64[0:64, :], in1=sb64[0:64, :], op=ALU.mult)
            kb.t_wait([kb.evt('Vc'), kb.evt('Sc')])
        if trunc == 'heads':
            return kb.evt('Sc')
        with contextlib.ExitStack() as vst:
            vsb = kb.sb('l0a_vsb', [128, 4, 512], BF16, stack=vst)
            ps1 = kb.ps('l0a_v1', [128, 512], stack=vst)
            ps2 = kb.ps('l0a_v2', [128, 512], stack=vst)
            vevac = {0: None, 1: None}
            for tt in range(16):
                for eg in range(4):
                    par = eg % 2
                    pv = ps1 if par == 0 else ps2
                    for dc in range(NC_):
                        mi = kb.T('matmul', pv[:, :],
                                  xg[:, dc, tt * 128:(tt + 1) * 128],
                                  d['wvo'][:, dc * 2048 + 512 * eg:
                                           dc * 2048 + 512 * eg + 512],
                                  start=(dc == 0), stop=False,
                                  waits=(([vevac[par]] if vevac[par]
                                          else [kb.evt('Sc'), kb.evt('Vc')]) +
                                         ([kb.evt('WV')] if (tt == 0 and
                                                             eg == 0) else []))
                                  if dc == 0 else ())
                    mi = kb.T('matmul', pv[:, :], ones_row[:],
                              d['bv'][:, 512 * eg:512 * eg + 512],
                              start=False, stop=True)
                    pe = kb.mark(mi, 'PE')
                    vevac[par] = kb.S('activation', out=vsb[:, eg, :],
                                      in_=pv[:, :],
                                      func=AF.Silu, bias=0.0, scale=1.0,
                                      waits=[pe] + ([kb.evt('VW')]
                                                    if eg == 0 else []))
                kb.dma(d['v_dram'][tt * 128:(tt + 1) * 128, :],
                       vsb[:].rearrange('p a b -> p (a b)'), 'VW',
                       queue='scalar', waits=[kb.evt('Sc')])
            kb.t_wait([kb.evt('Sc'), kb.evt('Vc')])
        with contextlib.ExitStack() as ast:
            psl_ = [kb.ps('l0a_pl%d' % i, [128, 512], stack=ast)
                    for i in range(6)]
            vt = kb.sb('l0a_vt', [128, 2, 2048], BF16, stack=ast)
            vtpe = {0: None, 1: None}
            for tt in range(16):
                par = tt % 2
                le = kb.dma(vt[:, par, :],
                            d['v_dram'][tt * 128:(tt + 1) * 128, :],
                            'VR%d' % par, queue='sync',
                            waits=([kb.evt('VW')] if tt == 0 else
                                   ([vtpe[par]] if vtpe[par]
                                    else [kb.evt('PE')])))
                for dc2 in range(2):
                    for eg in range(3):
                        mi = kb.T('matmul', psl_[dc2 * 3 + eg][:, :],
                                  lktok[:, tt, dc2 * 128:(dc2 + 1) * 128],
                                  vt[:, par, eg * 512:(eg + 1) * 512],
                                  start=(tt == 0), stop=(tt == 15),
                                  waits=[le] if (dc2 == 0 and eg == 0) else ())
                vtpe[par] = kb.mark(mi, 'PE')
            for dc2 in range(2):
                for eg in range(3):
                    kb.S('activation',
                         out=linkv[:, dc2, eg * 512:(eg + 1) * 512],
                         in_=psl_[dc2 * 3 + eg][:, :], func=AF.Copy, bias=0.0,
                         scale=1.0 / T, waits=[kb.evt('PE')])
            kb.t_wait([kb.evt('Sc')])
            for tt in range(16):
                par = tt % 2
                le = kb.dma(vt[:, par, 0:512],
                            d['v_dram'][tt * 128:(tt + 1) * 128, 1536:2048],
                            'VR%d' % par, queue='sync',
                            waits=([vtpe[par]] if vtpe[par]
                                   else [kb.evt('PE')]))
                for dc2 in range(2):
                    mi = kb.T('matmul', psl_[dc2][:, :],
                              lktok[:, tt, dc2 * 128:(dc2 + 1) * 128],
                              vt[:, par, 0:512],
                              start=(tt == 0), stop=(tt == 15),
                              waits=[le] if dc2 == 0 else ())
                vtpe[par] = kb.mark(mi, 'PE')
            for dc2 in range(2):
                kb.S('activation', out=linkv[:, dc2, 1536:2048],
                     in_=psl_[dc2][:, :], func=AF.Copy, bias=0.0,
                     scale=1.0 / T, waits=[kb.evt('PE')])
            kb.t_wait([kb.evt('Sc')])

        if trunc == 'phaseA':
            return kb.evt('Sc')
        kb.dma(d['wvo'][:], d['wo_dram'][:], 'WO', queue='gpsimd',
               waits=[kb.evt('PE'), kb.evt('Sc')])
        with contextlib.ExitStack() as bst:
            vsuper = kb.sb('l0a_vs', [64, 8, 2048], BF16, stack=bst)
            zsuper = kb.sb('l0a_zs', [128, 16, 512], BF16, stack=bst)
            gsb = kb.sb('l0a_gsb', [128, 2, 512], BF16, stack=bst)
            wg = [kb.sb('l0a_wg0', [128, NC_, 128], BF16, stack=bst),
                  kb.sb('l0a_wg1', [128, NC_, 128], BF16, stack=bst)]
            outsb = kb.sb('l0a_out', [128, NC_, 512], F32, stack=bst)
            psg = kb.ps('l0a_pg', [128, 512], stack=bst)
            psgB = kb.ps('l0a_pgB', [128, 512], stack=bst)
            psq = kb.ps('l0a_pq', [128, 512], stack=bst)
            psqB = kb.ps('l0a_pqB', [128, 512], stack=bst)
            pso = [kb.ps('l0a_po0', [128, 512], stack=bst),
                   kb.ps('l0a_po1', [128, 512], stack=bst)]
            gpe_h = {0: None, 1: None}
            ge_h = {0: None, 1: None}
            ze_h = {0: None, 1: None}
            oevac = {0: None, 1: None}
            for ts_ in range(4):
                sl = slice(ts_ * 512, (ts_ + 1) * 512)
                vle = kb.dma(vsuper[:],
                             d['v_dram'].rearrange('(a p) e -> p a e', p=64)[
                                 :, 8 * ts_:8 * ts_ + 8, :], 'VS',
                             queue='sync',
                             waits=[kb.evt('Vc'), kb.evt('PE')])
                for ec in range(16):
                    par = ec % 2
                    pg_ = psg if par == 0 else psgB
                    pq_ = psq if par == 0 else psqB
                    wge = kb.dma(wg[par][:],
                                 d['wgT_dram'][:, :, ec, :], 'WG%d' % par,
                                 queue='gpsimd',
                                 waits=([gpe_h[par]] if gpe_h[par]
                                        else [kb.evt('PE')]))
                    for dc in range(NC_):
                        mi = kb.T('matmul', pg_[:, :], wg[par][:, dc, :],
                                  xg[:, dc, sl], start=(dc == 0),
                                  stop=(dc == NC_ - 1),
                                  waits=([wge] +
                                         ([ge_h[par]] if ge_h[par]
                                          else [kb.evt('Sc')]))
                                  if dc == 0 else ())
                    gpe_h[par] = pe = kb.mark(mi, 'PE')
                    ge = kb.S('activation', out=gsb[:, par, :], in_=pg_[:, :],
                              func=AF.Silu, bias=d['bg'][:, ec:ec + 1],
                              scale=1.0,
                              waits=[pe] + ([ze_h[par]] if ze_h[par] else []))
                    ge_h[par] = ge
                    if trunc == 'gate' and ts_ == 0 and ec == 0:
                        return kb.evt('Sc')
                    for gg in range(8):
                        g = 8 * ts_ + gg
                        mi = kb.T('matmul', pq_[:, gg * 64:(gg + 1) * 64],
                                  vsuper[:, gg, ec * 128:(ec + 1) * 128],
                                  attn[:, g, :],
                                  start=(gg == 0), stop=False,
                                  waits=([vle] +
                                         ([ze_h[par]] if ze_h[par]
                                          else [kb.evt('Vc')]))
                                  if gg == 0 else ())
                    for dc2 in range(2):
                        mi = kb.T('matmul', pq_[:, :],
                                  linkv[:, dc2, ec * 128:(ec + 1) * 128],
                                  lq[:, dc2, sl], start=False,
                                  stop=(dc2 == 1))
                    pe = kb.mark(mi, 'PE')
                    ze = kb.V('tensor_tensor', out=zsuper[:, ec, :],
                              in0=gsb[:, par, :], in1=pq_[:, :], op=ALU.mult,
                              waits=[pe, ge])
                    ze_h[par] = ze
                    if trunc == 'ec%d' % ec and ts_ == 0:
                        return kb.evt('Vc')
                if trunc == 'ecs' and ts_ == 0:
                    return kb.evt('Vc')
                for fc in range(NC_):
                    par = fc % 2
                    po = pso[par]
                    for ec in range(16):
                        mi = kb.T('matmul', po[:, :],
                                  d['wvo'][:, ec * 1024 + fc * 128:
                                           ec * 1024 + fc * 128 + 128],
                                  zsuper[:, ec, :],
                                  start=(ec == 0), stop=False,
                                  waits=(([kb.evt('Vc')] if fc == 0 else []) +
                                         ([oevac[par]] if oevac[par]
                                          else [kb.evt('Sc')]) +
                                         ([kb.evt('WO')] if (ts_ == 0 and
                                                             fc == 0) else []))
                                  if ec == 0 else ())
                    mi = kb.T('matmul', po[:, :],
                              d['bo'][:, fc * 128:(fc + 1) * 128],
                              d['consts']['ones_row512'][:],
                              start=False, stop=True)
                    pe = kb.mark(mi, 'PE')
                    oevac[par] = kb.S('activation', out=outsb[:, fc, :],
                                      in_=po[:, :],
                                      func=AF.Copy, bias=0.0, scale=1.0,
                                      waits=[pe])
                if trunc == 'phaseB1' and ts_ == 0:
                    return kb.evt('Sc')
                with contextlib.ExitStack() as lst:
                    lsq = kb.sb('l0a_lsq%d' % ts_, [128, 512], F32, stack=lst)
                    lst_ = kb.sb('l0a_lst%d' % ts_, [1, 6, 512], F32, stack=lst)
                    xsr = kb.sb('l0a_xsr%d' % ts_, [128, 512], F32, stack=lst)
                    ysb = kb.sb('l0a_ysb%d' % ts_, [128, 512], F32, stack=lst)
                    ep1 = kb.sb('l0a_ep1%d' % ts_, [1, 1], F32, stack=lst)
                    pss = kb.ps('l0a_pss%d' % ts_, [128, 512], stack=lst)
                    psq2 = kb.ps('l0a_psq2%d' % ts_, [128, 512], stack=lst)
                    kb.V('memset', ep1[:], 1e-5)
                    pe = kb.evt('PE')
                    for c in range(NC_):
                        se = kb.V('tensor_tensor', out=lsq[:], in0=outsb[:, c, :],
                                  in1=outsb[:, c, :], op=ALU.mult,
                                  waits=[kb.evt('Sc'), pe])
                        kb.T('matmul', pss[0:1, :], d['consts']['ones_col'][:],
                             outsb[:, c, :], start=(c == 0),
                             stop=(c == NC_ - 1),
                             waits=[kb.evt('Sc')] if c == 0 else ())
                        mi = kb.T('matmul', psq2[0:1, :],
                                  d['consts']['ones_col'][:], lsq[:],
                                  start=(c == 0), stop=(c == NC_ - 1),
                                  waits=[se])
                        pe = kb.mark(mi, 'PE')
                    kb.V('tensor_scalar', out=lst_[:, 0, :], in0=pss[0:1, :],
                         scalar1=1.0 / D, scalar2=None, op0=ALU.mult,
                         waits=[pe])
                    kb.V('tensor_scalar', out=lst_[:, 1, :], in0=psq2[0:1, :],
                         scalar1=1.0 / D, scalar2=None, op0=ALU.mult)
                    kb.V('tensor_tensor', out=lst_[:, 2, :],
                         in0=lst_[:, 0, :], in1=lst_[:, 0, :], op=ALU.mult)
                    kb.V('tensor_tensor', out=lst_[:, 1, :],
                         in0=lst_[:, 1, :], in1=lst_[:, 2, :],
                         op=ALU.subtract)
                    se2 = kb.S('activation', out=lst_[:, 1, :],
                               in_=lst_[:, 1, :], func=AF.Sqrt, bias=ep1[:],
                               scale=1.0, waits=[kb.evt('Vc')])
                    rve = kb.V('reciprocal', out=lst_[:, 2, :],
                               in_=lst_[:, 1, :], waits=[se2])
                    kb.T('matmul', pss[:, :], ones_row[:], lst_[:, 0, :],
                         start=True, stop=True, waits=[rve])
                    mi = kb.T('matmul', psq2[:, :], ones_row[:],
                              lst_[:, 2, :], start=True, stop=True)
                    pe = kb.mark(mi, 'PE')
                    for c in range(NC_):
                        xre = kb.dma(xsr[:],
                                     d['x_in'].rearrange(
                                         '(c p) t -> p c t', p=128)[:, c, sl],
                                     'XR', queue='sync',
                                     waits=[kb.evt('Vc')])
                        kb.V('tensor_tensor', out=lsq[:], in0=outsb[:, c, :],
                             in1=pss[:, :], op=ALU.subtract,
                             waits=[pe] if c == 0 else ())
                        kb.V('tensor_tensor', out=lsq[:], in0=lsq[:],
                             in1=psq2[:, :], op=ALU.mult)
                        kb.V('tensor_scalar', out=lsq[:], in0=lsq[:],
                             scalar1=d['ln2w'][:, c:c + 1],
                             scalar2=d['ln2b'][:, c:c + 1],
                             op0=ALU.mult, op1=ALU.add)
                        ye = kb.V('tensor_tensor', out=ysb[:], in0=lsq[:],
                                  in1=xsr[:], op=ALU.add,
                                  waits=[xre, kb.evt('YO')])
                        kb.dma(d['y_out'].rearrange(
                            '(c p) t -> p c t', p=128)[:, c, sl], ysb[:],
                            'YO', queue='scalar', waits=[ye])
                    kb.t_wait([kb.evt('Vc')])
        return kb.evt('YO')


def emit_l2f(kb, d):
    NKEY = 128
    with contextlib.ExitStack() as st:
        xln = d['xln']
        wev = d['wevt']
        lev = d['lnevt']
        ones_row = d['consts']['ones_row']

        with contextlib.ExitStack() as cst:
            cb = kb.sb('l2f_cb', [128, 2, 2048], BF16, stack=cst)
            for i in range(128):
                b = i % 2
                kb.dma(cb[:, b, 0:1024],
                       d['down_dram'][i * 128:(i + 1) * 128, :], 'CV%d' % b,
                       queue='gpsimd',
                       waits=[kb.evt('CW')] if i >= 2 else [])
                kb.dma(cb[:, b, 1024:2048],
                       d['up_dram'][i * 128:(i + 1) * 128, :], 'CV%d' % b,
                       queue='gpsimd')
                kb.dma(d['comb_dram'][i * 128:(i + 1) * 128, :],
                       cb[:, b, :], 'CW', queue='scalar',
                       waits=[kb.evt('CV%d' % b)])
        for c in range(NC_):
            kb.dma(d['xlnfm_dram'][c * 128:(c + 1) * 128, :], xln[:, c, :],
                   'XW', queue='sync', waits=[lev] if c == 0 else ())

        with contextlib.ExitStack() as pst:
            qp = kb.sb('l2f_qp', [32, 8, T], BF16, stack=pst)
            xtok = kb.sb('l2f_xt', [128, 2, 1024], BF16, stack=pst)
            dsb = kb.sb('l2f_dsb', [128, 2, 8, 2048], BF16, stack=pst)
            junk = kb.sb('l2f_junk', [128, 1024], BF16, stack=pst)
            simf = kb.sb('l2f_simf', [128, 128], F32, stack=pst)
            m8p = kb.sb('l2f_m8', [128, 8], F32, stack=pst)
            sxs = kb.sb('l2f_sxs', [128, 2, 8], F32, stack=pst)
            ixs = kb.sb('l2f_ixs', [128, 2, 8], F32, stack=pst)
            c64 = kb.sb('l2f_c64', [128, 64], F32, stack=pst)
            i64 = kb.sb('l2f_i64', [128, 64], F32, stack=pst)
            i64i = kb.sb('l2f_i64i', [128, 64], I32, stack=pst)
            kidx = kb.sb('l2f_kidx', [128, 2, 8], I32, stack=pst)
            ksc = kb.sb('l2f_ksc', [128, 8], F32, stack=pst)
            ewa = kb.sb('l2f_ewa', [128, 8], F32, stack=pst)
            wk8 = kb.sb('l2f_wk8', [128, 2, 8], F32, stack=pst)
            m8b = kb.sb('l2f_m8b', [128, 8], F32, stack=pst)
            dgt = kb.sb('l2f_dgt', [128, 128], BF16, stack=pst)
            nm1 = kb.sb('l2f_nm1', [128, 1], F32, stack=pst)
            ssum = kb.sb('l2f_ss', [128, 1], F32, stack=pst)
            srec = kb.sb('l2f_sr', [128, 1], F32, stack=pst)
            estok = kb.sb('l2f_est', [128, 1024], BF16, stack=pst)
            psq = kb.ps('l2f_pq', [32, 512], stack=pst)
            pss = kb.ps('l2f_ps', [128, 4, 128], stack=pst)
            pse = kb.ps('l2f_pe', [128, 1024], stack=pst)

            for ph in range(8):
                for tc in range(4):
                    sl = slice(tc * 512, (tc + 1) * 512)
                    for dc in range(NC_):
                        mi = kb.T('matmul', psq[:, :],
                                  d['wqp'][:, dc, ph, :], xln[:, dc, sl],
                                  start=(dc == 0), stop=(dc == NC_ - 1),
                                  waits=(list(wev) + [lev])
                                  if (ph == 0 and tc == 0 and dc == 0)
                                  else ([kb.evt('Sc')] if dc == 0 else ()))
                    pe = kb.mark(mi, 'PE')
                    kb.S('activation', out=qp[:, ph, sl], in_=psq[:, :],
                         func=AF.Copy, bias=0.0, scale=1.0,
                         waits=[pe] + ([kb.evt('CW')]
                                       if (ph == 0 and tc == 0) else []))
            qpe = kb.evt('Sc')

            def select_front(u):
                tt, h = divmod(u, 4)
                par = u % 2
                tsl = slice(tt * 128, (tt + 1) * 128)
                if h == 0:
                    kb.dma(xtok[:, tt % 2, :], d['xlnfm_dram'][:, tsl], 'TL',
                           queue='sync',
                           waits=[kb.evt('XW'), kb.evt('CW')] if tt == 0
                           else [kb.evt('Vc')])
                for p in range(2):
                    mi = kb.T('matmul', pss[:, 2 * par + p, :],
                              qp[:, p * 4 + h, tsl],
                              d['keysT'][:, p * 4 + h, :],
                              start=True, stop=True,
                              waits=[qpe, kb.evt('Vc')])
                    pe = kb.mark(mi, 'PE')
                    kb.V('tensor_scalar', out=simf[:],
                         in0=pss[:, 2 * par + p, :],
                         scalar1=4.0, scalar2=None, op0=ALU.add, waits=[pe])
                    kb.V('tensor_scalar', out=simf[:].bitcast(I32),
                         in0=simf[:].bitcast(I32), scalar1=~127,
                         scalar2=None, op0=ALU.bitwise_and)
                    kb.V('tensor_tensor', out=simf[:].bitcast(I32),
                         in0=simf[:].bitcast(I32), in1=d['iotak'][:],
                         op=ALU.bitwise_or)
                    kb.V('max', out=m8p[:], in_=simf[:])
                    kb.V('tensor_scalar', out=kidx[:, par, :],
                         in0=m8p[:].bitcast(I32), scalar1=127,
                         scalar2=None, op0=ALU.bitwise_and)
                    kb.V('tensor_copy', out=ixs[:, p, :],
                         in_=kidx[:, par, :])
                    kb.V('tensor_scalar', out=m8p[:].bitcast(I32),
                         in0=m8p[:].bitcast(I32), scalar1=~127,
                         scalar2=None, op0=ALU.bitwise_and)
                    kb.V('tensor_scalar', out=sxs[:, p, :],
                         in0=m8p[:], scalar1=-4.0, scalar2=None, op0=ALU.add)
                kb.V('tensor_tensor', out=c64[:].rearrange(
                         'p (a b) -> p a b', a=8),
                     in0=sxs[:, 0, :].unsqueeze(2).to_broadcast((128, 8, 8)),
                     in1=sxs[:, 1, :].unsqueeze(1).to_broadcast((128, 8, 8)),
                     op=ALU.add)
                kb.V('scalar_tensor_tensor', out=i64[:].rearrange(
                         'p (a b) -> p a b', a=8),
                     in0=ixs[:, 0, :].unsqueeze(2).to_broadcast((128, 8, 8)),
                     scalar=float(NKEY),
                     in1=ixs[:, 1, :].unsqueeze(1).to_broadcast((128, 8, 8)),
                     op0=ALU.mult, op1=ALU.add)
                kb.V('tensor_copy', out=i64i[:], in_=i64[:])
                kb.V('tensor_scalar', out=c64[:], in0=c64[:], scalar1=8.0,
                     scalar2=None, op0=ALU.add)
                kb.V('tensor_scalar', out=c64[:].bitcast(I32),
                     in0=c64[:].bitcast(I32), scalar1=~16383, scalar2=None,
                     op0=ALU.bitwise_and)
                kb.V('tensor_tensor', out=c64[:].bitcast(I32),
                     in0=c64[:].bitcast(I32), in1=i64i[:],
                     op=ALU.bitwise_or)
                kb.V('max', out=m8p[:], in_=c64[:])
                kb.V('tensor_scalar', out=kidx[:, par, :],
                     in0=m8p[:].bitcast(I32), scalar1=16383, scalar2=None,
                     op0=ALU.bitwise_and)
                kb.V('tensor_scalar', out=m8p[:].bitcast(I32),
                     in0=m8p[:].bitcast(I32), scalar1=~16383, scalar2=None,
                     op0=ALU.bitwise_and)
                kb.V('tensor_scalar', out=ksc[:], in0=m8p[:], scalar1=-8.0,
                     scalar2=None, op0=ALU.add)
                kb.V('tensor_scalar', out=nm1[:], in0=ksc[:, 0:1],
                     scalar1=-1.0, scalar2=None, op0=ALU.mult)
                see = kb.S('activation', out=wk8[:, par, :], in_=ksc[:],
                           func=AF.Exp, bias=nm1[:], scale=1.0,
                           waits=[kb.evt('Vc')])
                kb.V('tensor_reduce', out=ssum[:], in_=wk8[:, par, :],
                     axis=AX.X, op=ALU.add, waits=[see])
                kb.V('reciprocal', out=srec[:], in_=ssum[:])
                kie = kb.V('tensor_scalar', out=wk8[:, par, :],
                           in0=wk8[:, par, :], scalar1=srec[:], scalar2=None,
                           op0=ALU.mult)
                ge = None
                for k in range(8):
                    ge = kb.idma(dsb[:, par, k, :], None, d['comb_dram'],
                                 bass.IndirectOffsetOnAxis(
                                     ap=kidx[:, par, k:k + 1], axis=0),
                                 'GA%d' % par,
                                 waits=[kie, kb.evt('CW'), kb.evt('PE')]
                                 if k == 0 else [])
                return ge

            def back_half(u, ge):
                tt, h = divmod(u, 4)
                par = u % 2
                tsl = slice(tt * 128, (tt + 1) * 128)
                for k in range(8):
                    kb.V('scalar_tensor_tensor', out=junk[:],
                         in0=dsb[:, par, k, 0:1024], scalar=1.0,
                         in1=xtok[:, tt % 2, :], op0=ALU.mult, op1=ALU.mult,
                         accum_out=ewa[:, k:k + 1],
                         waits=[ge, kb.evt('TL')] if k == 0 else ())
                swe = kb.S('activation', out=m8b[:], in_=ewa[:],
                           func=AF.Exp, bias=0.0, scale=-1.0,
                           waits=[kb.evt('Vc')])
                kb.V('tensor_scalar', out=m8b[:], in0=m8b[:], scalar1=1.0,
                     scalar2=None, op0=ALU.add, waits=[swe])
                kb.V('reciprocal', out=m8b[:], in_=m8b[:])
                kb.V('tensor_tensor', out=ewa[:], in0=ewa[:], in1=m8b[:],
                     op=ALU.mult)
                wke = kb.V('tensor_tensor', out=wk8[:, par, :],
                           in0=wk8[:, par, :], in1=ewa[:], op=ALU.mult)
                pv_prev = None
                for k in range(8):
                    de = kb.V('tensor_scalar', out=dgt[:], in0=d['idbf'][:],
                              scalar1=wk8[:, par, k:k + 1], scalar2=None,
                              op0=ALU.mult,
                              waits=([wke, kb.evt('PE')] if k == 0 else []) +
                              ([pv_prev] if pv_prev else []))
                    for nn in range(2):
                        mi = kb.T('matmul', pse[:, nn * 512:(nn + 1) * 512],
                                  dgt[:],
                                  dsb[:, par, k, 1024 + nn * 512:
                                      1024 + (nn + 1) * 512],
                                  start=(h == 0 and k == 0),
                                  stop=(h == 3 and k == 7),
                                  waits=[de] if nn == 0 else ())
                    pv_prev = kb.mark(mi, 'PE')
                if h == 3:
                    ee = kb.V('tensor_copy', out=estok[:], in_=pse[:, :],
                              waits=[pv_prev, kb.evt('EO')])
                    kb.dma(d['es_dram'][tsl, :], estok[:], 'EO',
                           queue='scalar', waits=[ee])

            pending = None
            for u in range(64):
                ge = select_front(u)
                if pending is not None:
                    back_half(*pending)
                pending = (u, ge)
            back_half(*pending)
            kb.t_wait([kb.evt('Vc')])

        with contextlib.ExitStack() as mst:
            wg = kb.sb('l2f_wg', [128, NC_, 16, 128], BF16, stack=mst)
            wu = kb.sb('l2f_wu', [128, NC_, 16, 128], BF16, stack=mst)
            wd = kb.sb('l2f_wd', [128, 16, NC_, 128], BF16, stack=mst)
            msb = kb.sb('l2f_m', [128, 16, 512], BF16, stack=mst)
            gsb = kb.sb('l2f_g', [128, 512], BF16, stack=mst)
            xsr = kb.sb('l2f_xsr', [128, 512], F32, stack=mst)
            esr = kb.sb('l2f_esr', [128, 512], BF16, stack=mst)
            ysb = kb.sb('l2f_ysb', [128, 512], F32, stack=mst)
            ps1 = kb.ps('l2f_m1', [128, 512], stack=mst)
            ps2 = kb.ps('l2f_m2', [128, 512], stack=mst)
            ps3 = kb.ps('l2f_m3', [128, 512], stack=mst)
            we1 = kb.dma(wg[:], d['wgT_dram'][:], 'MW', queue='gpsimd',
                         waits=[kb.evt('Vc'), kb.evt('PE'), kb.evt('EO'),
                                kb.evt('GA0'), kb.evt('GA1')])
            we2 = kb.dma(wu[:], d['wuT_dram'][:], 'MW', queue='gpsimd')
            we3 = kb.dma(wd[:], d['wdT_dram'][:], 'MW', queue='gpsimd')
            for tc in range(4):
                sl = slice(tc * 512, (tc + 1) * 512)
                for ic in range(16):
                    pg = ps1 if ic % 2 == 0 else ps2
                    for dc in range(NC_):
                        kb.T('matmul', pg[:, :], wg[:, dc, ic, :],
                             xln[:, dc, sl], start=(dc == 0),
                             stop=(dc == NC_ - 1),
                             waits=[we3, kb.evt('Sc')] if dc == 0 else ())
                    for dc in range(NC_):
                        mi = kb.T('matmul', ps3[:, :], wu[:, dc, ic, :],
                                  xln[:, dc, sl], start=(dc == 0),
                                  stop=(dc == NC_ - 1),
                                  waits=[kb.evt('Vc')] if dc == 0 else ())
                    pe = kb.mark(mi, 'PE')
                    ge = kb.S('activation', out=gsb[:], in_=pg[:, :],
                              func=AF.Silu, bias=0.0, scale=1.0, waits=[pe])
                    me = kb.V('tensor_tensor', out=msb[:, ic, :], in0=gsb[:],
                              in1=ps3[:, :], op=ALU.mult, waits=[ge])
                    kb.t_wait([me])
                for fc in range(NC_):
                    po = ps1 if fc % 2 == 0 else ps2
                    xre = kb.dma(xsr[:], d['x_in'].rearrange(
                        '(c p) t -> p c t', p=128)[:, fc, sl], 'XR',
                        queue='sync', waits=[kb.evt('Vc')])
                    ere = kb.dma(esr[:], d['es_dram'][sl,
                                                       fc * 128:
                                                       (fc + 1) * 128],
                                 'ER', queue='sync', transpose=True,
                                 waits=[kb.evt('EO')] if (tc == 0 and fc == 0)
                                 else [])
                    for ic in range(16):
                        mi = kb.T('matmul', po[:, :], wd[:, ic, fc, :],
                                  msb[:, ic, :], start=(ic == 0),
                                  stop=(ic == 15),
                                  waits=[kb.evt('Vc'), kb.evt('Sc')]
                                  if ic == 0 else ())
                    pe = kb.mark(mi, 'PE')
                    kb.V('scalar_tensor_tensor', out=ysb[:], in0=po[:, :],
                         scalar=1.0, in1=xsr[:], op0=ALU.mult, op1=ALU.add,
                         waits=[pe, xre, kb.evt('YO')])
                    ye = kb.V('tensor_tensor', out=ysb[:], in0=ysb[:],
                              in1=esr[:], op=ALU.add, waits=[ere])
                    kb.dma(d['y_out'].rearrange('(c p) t -> p c t', p=128)[
                        :, fc, sl], ysb[:], 'YO', queue='scalar',
                        waits=[ye])
        return kb.evt('YO')


def _f(a):
    return np.ascontiguousarray(a, dtype=np.float32)


def prep_weights(inp):
    w = {}
    w['identity'] = np.eye(128, dtype=np.float32)
    w['e8'] = _f(np.repeat(np.eye(8, dtype=np.float32), 4, axis=1))
    w['l0f_wlin'] = _f(inp['L0_ffn_W_lin'].T.reshape(NC_, 128, NC_, 128)
                       .transpose(1, 0, 2, 3))
    w['l0f_blin'] = _f(np.asarray(inp['L0_ffn_b_lin']).reshape(NC_, 128).T)
    w['l0f_gw'] = _f(inp['L0_ffn_gate_w'].T.reshape(NC_, 128, 8)
                     .transpose(1, 0, 2))
    w['l0f_leftc'] = _f(np.transpose(inp['L0_ffn_left'], (1, 0, 2))
                        .reshape(D, 32).reshape(NC_, 128, 32)
                        .transpose(1, 0, 2))
    w['l0f_rightc'] = _f(np.asarray(inp['L0_ffn_right']).reshape(32, D)
                         .reshape(32, NC_, 128))
    w['l3f_rw'] = _f(np.asarray(inp['L3_ffn_router_w']).reshape(NC_, 128, 8)
                     .transpose(1, 0, 2))
    w['l3f_rb'] = _f(np.asarray(inp['L3_ffn_router_b']).reshape(1, 8))
    w['l3f_ew'] = _f(np.asarray(inp['L3_ffn_expert_w'])
                     .reshape(8, NC_, 128, NC_, 128).transpose(0, 2, 1, 3, 4))
    w['l3f_ebrow'] = _f(np.asarray(inp['L3_ffn_expert_b'])
                        .reshape(1, 8, 1024))
    w['l1f_rw'] = _f(np.asarray(inp['L1_ffn_router_w']).reshape(NC_, 128, 8)
                     .transpose(1, 0, 2))
    w['l1f_w1'] = _f(np.asarray(inp['L1_ffn_w1'])
                     .reshape(8, NC_, 128, 32, 128).transpose(0, 2, 1, 3, 4))
    w['l1f_w2'] = _f(np.asarray(inp['L1_ffn_w2'])
                     .reshape(8, 32, 128, NC_, 128).transpose(0, 2, 1, 3, 4))
    w['l1f_b1'] = _f(np.asarray(inp['L1_ffn_b1']).reshape(8, 32, 128)
                     .transpose(0, 2, 1))
    w['l1f_b2'] = _f(np.asarray(inp['L1_ffn_b2']).reshape(8, NC_, 128)
                     .transpose(0, 2, 1))
    w['l3a_wq'] = _f(np.asarray(inp['L3_attn_W_qkv']).T
                     .reshape(NC_, 128, 24, 128).transpose(1, 0, 2, 3))
    w['l3a_woutT'] = _f(np.asarray(inp['L3_attn_W_out']).T
                        .reshape(NC_, 128, NC_, 128).transpose(1, 0, 2, 3))
    w['trimask'] = _f(np.triu(np.ones((128, 128), dtype=np.float32)))
    _wq2 = np.asarray(inp['L2_attn_W_qkv']).reshape(D, 16, 3, 64) \
        .transpose(0, 2, 1, 3).reshape(D, 3072)
    _bq2 = np.asarray(inp['L2_attn_b_qkv']).reshape(16, 3, 64) \
        .transpose(1, 0, 2).reshape(3072)
    w['l2a_wq'] = _f(_wq2.reshape(NC_, 128, 24, 128).transpose(1, 0, 2, 3))
    w['l2a_bqkv'] = _f(_bq2.reshape(24, 128).T)
    w['l2a_bqkv_row'] = _f(_bq2.reshape(1, 3072))
    w['l2a_wo'] = _f(np.asarray(inp['L2_attn_W_o'])
                     .reshape(NC_, 128, NC_, 128).transpose(1, 0, 2, 3))
    w['l2a_bo'] = _f(np.asarray(inp['L2_attn_b_o']).reshape(1, 1024))
    w['i2'] = _f(np.eye(2, dtype=np.float32))
    w['ones2'] = _f(np.ones((2, 128), dtype=np.float32))
    w['i2c'] = _f(np.concatenate([
        np.concatenate([np.ones((64, 1)), np.zeros((64, 1))], 1),
        np.concatenate([np.zeros((64, 1)), np.ones((64, 1))], 1)],
        0).astype(np.float32))
    _wq1 = np.asarray(inp['L1_attn_Wq']).reshape(NC_, 128, 8, 128)
    _wk1 = np.asarray(inp['L1_attn_Wk']).reshape(NC_, 128, 8, 128)
    _wqk = np.empty((128, NC_, 16, 128), np.float32)
    _wqk[:, :, 0::2, :] = _wq1.transpose(1, 0, 2, 3)
    _wqk[:, :, 1::2, :] = _wk1.transpose(1, 0, 2, 3)
    w['l1a_wqk'] = _f(_wqk)
    w['l1a_wv'] = _f(np.asarray(inp['L1_attn_Wv']).reshape(NC_, 128, NC_, 128)
                     .transpose(1, 0, 2, 3))
    w['l1a_wo'] = _f(np.asarray(inp['L1_attn_Wo']).reshape(NC_, 128, NC_, 128)
                     .transpose(1, 0, 2, 3))
    w['l1a_bo'] = _f(np.asarray(inp['L1_attn_bo']).reshape(1, 1024))
    w['l1a_thcol'] = _f(np.asarray(inp['L1_attn_theta']).reshape(8, 1))
    w['l1a_throw'] = _f(np.asarray(inp['L1_attn_theta']).reshape(1, 8))
    w['l1a_gam'] = _f(np.asarray(inp['L1_attn_gamma']).reshape(1, 1024))
    w['l1a_bet'] = _f(np.asarray(inp['L1_attn_beta']).reshape(1, 1024))
    w['l0a_ln1w'] = _f(np.asarray(inp['L0_attn_ln1_w']).reshape(NC_, 128).T)
    w['l0a_ln1b'] = _f(np.asarray(inp['L0_attn_ln1_b']).reshape(NC_, 128).T)
    w['l0a_wqkT'] = _f(np.asarray(inp['L0_attn_Wqk']).T
                       .reshape(NC_, 128, 2, 128).transpose(1, 0, 2, 3))
    w['l0a_bqk'] = _f(np.asarray(inp['L0_attn_bqk']).reshape(2, 128).T)
    w['l0a_gam'] = _f(np.asarray(inp['L0_attn_gamma']).reshape(4, 2, 128)
                      .transpose(2, 0, 1).reshape(128, 8))
    w['l0a_bet'] = _f(np.asarray(inp['L0_attn_beta']).reshape(4, 2, 128)
                      .transpose(2, 0, 1).reshape(128, 8))
    _idx = np.arange(64)
    _dist = np.clip(np.abs(_idx[:, None] - _idx[None, :]), 0, 32)
    w['l0a_bias64'] = _f(np.asarray(inp['L0_attn_pos_emb'])[_dist])
    _pos = np.arange(T, dtype=np.float64)
    _invf = 10000.0 ** (-np.arange(128, dtype=np.float64) / 128.0)
    _ang = _invf[:, None] * _pos[None, :]
    w['l0a_cosT'] = _f(np.cos(_ang))
    w['l0a_sinT'] = _f(np.sin(_ang))
    w['l0a_wvT'] = _f(np.asarray(inp['L0_attn_Wv']).T
                      .reshape(NC_, 128, 2048).transpose(1, 0, 2)
                      .reshape(128, NC_ * 2048))
    w['l0a_bv'] = _f(np.asarray(inp['L0_attn_bv']).reshape(1, 2048))
    w['l0a_wgT'] = _f(np.asarray(inp['L0_attn_Wg']).T
                      .reshape(NC_, 128, 16, 128).transpose(1, 0, 2, 3))
    w['l0a_bg'] = _f(np.asarray(inp['L0_attn_bg']).reshape(16, 128).T)
    w['l0a_woT'] = _f(np.asarray(inp['L0_attn_Wo']).T
                      .reshape(16, 128, NC_, 128).transpose(1, 0, 2, 3)
                      .reshape(128, 16 * NC_ * 128))
    w['l0a_bo'] = _f(np.asarray(inp['L0_attn_bo']).reshape(1, 1024))
    w['l0a_ln2w'] = _f(np.asarray(inp['L0_attn_ln2_w']).reshape(NC_, 128).T)
    w['l0a_ln2b'] = _f(np.asarray(inp['L0_attn_ln2_b']).reshape(NC_, 128).T)
    w['l2f_down'] = _f(inp['L2_ffn_down_embed'])
    w['l2f_up'] = _f(inp['L2_ffn_up_embed'])
    w['l2f_wqp'] = _f(np.asarray(inp['L2_ffn_w_queries']).T
                      .reshape(NC_, 128, 2, 4, 32)
                      .transpose(1, 0, 2, 3, 4).reshape(128, NC_, 8, 32))
    w['l2f_keysT'] = _f(np.asarray(inp['L2_ffn_keys'])
                        .transpose(3, 2, 0, 1).reshape(32, 8, 128))
    w['l2f_wgT'] = _f(np.asarray(inp['L2_ffn_w_gate']).T
                      .reshape(NC_, 128, 16, 128).transpose(1, 0, 2, 3))
    w['l2f_wuT'] = _f(np.asarray(inp['L2_ffn_w_up']).T
                      .reshape(NC_, 128, 16, 128).transpose(1, 0, 2, 3))
    w['l2f_wdT'] = _f(np.asarray(inp['L2_ffn_w_down']).T
                      .reshape(16, 128, NC_, 128).transpose(1, 0, 2, 3))
    w['iotak'] = np.ascontiguousarray(
        np.tile(np.arange(128, dtype=np.int32), (128, 1)))
    w['iota8_rep'] = _f(np.tile(np.arange(8, dtype=np.float32), (128, 1)))
    w['triu128'] = _f(np.triu(np.ones((128, 128), dtype=np.float32)))
    return w


L3F_WEIGHTS = [
    ('l3f_rw', [128, NC_, 8], BF16),
    ('l3f_rb', [1, 8], F32),
    ('l3f_ebrow', [1, NC_, 1024], F32),
    ('l3f_ew', [8, 128, NC_, NC_, 128], F32, 'dram'),
]

L1F_WEIGHTS = [
    ('l1f_rw', [128, NC_, 8], BF16),
    ('l1f_b1c', [128, 32], F32),
    ('l1f_b2c', [128, NC_], F32),
    ('l1f_w1c', [128, NC_, 32, 128], F32, 'dram'),
    ('l1f_w2c', [128, 32, NC_, 128], F32, 'dram'),
    ('l1f_rowidx', [128, 16], I32),
    ('iota8_rep', [128, 8], F32),
    ('triu128', [128, 128], BF16),
]

L3A_WEIGHTS = [
    ('l3a_wq', [128, NC_, 24, 128], BF16),
    ('l3a_woutT', [128, NC_, NC_, 128], BF16),
    ('trimask', [128, 128], BF16),
]

L2A_WEIGHTS = [
    ('l2a_wq', [128, NC_, 24, 128], BF16),
    ('l2a_bqkv', [128, 24], F32),
    ('l2a_bqkv_row', [1, 3072], F32),
    ('l2a_wo', [128, NC_, NC_, 128], BF16),
    ('l2a_bo', [1, 1024], F32),
    ('i2', [2, 2], F32),
    ('ones2', [2, 128], F32),
    ('i2c', [128, 2], BF16),
]

L1A_WEIGHTS = [
    ('l1a_wqk', [128, NC_, 16, 128], BF16),
    ('l1a_wv', [128, NC_, NC_, 128], BF16),
    ('l1a_wo', [128, NC_, NC_, 128], BF16),
    ('l1a_bo', [1, 1024], F32),
    ('l1a_thcol', [8, 1], F32),
    ('l1a_throw', [1, 8], F32),
    ('l1a_gam', [1, 1024], F32),
    ('l1a_bet', [1, 1024], F32),
    ('trimask', [128, 128], BF16),
    ('identity', [128, 128], BF16),
]

L0A_WEIGHTS = [
    ('l0a_wqkT', [128, NC_, 2, 128], BF16),
    ('l0a_bqk', [128, 2], F32),
    ('l0a_gam', [128, 8], F32),
    ('l0a_bet', [128, 8], F32),
    ('l0a_bias64', [64, 64], F32),
    ('l0a_bv', [1, 2048], F32),
    ('l0a_bg', [128, 16], F32),
    ('l0a_bo', [1, 1024], F32),
    ('l0a_ln2w', [128, NC_], F32),
    ('l0a_ln2b', [128, NC_], F32),
    ('identity', [128, 128], BF16),
    ('l0a_wgT', [128, NC_, 16, 128], F32, 'dram'),
    ('l0a_wvT', [128, NC_ * 2048], F32, 'dram'),
    ('l0a_woT', [128, 16 * NC_ * 128], F32, 'dram'),
]

L2F_WEIGHTS = [
    ('l2f_wqp', [128, NC_, 8, 32], BF16),
    ('l2f_keysT', [32, 8, 128], BF16),
    ('iotak', [128, 128], I32),
    ('identity', [128, 128], BF16),
    ('l2f_wgT', [128, NC_, 16, 128], F32, 'dram'),
    ('l2f_wuT', [128, NC_, 16, 128], F32, 'dram'),
    ('l2f_wdT', [128, 16, NC_, 128], F32, 'dram'),
    ('l2f_down', [16384, 1024], F32, 'dram'),
    ('l2f_up', [16384, 1024], F32, 'dram'),
]

L0F_WEIGHTS = [
    ('l0f_wlin', [128, NC_, NC_, 128], BF16),
    ('l0f_blin', [128, NC_], F32),
    ('l0f_gw', [128, NC_, 8], BF16),
    ('l0f_leftc', [128, NC_, 32], BF16),
    ('l0f_rightc', [32, NC_, 128], BF16),
    ('identity', [128, 128], BF16),
    ('e8', [8, 32], BF16),
]


def declare_and_load(kb, names_shapes, queue='gpsimd', sem='WL', waits=(), sfx=''):
    nc = kb.nc
    out = {}
    first = True
    for ent in names_shapes:
        if len(ent) == 4 and ent[3] == 'dram':
            name, shape, dt, _ = ent
            out[name] = nc.dram_tensor(name + sfx, list(shape), dt,
                                       kind='ExternalInput').ap()
            continue
        name, shape, dt = ent
        ddt = I32 if dt == I32 else F32
        dram = nc.dram_tensor(name + sfx, list(shape), ddt,
                              kind='ExternalInput').ap()
        sbuf = kb.sb(name + '_sb' + sfx, shape, dt)
        kb.dma(sbuf[:], dram[:], sem, queue=queue,
               waits=waits if first else ())
        first = False
        out[name] = sbuf
    return out


def emit_stage(kb, nc, block, x_in, y_out, fake_cc, sfx=''):
    if True:
        if block in ('l3a', 'l2a', 'l1a', 'l0a', 'l2f', 'l1f'):
            xln = kb.sb('xln' + sfx, [128, NC_, T], BF16)
            consts = {}
            consts['ones_col'] = kb.sb('ones_col' + sfx, [128, 1], F32)
            consts['ones_row'] = kb.sb('ones_row' + sfx, [1, 128], F32)
            consts['ones_col_bf'] = kb.sb('ones_col_bf' + sfx, [128, 1], BF16)
            kb.V('memset', consts['ones_col'][:], 1.0)
            kb.V('memset', consts['ones_col_bf'][:], 1.0)
            ocev = kb.V('memset', consts['ones_row'][:], 1.0)
            affine = None
            vs = vb = None
            if block == 'l0a':
                ln1w_d = nc.dram_tensor('l0a_ln1w' + sfx, [128, NC_], F32,
                                        kind='ExternalInput').ap()
                ln1b_d = nc.dram_tensor('l0a_ln1b' + sfx, [128, NC_], F32,
                                        kind='ExternalInput').ap()
                ln1w = kb.sb('ln1w_sb' + sfx, [128, NC_], F32)
                ln1b = kb.sb('ln1b_sb' + sfx, [128, NC_], F32)
                kb.dma(ln1w[:], ln1w_d[:], 'LW', queue='sync')
                lwe = kb.dma(ln1b[:], ln1b_d[:], 'LW', queue='sync')
                affine = (ln1w, ln1b)
                vs, vb = 1.0 + 1e-5, 1e-10
            with contextlib.ExitStack() as xst:
                x = kb.sb('x' + sfx, [128, NC_, T], F32, stack=xst)
                xe = kb.dma(x[:], x_in.rearrange('(c p) t -> p c t', p=128),
                            'XL')
                lev = emit_ln_fm(kb, x, xln, consts, block,
                                 x_evt=[xe, ocev] + ([lwe] if affine else []),
                                 var_scale=vs, var_bias=vb, affine=affine)
            sbufs = declare_and_load(kb, globals()[block.upper() + '_WEIGHTS'],
                                     waits=[lev], sfx=sfx)
            if block == 'l3a':
                dd = dict(xln=xln, wevt=[kb.evt('WL')], lnevt=lev,
                          consts=consts, wq=sbufs['l3a_wq'],
                          woutT=sbufs['l3a_woutT'],
                          trimask=sbufs['trimask'], x_in=x_in, y_out=y_out)
                emit_l3a(kb, dd)
            elif block == 'l0a':
                consts['ones_row512'] = kb.sb('ones_row512' + sfx, [1, 512], F32)
                kb.V('memset', consts['ones_row512'][:], 1.0)
                cosd = nc.dram_tensor('l0a_cosT' + sfx, [128, T], F32,
                                      kind='ExternalInput').ap()
                sind = nc.dram_tensor('l0a_sinT' + sfx, [128, T], F32,
                                      kind='ExternalInput').ap()
                wvo = kb.sb('wvo_sb' + sfx, [128, NC_ * 2048], BF16)
                kb.dma(wvo[:], sbufs['l0a_wvT'][:], 'WV', queue='gpsimd')
                dd = dict(xg=xln, wevt=[kb.evt('WL')], lnevt=lev,
                          consts=consts, wqkT=sbufs['l0a_wqkT'],
                          bqk=sbufs['l0a_bqk'], gam=sbufs['l0a_gam'],
                          bet=sbufs['l0a_bet'], bias64=sbufs['l0a_bias64'],
                          cos_dram=cosd, sin_dram=sind,
                          idbf=sbufs['identity'],
                          wvo=wvo, bv=sbufs['l0a_bv'],
                          wgT_dram=sbufs['l0a_wgT'], bg=sbufs['l0a_bg'],
                          bo=sbufs['l0a_bo'], ln2w=sbufs['l0a_ln2w'],
                          ln2b=sbufs['l0a_ln2b'], x_in=x_in, y_out=y_out)
                dd['v_dram'] = nc.dram_tensor('v_dram' + sfx, [T, 2048], BF16,
                                              kind='Internal').ap()
                dd['wo_dram'] = sbufs['l0a_woT']
                dd['wvo_handle'] = wvo
                emit_l0a(kb, dd, trunc=globals().get('_L0A_TRUNC'))
            elif block == 'l1f':
                cmask = nc.dram_tensor('coremask' + sfx, [8, 1], F32,
                                       kind='ExternalInput').ap()
                cm_sb = kb.sb('coremask_sb' + sfx, [8, 1], F32)
                kb.dma(cm_sb[:], cmask[:], 'WL', queue='gpsimd')
                consts['iota8_rep'] = sbufs['iota8_rep']
                consts['triu128'] = sbufs['triu128']
                dd = dict(xln=xln, wevt=[kb.evt('WL')], lnevt=lev,
                          consts=consts, rw=sbufs['l1f_rw'],
                          b1c=sbufs['l1f_b1c'], b2c=sbufs['l1f_b2c'],
                          w1c_dram=sbufs['l1f_w1c'],
                          w2c_dram=sbufs['l1f_w2c'],
                          rowidx=sbufs['l1f_rowidx'],
                          coremask=cm_sb, fake_cc=fake_cc,
                          x_in=x_in, y_out=y_out)
                for nm, shp, dt in (('xc_dram', [8 * T, D], BF16),
                                    ('xf_dram', [8 * T, D], BF16),
                                    ('yc_dram', [8 * T, D], BF16),
                                    ('yf_dram', [8 * T, D], BF16),
                                    ('xseg_dram', [T, D], BF16),
                                    ('yfm_dram', [D, T], BF16),
                                    ('yfin_tok', [T, D], BF16),
                                    ('xlnfm_dram', [D, T], BF16)):
                    dd[nm] = nc.dram_tensor(nm + sfx, shp, dt,
                                            kind='Internal').ap()
                for nm, shp in (('cnt_dram', [1, 8]), ('ag_dram', [8, 8])):
                    dd[nm] = nc.dram_tensor(nm + sfx, shp, F32).ap()
                emit_l1f(kb, dd)
            elif block == 'l2f':
                dd = dict(xln=xln, wevt=[kb.evt('WL')], lnevt=lev,
                          consts=consts, wqp=sbufs['l2f_wqp'],
                          keysT=sbufs['l2f_keysT'], iotak=sbufs['iotak'],
                          idbf=sbufs['identity'],
                          wgT_dram=sbufs['l2f_wgT'],
                          wuT_dram=sbufs['l2f_wuT'],
                          wdT_dram=sbufs['l2f_wdT'],
                          down_dram=sbufs['l2f_down'],
                          up_dram=sbufs['l2f_up'], x_in=x_in, y_out=y_out)
                for nm, shp, dt in (('comb_dram', [16384, 2048], BF16),
                                    ('es_dram', [T, D], BF16),
                                    ('xlnfm_dram', [D, T], BF16)):
                    dd[nm] = nc.dram_tensor(nm + sfx, shp, dt, kind='Internal').ap()
                emit_l2f(kb, dd)
            elif block == 'l1a':
                consts['ones_row512'] = kb.sb('ones_row512' + sfx, [1, 512], F32)
                kb.V('memset', consts['ones_row512'][:], 1.0)
                dd = dict(xln=xln, wevt=[kb.evt('WL')], lnevt=lev,
                          consts=consts, wqk=sbufs['l1a_wqk'],
                          wv=sbufs['l1a_wv'], wo=sbufs['l1a_wo'],
                          bo=sbufs['l1a_bo'], thcol=sbufs['l1a_thcol'],
                          throw=sbufs['l1a_throw'], gam=sbufs['l1a_gam'],
                          bet=sbufs['l1a_bet'], trimask=sbufs['trimask'],
                          idbf=sbufs['identity'], x_in=x_in, y_out=y_out)
                dd['z_dram'] = nc.dram_tensor('z_dram' + sfx, [D, T], BF16,
                                              kind='Internal').ap()
                emit_l1a(kb, dd)
            elif block == 'l2a':
                consts['ones_row512'] = kb.sb('ones_row512' + sfx, [1, 512], F32)
                kb.V('memset', consts['ones_row512'][:], 1.0)
                dd = dict(xln=xln, wevt=[kb.evt('WL')], lnevt=lev,
                          consts=consts, wq=sbufs['l2a_wq'],
                          bqkv=sbufs['l2a_bqkv'],
                          bqkv_row=sbufs['l2a_bqkv_row'],
                          wo=sbufs['l2a_wo'], bo=sbufs['l2a_bo'],
                          i2=sbufs['i2'], ones2=sbufs['ones2'],
                          i2c=sbufs['i2c'], x_in=x_in, y_out=y_out)
                emit_l2a(kb, dd)
            return
        x = kb.sb('x' + sfx, [128, NC_, T], F32)
        xln = kb.sb('xln' + sfx, [128, NC_, T], BF16)
        xe = kb.dma(x[:], x_in.rearrange('(c p) t -> p c t', p=128), 'XL')

        consts = {}
        consts['ones_col'] = kb.sb('ones_col' + sfx, [128, 1], F32)
        consts['ones_row'] = kb.sb('ones_row' + sfx, [1, 128], F32)
        consts['ones_col_bf'] = kb.sb('ones_col_bf' + sfx, [128, 1], BF16)
        kb.V('memset', consts['ones_col'][:], 1.0)
        kb.V('memset', consts['ones_col_bf'][:], 1.0)
        ocev = kb.V('memset', consts['ones_row'][:], 1.0)

        if block == 'l0f':
            sbufs = declare_and_load(kb, L0F_WEIGHTS, sfx=sfx)
            lev = emit_ln_fm(kb, x, xln, consts, 'l0f', x_evt=[xe, ocev])
            dd = dict(x=x, xln=xln, wevt=[kb.evt('WL')], lnevt=lev,
                      consts=consts, idbf=sbufs['identity'], e8=sbufs['e8'],
                      wlin=sbufs['l0f_wlin'], blin=sbufs['l0f_blin'],
                      gw=sbufs['l0f_gw'], leftc=sbufs['l0f_leftc'],
                      rightc=sbufs['l0f_rightc'])
            oev = emit_l0f(kb, dd)
        elif block == 'l3f':
            sbufs = declare_and_load(kb, L3F_WEIGHTS, sfx=sfx)
            lev = emit_ln_fm(kb, x, xln, consts, 'l3f', x_evt=[xe, ocev])
            dd = dict(x=x, xln=xln, wevt=[kb.evt('WL')], lnevt=lev,
                      consts=consts, rw=sbufs['l3f_rw'], rb=sbufs['l3f_rb'],
                      ebrow=sbufs['l3f_ebrow'], ew_dram=sbufs['l3f_ew'])
            dd['yfin_tok'] = nc.dram_tensor('yfin_tok' + sfx, [T, D], BF16,
                                            kind='Internal').ap()
            oev = emit_l3f(kb, dd)
        else:
            raise ValueError(block)

        kb.dma(y_out.rearrange('(c p) t -> p c t', p=128), x[:], 'ST',
               waits=[oev])
    return




def _stage_barrier(kb):
    for eng in (kb.nc.tensor, kb.nc.vector, kb.nc.scalar, kb.nc.gpsimd,
                kb.nc.sync):
        for name, c in list(kb.cnt.items()):
            if c > 0:
                eng.wait_ge(kb.sem(name), c)


def build_block_program(block, fake_cc=False):
    nc = bass.Bass('TRN2', target_bir_lowering=False)
    with contextlib.ExitStack() as stack:
        kb = KB(nc, stack)
        x_in = nc.dram_tensor('x_fm', [D, T], F32, kind='ExternalInput').ap()
        y_out = nc.dram_tensor('y_fm', [D, T], F32,
                               kind='ExternalOutput').ap()
        emit_stage(kb, nc, block, x_in, y_out, fake_cc)
        for name in list(kb.cnt):
            if kb.cnt[name] > 0:
                kb.nc.sync.wait_ge(kb.sem(name), kb.cnt[name])
    return nc


def build_fused_program(fake_cc=False):
    nc = bass.Bass('TRN2', target_bir_lowering=False)
    with contextlib.ExitStack() as stack:
        kb = KB(nc, stack)
        x_ext = nc.dram_tensor('x_fm', [D, T], F32, kind='ExternalInput').ap()
        y_ext = nc.dram_tensor('y_fm', [D, T], F32,
                               kind='ExternalOutput').ap()
        chain = [x_ext]
        for k in range(7):
            chain.append(nc.dram_tensor('xchain%d' % k, [D, T], F32,
                                        kind='Internal').ap())
        chain.append(y_ext)
        outer = kb.stack
        for k, block in enumerate(BLOCK_ORDER):
            with contextlib.ExitStack() as sst:
                kb.stack = sst
                emit_stage(kb, nc, block, chain[k], chain[k + 1], fake_cc,
                           sfx='_s%d' % k)
                _stage_barrier(kb)
            kb.stack = outer
        for name in list(kb.cnt):
            if kb.cnt[name] > 0:
                kb.nc.sync.wait_ge(kb.sem(name), kb.cnt[name])
    return nc

BLOCK_ORDER = ['l0a', 'l0f', 'l1a', 'l1f', 'l2a', 'l2f', 'l3a', 'l3f']

BLOCK_EXTRAS = {
    'l0a': ['l0a_ln1w', 'l0a_ln1b', 'l0a_cosT', 'l0a_sinT'],
}

_PROG_CACHE = {}


def _get_prog(block):
    if block not in _PROG_CACHE:
        _PROG_CACHE[block] = build_block_program(block)
    return _PROG_CACHE[block]


def _block_input_names(block):
    names = [e[0] for e in globals()[block.upper() + '_WEIGHTS']
             if e[0] not in ('l1f_w1c', 'l1f_w2c', 'l1f_b1c', 'l1f_b2c',
                             'l1f_rowidx')]
    names += BLOCK_EXTRAS.get(block, [])
    return names


def _percore_extras(m, block, c, w, sfx=''):
    if block == 'l1f':
        cm = np.zeros((8, 1), np.float32)
        cm[:c] = 1.0
        m['coremask' + sfx] = cm
        m['l1f_w1c' + sfx] = w['l1f_w1'][c]
        m['l1f_w2c' + sfx] = w['l1f_w2'][c]
        m['l1f_b1c' + sfx] = w['l1f_b1'][c]
        m['l1f_b2c' + sfx] = w['l1f_b2'][c]
        m['l1f_rowidx' + sfx] = np.ascontiguousarray(
            np.arange(c * T, (c + 1) * T, dtype=np.int32)
            .reshape(16, 128).T)


def kernel(**inputs):
    w = prep_weights(inputs)
    if 'prog' not in _PROG_CACHE:
        _PROG_CACHE['prog'] = build_fused_program()
    nc = _PROG_CACHE['prog']
    x = np.asarray(inputs['L0_attn_x'], dtype=np.float32)
    maps = []
    for c in range(8):
        m = {'x_fm': np.ascontiguousarray(x[c].T)}
        for k, block in enumerate(BLOCK_ORDER):
            sfx = '_s%d' % k
            for n in _block_input_names(block):
                m[n + sfx] = w[n]
            _percore_extras(m, block, c, w, sfx)
        maps.append(m)
    res = run_bass_kernel_spmd(nc, maps, core_ids=list(range(8)))
    out = np.stack([np.ascontiguousarray(np.asarray(
        res.results[c]['y_fm']).T) for c in range(8)])
    return out.astype(np.float32)


def kernel_unfused(**inputs):
    w = prep_weights(inputs)
    x = np.asarray(inputs['L0_attn_x'], dtype=np.float32)
    xfm = [np.ascontiguousarray(x[c].T) for c in range(8)]
    for block in BLOCK_ORDER:
        nc = _get_prog(block)
        maps = []
        for c in range(8):
            m = {'x_fm': xfm[c]}
            for n in _block_input_names(block):
                m[n] = w[n]
            _percore_extras(m, block, c, w)
            maps.append(m)
        res = run_bass_kernel_spmd(nc, maps, core_ids=list(range(8)))
        xfm = [np.asarray(res.results[c]['y_fm']) for c in range(8)]
    out = np.stack([np.ascontiguousarray(xfm[c].T) for c in range(8)])
    return out.astype(np.float32)
